# version 06
# baseline (speedup 1.0000x reference)
import numpy as np
import ml_dtypes
import concourse.bass as bass
import concourse.mybir as mybir
from concourse.bass_utils import run_bass_kernel_spmd

F32 = mybir.dt.float32
BF16 = mybir.dt.bfloat16
AF = mybir.ActivationFunctionType
ALU = mybir.AluOpType
AX = mybir.AxisListType

D = 1024
T = 4096
TOWN = 2048
NSLAB = 16
NT = 32
EPS = 1e-5


def _tmap():
    q = np.arange(2048)
    P, u, j = q // 1024, (q % 1024) // 128, q % 128
    return j * 16 + 2 * u + P


_TMAP = _tmap()


class _K:

    def __init__(self):
        self.ops = {e: [] for e in ("sync", "tensor", "vector", "scalar",
                                    "gpsimd")}
        self.ct = {}

    def E(self, eng, fn):
        self.ops[eng].append(fn)

    def inc(self, sem, n):
        self.ct[sem] = self.ct.get(sem, 0) + n
        return self.ct[sem]

    def val(self, sem):
        return self.ct.get(sem, 0)


def build_program(stop_after=None):
    nc = bass.Bass("TRN2", target_bir_lowering=False,
                   detect_race_conditions=False)
    K = _K()
    E, inc, val = K.E, K.inc, K.val

    dr = {}
    for nm, shp in [("xp", [T, D]), ("x0", [TOWN, D]),
                    ("wq", [D, D]), ("wk", [D, D]), ("wv", [D, D]),
                    ("wo", [D, D]), ("bq", [D]), ("bk", [D]), ("bv", [D]),
                    ("bo", [D]), ("cosr", [64]), ("sinr", [64]),
                    ("w1", [D, 4096]), ("b1", [4096]), ("w2", [4096, D]),
                    ("b2", [D]), ("eyeC", [128, 128]), ("eyeU", [128, 128]),
                    ("eyeD", [128, 128])]:
        dr[nm] = nc.dram_tensor(nm, shp, F32, kind="ExternalInput").ap()
    out_d = nc.dram_tensor("out", [TOWN, D], F32, kind="ExternalOutput").ap()
    x1_dram = nc.dram_tensor("x1s", [TOWN, D], F32, kind="Internal").ap()

    NBF = 81088
    NF32 = 12352

    bo_ = [0]

    def ab(n):
        o = bo_[0]
        bo_[0] += n
        return o

    o_woT = ab(8 * 1024)
    o_ocat = ab(8 * 2048)
    o_otmp = ab(2048)
    o_xinb = ab(4 * 1024)
    o_onesb = ab(128)
    o_brow = ab(3 * 1024)
    _pbf = bo_[0]
    o_wqT = ab(8 * 1024)
    o_wkT = ab(8 * 1024)
    o_wvT = ab(8 * 1024)
    o_xinT = [ab(8 * 256), ab(8 * 256)]
    o_Ks = [ab(8 * 256), ab(8 * 256)]
    o_Vp = [ab(2 * 1040), ab(2 * 1040)]
    o_QC = [ab(1024), ab(1024)]
    o_QW = [ab(1024), ab(1024)]
    o_PT = [ab(2048), ab(2048), ab(2048)]
    _endA = bo_[0]
    bo_[0] = _pbf
    o_xin2 = ab(16 * 1024)
    o_gT = ab(32 * 512)
    o_xin2T = [ab(8 * 512), ab(8 * 512)]
    o_w1b = [ab(8 * 128), ab(8 * 128)]
    o_w2b = [ab(1024), ab(1024)]
    _endB = bo_[0]
    assert max(_endA, _endB) <= NBF, (_endA, _endB)

    fo_ = [0]

    def af(n):
        o = fo_[0]
        fo_[0] += n
        return o

    o_oden = af(2048)
    o_dspr = af(16)
    o_stat = af(64)
    o_ln1 = af(6 * 32)
    o_bqr = af(8)
    o_bkr = af(8)
    o_b1c = af(32)
    o_cs = af(8)
    o_onesf = af(64)
    o_mrt = af(128)
    o_mrtq = af(128)
    o_eye = af(3 * 128)
    _pf = fo_[0]
    o_xpb = [af(1024), af(1024)]
    o_wst = af(1024)
    _endFA = fo_[0]
    fo_[0] = _pf
    o_x1rb = [af(1024), af(1024)]
    o_outb = af(1024)
    o_junk = af(1024)
    _endFA2 = fo_[0]
    fo_[0] = max(_endFA, _endFA2)
    o_x0b = [af(1024), af(1024)]
    o_x1b = [af(1024), af(1024)]
    o_tmpL = af(1024)
    assert fo_[0] <= NF32, fo_[0]

    import contextlib
    with contextlib.ExitStack() as _st:
        ABh = _st.enter_context(nc.sbuf_tensor("AB", [128, NBF], BF16))
        AFh = _st.enter_context(nc.sbuf_tensor("AF", [128, NF32], F32))
        ps = _st.enter_context(nc.psum_tensor("ps", [128, 4096], F32))
        S = {}
        for _snm in ("wg", "wq2", "wvt", "t0", "t1", "x0", "x1", "st0",
                     "st1", "w50", "w51", "tr", "te", "sc", "pv", "ex",
                     "ve", "ap", "out"):
            S[_snm] = _st.enter_context(nc.semaphore("s_" + _snm))
        block = _st.enter_context(nc.Block())

        def B(off, n):
            return ABh[:, off:off + n]

        def Fv(off, n):
            return AFh[:, off:off + n]

        woT = B(o_woT, 8192).rearrange("p (c n) -> p c n", c=8)
        ocat = B(o_ocat, 16384).rearrange("p (c n) -> p c n", c=8)
        otmp = B(o_otmp, 2048)
        xinb = B(o_xinb, 4096)
        onesb = B(o_onesb, 128)
        brow = B(o_brow, 3072)
        wqT = B(o_wqT, 8192).rearrange("p (c n) -> p c n", c=8)
        wkT = B(o_wkT, 8192).rearrange("p (c n) -> p c n", c=8)
        wvT = B(o_wvT, 8192).rearrange("p (c n) -> p c n", c=8)
        xinT = [B(o, 2048).rearrange("p (c n) -> p c n", c=8) for o in o_xinT]
        Ks = [B(o, 2048).rearrange("p (c n) -> p c n", c=8) for o in o_Ks]
        Vp = [B(o, 2080).rearrange("p (c n) -> p c n", c=2) for o in o_Vp]
        QC = [B(o, 1024) for o in o_QC]
        QW = [B(o, 1024) for o in o_QW]
        PT = [B(o, 2048) for o in o_PT]
        xin2 = B(o_xin2, 16384).rearrange("p (c n) -> p c n", c=16)
        gT = B(o_gT, 16384).rearrange("p (c n) -> p c n", c=32)
        xin2T = [B(o, 4096).rearrange("p (c n) -> p c n", c=8)
                 for o in o_xin2T]
        w1b = [B(o, 1024).rearrange("p (c n) -> p c n", c=8) for o in o_w1b]
        w2b = [B(o, 1024) for o in o_w2b]
        jkb = B(o_otmp, 1024)

        oden = Fv(o_oden, 2048)
        dspr = Fv(o_eye, 256)
        stat = Fv(o_stat, 64)
        ln1 = Fv(o_ln1, 192).rearrange("p (c n) -> p c n", c=6)
        bqr = Fv(o_bqr, 8)
        bkr = Fv(o_bkr, 8)
        b1c = Fv(o_b1c, 32)
        cs = Fv(o_cs, 8)
        onesf = Fv(o_onesf, 64)
        mrt = Fv(o_mrt, 128)
        mrtq = Fv(o_mrtq, 128)
        eye = Fv(o_eye, 384)
        xpb = [Fv(o, 1024) for o in o_xpb]
        wst = Fv(o_wst, 1024)
        x1rb = [Fv(o, 1024) for o in o_x1rb]
        outb = Fv(o_outb, 1024)
        junk = Fv(o_junk, 1024)
        x0b = [Fv(o, 1024) for o in o_x0b]
        x1b = [Fv(o, 1024) for o in o_x1b]
        tmpL = Fv(o_tmpL, 1024)

        def W(eng, sem, v):
            if v > 0:
                E(eng, lambda e, s=S[sem], v=v: e.wait_ge(s, v))

        def gdma(dst, src, slow=False):
            v = inc("wg", 16)
            E("gpsimd", lambda e, d=dst, s=src, slow=slow:
              e.dma_start(out=d, in_=s, allow_slow_non_contiguous=slow)
              .then_inc(S["wg"], 16))
            return v

        gdma(cs[0:64, 0:1], dr["cosr"][:].rearrange("(a c) -> c a", a=1))
        gdma(cs[64:128, 0:1], dr["cosr"][:].rearrange("(a c) -> c a", a=1))
        gdma(cs[0:64, 3:4], dr["sinr"][:].rearrange("(a c) -> c a", a=1))
        gdma(cs[64:128, 3:4], dr["sinr"][:].rearrange("(a c) -> c a", a=1))
        gdma(eye[:, 0:128], dr["eyeC"][:, :])
        gdma(eye[:, 128:256], dr["eyeU"][:, :])
        gdma(eye[:, 256:384], dr["eyeD"][:, :])
        gdma(bqr[:, 0:8], dr["bq"][:].rearrange("(c p) -> p c", p=128), slow=True)
        gdma(bkr[:, 0:8], dr["bk"][:].rearrange("(c p) -> p c", p=128), slow=True)
        gdma(b1c[:, 0:32], dr["b1"][:].rearrange("(c p) -> p c", p=128), slow=True)
        gdma(brow[0:1, 0:1024], dr["bv"][:].rearrange("(a n) -> a n", a=1))
        gdma(brow[0:1, 1024:2048], dr["bo"][:].rearrange("(a n) -> a n", a=1))
        gdma(brow[0:1, 2048:3072], dr["b2"][:].rearrange("(a n) -> a n", a=1))
        v_bias = val("wg")

        E("vector", lambda e: e.memset(cs[:, 1:3], 0.0).then_inc(S["ve"], 1))
        v_cz = inc("ve", 1)
        W("sync", "ve", v_cz)
        W("sync", "wg", v_bias)
        E("sync", lambda e: e.dma_start(out=cs[0:127, 1:2], in_=cs[1:128, 3:4])
          .then_inc(S["tr"], 16))
        E("sync", lambda e: e.dma_start(out=cs[1:128, 2:3], in_=cs[0:127, 3:4])
          .then_inc(S["tr"], 16))
        v_csh = inc("tr", 32)

        E("vector", lambda e: e.memset(onesb[:, :], 1.0))
        E("vector", lambda e: e.memset(onesf[:, :], 1.0))
        E("vector", lambda e: e.memset(Vp[0][:, :, :], 1.0))
        E("vector", lambda e: e.memset(Vp[1][:, :, :], 1.0))
        W("vector", "wg", v_bias)
        W("vector", "tr", v_csh)
        E("vector", lambda e: e.tensor_scalar(mrt[:, :], eye[:, 0:128],
                                              cs[:, 0:1], None, ALU.mult))
        E("vector", lambda e: e.tensor_scalar(eye[:, 128:256], eye[:, 128:256],
                                              cs[:, 1:2], None, ALU.mult))
        E("vector", lambda e: e.tensor_add(mrt[:, :], mrt[:, :],
                                           eye[:, 128:256]))
        E("vector", lambda e: e.tensor_scalar(eye[:, 256:384], eye[:, 256:384],
                                              cs[:, 2:3], None, ALU.mult))
        E("vector", lambda e: e.tensor_sub(mrt[:, :], mrt[:, :],
                                           eye[:, 256:384]))
        E("vector", lambda e: e.tensor_scalar(mrtq[:, :], mrt[:, :], 0.125,
                                              None, ALU.mult)
          .then_inc(S["ve"], 1))
        v_mrot = inc("ve", 1)

        v_wv_tr = {}
        for wname, dstT in (("wv", wvT), ("wo", woT)):
            for cc in range(8):
                vg = inc("wq2", 16)
                E("gpsimd", lambda e, wname=wname, cc=cc:
                  e.dma_start(out=xinb[:, 0:1024],
                              in_=dr[wname][cc * 128:(cc + 1) * 128, :])
                  .then_inc(S["wq2"], 16))
                W("sync", "wq2", vg)
                E("sync", lambda e, dstT=dstT, cc=cc:
                  e.dma_start(out=dstT[:, :, cc * 128:cc * 128 + 128],
                              in_=xinb[:, 0:1024], transpose=True)
                  .then_inc(S["wvt"], 16))
                vtr = inc("wvt", 16)
                W("gpsimd", "wvt", vtr)
        v_wvwo = val("wvt")

        rot_ap = {}

        def rot_weight(wname, rhs_ap, dstT, bias_col, prev_te, prev_ve):
            for cc in range(8):
                if cc >= 1:
                    W("gpsimd", "te", rot_ap[("te", wname, cc - 1)])
                elif prev_te:
                    W("gpsimd", "te", prev_te)
                vw = inc("wq2", 16)
                E("gpsimd", lambda e, wname=wname, cc=cc:
                  e.dma_start(out=wst[:, :],
                              in_=dr[wname][cc * 128:(cc + 1) * 128, :])
                  .then_inc(S["wq2"], 16))
                W("tensor", "ve", v_mrot)
                W("tensor", "wq2", vw)
                if cc >= 2:
                    W("tensor", "ap", rot_ap[("ap", wname, cc - 2)])
                reg = (cc % 2) * 1024
                for fc in range(8):
                    last = fc == 7
                    E("tensor", lambda e, cc=cc, fc=fc, reg=reg, last=last,
                      rhs_ap=rhs_ap:
                      (e.matmul(ps[:, reg + fc * 128:reg + fc * 128 + 128],
                                wst[:, fc * 128:(fc + 1) * 128],
                                rhs_ap, start=True, stop=True)
                       .then_inc(S["te"], 1)) if last else
                      e.matmul(ps[:, reg + fc * 128:reg + fc * 128 + 128],
                               wst[:, fc * 128:(fc + 1) * 128],
                               rhs_ap, start=True, stop=True))
                rot_ap[("te", wname, cc)] = inc("te", 1)
                W("scalar", "te", rot_ap[("te", wname, cc)])
                E("scalar", lambda e, cc=cc, reg=reg, dstT=dstT:
                  e.activation(dstT[:, :, cc * 128:cc * 128 + 128],
                               ps[:, reg:reg + 1024]
                               .rearrange("p (c n) -> p c n", c=8),
                               AF.Identity).then_inc(S["ap"], 1))
                rot_ap[("ap", wname, cc)] = inc("ap", 1)
            W("tensor", "wg", v_bias)
            if prev_ve:
                W("tensor", "ve", prev_ve)
            for cc in range(8):
                last = cc == 7
                E("tensor", lambda e, cc=cc, rhs_ap=rhs_ap, bias_col=bias_col,
                  last=last:
                  (e.matmul(ps[:, 2048 + cc:2048 + cc + 1], rhs_ap,
                            bias_col[:, cc:cc + 1], start=True, stop=True)
                   .then_inc(S["te"], 1)) if last else
                  e.matmul(ps[:, 2048 + cc:2048 + cc + 1], rhs_ap,
                           bias_col[:, cc:cc + 1], start=True, stop=True))
            vt = inc("te", 1)
            W("vector", "te", vt)
            E("vector", lambda e, bias_col=bias_col:
              e.tensor_copy(bias_col[:, 0:8], ps[:, 2048:2056])
              .then_inc(S["ve"], 1))
            return inc("ve", 1)

        v_bqrot = rot_weight("wq", mrtq[:, :], wqT, bqr, 0, 0)
        v_bkrot = rot_weight("wk", mrt[:, :], wkT, bkr,
                             rot_ap[("te", "wq", 7)], v_bqrot)
        v_wrot_ap = val("ap")

        E("vector", lambda e: e.memset(ln1[:, :, :], 0.0).then_inc(S["ve"], 1))
        v_lnz = inc("ve", 1)
        pp_ve = {}
        pp_ap = {}
        v_ln1_c = {}
        for i in range(NT):
            if i >= 2:
                W("sync", "ve", pp_ve[i - 2])
                W("sync", "ap", pp_ap[i - 2])
            xs = "x%d" % (i % 2)
            vx = inc(xs, 16)
            E("sync", lambda e, i=i, xs=xs:
              e.dma_start(out=xpb[i % 2][:, :],
                          in_=dr["xp"][i * 128:(i + 1) * 128, :])
              .then_inc(S[xs], 16))
            W("vector", xs, vx)
            if i == 0:
                W("vector", "ve", v_lnz)
            E("vector", lambda e, i=i:
              e.reduce_sum(ln1[:, 0, i:i + 1], xpb[i % 2][:, :], axis=AX.X)
              .then_inc(S["ve"], 1))
            pp_ve[i] = inc("ve", 1)
            W("scalar", xs, vx)
            if i == 0:
                W("scalar", "ve", v_lnz)
            E("scalar", lambda e, i=i:
              e.activation(jkb[:, :], xpb[i % 2][:, :], AF.Square,
                           accum_out=ln1[:, 1, i:i + 1]).then_inc(S["ap"], 1))
            pp_ap[i] = inc("ap", 1)
            if i % 8 == 7:
                c = i // 8
                lo, hi = c * 8, c * 8 + 8
                W("vector", "ap", pp_ap[i])
                E("vector", lambda e, lo=lo, hi=hi:
                  e.tensor_scalar(ln1[:, 2, lo:hi], ln1[:, 0, lo:hi],
                                  -1.0 / D, None, ALU.mult))
                E("vector", lambda e, lo=lo, hi=hi:
                  e.tensor_scalar(ln1[:, 3, lo:hi], ln1[:, 1, lo:hi],
                                  1.0 / D, None, ALU.mult)
                  .then_inc(S["ve"], 1))
                W("vector", "ve", inc("ve", 1))
                E("vector", lambda e, lo=lo, hi=hi:
                  e.tensor_mul(ln1[:, 4, lo:hi], ln1[:, 2, lo:hi],
                               ln1[:, 2, lo:hi]).then_inc(S["ve"], 1))
                W("vector", "ve", inc("ve", 1))
                E("vector", lambda e, lo=lo, hi=hi:
                  e.scalar_tensor_tensor(ln1[:, 3, lo:hi], ln1[:, 3, lo:hi],
                                         EPS, ln1[:, 4, lo:hi], ALU.add,
                                         ALU.subtract).then_inc(S["ve"], 1))
                vv = inc("ve", 1)
                W("scalar", "ve", vv)
                E("scalar", lambda e, lo=lo, hi=hi:
                  e.activation(ln1[:, 4, lo:hi], ln1[:, 3, lo:hi], AF.Sqrt)
                  .then_inc(S["ap"], 1))
                va = inc("ap", 1)
                W("vector", "ap", va)
                E("vector", lambda e, lo=lo, hi=hi:
                  e.reciprocal(ln1[:, 3, lo:hi], ln1[:, 4, lo:hi])
                  .then_inc(S["ve"], 1))
                W("vector", "ve", inc("ve", 1))
                E("vector", lambda e, lo=lo, hi=hi:
                  e.tensor_mul(ln1[:, 2, lo:hi], ln1[:, 2, lo:hi],
                               ln1[:, 3, lo:hi]).then_inc(S["ve"], 1))
                v_ln1_c[c] = inc("ve", 1)
        v_ln1 = v_ln1_c[3]

        ap_apply = {}
        tr_xin = {}

        def ln1_apply_and_transpose(s, xinT_guard=None):
            for k in range(2):
                i = 2 * s + k
                if i >= 2:
                    W("sync", "ap", ap_apply[i - 2])
                xs = "st%d" % (i % 2)
                vx = inc(xs, 16)
                E("sync", lambda e, i=i, xs=xs:
                  e.dma_start(out=x0b[i % 2][:, :],
                              in_=dr["xp"][i * 128:(i + 1) * 128, :])
                  .then_inc(S[xs], 16))
                W("scalar", xs, vx)
                W("scalar", "ve", v_ln1_c[i // 8])
                if s >= 2:
                    W("scalar", "t%d" % (s % 2), tr_xin[(s - 2, k)])
                else:
                    W("scalar", "wvt", v_wvwo)
                E("scalar", lambda e, i=i:
                  e.activation(xinb[:, (i % 4) * 1024:(i % 4) * 1024 + 1024],
                               x0b[i % 2][:, :], AF.Identity,
                               bias=ln1[:, 2, i:i + 1],
                               scale=ln1[:, 3, i:i + 1]).then_inc(S["ap"], 1))
                ap_apply[i] = inc("ap", 1)
                W("sync", "ap", ap_apply[i])
                if xinT_guard is not None:
                    W("sync", "ap", xinT_guard)
                ts_ = "t%d" % (s % 2)
                E("sync", lambda e, s=s, k=k, i=i, ts_=ts_:
                  e.dma_start(out=xinT[s % 2][:, :, k * 128:k * 128 + 128],
                              in_=xinb[:, (i % 4) * 1024:(i % 4) * 1024 + 1024],
                              transpose=True).then_inc(S[ts_], 16))
                tr_xin[(s, k)] = inc(ts_, 16)

        def proj_slab(s):
            sb = s % 2
            W("tensor", "t%d" % sb, tr_xin[(s, 1)])
            W("tensor", "ap", v_wrot_ap)
            W("tensor", "wvt", v_wvwo)
            W("tensor", "ve", v_bkrot)
            ev = {}
            for cc in range(8):
                reg = 2048 + (cc % 2) * 1024
                if cc >= 2:
                    W("tensor", "ve", ev[cc - 2])
                for fc in range(8):
                    last = fc == 7
                    E("tensor", lambda e, cc=cc, fc=fc, reg=reg, sb=sb,
                      last=last:
                      (e.matmul(ps[:, reg:reg + 256],
                                wkT[:, fc, cc * 128:cc * 128 + 128],
                                xinT[sb][:, fc, 0:256],
                                start=(fc == 0), stop=True)
                       .then_inc(S["te"], 1)) if last else
                      e.matmul(ps[:, reg:reg + 256],
                               wkT[:, fc, cc * 128:cc * 128 + 128],
                               xinT[sb][:, fc, 0:256],
                               start=(fc == 0), stop=False))
                vt = inc("te", 1)
                W("vector", "te", vt)
                E("vector", lambda e, cc=cc, reg=reg, sb=sb:
                  e.tensor_scalar(Ks[sb][:, cc, :], ps[:, reg:reg + 256],
                                  bkr[:, cc:cc + 1], None, ALU.add)
                  .then_inc(S["ve"], 1))
                ev[cc] = inc("ve", 1)
            for rt in range(2):
                for n2 in range(2):
                    idx = rt * 2 + n2
                    reg = 2048 + (idx % 2) * 1024
                    W("tensor", "ve", ev[6 + idx] if idx < 2 else ev[8 + idx - 2])
                    E("tensor", lambda e, n2=n2, reg=reg:
                      e.matmul(ps[:, reg:reg + 512], onesb[0:1, 0:128],
                               brow[0:1, n2 * 512:n2 * 512 + 512],
                               start=True, stop=False))
                    for fc in range(8):
                        last = fc == 7
                        E("tensor", lambda e, rt=rt, n2=n2, fc=fc, reg=reg,
                          sb=sb, last=last:
                          (e.matmul(ps[:, reg:reg + 512],
                                    xinT[sb][:, fc, rt * 128:rt * 128 + 128],
                                    wvT[:, fc, n2 * 512:n2 * 512 + 512],
                                    start=False, stop=True)
                           .then_inc(S["te"], 1)) if last else
                          e.matmul(ps[:, reg:reg + 512],
                                   xinT[sb][:, fc, rt * 128:rt * 128 + 128],
                                   wvT[:, fc, n2 * 512:n2 * 512 + 512],
                                   start=False, stop=False))
                    vt = inc("te", 1)
                    W("vector", "te", vt)
                    E("vector", lambda e, rt=rt, n2=n2, reg=reg, sb=sb:
                      e.tensor_copy(
                          Vp[sb][:, rt, :].rearrange("p (a b) -> p a b", b=65)
                          [:, n2 * 8:(n2 + 1) * 8, 0:64],
                          ps[:, reg:reg + 512]
                          .rearrange("p (a b) -> p a b", b=64))
                      .then_inc(S["ve"], 1))
                    ev[8 + idx] = inc("ve", 1)
            v_v = val("ve")
            qap = {}
            for cc in range(8):
                reg = 2048 + (cc % 2) * 1024
                if cc < 2:
                    W("tensor", "ve", ev[10 + cc])
                else:
                    W("tensor", "ap", qap[cc - 2])
                for fc in range(8):
                    last = fc == 7
                    E("tensor", lambda e, cc=cc, fc=fc, reg=reg, sb=sb,
                      last=last:
                      (e.matmul(ps[:, reg:reg + 128],
                                wqT[:, fc, cc * 128:cc * 128 + 128],
                                xinT[sb][:, fc, 0:128],
                                start=(fc == 0), stop=True)
                       .then_inc(S["te"], 1)) if last else
                      e.matmul(ps[:, reg:reg + 128],
                               wqT[:, fc, cc * 128:cc * 128 + 128],
                               xinT[sb][:, fc, 0:128],
                               start=(fc == 0), stop=False))
                vt = inc("te", 1)
                W("scalar", "te", vt)
                E("scalar", lambda e, cc=cc, reg=reg, sb=sb:
                  e.activation(QC[sb][0:64, cc * 128:cc * 128 + 128],
                               ps[0:64, reg:reg + 128], AF.Identity,
                               bias=bqr[0:64, cc:cc + 1]))
                E("scalar", lambda e, cc=cc, reg=reg, sb=sb:
                  e.activation(QC[sb][64:128, cc * 128:cc * 128 + 128],
                               ps[64:128, reg:reg + 128], AF.Identity,
                               bias=bqr[64:128, cc:cc + 1])
                  .then_inc(S["ap"], 2))
                qap[cc] = inc("ap", 2)
            vq = val("ap")
            W("sync", "ap", vq)
            ts_ = "t%d" % sb
            E("sync", lambda e, sb=sb, ts_=ts_:
              e.dma_start(out=QW[sb][0:64, :], in_=QC[sb][64:128, :])
              .then_inc(S[ts_], 16))
            E("sync", lambda e, sb=sb, ts_=ts_:
              e.dma_start(out=QW[sb][64:128, :], in_=QC[sb][0:64, :])
              .then_inc(S[ts_], 16))
            vqw = inc(ts_, 32)
            return dict(k=ev[7], v=v_v, q=vq, qw=vqw, sb=sb)

        def attention_slab(s, pr, prev_mult, prev_pv, guard):
            sb = s % 2
            sc_d, ex_d, pv_d = {}, {}, {}

            def tile_of(n):
                P, g, t = n // 32, (n % 32) // 2, n % 2
                m, kh = g // 2, g % 2
                t_lo = 2 * m + t
                return P, t_lo, kh

            def scores(n):
                P, t_lo, kh = tile_of(n)
                pb = (t_lo % 2) * 64
                src_q = QC[sb] if (P == t_lo % 2) else QW[sb]
                bb = 4 + (n % 2) * 2
                if n == 0:
                    W("tensor", "ve", pr["k"])
                    W("tensor", "ap", pr["q"])
                    W("tensor", "t%d" % pr["sb"], pr["qw"])
                    if guard is not pr:
                        W("tensor", "ap", guard["q"])
                        W("tensor", "ve", guard["v"])
                if n >= 2:
                    W("tensor", "ex", ex_d[n - 2])
                for half in range(2):
                    last = half == 1
                    E("tensor", lambda e, P=P, t_lo=t_lo, kh=kh, pb=pb,
                      src_q=src_q, bb=bb, half=half, sb=sb, last=last:
                      (e.matmul(ps[:, (bb + half) * 512:(bb + half) * 512 + 512],
                                Ks[sb][pb:pb + 64, t_lo // 2,
                                       kh * 128:kh * 128 + 128],
                                src_q[pb:pb + 64,
                                      half * 512:half * 512 + 512],
                                start=True, stop=True)
                       .then_inc(S["sc"], 1)) if last else
                      e.matmul(ps[:, (bb + half) * 512:(bb + half) * 512 + 512],
                               Ks[sb][pb:pb + 64, t_lo // 2,
                                      kh * 128:kh * 128 + 128],
                               src_q[pb:pb + 64,
                                     half * 512:half * 512 + 512],
                               start=True, stop=True))
                sc_d[n] = inc("sc", 1)

            def expc(n):
                bb = 4 + (n % 2) * 2
                po = (n % 2) * 1024
                W("scalar", "sc", sc_d[n])
                if n >= 6:
                    W("scalar", "pv", pv_d[n - 6])
                elif prev_pv:
                    W("scalar", "pv", prev_pv)
                E("scalar", lambda e, n=n, bb=bb, po=po:
                  e.activation(PT[(n // 2) % 3][:, po:po + 1024],
                               ps[:, bb * 512:bb * 512 + 1024],
                               AF.Exp).then_inc(S["ex"], 1))
                ex_d[n] = inc("ex", 1)

            def pv(n):
                P, t_lo, kh = tile_of(n)
                po = (n % 2) * 1024
                W("tensor", "ex", ex_d[n])
                if n == 0:
                    W("tensor", "ve", prev_mult)
                    W("tensor", "ve", pr["v"])
                first = (n % 32) == 0
                last_g = (n % 32) == 31
                for half in range(2):
                    last = half == 1
                    E("tensor", lambda e, P=P, t_lo=t_lo, kh=kh, po=po, n=n,
                      half=half, sb=sb, first=first, last_g=last_g, last=last:
                      (e.matmul(ps[0:65, P * 1024 + half * 512:
                                   P * 1024 + half * 512 + 512],
                                Vp[sb][:, kh, t_lo * 65:t_lo * 65 + 65],
                                PT[(n // 2) % 3][:, po + half * 512:
                                                 po + half * 512 + 512],
                                start=first, stop=last_g)
                       .then_inc(S["pv"], 1)) if last else
                      e.matmul(ps[0:65, P * 1024 + half * 512:
                                  P * 1024 + half * 512 + 512],
                               Vp[sb][:, kh, t_lo * 65:t_lo * 65 + 65],
                               PT[(n // 2) % 3][:, po + half * 512:
                                                po + half * 512 + 512],
                               start=first, stop=last_g))
                pv_d[n] = inc("pv", 1)

            scores(0)
            scores(1)
            expc(0)
            expc(1)
            for n in range(2, 64, 2):
                scores(n)
                scores(n + 1)
                expc(n)
                expc(n + 1)
                pv(n - 2)
                pv(n - 1)
            pv(62)
            pv(63)
            return pv_d, ex_d

        def head_finish(s, pv_d):
            W("vector", "pv", pv_d[63])
            if s % 2 == 0:
                E("vector", lambda e, s=s:
                  e.tensor_copy(ocat[0:64, s // 2, :], ps[0:64, 0:2048])
                  .then_inc(S["ve"], 1))
            else:
                E("vector", lambda e:
                  e.tensor_copy(otmp[0:64, :], ps[0:64, 0:2048])
                  .then_inc(S["ve"], 1))
            inc("ve", 1)
            E("vector", lambda e: e.tensor_copy(oden[64:65, :],
                                                ps[64:65, 0:2048])
              .then_inc(S["ve"], 1))
            ve_evac = inc("ve", 1)
            if s % 2 == 1:
                W("sync", "ve", ve_evac)
                E("sync", lambda e, s=s:
                  e.dma_start(out=ocat[64:128, s // 2, :], in_=otmp[0:64, :])
                  .then_inc(S["tr"], 16))
                inc("tr", 16)
            W("sync", "ve", ve_evac)
            E("sync", lambda e, s=s:
              e.dma_start(out=dspr[:, s * 16:(s + 1) * 16],
                          in_=oden[64:65, :]).then_inc(S["tr"], 16))
            v_spread = inc("tr", 16)
            return dict(evac=ve_evac, spread=v_spread)

        def head_dbcast_mult(s, hf):
            W("tensor", "tr", hf["back"])
            W("tensor", "ve", hf["evac"])
            for c in range(4):
                last = c == 3
                E("tensor", lambda e, c=c, last=last:
                  (e.matmul(ps[0:64, c * 512:(c + 1) * 512], onesf[64:65, 0:64],
                            oden[64:65, c * 512:(c + 1) * 512],
                            start=True, stop=True).then_inc(S["te"], 1))
                  if last else
                  e.matmul(ps[0:64, c * 512:(c + 1) * 512], onesf[64:65, 0:64],
                           oden[64:65, c * 512:(c + 1) * 512],
                           start=True, stop=True))
            vt = inc("te", 1)
            W("vector", "te", vt)
            if s % 2 == 0:
                E("vector", lambda e, s=s:
                  e.tensor_mul(ocat[0:64, s // 2, :], oden[0:64, :],
                               ps[0:64, 0:2048]).then_inc(S["ve"], 1))
                vm = inc("ve", 1)
                return vm, None
            E("vector", lambda e:
              e.tensor_mul(otmp[0:64, :], oden[0:64, :], ps[0:64, 0:2048])
              .then_inc(S["ve"], 1))
            vm = inc("ve", 1)
            W("sync", "ve", vm)
            E("sync", lambda e, s=s:
              e.dma_start(out=ocat[64:128, s // 2, :], in_=otmp[0:64, :])
              .then_inc(S["tr"], 16))
            return vm, inc("tr", 16)

        prs = {}
        ln1_apply_and_transpose(0)
        prs[0] = proj_slab(0)
        ln1_apply_and_transpose(1)
        prs[1] = proj_slab(1)
        ln1_apply_and_transpose(2, xinT_guard=prs[0]["q"])
        ln1_apply_and_transpose(3, xinT_guard=prs[1]["q"])
        prev_mult = 0
        prev_pv = None
        for s in range(NSLAB):
            guard = prs[min(s + 1, NSLAB - 1)]
            pv_d, ex_d = attention_slab(s, prs[s], prev_mult, prev_pv, guard)
            hf = head_finish(s, pv_d)
            if s + 2 < NSLAB:
                prs[s + 2] = proj_slab(s + 2)
            if s + 4 < NSLAB:
                ln1_apply_and_transpose(s + 4, xinT_guard=prs[s + 2]["q"])
            prev_mult = hf["evac"]
            prev_pv = pv_d[63]
            last_spread = hf["spread"]
        W("vector", "tr", last_spread)
        E("vector", lambda e: e.reciprocal(dspr[:, 0:256], dspr[:, 0:256])
          .then_inc(S["ve"], 1))
        v_rec = inc("ve", 1)
        mult_done = {}
        for g in range(8):
            for hh in range(2):
                h = 2 * g + hh
                if g == 0 and hh == 0:
                    W("sync", "ve", v_rec)
                else:
                    W("sync", "te", val("te"))
                E("sync", lambda e, h=h:
                  e.dma_start(out=oden[64:65, :],
                              in_=dspr[:, h * 16:(h + 1) * 16])
                  .then_inc(S["tr"], 16))
                vback = inc("tr", 16)
                W("tensor", "tr", vback)
                if hh == 0:
                    if g == 0:
                        W("tensor", "ve", val("ve"))
                    else:
                        W("tensor", "ve", mult_done[g - 1])
                pb = hh * 64
                for c in range(4):
                    last = c == 3
                    E("tensor", lambda e, pb=pb, c=c, last=last:
                      (e.matmul(ps[pb:pb + 64, c * 512:(c + 1) * 512],
                                onesf[64:65, 0:64],
                                oden[64:65, c * 512:(c + 1) * 512],
                                start=True, stop=True).then_inc(S["te"], 1))
                      if last else
                      e.matmul(ps[pb:pb + 64, c * 512:(c + 1) * 512],
                               onesf[64:65, 0:64],
                               oden[64:65, c * 512:(c + 1) * 512],
                               start=True, stop=True))
                inc("te", 1)
            vt = val("te")
            W("vector", "te", vt)
            E("vector", lambda e, g=g:
              e.tensor_mul(ocat[:, g, :], ocat[:, g, :], ps[:, 0:2048])
              .then_inc(S["ve"], 1))
            mult_done[g] = inc("ve", 1)
        v_att_ve = val("ve")
        v_att_tr = val("tr")
        v_att_pv = val("pv")
        if stop_after == "attn":
            W("gpsimd", "ve", v_att_ve)
            W("gpsimd", "tr", v_att_tr)
            E("gpsimd", lambda e:
              e.dma_start(out=out_d[:, :],
                          in_=ocat[:, :, :]).then_inc(S["out"], 16))
            E("gpsimd", lambda e: e.wait_ge(S["out"], 16))
            _finish(block, K)
            return nc
        if stop_after == "proj0":
            W("gpsimd", "ve", prs[1]["v"])
            W("gpsimd", "ap", prs[1]["q"])
            E("gpsimd", lambda e:
              e.dma_start(out=out_d[0:128, 0:1024],
                          in_=QC[0][:, :]).then_inc(S["out"], 16))
            E("gpsimd", lambda e:
              e.dma_start(out=out_d[128:256, 0:1024],
                          in_=QC[1][:, :]).then_inc(S["out"], 16))
            E("gpsimd", lambda e:
              e.dma_start(out=out_d[256:384, 0:512],
                          in_=Ks[0][:, 0:2, :]).then_inc(S["out"], 16))
            E("gpsimd", lambda e:
              e.dma_start(out=out_d[384:512, 0:1024],
                          in_=Vp[0][:, 0, 0:1024]).then_inc(S["out"], 16))
            E("gpsimd", lambda e:
              e.dma_start(out=out_d[512:640, 0:512],
                          in_=xinT[0][:, 0:2, 0:256])
              .then_inc(S["out"], 16))
            E("gpsimd", lambda e: e.wait_ge(S["out"], 80))
            _finish(block, K)
            return nc

        psA_free = {}
        x1_store = {}
        xin2_done = {}
        x0_used = {}
        for tt in range(16):
            reg = (tt % 4) * 1024
            if tt == 0:
                W("tensor", "ve", v_att_ve)
                W("tensor", "tr", v_att_tr)
            if tt >= 4:
                W("tensor", "ve", psA_free[tt - 4][0])
                W("tensor", "ap", psA_free[tt - 4][1])
            E("tensor", lambda e, reg=reg:
              e.matmul(ps[:, reg:reg + 512], onesb[0:1, 0:128],
                       brow[0:1, 1024:1536], start=True, stop=False))
            E("tensor", lambda e, reg=reg:
              e.matmul(ps[:, reg + 512:reg + 1024], onesb[0:1, 0:128],
                       brow[0:1, 1536:2048], start=True, stop=False))
            for g in range(8):
                for n2 in range(2):
                    last = g == 7 and n2 == 1
                    E("tensor", lambda e, tt=tt, g=g, n2=n2, reg=reg,
                      last=last:
                      (e.matmul(ps[:, reg + n2 * 512:reg + n2 * 512 + 512],
                                ocat[:, g, tt * 128:tt * 128 + 128],
                                woT[:, g, n2 * 512:n2 * 512 + 512],
                                start=False, stop=(g == 7))
                       .then_inc(S["te"], 1)) if last else
                      e.matmul(ps[:, reg + n2 * 512:reg + n2 * 512 + 512],
                               ocat[:, g, tt * 128:tt * 128 + 128],
                               woT[:, g, n2 * 512:n2 * 512 + 512],
                               start=False, stop=(g == 7)))
            vt = inc("te", 1)
            if stop_after == "p4mm":
                psA_free[tt] = (0, 0)
                x0_used[tt] = 0
                x1_store[tt] = 0
                xin2_done[tt] = 0
                continue
            if tt >= 2:
                W("sync", "ve", x0_used[tt - 2])
            xs = "x%d" % (tt % 2)
            vx0 = inc(xs, 16)
            E("sync", lambda e, tt=tt, xs=xs:
              e.dma_start(out=x0b[tt % 2][:, :],
                          in_=dr["x0"][tt * 128:(tt + 1) * 128, :])
              .then_inc(S[xs], 16))
            W("vector", "te", vt)
            E("vector", lambda e, reg=reg:
              e.reduce_sum(stat[:, 0:1], ps[:, reg:reg + 1024], axis=AX.X))
            E("vector", lambda e: e.memset(stat[:, 1:2], 0.0)
              .then_inc(S["ve"], 1))
            vv0 = inc("ve", 1)
            W("scalar", "te", vt)
            W("scalar", "ve", vv0)
            E("scalar", lambda e, reg=reg:
              e.activation(junk[:, :], ps[:, reg:reg + 1024], AF.Square,
                           accum_out=stat[:, 1:2]).then_inc(S["ap"], 1))
            va0 = inc("ap", 1)
            W("vector", "ap", va0)
            E("vector", lambda e: e.tensor_scalar(stat[:, 2:3], stat[:, 0:1],
                                                  -1.0 / D, None, ALU.mult))
            E("vector", lambda e: e.tensor_scalar(stat[:, 3:4], stat[:, 1:2],
                                                  1.0 / D, None, ALU.mult)
              .then_inc(S["ve"], 1))
            W("vector", "ve", inc("ve", 1))
            E("vector", lambda e: e.tensor_mul(stat[:, 4:5], stat[:, 2:3],
                                               stat[:, 2:3])
              .then_inc(S["ve"], 1))
            W("vector", "ve", inc("ve", 1))
            E("vector", lambda e: e.scalar_tensor_tensor(
                stat[:, 3:4], stat[:, 3:4], EPS, stat[:, 4:5], ALU.add,
                ALU.subtract).then_inc(S["ve"], 1))
            vv = inc("ve", 1)
            W("scalar", "ve", vv)
            E("scalar", lambda e: e.activation(stat[:, 4:5], stat[:, 3:4],
                                               AF.Sqrt).then_inc(S["ap"], 1))
            va = inc("ap", 1)
            W("vector", "ap", va)
            E("vector", lambda e: e.reciprocal(stat[:, 5:6], stat[:, 4:5])
              .then_inc(S["ve"], 1))
            W("vector", "ve", inc("ve", 1))
            E("vector", lambda e: e.tensor_mul(stat[:, 6:7], stat[:, 2:3],
                                               stat[:, 5:6])
              .then_inc(S["ve"], 1))
            vv = inc("ve", 1)
            W("scalar", "ve", vv)
            E("scalar", lambda e, reg=reg:
              e.activation(tmpL[:, :], ps[:, reg:reg + 1024], AF.Identity,
                           bias=stat[:, 6:7], scale=stat[:, 5:6])
              .then_inc(S["ap"], 1))
            va = inc("ap", 1)
            psA_free[tt] = (vv, va)
            W("vector", "ap", va)
            W("vector", xs, vx0)
            if tt >= 2:
                W("vector", "st%d" % (tt % 2), x1_store[tt - 2])
            E("vector", lambda e, tt=tt:
              e.tensor_add(x1b[tt % 2][:, :], tmpL[:, :], x0b[tt % 2][:, :])
              .then_inc(S["ve"], 1))
            vx1 = inc("ve", 1)
            x0_used[tt] = vx1
            W("sync", "ve", vx1)
            sts = "st%d" % (tt % 2)
            E("sync", lambda e, tt=tt, sts=sts:
              e.dma_start(out=x1_dram[tt * 128:(tt + 1) * 128, :],
                          in_=x1b[tt % 2][:, :]).then_inc(S[sts], 16))
            x1_store[tt] = inc(sts, 16)
            E("vector", lambda e, tt=tt:
              e.reduce_sum(stat[:, 8:9], x1b[tt % 2][:, :], axis=AX.X))
            E("vector", lambda e, tt=tt:
              e.tensor_mul(junk[:, :], x1b[tt % 2][:, :], x1b[tt % 2][:, :]))
            E("vector", lambda e:
              e.reduce_sum(stat[:, 9:10], junk[:, :], axis=AX.X))
            E("vector", lambda e:
              e.tensor_scalar(stat[:, 10:11], stat[:, 8:9], -1.0 / D, None,
                              ALU.mult))
            E("vector", lambda e:
              e.tensor_scalar(stat[:, 11:12], stat[:, 9:10], 1.0 / D, None,
                              ALU.mult).then_inc(S["ve"], 1))
            W("vector", "ve", inc("ve", 1))
            E("vector", lambda e: e.tensor_mul(stat[:, 12:13], stat[:, 10:11],
                                               stat[:, 10:11])
              .then_inc(S["ve"], 1))
            W("vector", "ve", inc("ve", 1))
            E("vector", lambda e: e.scalar_tensor_tensor(
                stat[:, 11:12], stat[:, 11:12], EPS, stat[:, 12:13], ALU.add,
                ALU.subtract).then_inc(S["ve"], 1))
            vv = inc("ve", 1)
            W("scalar", "ve", vv)
            E("scalar", lambda e: e.activation(stat[:, 12:13], stat[:, 11:12],
                                               AF.Sqrt).then_inc(S["ap"], 1))
            va = inc("ap", 1)
            W("vector", "ap", va)
            E("vector", lambda e: e.reciprocal(stat[:, 13:14], stat[:, 12:13])
              .then_inc(S["ve"], 1))
            W("vector", "ve", inc("ve", 1))
            E("vector", lambda e: e.tensor_mul(stat[:, 14:15], stat[:, 10:11],
                                               stat[:, 13:14])
              .then_inc(S["ve"], 1))
            vv = inc("ve", 1)
            W("scalar", "ve", vv)
            E("scalar", lambda e, tt=tt:
              e.activation(xin2[:, tt, :], x1b[tt % 2][:, :], AF.Identity,
                           bias=stat[:, 14:15], scale=stat[:, 13:14])
              .then_inc(S["ap"], 1))
            xin2_done[tt] = inc("ap", 1)

        if stop_after == "p4mm":
            W("vector", "te", val("te"))
            vv_ = inc("ve", 1)
            E("vector", lambda e: e.tensor_copy(oden[0:64, 0:1024],
                                                ps[0:64, 0:1024])
              .then_inc(S["ve"], 1))
            W("sync", "ve", val("ve"))
            W("sync", "te", val("te"))
            E("sync", lambda e:
              e.dma_start(out=out_d[0:64, :], in_=oden[0:64, 0:1024])
              .then_inc(S["out"], 16))
            E("sync", lambda e: e.wait_ge(S["out"], 16))
            _finish(block, K)
            return nc

        if stop_after == "p4ln2":
            W("gpsimd", "st0", val("st0"))
            W("gpsimd", "st1", val("st1"))
            E("gpsimd", lambda e:
              e.dma_start(out=out_d[:, :], in_=x1_dram[:, :])
              .then_inc(S["out"], 16))
            E("gpsimd", lambda e: e.wait_ge(S["out"], 16))
            _finish(block, K)
            return nc

        if stop_after == "p4":
            W("gpsimd", "st0", val("st0"))
            W("gpsimd", "st1", val("st1"))
            E("gpsimd", lambda e:
              e.dma_start(out=out_d[:, :], in_=x1_dram[:, :])
              .then_inc(S["out"], 16))
            E("gpsimd", lambda e: e.wait_ge(S["out"], 16))
            _finish(block, K)
            return nc

        w1_te = {}
        w2_te = {}
        gelu_done = {}
        psF_ap = {}
        out_tot = 0
        x1r_used = {}
        out_used = 0
        for tb in range(4):
            for k in range(4):
                tt = tb * 4 + k
                W("sync", "ap", xin2_done[tt])
                if tb >= 2:
                    W("sync", "te", w1_te[(tb - 2, 31)])
                ts_ = "t%d" % (tb % 2)
                E("sync", lambda e, tb=tb, k=k, tt=tt, ts_=ts_:
                  e.dma_start(out=xin2T[tb % 2][:, :, k * 128:k * 128 + 128],
                              in_=xin2[:, tt, :], transpose=True)
                  .then_inc(S[ts_], 16))
                inc(ts_, 16)
            v_x2t = val("t%d" % (tb % 2))
            if stop_after == "p5tr":
                break
            for jc in range(32):
                if (tb, jc - 2) in w1_te:
                    W("gpsimd", "te", w1_te[(tb, jc - 2)])
                elif tb > 0 and jc < 2:
                    W("gpsimd", "te", w1_te[(tb - 1, 30 + jc)])
                else:
                    W("gpsimd", "pv", v_att_pv)
                ws = "w5%d" % (jc % 2)
                vw = inc(ws, 16)
                E("gpsimd", lambda e, jc=jc, ws=ws:
                  e.dma_start(out=w1b[jc % 2][:, :, :],
                              in_=dr["w1"][:, jc * 128:(jc + 1) * 128]
                              .rearrange("(c p) j -> p c j", p=128))
                  .then_inc(S[ws], 16))
                reg = (jc % 2) * 512
                if stop_after == "p5w1":
                    continue
                W("tensor", ws, vw)
                W("tensor", "t%d" % (tb % 2), v_x2t)
                if (tb, jc - 2) in gelu_done:
                    W("tensor", "ap", gelu_done[(tb, jc - 2)])
                elif tb > 0 and jc < 2:
                    W("tensor", "ap", psF_ap[(tb - 1, 0)])
                for fc in range(8):
                    last = fc == 7
                    E("tensor", lambda e, jc=jc, fc=fc, reg=reg, tb=tb,
                      last=last:
                      (e.matmul(ps[:, reg:reg + 512], w1b[jc % 2][:, fc, :],
                                xin2T[tb % 2][:, fc, :],
                                start=(fc == 0), stop=True)
                       .then_inc(S["te"], 1)) if last else
                      e.matmul(ps[:, reg:reg + 512], w1b[jc % 2][:, fc, :],
                               xin2T[tb % 2][:, fc, :],
                               start=(fc == 0), stop=False))
                w1_te[(tb, jc)] = inc("te", 1)
                W("scalar", "te", w1_te[(tb, jc)])
                _gdst = (jkb[:, 0:512] if stop_after == "p5nog"
                         else gT[:, jc, :])
                E("scalar", lambda e, jc=jc, reg=reg, _gdst=_gdst:
                  e.activation(_gdst, ps[:, reg:reg + 512], AF.Gelu,
                               bias=b1c[:, jc:jc + 1]).then_inc(S["ap"], 1))
                gelu_done[(tb, jc)] = inc("ap", 1)
            if stop_after in ("p5w1", "p5nog", "p5mm1"):
                break
            W("tensor", "ap", gelu_done[(tb, 31)])
            for k in range(4):
                E("tensor", lambda e, k=k:
                  e.matmul(ps[:, k * 1024:k * 1024 + 512], onesb[0:1, 0:128],
                           brow[0:1, 2048:2560], start=True, stop=False))
                E("tensor", lambda e, k=k:
                  e.matmul(ps[:, k * 1024 + 512:k * 1024 + 1024],
                           onesb[0:1, 0:128], brow[0:1, 2560:3072],
                           start=True, stop=False))
            for jc in range(32):
                if (tb, jc - 2) in w2_te:
                    W("gpsimd", "te", w2_te[(tb, jc - 2)])
                else:
                    W("gpsimd", "te", w1_te[(tb, 30 + jc)])
                ws = "w5%d" % (jc % 2)
                vw = inc(ws, 16)
                E("gpsimd", lambda e, jc=jc, ws=ws:
                  e.dma_start(out=w2b[jc % 2][:, :],
                              in_=dr["w2"][jc * 128:(jc + 1) * 128, :])
                  .then_inc(S[ws], 16))
                W("tensor", ws, vw)
                for k in range(4):
                    for n2 in range(2):
                        last = k == 3 and n2 == 1
                        E("tensor", lambda e, jc=jc, k=k, n2=n2, last=last:
                          (e.matmul(ps[:, k * 1024 + n2 * 512:
                                       k * 1024 + n2 * 512 + 512],
                                    gT[:, jc, k * 128:k * 128 + 128],
                                    w2b[jc % 2][:, n2 * 512:n2 * 512 + 512],
                                    start=False, stop=(jc == 31))
                           .then_inc(S["te"], 1)) if last else
                          e.matmul(ps[:, k * 1024 + n2 * 512:
                                      k * 1024 + n2 * 512 + 512],
                                   gT[:, jc, k * 128:k * 128 + 128],
                                   w2b[jc % 2][:, n2 * 512:n2 * 512 + 512],
                                   start=False, stop=(jc == 31)))
                w2_te[(tb, jc)] = inc("te", 1)
            v_mm2 = val("te")
            if stop_after == "p5mm2":
                break
            last_ap = 0
            for k in range(4):
                tt = tb * 4 + k
                if tt >= 2:
                    W("sync", "ve", x1r_used.get(tt - 2, 0))
                W("sync", "st%d" % (tt % 2), x1_store[tt])
                xs = "x%d" % (tt % 2)
                vx1 = inc(xs, 16)
                E("sync", lambda e, tt=tt, xs=xs:
                  e.dma_start(out=x1rb[tt % 2][:, :],
                              in_=x1_dram[tt * 128:(tt + 1) * 128, :])
                  .then_inc(S[xs], 16))
                W("vector", "te", v_mm2)
                E("vector", lambda e, k=k:
                  e.reduce_sum(stat[:, 0:1], ps[:, k * 1024:k * 1024 + 1024],
                               axis=AX.X))
                E("vector", lambda e: e.memset(stat[:, 1:2], 0.0)
                  .then_inc(S["ve"], 1))
                vv0 = inc("ve", 1)
                W("scalar", "te", v_mm2)
                W("scalar", "ve", vv0)
                E("scalar", lambda e, k=k:
                  e.activation(junk[:, :], ps[:, k * 1024:k * 1024 + 1024],
                               AF.Square, accum_out=stat[:, 1:2])
                  .then_inc(S["ap"], 1))
                va0 = inc("ap", 1)
                W("vector", "ap", va0)
                E("vector", lambda e:
                  e.tensor_scalar(stat[:, 2:3], stat[:, 0:1], -1.0 / D, None,
                                  ALU.mult))
                E("vector", lambda e:
                  e.tensor_scalar(stat[:, 3:4], stat[:, 1:2], 1.0 / D, None,
                                  ALU.mult).then_inc(S["ve"], 1))
                W("vector", "ve", inc("ve", 1))
                E("vector", lambda e:
                  e.tensor_mul(stat[:, 4:5], stat[:, 2:3], stat[:, 2:3])
                  .then_inc(S["ve"], 1))
                W("vector", "ve", inc("ve", 1))
                E("vector", lambda e: e.scalar_tensor_tensor(
                    stat[:, 3:4], stat[:, 3:4], EPS, stat[:, 4:5], ALU.add,
                    ALU.subtract).then_inc(S["ve"], 1))
                vv = inc("ve", 1)
                W("scalar", "ve", vv)
                E("scalar", lambda e: e.activation(stat[:, 4:5], stat[:, 3:4],
                                                   AF.Sqrt)
                  .then_inc(S["ap"], 1))
                va = inc("ap", 1)
                W("vector", "ap", va)
                E("vector", lambda e: e.reciprocal(stat[:, 5:6], stat[:, 4:5])
                  .then_inc(S["ve"], 1))
                W("vector", "ve", inc("ve", 1))
                E("vector", lambda e:
                  e.tensor_mul(stat[:, 6:7], stat[:, 2:3], stat[:, 5:6])
                  .then_inc(S["ve"], 1))
                vv = inc("ve", 1)
                W("scalar", "ve", vv)
                E("scalar", lambda e, k=k:
                  e.activation(tmpL[:, :], ps[:, k * 1024:k * 1024 + 1024],
                               AF.Identity, bias=stat[:, 6:7],
                               scale=stat[:, 5:6]).then_inc(S["ap"], 1))
                va = inc("ap", 1)
                last_ap = va
                psF_ap[(tb, k)] = va
                W("vector", "ap", va)
                W("vector", xs, vx1)
                if out_used:
                    W("vector", "out", out_used)
                E("vector", lambda e, tt=tt:
                  e.tensor_add(outb[:, :], tmpL[:, :], x1rb[tt % 2][:, :])
                  .then_inc(S["ve"], 1))
                vo = inc("ve", 1)
                x1r_used[tt] = vo
                W("sync", "ve", vo)
                E("sync", lambda e, tt=tt:
                  e.dma_start(out=out_d[tt * 128:(tt + 1) * 128, :],
                              in_=outb[:, :]).then_inc(S["out"], 16))
                out_tot = inc("out", 16)
                out_used = out_tot


        if stop_after in ("p5tr", "p5w1", "p5nog", "p5mm1", "p5mm2"):
            W("sync", "st0", val("st0"))
            W("sync", "st1", val("st1"))
            vv_ = inc("ve", 1)
            E("vector", lambda e: e.tensor_copy(oden[0:64, 0:1024],
                                                junk[0:64, :])
              .then_inc(S["ve"], 1))
            W("sync", "ve", val("ve"))
            if stop_after in ("p5tr", "p5w1", "p5nog", "p5mm1"):
                W("sync", "ap", val("ap"))
            else:
                W("sync", "te", val("te"))
            E("sync", lambda e:
              e.dma_start(out=out_d[0:64, :], in_=oden[0:64, 0:1024])
              .then_inc(S["out"], 16))
            E("sync", lambda e: e.wait_ge(S["out"], 16))
            _finish(block, K)
            return nc
        W("sync", "out", out_tot)
        _finish(block, K)
        return nc


def _finish(block, K):
    @block.sync
    def _(e):
        for f in K.ops["sync"]:
            f(e)

    @block.tensor
    def _(e):
        for f in K.ops["tensor"]:
            f(e)

    @block.vector
    def _(e):
        for f in K.ops["vector"]:
            f(e)

    @block.scalar
    def _(e):
        for f in K.ops["scalar"]:
            f(e)

    @block.gpsimd
    def _(e):
        for f in K.ops["gpsimd"]:
            f(e)


_NC_CACHE = {}


def _get_nc():
    if "nc" not in _NC_CACHE:
        _NC_CACHE["nc"] = build_program()
    return _NC_CACHE["nc"]


def kernel(L0_attn_x, L0_attn_wq, L0_attn_bq, L0_attn_wk, L0_attn_bk,
           L0_attn_wv, L0_attn_bv, L0_attn_wo, L0_attn_bo, L0_attn_cos,
           L0_attn_sin, L0_ffn_w1, L0_ffn_b1, L0_ffn_w2, L0_ffn_b2):
    x = np.ascontiguousarray(np.asarray(L0_attn_x, dtype=np.float32))
    B = x.shape[0]
    eyeC = np.eye(128, dtype=np.float32)
    eyeU = np.zeros((128, 128), np.float32)
    eyeD = np.zeros((128, 128), np.float32)
    for p in range(0, 128, 2):
        eyeU[p, p + 1] = 1.0
    for p in range(1, 128, 2):
        eyeD[p, p - 1] = 1.0
    shared = dict(
        wq=np.ascontiguousarray(L0_attn_wq, np.float32),
        wk=np.ascontiguousarray(L0_attn_wk, np.float32),
        wv=np.ascontiguousarray(L0_attn_wv, np.float32),
        wo=np.ascontiguousarray(L0_attn_wo, np.float32),
        bq=np.ascontiguousarray(L0_attn_bq, np.float32),
        bk=np.ascontiguousarray(L0_attn_bk, np.float32),
        bv=np.ascontiguousarray(L0_attn_bv, np.float32),
        bo=np.ascontiguousarray(L0_attn_bo, np.float32),
        cosr=np.ascontiguousarray(np.asarray(L0_attn_cos)[3199], np.float32),
        sinr=np.ascontiguousarray(np.asarray(L0_attn_sin)[3199], np.float32),
        w1=np.ascontiguousarray(L0_ffn_w1, np.float32),
        b1=np.ascontiguousarray(L0_ffn_b1, np.float32),
        w2=np.ascontiguousarray(L0_ffn_w2, np.float32),
        b2=np.ascontiguousarray(L0_ffn_b2, np.float32),
        eyeC=eyeC, eyeU=eyeU, eyeD=eyeD,
    )
    in_maps = []
    for c in range(8):
        b, half = c // 2, c % 2
        xb = x[b]
        xpc = xb.reshape(16, 2, 128, D)[:, [half, 1 - half]].reshape(T, D)
        x0c = xb[half * TOWN + _TMAP]
        m = dict(shared)
        m["xp"] = np.ascontiguousarray(xpc)
        m["x0"] = np.ascontiguousarray(x0c)
        in_maps.append(m)

    nc = _get_nc()
    res = run_bass_kernel_spmd(nc, in_maps, core_ids=list(range(8)))
    out = np.empty((B, T, D), np.float32)
    inv = np.empty_like(_TMAP)
    inv[_TMAP] = np.arange(2048)
    for c in range(8):
        b, half = c // 2, c % 2
        out[b, half * TOWN:(half + 1) * TOWN] = res.results[c]["out"][inv]
    return out


# version 07
# speedup vs baseline: 1.0657x; 1.0657x over previous
import numpy as np
import ml_dtypes
import concourse.bass as bass
import concourse.mybir as mybir
from concourse.bass_utils import run_bass_kernel_spmd

F32 = mybir.dt.float32
BF16 = mybir.dt.bfloat16
AF = mybir.ActivationFunctionType
ALU = mybir.AluOpType
AX = mybir.AxisListType

D = 1024
T = 4096
TOWN = 2048
NSLAB = 16
NT = 32
EPS = 1e-5


def _tmap():
    q = np.arange(2048)
    P, u, j = q // 1024, (q % 1024) // 128, q % 128
    return j * 16 + 2 * u + P


_TMAP = _tmap()


class _K:

    def __init__(self):
        self.ops = {e: [] for e in ("sync", "tensor", "vector", "scalar",
                                    "gpsimd")}
        self.ct = {}

    def E(self, eng, fn):
        self.ops[eng].append(fn)

    def inc(self, sem, n):
        self.ct[sem] = self.ct.get(sem, 0) + n
        return self.ct[sem]

    def val(self, sem):
        return self.ct.get(sem, 0)


def build_program(stop_after=None):
    nc = bass.Bass("TRN2", target_bir_lowering=False,
                   detect_race_conditions=False)
    K = _K()
    E, inc, val = K.E, K.inc, K.val

    dr = {}
    for nm, shp in [("xp", [T, D]), ("x0", [TOWN, D]),
                    ("wq", [D, D]), ("wk", [D, D]), ("wv", [D, D]),
                    ("wo", [D, D]), ("bq", [D]), ("bk", [D]), ("bv", [D]),
                    ("bo", [D]), ("cosr", [64]), ("sinr", [64]),
                    ("w1", [D, 4096]), ("b1", [4096]), ("w2", [4096, D]),
                    ("b2", [D]), ("eyeC", [128, 128]), ("eyeU", [128, 128]),
                    ("eyeD", [128, 128])]:
        dr[nm] = nc.dram_tensor(nm, shp, F32, kind="ExternalInput").ap()
    out_d = nc.dram_tensor("out", [TOWN, D], F32, kind="ExternalOutput").ap()
    x1_dram = nc.dram_tensor("x1s", [TOWN, D], F32, kind="Internal").ap()

    NBF = 81088
    NF32 = 12352

    bo_ = [0]

    def ab(n):
        o = bo_[0]
        bo_[0] += n
        return o

    o_woT = ab(8 * 1024)
    o_ocat = ab(8 * 2048)
    o_otmp = ab(2048)
    o_xinb = ab(4 * 1024)
    o_onesb = ab(128)
    o_brow = ab(3 * 1024)
    _pbf = bo_[0]
    o_wqT = ab(8 * 1024)
    o_wkT = ab(8 * 1024)
    o_wvT = ab(8 * 1024)
    o_xinT = [ab(8 * 256), ab(8 * 256)]
    o_Ks = [ab(8 * 256), ab(8 * 256)]
    o_Vp = [ab(2 * 1040), ab(2 * 1040)]
    o_QC = [ab(1024), ab(1024)]
    o_QW = [ab(1024), ab(1024)]
    o_PT = [ab(2048), ab(2048), ab(2048)]
    _endA = bo_[0]
    bo_[0] = _pbf
    o_xin2 = ab(16 * 1024)
    o_gT = ab(32 * 512)
    o_xin2T = [ab(8 * 512), ab(8 * 512)]
    o_w1b = [ab(8 * 128), ab(8 * 128)]
    o_w2b = [ab(1024), ab(1024)]
    _endB = bo_[0]
    assert max(_endA, _endB) <= NBF, (_endA, _endB)

    fo_ = [0]

    def af(n):
        o = fo_[0]
        fo_[0] += n
        return o

    o_oden = af(2048)
    o_dspr = af(16)
    o_stat = af(64)
    o_ln1 = af(6 * 32)
    o_bqr = af(8)
    o_bkr = af(8)
    o_b1c = af(32)
    o_cs = af(8)
    o_onesf = af(64)
    o_mrt = af(128)
    o_mrtq = af(128)
    o_eye = af(3 * 128)
    _pf = fo_[0]
    o_xpb = [af(1024), af(1024)]
    o_wst = af(1024)
    _endFA = fo_[0]
    fo_[0] = _pf
    o_x1rb = [af(1024), af(1024)]
    o_outb = af(1024)
    o_junk = af(1024)
    _endFA2 = fo_[0]
    fo_[0] = max(_endFA, _endFA2)
    o_x0b = [af(1024), af(1024)]
    o_x1b = [af(1024), af(1024)]
    o_tmpL = af(1024)
    assert fo_[0] <= NF32, fo_[0]

    import contextlib
    with contextlib.ExitStack() as _st:
        ABh = _st.enter_context(nc.sbuf_tensor("AB", [128, NBF], BF16))
        AFh = _st.enter_context(nc.sbuf_tensor("AF", [128, NF32], F32))
        ps = _st.enter_context(nc.psum_tensor("ps", [128, 4096], F32))
        S = {}
        for _snm in ("wg", "wq2", "wvt", "t0", "t1", "x0", "x1", "st0",
                     "st1", "w50", "w51", "tr", "te", "sc", "pv", "ex",
                     "ve", "ap", "out"):
            S[_snm] = _st.enter_context(nc.semaphore("s_" + _snm))
        block = _st.enter_context(nc.Block())

        def B(off, n):
            return ABh[:, off:off + n]

        def Fv(off, n):
            return AFh[:, off:off + n]

        woT = B(o_woT, 8192).rearrange("p (c n) -> p c n", c=8)
        ocat = B(o_ocat, 16384).rearrange("p (c n) -> p c n", c=8)
        otmp = B(o_otmp, 2048)
        xinb = B(o_xinb, 4096)
        onesb = B(o_onesb, 128)
        brow = B(o_brow, 3072)
        wqT = B(o_wqT, 8192).rearrange("p (c n) -> p c n", c=8)
        wkT = B(o_wkT, 8192).rearrange("p (c n) -> p c n", c=8)
        wvT = B(o_wvT, 8192).rearrange("p (c n) -> p c n", c=8)
        xinT = [B(o, 2048).rearrange("p (c n) -> p c n", c=8) for o in o_xinT]
        Ks = [B(o, 2048).rearrange("p (c n) -> p c n", c=8) for o in o_Ks]
        Vp = [B(o, 2080).rearrange("p (c n) -> p c n", c=2) for o in o_Vp]
        QC = [B(o, 1024) for o in o_QC]
        QW = [B(o, 1024) for o in o_QW]
        PT = [B(o, 2048) for o in o_PT]
        xin2 = B(o_xin2, 16384).rearrange("p (c n) -> p c n", c=16)
        gT = B(o_gT, 16384).rearrange("p (c n) -> p c n", c=32)
        xin2T = [B(o, 4096).rearrange("p (c n) -> p c n", c=8)
                 for o in o_xin2T]
        w1b = [B(o, 1024).rearrange("p (c n) -> p c n", c=8) for o in o_w1b]
        w2b = [B(o, 1024) for o in o_w2b]
        jkb = B(o_otmp, 1024)

        oden = Fv(o_oden, 2048)
        dspr = Fv(o_eye, 256)
        stat = Fv(o_stat, 64)
        ln1 = Fv(o_ln1, 192).rearrange("p (c n) -> p c n", c=6)
        bqr = Fv(o_bqr, 8)
        bkr = Fv(o_bkr, 8)
        b1c = Fv(o_b1c, 32)
        cs = Fv(o_cs, 8)
        onesf = Fv(o_onesf, 64)
        mrt = Fv(o_mrt, 128)
        mrtq = Fv(o_mrtq, 128)
        eye = Fv(o_eye, 384)
        xpb = [Fv(o, 1024) for o in o_xpb]
        wst = Fv(o_wst, 1024)
        x1rb = [Fv(o, 1024) for o in o_x1rb]
        outb = Fv(o_outb, 1024)
        junk = Fv(o_junk, 1024)
        x0b = [Fv(o, 1024) for o in o_x0b]
        x1b = [Fv(o, 1024) for o in o_x1b]
        tmpL = Fv(o_tmpL, 1024)

        def W(eng, sem, v):
            if v > 0:
                E(eng, lambda e, s=S[sem], v=v: e.wait_ge(s, v))

        def gdma(dst, src, slow=False):
            v = inc("wg", 16)
            E("gpsimd", lambda e, d=dst, s=src, slow=slow:
              e.dma_start(out=d, in_=s, allow_slow_non_contiguous=slow)
              .then_inc(S["wg"], 16))
            return v

        gdma(cs[0:64, 0:1], dr["cosr"][:].rearrange("(a c) -> c a", a=1))
        gdma(cs[64:128, 0:1], dr["cosr"][:].rearrange("(a c) -> c a", a=1))
        gdma(cs[0:64, 3:4], dr["sinr"][:].rearrange("(a c) -> c a", a=1))
        gdma(cs[64:128, 3:4], dr["sinr"][:].rearrange("(a c) -> c a", a=1))
        gdma(eye[:, 0:128], dr["eyeC"][:, :])
        gdma(eye[:, 128:256], dr["eyeU"][:, :])
        gdma(eye[:, 256:384], dr["eyeD"][:, :])
        gdma(bqr[:, 0:8], dr["bq"][:].rearrange("(c p) -> p c", p=128), slow=True)
        gdma(bkr[:, 0:8], dr["bk"][:].rearrange("(c p) -> p c", p=128), slow=True)
        gdma(b1c[:, 0:32], dr["b1"][:].rearrange("(c p) -> p c", p=128), slow=True)
        gdma(brow[0:1, 0:1024], dr["bv"][:].rearrange("(a n) -> a n", a=1))
        gdma(brow[0:1, 1024:2048], dr["bo"][:].rearrange("(a n) -> a n", a=1))
        gdma(brow[0:1, 2048:3072], dr["b2"][:].rearrange("(a n) -> a n", a=1))
        v_bias = val("wg")

        E("vector", lambda e: e.memset(cs[:, 1:3], 0.0).then_inc(S["ve"], 1))
        v_cz = inc("ve", 1)
        W("sync", "ve", v_cz)
        W("sync", "wg", v_bias)
        E("sync", lambda e: e.dma_start(out=cs[0:127, 1:2], in_=cs[1:128, 3:4])
          .then_inc(S["tr"], 16))
        E("sync", lambda e: e.dma_start(out=cs[1:128, 2:3], in_=cs[0:127, 3:4])
          .then_inc(S["tr"], 16))
        v_csh = inc("tr", 32)

        E("vector", lambda e: e.memset(onesb[:, :], 1.0))
        E("vector", lambda e: e.memset(onesf[:, :], 1.0))
        E("vector", lambda e: e.memset(Vp[0][:, :, :], 1.0))
        E("vector", lambda e: e.memset(Vp[1][:, :, :], 1.0))
        W("vector", "wg", v_bias)
        W("vector", "tr", v_csh)
        E("vector", lambda e: e.tensor_scalar(mrt[:, :], eye[:, 0:128],
                                              cs[:, 0:1], None, ALU.mult))
        E("vector", lambda e: e.tensor_scalar(eye[:, 128:256], eye[:, 128:256],
                                              cs[:, 1:2], None, ALU.mult))
        E("vector", lambda e: e.tensor_add(mrt[:, :], mrt[:, :],
                                           eye[:, 128:256]))
        E("vector", lambda e: e.tensor_scalar(eye[:, 256:384], eye[:, 256:384],
                                              cs[:, 2:3], None, ALU.mult))
        E("vector", lambda e: e.tensor_sub(mrt[:, :], mrt[:, :],
                                           eye[:, 256:384]))
        E("vector", lambda e: e.tensor_scalar(mrtq[:, :], mrt[:, :], 0.125,
                                              None, ALU.mult)
          .then_inc(S["ve"], 1))
        v_mrot = inc("ve", 1)

        v_wv_tr = {}
        for wname, dstT in (("wv", wvT), ("wo", woT)):
            for cc in range(8):
                vg = inc("wq2", 16)
                E("gpsimd", lambda e, wname=wname, cc=cc:
                  e.dma_start(out=xinb[:, 0:1024],
                              in_=dr[wname][cc * 128:(cc + 1) * 128, :])
                  .then_inc(S["wq2"], 16))
                W("sync", "wq2", vg)
                E("sync", lambda e, dstT=dstT, cc=cc:
                  e.dma_start(out=dstT[:, :, cc * 128:cc * 128 + 128],
                              in_=xinb[:, 0:1024], transpose=True)
                  .then_inc(S["wvt"], 16))
                vtr = inc("wvt", 16)
                W("gpsimd", "wvt", vtr)
        v_wvwo = val("wvt")

        rot_ap = {}

        def rot_weight(wname, rhs_ap, dstT, bias_col, prev_te, prev_ve):
            for cc in range(8):
                if cc >= 1:
                    W("gpsimd", "te", rot_ap[("te", wname, cc - 1)])
                elif prev_te:
                    W("gpsimd", "te", prev_te)
                vw = inc("wq2", 16)
                E("gpsimd", lambda e, wname=wname, cc=cc:
                  e.dma_start(out=wst[:, :],
                              in_=dr[wname][cc * 128:(cc + 1) * 128, :])
                  .then_inc(S["wq2"], 16))
                W("tensor", "ve", v_mrot)
                W("tensor", "wq2", vw)
                if cc >= 2:
                    W("tensor", "ap", rot_ap[("ap", wname, cc - 2)])
                reg = (cc % 2) * 1024
                for fc in range(8):
                    last = fc == 7
                    E("tensor", lambda e, cc=cc, fc=fc, reg=reg, last=last,
                      rhs_ap=rhs_ap:
                      (e.matmul(ps[:, reg + fc * 128:reg + fc * 128 + 128],
                                wst[:, fc * 128:(fc + 1) * 128],
                                rhs_ap, start=True, stop=True)
                       .then_inc(S["te"], 1)) if last else
                      e.matmul(ps[:, reg + fc * 128:reg + fc * 128 + 128],
                               wst[:, fc * 128:(fc + 1) * 128],
                               rhs_ap, start=True, stop=True))
                rot_ap[("te", wname, cc)] = inc("te", 1)
                W("scalar", "te", rot_ap[("te", wname, cc)])
                E("scalar", lambda e, cc=cc, reg=reg, dstT=dstT:
                  e.activation(dstT[:, :, cc * 128:cc * 128 + 128],
                               ps[:, reg:reg + 1024]
                               .rearrange("p (c n) -> p c n", c=8),
                               AF.Identity).then_inc(S["ap"], 1))
                rot_ap[("ap", wname, cc)] = inc("ap", 1)
            W("tensor", "wg", v_bias)
            if prev_ve:
                W("tensor", "ve", prev_ve)
            for cc in range(8):
                last = cc == 7
                E("tensor", lambda e, cc=cc, rhs_ap=rhs_ap, bias_col=bias_col,
                  last=last:
                  (e.matmul(ps[:, 2048 + cc:2048 + cc + 1], rhs_ap,
                            bias_col[:, cc:cc + 1], start=True, stop=True)
                   .then_inc(S["te"], 1)) if last else
                  e.matmul(ps[:, 2048 + cc:2048 + cc + 1], rhs_ap,
                           bias_col[:, cc:cc + 1], start=True, stop=True))
            vt = inc("te", 1)
            W("vector", "te", vt)
            E("vector", lambda e, bias_col=bias_col:
              e.tensor_copy(bias_col[:, 0:8], ps[:, 2048:2056])
              .then_inc(S["ve"], 1))
            return inc("ve", 1)

        v_bqrot = rot_weight("wq", mrtq[:, :], wqT, bqr, 0, 0)
        v_bkrot = rot_weight("wk", mrt[:, :], wkT, bkr,
                             rot_ap[("te", "wq", 7)], v_bqrot)
        v_wrot_ap = val("ap")

        E("vector", lambda e: e.memset(ln1[:, :, :], 0.0).then_inc(S["ve"], 1))
        v_lnz = inc("ve", 1)
        pp_ve = {}
        pp_ap = {}
        v_ln1_c = {}
        for i in range(NT):
            if i >= 2:
                W("sync", "ve", pp_ve[i - 2])
                W("sync", "ap", pp_ap[i - 2])
            xs = "x%d" % (i % 2)
            vx = inc(xs, 16)
            E("sync", lambda e, i=i, xs=xs:
              e.dma_start(out=xpb[i % 2][:, :],
                          in_=dr["xp"][i * 128:(i + 1) * 128, :])
              .then_inc(S[xs], 16))
            W("vector", xs, vx)
            if i == 0:
                W("vector", "ve", v_lnz)
            E("vector", lambda e, i=i:
              e.reduce_sum(ln1[:, 0, i:i + 1], xpb[i % 2][:, :], axis=AX.X)
              .then_inc(S["ve"], 1))
            pp_ve[i] = inc("ve", 1)
            W("scalar", xs, vx)
            if i == 0:
                W("scalar", "ve", v_lnz)
            E("scalar", lambda e, i=i:
              e.activation(jkb[:, :], xpb[i % 2][:, :], AF.Square,
                           accum_out=ln1[:, 1, i:i + 1]).then_inc(S["ap"], 1))
            pp_ap[i] = inc("ap", 1)
            if i % 8 == 7:
                c = i // 8
                lo, hi = c * 8, c * 8 + 8
                W("vector", "ap", pp_ap[i])
                E("vector", lambda e, lo=lo, hi=hi:
                  e.tensor_scalar(ln1[:, 2, lo:hi], ln1[:, 0, lo:hi],
                                  -1.0 / D, None, ALU.mult))
                E("vector", lambda e, lo=lo, hi=hi:
                  e.tensor_scalar(ln1[:, 3, lo:hi], ln1[:, 1, lo:hi],
                                  1.0 / D, None, ALU.mult)
                  .then_inc(S["ve"], 1))
                W("vector", "ve", inc("ve", 1))
                E("vector", lambda e, lo=lo, hi=hi:
                  e.tensor_mul(ln1[:, 4, lo:hi], ln1[:, 2, lo:hi],
                               ln1[:, 2, lo:hi]).then_inc(S["ve"], 1))
                W("vector", "ve", inc("ve", 1))
                E("vector", lambda e, lo=lo, hi=hi:
                  e.scalar_tensor_tensor(ln1[:, 3, lo:hi], ln1[:, 3, lo:hi],
                                         EPS, ln1[:, 4, lo:hi], ALU.add,
                                         ALU.subtract).then_inc(S["ve"], 1))
                vv = inc("ve", 1)
                W("scalar", "ve", vv)
                E("scalar", lambda e, lo=lo, hi=hi:
                  e.activation(ln1[:, 4, lo:hi], ln1[:, 3, lo:hi], AF.Sqrt)
                  .then_inc(S["ap"], 1))
                va = inc("ap", 1)
                W("vector", "ap", va)
                E("vector", lambda e, lo=lo, hi=hi:
                  e.reciprocal(ln1[:, 3, lo:hi], ln1[:, 4, lo:hi])
                  .then_inc(S["ve"], 1))
                W("vector", "ve", inc("ve", 1))
                E("vector", lambda e, lo=lo, hi=hi:
                  e.tensor_mul(ln1[:, 2, lo:hi], ln1[:, 2, lo:hi],
                               ln1[:, 3, lo:hi]).then_inc(S["ve"], 1))
                v_ln1_c[c] = inc("ve", 1)
        v_ln1 = v_ln1_c[3]

        ap_apply = {}
        tr_xin = {}

        def ln1_apply_and_transpose(s, xinT_guard=None):
            for k in range(2):
                i = 2 * s + k
                if i >= 2:
                    W("sync", "ve", ap_apply[i - 2])
                xs = "st%d" % (i % 2)
                vx = inc(xs, 16)
                E("sync", lambda e, i=i, xs=xs:
                  e.dma_start(out=x0b[i % 2][:, :],
                              in_=dr["xp"][i * 128:(i + 1) * 128, :])
                  .then_inc(S[xs], 16))
                W("vector", xs, vx)
                W("vector", "ve", v_ln1_c[i // 8])
                if s >= 2:
                    W("vector", "t%d" % (s % 2), tr_xin[(s - 2, k)])
                else:
                    W("vector", "wvt", v_wvwo)
                E("vector", lambda e, i=i:
                  e.tensor_scalar(xinb[:, (i % 4) * 1024:(i % 4) * 1024 + 1024],
                                  x0b[i % 2][:, :], ln1[:, 3, i:i + 1],
                                  ln1[:, 2, i:i + 1], ALU.mult, ALU.add)
                  .then_inc(S["ve"], 1))
                ap_apply[i] = inc("ve", 1)
                W("sync", "ve", ap_apply[i])
                if xinT_guard is not None:
                    W("sync", "ve", xinT_guard)
                ts_ = "t%d" % (s % 2)
                E("sync", lambda e, s=s, k=k, i=i, ts_=ts_:
                  e.dma_start(out=xinT[s % 2][:, :, k * 128:k * 128 + 128],
                              in_=xinb[:, (i % 4) * 1024:(i % 4) * 1024 + 1024],
                              transpose=True).then_inc(S[ts_], 16))
                tr_xin[(s, k)] = inc(ts_, 16)

        def proj_slab(s):
            sb = s % 2
            W("tensor", "t%d" % sb, tr_xin[(s, 1)])
            W("tensor", "ap", v_wrot_ap)
            W("tensor", "wvt", v_wvwo)
            W("tensor", "ve", v_bkrot)
            ev = {}
            for cc in range(8):
                reg = 2048 + (cc % 2) * 1024
                if cc >= 2:
                    W("tensor", "ve", ev[cc - 2])
                for fc in range(8):
                    last = fc == 7
                    E("tensor", lambda e, cc=cc, fc=fc, reg=reg, sb=sb,
                      last=last:
                      (e.matmul(ps[:, reg:reg + 256],
                                wkT[:, fc, cc * 128:cc * 128 + 128],
                                xinT[sb][:, fc, 0:256],
                                start=(fc == 0), stop=True)
                       .then_inc(S["te"], 1)) if last else
                      e.matmul(ps[:, reg:reg + 256],
                               wkT[:, fc, cc * 128:cc * 128 + 128],
                               xinT[sb][:, fc, 0:256],
                               start=(fc == 0), stop=False))
                vt = inc("te", 1)
                W("vector", "te", vt)
                E("vector", lambda e, cc=cc, reg=reg, sb=sb:
                  e.tensor_scalar(Ks[sb][:, cc, :], ps[:, reg:reg + 256],
                                  bkr[:, cc:cc + 1], None, ALU.add)
                  .then_inc(S["ve"], 1))
                ev[cc] = inc("ve", 1)
            for rt in range(2):
                for n2 in range(2):
                    idx = rt * 2 + n2
                    reg = 2048 + (idx % 2) * 1024
                    W("tensor", "ve", ev[6 + idx] if idx < 2 else ev[8 + idx - 2])
                    E("tensor", lambda e, n2=n2, reg=reg:
                      e.matmul(ps[:, reg:reg + 512], onesb[0:1, 0:128],
                               brow[0:1, n2 * 512:n2 * 512 + 512],
                               start=True, stop=False))
                    for fc in range(8):
                        last = fc == 7
                        E("tensor", lambda e, rt=rt, n2=n2, fc=fc, reg=reg,
                          sb=sb, last=last:
                          (e.matmul(ps[:, reg:reg + 512],
                                    xinT[sb][:, fc, rt * 128:rt * 128 + 128],
                                    wvT[:, fc, n2 * 512:n2 * 512 + 512],
                                    start=False, stop=True)
                           .then_inc(S["te"], 1)) if last else
                          e.matmul(ps[:, reg:reg + 512],
                                   xinT[sb][:, fc, rt * 128:rt * 128 + 128],
                                   wvT[:, fc, n2 * 512:n2 * 512 + 512],
                                   start=False, stop=False))
                    vt = inc("te", 1)
                    W("vector", "te", vt)
                    E("vector", lambda e, rt=rt, n2=n2, reg=reg, sb=sb:
                      e.tensor_copy(
                          Vp[sb][:, rt, :].rearrange("p (a b) -> p a b", b=65)
                          [:, n2 * 8:(n2 + 1) * 8, 0:64],
                          ps[:, reg:reg + 512]
                          .rearrange("p (a b) -> p a b", b=64))
                      .then_inc(S["ve"], 1))
                    ev[8 + idx] = inc("ve", 1)
            v_v = val("ve")
            qap = {}
            for cc in range(8):
                reg = 2048 + (cc % 2) * 1024
                if cc < 2:
                    W("tensor", "ve", ev[10 + cc])
                else:
                    W("tensor", "ve", qap[cc - 2])
                for fc in range(8):
                    last = fc == 7
                    E("tensor", lambda e, cc=cc, fc=fc, reg=reg, sb=sb,
                      last=last:
                      (e.matmul(ps[:, reg:reg + 128],
                                wqT[:, fc, cc * 128:cc * 128 + 128],
                                xinT[sb][:, fc, 0:128],
                                start=(fc == 0), stop=True)
                       .then_inc(S["te"], 1)) if last else
                      e.matmul(ps[:, reg:reg + 128],
                               wqT[:, fc, cc * 128:cc * 128 + 128],
                               xinT[sb][:, fc, 0:128],
                               start=(fc == 0), stop=False))
                vt = inc("te", 1)
                W("vector", "te", vt)
                E("vector", lambda e, cc=cc, reg=reg, sb=sb:
                  e.tensor_scalar(QC[sb][0:64, cc * 128:cc * 128 + 128],
                                  ps[0:64, reg:reg + 128],
                                  bqr[0:64, cc:cc + 1], None, ALU.add))
                E("vector", lambda e, cc=cc, reg=reg, sb=sb:
                  e.tensor_scalar(QC[sb][64:128, cc * 128:cc * 128 + 128],
                                  ps[64:128, reg:reg + 128],
                                  bqr[64:128, cc:cc + 1], None, ALU.add)
                  .then_inc(S["ve"], 2))
                qap[cc] = inc("ve", 2)
            vq = val("ve")
            W("sync", "ve", vq)
            ts_ = "t%d" % sb
            E("sync", lambda e, sb=sb, ts_=ts_:
              e.dma_start(out=QW[sb][0:64, :], in_=QC[sb][64:128, :])
              .then_inc(S[ts_], 16))
            E("sync", lambda e, sb=sb, ts_=ts_:
              e.dma_start(out=QW[sb][64:128, :], in_=QC[sb][0:64, :])
              .then_inc(S[ts_], 16))
            vqw = inc(ts_, 32)
            return dict(k=ev[7], v=v_v, q=vq, qw=vqw, sb=sb)

        def attention_slab(s, pr, prev_mult, prev_pv, guard):
            sb = s % 2
            sc_d, ex_d, pv_d = {}, {}, {}

            def tile_of(n):
                P, g, t = n // 32, (n % 32) // 2, n % 2
                m, kh = g // 2, g % 2
                t_lo = 2 * m + t
                return P, t_lo, kh

            def scores(n):
                P, t_lo, kh = tile_of(n)
                pb = (t_lo % 2) * 64
                src_q = QC[sb] if (P == t_lo % 2) else QW[sb]
                bb = 4 + (n % 2) * 2
                if n == 0:
                    W("tensor", "ve", pr["k"])
                    W("tensor", "ve", pr["q"])
                    W("tensor", "t%d" % pr["sb"], pr["qw"])
                    if guard is not pr:
                        W("tensor", "ve", guard["q"])
                        W("tensor", "ve", guard["v"])
                if n >= 2:
                    W("tensor", "ex", ex_d[n - 2])
                for half in range(2):
                    last = half == 1
                    E("tensor", lambda e, P=P, t_lo=t_lo, kh=kh, pb=pb,
                      src_q=src_q, bb=bb, half=half, sb=sb, last=last:
                      (e.matmul(ps[:, (bb + half) * 512:(bb + half) * 512 + 512],
                                Ks[sb][pb:pb + 64, t_lo // 2,
                                       kh * 128:kh * 128 + 128],
                                src_q[pb:pb + 64,
                                      half * 512:half * 512 + 512],
                                start=True, stop=True)
                       .then_inc(S["sc"], 1)) if last else
                      e.matmul(ps[:, (bb + half) * 512:(bb + half) * 512 + 512],
                               Ks[sb][pb:pb + 64, t_lo // 2,
                                      kh * 128:kh * 128 + 128],
                               src_q[pb:pb + 64,
                                     half * 512:half * 512 + 512],
                               start=True, stop=True))
                sc_d[n] = inc("sc", 1)

            def expc(n):
                bb = 4 + (n % 2) * 2
                po = (n % 2) * 1024
                W("scalar", "sc", sc_d[n])
                if n >= 6:
                    W("scalar", "pv", pv_d[n - 6])
                elif prev_pv:
                    W("scalar", "pv", prev_pv)
                E("scalar", lambda e, n=n, bb=bb, po=po:
                  e.activation(PT[(n // 2) % 3][:, po:po + 1024],
                               ps[:, bb * 512:bb * 512 + 1024],
                               AF.Exp).then_inc(S["ex"], 1))
                ex_d[n] = inc("ex", 1)

            def pv(n):
                P, t_lo, kh = tile_of(n)
                po = (n % 2) * 1024
                W("tensor", "ex", ex_d[n])
                if n == 0:
                    W("tensor", "ve", prev_mult)
                    W("tensor", "ve", pr["v"])
                first = (n % 32) == 0
                last_g = (n % 32) == 31
                for half in range(2):
                    last = half == 1
                    E("tensor", lambda e, P=P, t_lo=t_lo, kh=kh, po=po, n=n,
                      half=half, sb=sb, first=first, last_g=last_g, last=last:
                      (e.matmul(ps[0:65, P * 1024 + half * 512:
                                   P * 1024 + half * 512 + 512],
                                Vp[sb][:, kh, t_lo * 65:t_lo * 65 + 65],
                                PT[(n // 2) % 3][:, po + half * 512:
                                                 po + half * 512 + 512],
                                start=first, stop=last_g)
                       .then_inc(S["pv"], 1)) if last else
                      e.matmul(ps[0:65, P * 1024 + half * 512:
                                  P * 1024 + half * 512 + 512],
                               Vp[sb][:, kh, t_lo * 65:t_lo * 65 + 65],
                               PT[(n // 2) % 3][:, po + half * 512:
                                                po + half * 512 + 512],
                               start=first, stop=last_g))
                pv_d[n] = inc("pv", 1)

            scores(0)
            scores(1)
            expc(0)
            expc(1)
            for n in range(2, 64, 2):
                scores(n)
                scores(n + 1)
                expc(n)
                expc(n + 1)
                pv(n - 2)
                pv(n - 1)
            pv(62)
            pv(63)
            return pv_d, ex_d

        def head_finish(s, pv_d):
            W("vector", "pv", pv_d[63])
            if s % 2 == 0:
                E("vector", lambda e, s=s:
                  e.tensor_copy(ocat[0:64, s // 2, :], ps[0:64, 0:2048])
                  .then_inc(S["ve"], 1))
            else:
                E("vector", lambda e:
                  e.tensor_copy(otmp[0:64, :], ps[0:64, 0:2048])
                  .then_inc(S["ve"], 1))
            inc("ve", 1)
            E("vector", lambda e: e.tensor_copy(oden[64:65, :],
                                                ps[64:65, 0:2048])
              .then_inc(S["ve"], 1))
            ve_evac = inc("ve", 1)
            if s % 2 == 1:
                W("sync", "ve", ve_evac)
                E("sync", lambda e, s=s:
                  e.dma_start(out=ocat[64:128, s // 2, :], in_=otmp[0:64, :])
                  .then_inc(S["tr"], 16))
                inc("tr", 16)
            W("sync", "ve", ve_evac)
            E("sync", lambda e, s=s:
              e.dma_start(out=dspr[:, s * 16:(s + 1) * 16],
                          in_=oden[64:65, :]).then_inc(S["tr"], 16))
            v_spread = inc("tr", 16)
            return dict(evac=ve_evac, spread=v_spread)

        def head_dbcast_mult(s, hf):
            W("tensor", "tr", hf["back"])
            W("tensor", "ve", hf["evac"])
            for c in range(4):
                last = c == 3
                E("tensor", lambda e, c=c, last=last:
                  (e.matmul(ps[0:64, c * 512:(c + 1) * 512], onesf[64:65, 0:64],
                            oden[64:65, c * 512:(c + 1) * 512],
                            start=True, stop=True).then_inc(S["te"], 1))
                  if last else
                  e.matmul(ps[0:64, c * 512:(c + 1) * 512], onesf[64:65, 0:64],
                           oden[64:65, c * 512:(c + 1) * 512],
                           start=True, stop=True))
            vt = inc("te", 1)
            W("vector", "te", vt)
            if s % 2 == 0:
                E("vector", lambda e, s=s:
                  e.tensor_mul(ocat[0:64, s // 2, :], oden[0:64, :],
                               ps[0:64, 0:2048]).then_inc(S["ve"], 1))
                vm = inc("ve", 1)
                return vm, None
            E("vector", lambda e:
              e.tensor_mul(otmp[0:64, :], oden[0:64, :], ps[0:64, 0:2048])
              .then_inc(S["ve"], 1))
            vm = inc("ve", 1)
            W("sync", "ve", vm)
            E("sync", lambda e, s=s:
              e.dma_start(out=ocat[64:128, s // 2, :], in_=otmp[0:64, :])
              .then_inc(S["tr"], 16))
            return vm, inc("tr", 16)

        prs = {}
        ln1_apply_and_transpose(0)
        prs[0] = proj_slab(0)
        ln1_apply_and_transpose(1)
        prs[1] = proj_slab(1)
        ln1_apply_and_transpose(2, xinT_guard=prs[0]["q"])
        ln1_apply_and_transpose(3, xinT_guard=prs[1]["q"])
        prev_mult = 0
        prev_pv = None
        for s in range(NSLAB):
            guard = prs[min(s + 1, NSLAB - 1)]
            pv_d, ex_d = attention_slab(s, prs[s], prev_mult, prev_pv, guard)
            hf = head_finish(s, pv_d)
            if s + 2 < NSLAB:
                prs[s + 2] = proj_slab(s + 2)
            if s + 4 < NSLAB:
                ln1_apply_and_transpose(s + 4, xinT_guard=prs[s + 2]["q"])
            prev_mult = hf["evac"]
            prev_pv = pv_d[63]
            last_spread = hf["spread"]
        W("vector", "tr", last_spread)
        E("vector", lambda e: e.reciprocal(dspr[:, 0:256], dspr[:, 0:256])
          .then_inc(S["ve"], 1))
        v_rec = inc("ve", 1)
        mult_done = {}
        for g in range(8):
            for hh in range(2):
                h = 2 * g + hh
                if g == 0 and hh == 0:
                    W("sync", "ve", v_rec)
                else:
                    W("sync", "te", val("te"))
                E("sync", lambda e, h=h:
                  e.dma_start(out=oden[64:65, :],
                              in_=dspr[:, h * 16:(h + 1) * 16])
                  .then_inc(S["tr"], 16))
                vback = inc("tr", 16)
                W("tensor", "tr", vback)
                if hh == 0:
                    if g == 0:
                        W("tensor", "ve", val("ve"))
                    else:
                        W("tensor", "ve", mult_done[g - 1])
                pb = hh * 64
                for c in range(4):
                    last = c == 3
                    E("tensor", lambda e, pb=pb, c=c, last=last:
                      (e.matmul(ps[pb:pb + 64, c * 512:(c + 1) * 512],
                                onesf[64:65, 0:64],
                                oden[64:65, c * 512:(c + 1) * 512],
                                start=True, stop=True).then_inc(S["te"], 1))
                      if last else
                      e.matmul(ps[pb:pb + 64, c * 512:(c + 1) * 512],
                               onesf[64:65, 0:64],
                               oden[64:65, c * 512:(c + 1) * 512],
                               start=True, stop=True))
                inc("te", 1)
            vt = val("te")
            W("vector", "te", vt)
            E("vector", lambda e, g=g:
              e.tensor_mul(ocat[:, g, :], ocat[:, g, :], ps[:, 0:2048])
              .then_inc(S["ve"], 1))
            mult_done[g] = inc("ve", 1)
        v_att_ve = val("ve")
        v_att_tr = val("tr")
        v_att_pv = val("pv")
        if stop_after == "attn":
            W("gpsimd", "ve", v_att_ve)
            W("gpsimd", "tr", v_att_tr)
            E("gpsimd", lambda e:
              e.dma_start(out=out_d[:, :],
                          in_=ocat[:, :, :]).then_inc(S["out"], 16))
            E("gpsimd", lambda e: e.wait_ge(S["out"], 16))
            _finish(block, K)
            return nc
        if stop_after == "proj0":
            W("gpsimd", "ve", prs[1]["v"])
            W("gpsimd", "ap", prs[1]["q"])
            E("gpsimd", lambda e:
              e.dma_start(out=out_d[0:128, 0:1024],
                          in_=QC[0][:, :]).then_inc(S["out"], 16))
            E("gpsimd", lambda e:
              e.dma_start(out=out_d[128:256, 0:1024],
                          in_=QC[1][:, :]).then_inc(S["out"], 16))
            E("gpsimd", lambda e:
              e.dma_start(out=out_d[256:384, 0:512],
                          in_=Ks[0][:, 0:2, :]).then_inc(S["out"], 16))
            E("gpsimd", lambda e:
              e.dma_start(out=out_d[384:512, 0:1024],
                          in_=Vp[0][:, 0, 0:1024]).then_inc(S["out"], 16))
            E("gpsimd", lambda e:
              e.dma_start(out=out_d[512:640, 0:512],
                          in_=xinT[0][:, 0:2, 0:256])
              .then_inc(S["out"], 16))
            E("gpsimd", lambda e: e.wait_ge(S["out"], 80))
            _finish(block, K)
            return nc

        psA_free = {}
        x1_store = {}
        xin2_done = {}
        x0_used = {}
        for tt in range(16):
            reg = (tt % 4) * 1024
            if tt == 0:
                W("tensor", "ve", v_att_ve)
                W("tensor", "tr", v_att_tr)
            if tt >= 4:
                W("tensor", "ve", psA_free[tt - 4][0])
                W("tensor", "ap", psA_free[tt - 4][1])
            E("tensor", lambda e, reg=reg:
              e.matmul(ps[:, reg:reg + 512], onesb[0:1, 0:128],
                       brow[0:1, 1024:1536], start=True, stop=False))
            E("tensor", lambda e, reg=reg:
              e.matmul(ps[:, reg + 512:reg + 1024], onesb[0:1, 0:128],
                       brow[0:1, 1536:2048], start=True, stop=False))
            for g in range(8):
                for n2 in range(2):
                    last = g == 7 and n2 == 1
                    E("tensor", lambda e, tt=tt, g=g, n2=n2, reg=reg,
                      last=last:
                      (e.matmul(ps[:, reg + n2 * 512:reg + n2 * 512 + 512],
                                ocat[:, g, tt * 128:tt * 128 + 128],
                                woT[:, g, n2 * 512:n2 * 512 + 512],
                                start=False, stop=(g == 7))
                       .then_inc(S["te"], 1)) if last else
                      e.matmul(ps[:, reg + n2 * 512:reg + n2 * 512 + 512],
                               ocat[:, g, tt * 128:tt * 128 + 128],
                               woT[:, g, n2 * 512:n2 * 512 + 512],
                               start=False, stop=(g == 7)))
            vt = inc("te", 1)
            if stop_after == "p4mm":
                psA_free[tt] = (0, 0)
                x0_used[tt] = 0
                x1_store[tt] = 0
                xin2_done[tt] = 0
                continue
            if tt >= 2:
                W("sync", "ve", x0_used[tt - 2])
            xs = "x%d" % (tt % 2)
            vx0 = inc(xs, 16)
            E("sync", lambda e, tt=tt, xs=xs:
              e.dma_start(out=x0b[tt % 2][:, :],
                          in_=dr["x0"][tt * 128:(tt + 1) * 128, :])
              .then_inc(S[xs], 16))
            W("vector", "te", vt)
            E("vector", lambda e, reg=reg:
              e.reduce_sum(stat[:, 0:1], ps[:, reg:reg + 1024], axis=AX.X))
            E("vector", lambda e: e.memset(stat[:, 1:2], 0.0)
              .then_inc(S["ve"], 1))
            vv0 = inc("ve", 1)
            W("scalar", "te", vt)
            W("scalar", "ve", vv0)
            E("scalar", lambda e, reg=reg:
              e.activation(junk[:, :], ps[:, reg:reg + 1024], AF.Square,
                           accum_out=stat[:, 1:2]).then_inc(S["ap"], 1))
            va0 = inc("ap", 1)
            W("vector", "ap", va0)
            E("vector", lambda e: e.tensor_scalar(stat[:, 2:3], stat[:, 0:1],
                                                  -1.0 / D, None, ALU.mult))
            E("vector", lambda e: e.tensor_scalar(stat[:, 3:4], stat[:, 1:2],
                                                  1.0 / D, None, ALU.mult)
              .then_inc(S["ve"], 1))
            W("vector", "ve", inc("ve", 1))
            E("vector", lambda e: e.tensor_mul(stat[:, 4:5], stat[:, 2:3],
                                               stat[:, 2:3])
              .then_inc(S["ve"], 1))
            W("vector", "ve", inc("ve", 1))
            E("vector", lambda e: e.scalar_tensor_tensor(
                stat[:, 3:4], stat[:, 3:4], EPS, stat[:, 4:5], ALU.add,
                ALU.subtract).then_inc(S["ve"], 1))
            vv = inc("ve", 1)
            W("scalar", "ve", vv)
            E("scalar", lambda e: e.activation(stat[:, 4:5], stat[:, 3:4],
                                               AF.Sqrt).then_inc(S["ap"], 1))
            va = inc("ap", 1)
            W("vector", "ap", va)
            E("vector", lambda e: e.reciprocal(stat[:, 5:6], stat[:, 4:5])
              .then_inc(S["ve"], 1))
            W("vector", "ve", inc("ve", 1))
            E("vector", lambda e: e.tensor_mul(stat[:, 6:7], stat[:, 2:3],
                                               stat[:, 5:6])
              .then_inc(S["ve"], 1))
            vv = inc("ve", 1)
            W("scalar", "ve", vv)
            E("scalar", lambda e, reg=reg:
              e.activation(tmpL[:, :], ps[:, reg:reg + 1024], AF.Identity,
                           bias=stat[:, 6:7], scale=stat[:, 5:6])
              .then_inc(S["ap"], 1))
            va = inc("ap", 1)
            psA_free[tt] = (vv, va)
            W("vector", "ap", va)
            W("vector", xs, vx0)
            if tt >= 2:
                W("vector", "st%d" % (tt % 2), x1_store[tt - 2])
            E("vector", lambda e, tt=tt:
              e.tensor_add(x1b[tt % 2][:, :], tmpL[:, :], x0b[tt % 2][:, :])
              .then_inc(S["ve"], 1))
            vx1 = inc("ve", 1)
            x0_used[tt] = vx1
            W("sync", "ve", vx1)
            sts = "st%d" % (tt % 2)
            E("sync", lambda e, tt=tt, sts=sts:
              e.dma_start(out=x1_dram[tt * 128:(tt + 1) * 128, :],
                          in_=x1b[tt % 2][:, :]).then_inc(S[sts], 16))
            x1_store[tt] = inc(sts, 16)
            E("vector", lambda e, tt=tt:
              e.reduce_sum(stat[:, 8:9], x1b[tt % 2][:, :], axis=AX.X))
            E("vector", lambda e, tt=tt:
              e.tensor_mul(junk[:, :], x1b[tt % 2][:, :], x1b[tt % 2][:, :]))
            E("vector", lambda e:
              e.reduce_sum(stat[:, 9:10], junk[:, :], axis=AX.X))
            E("vector", lambda e:
              e.tensor_scalar(stat[:, 10:11], stat[:, 8:9], -1.0 / D, None,
                              ALU.mult))
            E("vector", lambda e:
              e.tensor_scalar(stat[:, 11:12], stat[:, 9:10], 1.0 / D, None,
                              ALU.mult).then_inc(S["ve"], 1))
            W("vector", "ve", inc("ve", 1))
            E("vector", lambda e: e.tensor_mul(stat[:, 12:13], stat[:, 10:11],
                                               stat[:, 10:11])
              .then_inc(S["ve"], 1))
            W("vector", "ve", inc("ve", 1))
            E("vector", lambda e: e.scalar_tensor_tensor(
                stat[:, 11:12], stat[:, 11:12], EPS, stat[:, 12:13], ALU.add,
                ALU.subtract).then_inc(S["ve"], 1))
            vv = inc("ve", 1)
            W("scalar", "ve", vv)
            E("scalar", lambda e: e.activation(stat[:, 12:13], stat[:, 11:12],
                                               AF.Sqrt).then_inc(S["ap"], 1))
            va = inc("ap", 1)
            W("vector", "ap", va)
            E("vector", lambda e: e.reciprocal(stat[:, 13:14], stat[:, 12:13])
              .then_inc(S["ve"], 1))
            W("vector", "ve", inc("ve", 1))
            E("vector", lambda e: e.tensor_mul(stat[:, 14:15], stat[:, 10:11],
                                               stat[:, 13:14])
              .then_inc(S["ve"], 1))
            vv = inc("ve", 1)
            W("scalar", "ve", vv)
            E("scalar", lambda e, tt=tt:
              e.activation(xin2[:, tt, :], x1b[tt % 2][:, :], AF.Identity,
                           bias=stat[:, 14:15], scale=stat[:, 13:14])
              .then_inc(S["ap"], 1))
            xin2_done[tt] = inc("ap", 1)

        if stop_after == "p4mm":
            W("vector", "te", val("te"))
            vv_ = inc("ve", 1)
            E("vector", lambda e: e.tensor_copy(oden[0:64, 0:1024],
                                                ps[0:64, 0:1024])
              .then_inc(S["ve"], 1))
            W("sync", "ve", val("ve"))
            W("sync", "te", val("te"))
            E("sync", lambda e:
              e.dma_start(out=out_d[0:64, :], in_=oden[0:64, 0:1024])
              .then_inc(S["out"], 16))
            E("sync", lambda e: e.wait_ge(S["out"], 16))
            _finish(block, K)
            return nc

        if stop_after == "p4ln2":
            W("gpsimd", "st0", val("st0"))
            W("gpsimd", "st1", val("st1"))
            E("gpsimd", lambda e:
              e.dma_start(out=out_d[:, :], in_=x1_dram[:, :])
              .then_inc(S["out"], 16))
            E("gpsimd", lambda e: e.wait_ge(S["out"], 16))
            _finish(block, K)
            return nc

        if stop_after == "p4":
            W("gpsimd", "st0", val("st0"))
            W("gpsimd", "st1", val("st1"))
            E("gpsimd", lambda e:
              e.dma_start(out=out_d[:, :], in_=x1_dram[:, :])
              .then_inc(S["out"], 16))
            E("gpsimd", lambda e: e.wait_ge(S["out"], 16))
            _finish(block, K)
            return nc

        w1_te = {}
        w2_te = {}
        gelu_done = {}
        psF_ap = {}
        out_tot = 0
        x1r_used = {}
        out_used = 0
        for tb in range(4):
            for k in range(4):
                tt = tb * 4 + k
                W("sync", "ap", xin2_done[tt])
                if tb >= 2:
                    W("sync", "te", w1_te[(tb - 2, 31)])
                ts_ = "t%d" % (tb % 2)
                E("sync", lambda e, tb=tb, k=k, tt=tt, ts_=ts_:
                  e.dma_start(out=xin2T[tb % 2][:, :, k * 128:k * 128 + 128],
                              in_=xin2[:, tt, :], transpose=True)
                  .then_inc(S[ts_], 16))
                inc(ts_, 16)
            v_x2t = val("t%d" % (tb % 2))
            if stop_after == "p5tr":
                break
            for jc in range(32):
                if (tb, jc - 2) in w1_te:
                    W("gpsimd", "te", w1_te[(tb, jc - 2)])
                elif tb > 0 and jc < 2:
                    W("gpsimd", "te", w1_te[(tb - 1, 30 + jc)])
                else:
                    W("gpsimd", "pv", v_att_pv)
                ws = "w5%d" % (jc % 2)
                vw = inc(ws, 16)
                E("gpsimd", lambda e, jc=jc, ws=ws:
                  e.dma_start(out=w1b[jc % 2][:, :, :],
                              in_=dr["w1"][:, jc * 128:(jc + 1) * 128]
                              .rearrange("(c p) j -> p c j", p=128))
                  .then_inc(S[ws], 16))
                reg = (jc % 2) * 512
                if stop_after == "p5w1":
                    continue
                W("tensor", ws, vw)
                W("tensor", "t%d" % (tb % 2), v_x2t)
                if (tb, jc - 2) in gelu_done:
                    W("tensor", "ap", gelu_done[(tb, jc - 2)])
                elif tb > 0 and jc < 2:
                    W("tensor", "ap", psF_ap[(tb - 1, 0)])
                for fc in range(8):
                    last = fc == 7
                    E("tensor", lambda e, jc=jc, fc=fc, reg=reg, tb=tb,
                      last=last:
                      (e.matmul(ps[:, reg:reg + 512], w1b[jc % 2][:, fc, :],
                                xin2T[tb % 2][:, fc, :],
                                start=(fc == 0), stop=True)
                       .then_inc(S["te"], 1)) if last else
                      e.matmul(ps[:, reg:reg + 512], w1b[jc % 2][:, fc, :],
                               xin2T[tb % 2][:, fc, :],
                               start=(fc == 0), stop=False))
                w1_te[(tb, jc)] = inc("te", 1)
                W("scalar", "te", w1_te[(tb, jc)])
                _gdst = (jkb[:, 0:512] if stop_after == "p5nog"
                         else gT[:, jc, :])
                E("scalar", lambda e, jc=jc, reg=reg, _gdst=_gdst:
                  e.activation(_gdst, ps[:, reg:reg + 512], AF.Gelu,
                               bias=b1c[:, jc:jc + 1]).then_inc(S["ap"], 1))
                gelu_done[(tb, jc)] = inc("ap", 1)
            if stop_after in ("p5w1", "p5nog", "p5mm1"):
                break
            W("tensor", "ap", gelu_done[(tb, 31)])
            for k in range(4):
                E("tensor", lambda e, k=k:
                  e.matmul(ps[:, k * 1024:k * 1024 + 512], onesb[0:1, 0:128],
                           brow[0:1, 2048:2560], start=True, stop=False))
                E("tensor", lambda e, k=k:
                  e.matmul(ps[:, k * 1024 + 512:k * 1024 + 1024],
                           onesb[0:1, 0:128], brow[0:1, 2560:3072],
                           start=True, stop=False))
            for jc in range(32):
                if (tb, jc - 2) in w2_te:
                    W("gpsimd", "te", w2_te[(tb, jc - 2)])
                else:
                    W("gpsimd", "te", w1_te[(tb, 30 + jc)])
                ws = "w5%d" % (jc % 2)
                vw = inc(ws, 16)
                E("gpsimd", lambda e, jc=jc, ws=ws:
                  e.dma_start(out=w2b[jc % 2][:, :],
                              in_=dr["w2"][jc * 128:(jc + 1) * 128, :])
                  .then_inc(S[ws], 16))
                W("tensor", ws, vw)
                for k in range(4):
                    for n2 in range(2):
                        last = k == 3 and n2 == 1
                        E("tensor", lambda e, jc=jc, k=k, n2=n2, last=last:
                          (e.matmul(ps[:, k * 1024 + n2 * 512:
                                       k * 1024 + n2 * 512 + 512],
                                    gT[:, jc, k * 128:k * 128 + 128],
                                    w2b[jc % 2][:, n2 * 512:n2 * 512 + 512],
                                    start=False, stop=(jc == 31))
                           .then_inc(S["te"], 1)) if last else
                          e.matmul(ps[:, k * 1024 + n2 * 512:
                                      k * 1024 + n2 * 512 + 512],
                                   gT[:, jc, k * 128:k * 128 + 128],
                                   w2b[jc % 2][:, n2 * 512:n2 * 512 + 512],
                                   start=False, stop=(jc == 31)))
                w2_te[(tb, jc)] = inc("te", 1)
            v_mm2 = val("te")
            if stop_after == "p5mm2":
                break
            last_ap = 0
            for k in range(4):
                tt = tb * 4 + k
                if tt >= 2:
                    W("sync", "ve", x1r_used.get(tt - 2, 0))
                W("sync", "st%d" % (tt % 2), x1_store[tt])
                xs = "x%d" % (tt % 2)
                vx1 = inc(xs, 16)
                E("sync", lambda e, tt=tt, xs=xs:
                  e.dma_start(out=x1rb[tt % 2][:, :],
                              in_=x1_dram[tt * 128:(tt + 1) * 128, :])
                  .then_inc(S[xs], 16))
                W("vector", "te", v_mm2)
                E("vector", lambda e, k=k:
                  e.reduce_sum(stat[:, 0:1], ps[:, k * 1024:k * 1024 + 1024],
                               axis=AX.X))
                E("vector", lambda e: e.memset(stat[:, 1:2], 0.0)
                  .then_inc(S["ve"], 1))
                vv0 = inc("ve", 1)
                W("scalar", "te", v_mm2)
                W("scalar", "ve", vv0)
                E("scalar", lambda e, k=k:
                  e.activation(junk[:, :], ps[:, k * 1024:k * 1024 + 1024],
                               AF.Square, accum_out=stat[:, 1:2])
                  .then_inc(S["ap"], 1))
                va0 = inc("ap", 1)
                W("vector", "ap", va0)
                E("vector", lambda e:
                  e.tensor_scalar(stat[:, 2:3], stat[:, 0:1], -1.0 / D, None,
                                  ALU.mult))
                E("vector", lambda e:
                  e.tensor_scalar(stat[:, 3:4], stat[:, 1:2], 1.0 / D, None,
                                  ALU.mult).then_inc(S["ve"], 1))
                W("vector", "ve", inc("ve", 1))
                E("vector", lambda e:
                  e.tensor_mul(stat[:, 4:5], stat[:, 2:3], stat[:, 2:3])
                  .then_inc(S["ve"], 1))
                W("vector", "ve", inc("ve", 1))
                E("vector", lambda e: e.scalar_tensor_tensor(
                    stat[:, 3:4], stat[:, 3:4], EPS, stat[:, 4:5], ALU.add,
                    ALU.subtract).then_inc(S["ve"], 1))
                vv = inc("ve", 1)
                W("scalar", "ve", vv)
                E("scalar", lambda e: e.activation(stat[:, 4:5], stat[:, 3:4],
                                                   AF.Sqrt)
                  .then_inc(S["ap"], 1))
                va = inc("ap", 1)
                W("vector", "ap", va)
                E("vector", lambda e: e.reciprocal(stat[:, 5:6], stat[:, 4:5])
                  .then_inc(S["ve"], 1))
                W("vector", "ve", inc("ve", 1))
                E("vector", lambda e:
                  e.tensor_mul(stat[:, 6:7], stat[:, 2:3], stat[:, 5:6])
                  .then_inc(S["ve"], 1))
                vv = inc("ve", 1)
                W("scalar", "ve", vv)
                E("scalar", lambda e, k=k:
                  e.activation(tmpL[:, :], ps[:, k * 1024:k * 1024 + 1024],
                               AF.Identity, bias=stat[:, 6:7],
                               scale=stat[:, 5:6]).then_inc(S["ap"], 1))
                va = inc("ap", 1)
                last_ap = va
                psF_ap[(tb, k)] = va
                W("vector", "ap", va)
                W("vector", xs, vx1)
                if out_used:
                    W("vector", "out", out_used)
                E("vector", lambda e, tt=tt:
                  e.tensor_add(outb[:, :], tmpL[:, :], x1rb[tt % 2][:, :])
                  .then_inc(S["ve"], 1))
                vo = inc("ve", 1)
                x1r_used[tt] = vo
                W("sync", "ve", vo)
                E("sync", lambda e, tt=tt:
                  e.dma_start(out=out_d[tt * 128:(tt + 1) * 128, :],
                              in_=outb[:, :]).then_inc(S["out"], 16))
                out_tot = inc("out", 16)
                out_used = out_tot


        if stop_after in ("p5tr", "p5w1", "p5nog", "p5mm1", "p5mm2"):
            W("sync", "st0", val("st0"))
            W("sync", "st1", val("st1"))
            vv_ = inc("ve", 1)
            E("vector", lambda e: e.tensor_copy(oden[0:64, 0:1024],
                                                junk[0:64, :])
              .then_inc(S["ve"], 1))
            W("sync", "ve", val("ve"))
            if stop_after in ("p5tr", "p5w1", "p5nog", "p5mm1"):
                W("sync", "ap", val("ap"))
            else:
                W("sync", "te", val("te"))
            E("sync", lambda e:
              e.dma_start(out=out_d[0:64, :], in_=oden[0:64, 0:1024])
              .then_inc(S["out"], 16))
            E("sync", lambda e: e.wait_ge(S["out"], 16))
            _finish(block, K)
            return nc
        W("sync", "out", out_tot)
        _finish(block, K)
        return nc


def _finish(block, K):
    @block.sync
    def _(e):
        for f in K.ops["sync"]:
            f(e)

    @block.tensor
    def _(e):
        for f in K.ops["tensor"]:
            f(e)

    @block.vector
    def _(e):
        for f in K.ops["vector"]:
            f(e)

    @block.scalar
    def _(e):
        for f in K.ops["scalar"]:
            f(e)

    @block.gpsimd
    def _(e):
        for f in K.ops["gpsimd"]:
            f(e)


_NC_CACHE = {}


def _get_nc():
    if "nc" not in _NC_CACHE:
        _NC_CACHE["nc"] = build_program()
    return _NC_CACHE["nc"]


def kernel(L0_attn_x, L0_attn_wq, L0_attn_bq, L0_attn_wk, L0_attn_bk,
           L0_attn_wv, L0_attn_bv, L0_attn_wo, L0_attn_bo, L0_attn_cos,
           L0_attn_sin, L0_ffn_w1, L0_ffn_b1, L0_ffn_w2, L0_ffn_b2):
    x = np.ascontiguousarray(np.asarray(L0_attn_x, dtype=np.float32))
    B = x.shape[0]
    eyeC = np.eye(128, dtype=np.float32)
    eyeU = np.zeros((128, 128), np.float32)
    eyeD = np.zeros((128, 128), np.float32)
    for p in range(0, 128, 2):
        eyeU[p, p + 1] = 1.0
    for p in range(1, 128, 2):
        eyeD[p, p - 1] = 1.0
    shared = dict(
        wq=np.ascontiguousarray(L0_attn_wq, np.float32),
        wk=np.ascontiguousarray(L0_attn_wk, np.float32),
        wv=np.ascontiguousarray(L0_attn_wv, np.float32),
        wo=np.ascontiguousarray(L0_attn_wo, np.float32),
        bq=np.ascontiguousarray(L0_attn_bq, np.float32),
        bk=np.ascontiguousarray(L0_attn_bk, np.float32),
        bv=np.ascontiguousarray(L0_attn_bv, np.float32),
        bo=np.ascontiguousarray(L0_attn_bo, np.float32),
        cosr=np.ascontiguousarray(np.asarray(L0_attn_cos)[3199], np.float32),
        sinr=np.ascontiguousarray(np.asarray(L0_attn_sin)[3199], np.float32),
        w1=np.ascontiguousarray(L0_ffn_w1, np.float32),
        b1=np.ascontiguousarray(L0_ffn_b1, np.float32),
        w2=np.ascontiguousarray(L0_ffn_w2, np.float32),
        b2=np.ascontiguousarray(L0_ffn_b2, np.float32),
        eyeC=eyeC, eyeU=eyeU, eyeD=eyeD,
    )
    in_maps = []
    for c in range(8):
        b, half = c // 2, c % 2
        xb = x[b]
        xpc = xb.reshape(16, 2, 128, D)[:, [half, 1 - half]].reshape(T, D)
        x0c = xb[half * TOWN + _TMAP]
        m = dict(shared)
        m["xp"] = np.ascontiguousarray(xpc)
        m["x0"] = np.ascontiguousarray(x0c)
        in_maps.append(m)

    nc = _get_nc()
    res = run_bass_kernel_spmd(nc, in_maps, core_ids=list(range(8)))
    out = np.empty((B, T, D), np.float32)
    inv = np.empty_like(_TMAP)
    inv[_TMAP] = np.arange(2048)
    for c in range(8):
        b, half = c // 2, c % 2
        out[b, half * TOWN:(half + 1) * TOWN] = res.results[c]["out"][inv]
    return out


# version 08
# speedup vs baseline: 1.0796x; 1.0131x over previous
import numpy as np
import ml_dtypes
import concourse.bass as bass
import concourse.mybir as mybir
from concourse.bass_utils import run_bass_kernel_spmd

F32 = mybir.dt.float32
BF16 = mybir.dt.bfloat16
AF = mybir.ActivationFunctionType
ALU = mybir.AluOpType
AX = mybir.AxisListType

D = 1024
T = 4096
TOWN = 2048
NSLAB = 16
NT = 32
EPS = 1e-5


def _tmap():
    q = np.arange(2048)
    P, u, j = q // 1024, (q % 1024) // 128, q % 128
    return j * 16 + 2 * u + P


_TMAP = _tmap()


class _K:

    def __init__(self):
        self.ops = {e: [] for e in ("sync", "tensor", "vector", "scalar",
                                    "gpsimd")}
        self.ct = {}

    def E(self, eng, fn):
        self.ops[eng].append(fn)

    def inc(self, sem, n):
        self.ct[sem] = self.ct.get(sem, 0) + n
        return self.ct[sem]

    def val(self, sem):
        return self.ct.get(sem, 0)


def build_program(stop_after=None):
    nc = bass.Bass("TRN2", target_bir_lowering=False,
                   detect_race_conditions=False)
    K = _K()
    E, inc, val = K.E, K.inc, K.val

    dr = {}
    for nm, shp in [("xp", [T, D]), ("x0", [TOWN, D]),
                    ("wq", [D, D]), ("wk", [D, D]), ("wv", [D, D]),
                    ("wo", [D, D]), ("bq", [D]), ("bk", [D]), ("bv", [D]),
                    ("bo", [D]), ("cosr", [64]), ("sinr", [64]),
                    ("w1", [D, 4096]), ("b1", [4096]), ("w2", [4096, D]),
                    ("b2", [D]), ("eyeC", [128, 128]), ("eyeU", [128, 128]),
                    ("eyeD", [128, 128])]:
        dr[nm] = nc.dram_tensor(nm, shp, F32, kind="ExternalInput").ap()
    out_d = nc.dram_tensor("out", [TOWN, D], F32, kind="ExternalOutput").ap()
    x1_dram = nc.dram_tensor("x1s", [TOWN, D], F32, kind="Internal").ap()

    NBF = 81088
    NF32 = 12352

    bo_ = [0]

    def ab(n):
        o = bo_[0]
        bo_[0] += n
        return o

    o_woT = ab(8 * 1024)
    o_ocat = ab(8 * 2048)
    o_otmp = ab(2048)
    o_xinb = ab(4 * 1024)
    o_onesb = ab(128)
    o_brow = ab(3 * 1024)
    _pbf = bo_[0]
    o_wqT = ab(8 * 1024)
    o_wkT = ab(8 * 1024)
    o_wvT = ab(8 * 1024)
    o_xinT = [ab(8 * 256), ab(8 * 256)]
    o_Ks = [ab(8 * 256), ab(8 * 256)]
    o_Vp = [ab(2 * 1040), ab(2 * 1040)]
    o_QC = [ab(1024), ab(1024)]
    o_QW = [ab(1024), ab(1024)]
    o_PT = [ab(2048), ab(2048), ab(2048)]
    _endA = bo_[0]
    bo_[0] = _pbf
    o_xin2 = ab(16 * 1024)
    o_gT = ab(32 * 512)
    o_xin2T = [ab(8 * 512), ab(8 * 512)]
    o_w1b = [ab(8 * 128), ab(8 * 128)]
    o_w2b = [ab(1024), ab(1024)]
    _endB = bo_[0]
    assert max(_endA, _endB) <= NBF, (_endA, _endB)

    fo_ = [0]

    def af(n):
        o = fo_[0]
        fo_[0] += n
        return o

    o_oden = af(2048)
    o_dspr = af(16)
    o_stat = af(64)
    o_ln1 = af(6 * 32)
    o_bqr = af(8)
    o_bkr = af(8)
    o_b1c = af(32)
    o_cs = af(8)
    o_onesf = af(64)
    o_mrt = af(128)
    o_mrtq = af(128)
    o_eye = af(3 * 128)
    _pf = fo_[0]
    o_xpb = [af(1024), af(1024)]
    o_wst = af(1024)
    _endFA = fo_[0]
    fo_[0] = _pf
    o_x1rb = [af(1024), af(1024)]
    o_outb = af(1024)
    o_junk = af(1024)
    _endFA2 = fo_[0]
    fo_[0] = max(_endFA, _endFA2)
    o_x0b = [af(1024), af(1024)]
    o_x1b = [af(1024), af(1024)]
    o_tmpL = af(1024)
    assert fo_[0] <= NF32, fo_[0]

    import contextlib
    with contextlib.ExitStack() as _st:
        ABh = _st.enter_context(nc.sbuf_tensor("AB", [128, NBF], BF16))
        AFh = _st.enter_context(nc.sbuf_tensor("AF", [128, NF32], F32))
        ps = _st.enter_context(nc.psum_tensor("ps", [128, 4096], F32))
        S = {}
        for _snm in ("wg", "wq2", "wvt", "t0", "t1", "x0", "x1", "st0",
                     "st1", "w50", "w51", "tr", "te", "sc", "pv", "ex",
                     "ve", "ap", "out"):
            S[_snm] = _st.enter_context(nc.semaphore("s_" + _snm))
        block = _st.enter_context(nc.Block())

        def B(off, n):
            return ABh[:, off:off + n]

        def Fv(off, n):
            return AFh[:, off:off + n]

        woT = B(o_woT, 8192).rearrange("p (c n) -> p c n", c=8)
        ocat = B(o_ocat, 16384).rearrange("p (c n) -> p c n", c=8)
        otmp = B(o_otmp, 2048)
        xinb = B(o_xinb, 4096)
        onesb = B(o_onesb, 128)
        brow = B(o_brow, 3072)
        wqT = B(o_wqT, 8192).rearrange("p (c n) -> p c n", c=8)
        wkT = B(o_wkT, 8192).rearrange("p (c n) -> p c n", c=8)
        wvT = B(o_wvT, 8192).rearrange("p (c n) -> p c n", c=8)
        xinT = [B(o, 2048).rearrange("p (c n) -> p c n", c=8) for o in o_xinT]
        Ks = [B(o, 2048).rearrange("p (c n) -> p c n", c=8) for o in o_Ks]
        Vp = [B(o, 2080).rearrange("p (c n) -> p c n", c=2) for o in o_Vp]
        QC = [B(o, 1024) for o in o_QC]
        QW = [B(o, 1024) for o in o_QW]
        PT = [B(o, 2048) for o in o_PT]
        xin2 = B(o_xin2, 16384).rearrange("p (c n) -> p c n", c=16)
        gT = B(o_gT, 16384).rearrange("p (c n) -> p c n", c=32)
        xin2T = [B(o, 4096).rearrange("p (c n) -> p c n", c=8)
                 for o in o_xin2T]
        w1b = [B(o, 1024).rearrange("p (c n) -> p c n", c=8) for o in o_w1b]
        w2b = [B(o, 1024) for o in o_w2b]
        jkb = B(o_otmp, 1024)

        oden = Fv(o_oden, 2048)
        dspr = Fv(o_eye, 256)
        stat = Fv(o_stat, 64)
        ln1 = Fv(o_ln1, 192).rearrange("p (c n) -> p c n", c=6)
        bqr = Fv(o_bqr, 8)
        bkr = Fv(o_bkr, 8)
        b1c = Fv(o_b1c, 32)
        cs = Fv(o_cs, 8)
        onesf = Fv(o_onesf, 64)
        mrt = Fv(o_mrt, 128)
        mrtq = Fv(o_mrtq, 128)
        eye = Fv(o_eye, 384)
        xpb = [Fv(o, 1024) for o in o_xpb]
        wst = Fv(o_wst, 1024)
        x1rb = [Fv(o, 1024) for o in o_x1rb]
        outb = Fv(o_outb, 1024)
        junk = Fv(o_junk, 1024)
        x0b = [Fv(o, 1024) for o in o_x0b]
        x1b = [Fv(o, 1024) for o in o_x1b]
        tmpL = Fv(o_tmpL, 1024)

        def W(eng, sem, v):
            if v > 0:
                E(eng, lambda e, s=S[sem], v=v: e.wait_ge(s, v))

        def gdma(dst, src, slow=False):
            v = inc("wg", 16)
            E("gpsimd", lambda e, d=dst, s=src, slow=slow:
              e.dma_start(out=d, in_=s, allow_slow_non_contiguous=slow)
              .then_inc(S["wg"], 16))
            return v

        gdma(cs[0:64, 0:1], dr["cosr"][:].rearrange("(a c) -> c a", a=1))
        gdma(cs[64:128, 0:1], dr["cosr"][:].rearrange("(a c) -> c a", a=1))
        gdma(cs[0:64, 3:4], dr["sinr"][:].rearrange("(a c) -> c a", a=1))
        gdma(cs[64:128, 3:4], dr["sinr"][:].rearrange("(a c) -> c a", a=1))
        gdma(eye[:, 0:128], dr["eyeC"][:, :])
        gdma(eye[:, 128:256], dr["eyeU"][:, :])
        gdma(eye[:, 256:384], dr["eyeD"][:, :])
        gdma(bqr[:, 0:8], dr["bq"][:].rearrange("(c p) -> p c", p=128), slow=True)
        gdma(bkr[:, 0:8], dr["bk"][:].rearrange("(c p) -> p c", p=128), slow=True)
        gdma(b1c[:, 0:32], dr["b1"][:].rearrange("(c p) -> p c", p=128), slow=True)
        gdma(brow[0:1, 0:1024], dr["bv"][:].rearrange("(a n) -> a n", a=1))
        gdma(brow[0:1, 1024:2048], dr["bo"][:].rearrange("(a n) -> a n", a=1))
        gdma(brow[0:1, 2048:3072], dr["b2"][:].rearrange("(a n) -> a n", a=1))
        v_bias = val("wg")

        E("vector", lambda e: e.memset(cs[:, 1:3], 0.0).then_inc(S["ve"], 1))
        v_cz = inc("ve", 1)
        W("sync", "ve", v_cz)
        W("sync", "wg", v_bias)
        E("sync", lambda e: e.dma_start(out=cs[0:127, 1:2], in_=cs[1:128, 3:4])
          .then_inc(S["tr"], 16))
        E("sync", lambda e: e.dma_start(out=cs[1:128, 2:3], in_=cs[0:127, 3:4])
          .then_inc(S["tr"], 16))
        v_csh = inc("tr", 32)

        E("vector", lambda e: e.memset(onesb[:, :], 1.0))
        E("vector", lambda e: e.memset(onesf[:, :], 1.0))
        E("vector", lambda e: e.memset(Vp[0][:, :, :], 1.0))
        E("vector", lambda e: e.memset(Vp[1][:, :, :], 1.0))
        W("vector", "wg", v_bias)
        W("vector", "tr", v_csh)
        E("vector", lambda e: e.tensor_scalar(mrt[:, :], eye[:, 0:128],
                                              cs[:, 0:1], None, ALU.mult))
        E("vector", lambda e: e.tensor_scalar(eye[:, 128:256], eye[:, 128:256],
                                              cs[:, 1:2], None, ALU.mult))
        E("vector", lambda e: e.tensor_add(mrt[:, :], mrt[:, :],
                                           eye[:, 128:256]))
        E("vector", lambda e: e.tensor_scalar(eye[:, 256:384], eye[:, 256:384],
                                              cs[:, 2:3], None, ALU.mult))
        E("vector", lambda e: e.tensor_sub(mrt[:, :], mrt[:, :],
                                           eye[:, 256:384]))
        E("vector", lambda e: e.tensor_scalar(mrtq[:, :], mrt[:, :], 0.125,
                                              None, ALU.mult)
          .then_inc(S["ve"], 1))
        v_mrot = inc("ve", 1)

        v_wv_tr = {}
        for wname, dstT in (("wv", wvT), ("wo", woT)):
            for cc in range(8):
                vg = inc("wq2", 16)
                E("gpsimd", lambda e, wname=wname, cc=cc:
                  e.dma_start(out=xinb[:, 0:1024],
                              in_=dr[wname][cc * 128:(cc + 1) * 128, :])
                  .then_inc(S["wq2"], 16))
                W("sync", "wq2", vg)
                E("sync", lambda e, dstT=dstT, cc=cc:
                  e.dma_start(out=dstT[:, :, cc * 128:cc * 128 + 128],
                              in_=xinb[:, 0:1024], transpose=True)
                  .then_inc(S["wvt"], 16))
                vtr = inc("wvt", 16)
                W("gpsimd", "wvt", vtr)
        v_wvwo = val("wvt")

        rot_ap = {}

        def rot_weight(wname, rhs_ap, dstT, bias_col, prev_te, prev_ve):
            for cc in range(8):
                if cc >= 1:
                    W("gpsimd", "te", rot_ap[("te", wname, cc - 1)])
                elif prev_te:
                    W("gpsimd", "te", prev_te)
                vw = inc("wq2", 16)
                E("gpsimd", lambda e, wname=wname, cc=cc:
                  e.dma_start(out=wst[:, :],
                              in_=dr[wname][cc * 128:(cc + 1) * 128, :])
                  .then_inc(S["wq2"], 16))
                W("tensor", "ve", v_mrot)
                W("tensor", "wq2", vw)
                if cc >= 2:
                    W("tensor", "ap", rot_ap[("ap", wname, cc - 2)])
                reg = (cc % 2) * 1024
                for fc in range(8):
                    last = fc == 7
                    E("tensor", lambda e, cc=cc, fc=fc, reg=reg, last=last,
                      rhs_ap=rhs_ap:
                      (e.matmul(ps[:, reg + fc * 128:reg + fc * 128 + 128],
                                wst[:, fc * 128:(fc + 1) * 128],
                                rhs_ap, start=True, stop=True)
                       .then_inc(S["te"], 1)) if last else
                      e.matmul(ps[:, reg + fc * 128:reg + fc * 128 + 128],
                               wst[:, fc * 128:(fc + 1) * 128],
                               rhs_ap, start=True, stop=True))
                rot_ap[("te", wname, cc)] = inc("te", 1)
                W("scalar", "te", rot_ap[("te", wname, cc)])
                E("scalar", lambda e, cc=cc, reg=reg, dstT=dstT:
                  e.activation(dstT[:, :, cc * 128:cc * 128 + 128],
                               ps[:, reg:reg + 1024]
                               .rearrange("p (c n) -> p c n", c=8),
                               AF.Identity).then_inc(S["ap"], 1))
                rot_ap[("ap", wname, cc)] = inc("ap", 1)
            W("tensor", "wg", v_bias)
            if prev_ve:
                W("tensor", "ve", prev_ve)
            for cc in range(8):
                last = cc == 7
                E("tensor", lambda e, cc=cc, rhs_ap=rhs_ap, bias_col=bias_col,
                  last=last:
                  (e.matmul(ps[:, 2048 + cc:2048 + cc + 1], rhs_ap,
                            bias_col[:, cc:cc + 1], start=True, stop=True)
                   .then_inc(S["te"], 1)) if last else
                  e.matmul(ps[:, 2048 + cc:2048 + cc + 1], rhs_ap,
                           bias_col[:, cc:cc + 1], start=True, stop=True))
            vt = inc("te", 1)
            W("vector", "te", vt)
            E("vector", lambda e, bias_col=bias_col:
              e.tensor_copy(bias_col[:, 0:8], ps[:, 2048:2056])
              .then_inc(S["ve"], 1))
            return inc("ve", 1)

        v_bqrot = rot_weight("wq", mrtq[:, :], wqT, bqr, 0, 0)
        v_bkrot = rot_weight("wk", mrt[:, :], wkT, bkr,
                             rot_ap[("te", "wq", 7)], v_bqrot)
        v_wrot_ap = val("ap")

        E("vector", lambda e: e.memset(ln1[:, :, :], 0.0).then_inc(S["ve"], 1))
        v_lnz = inc("ve", 1)
        pp_ve = {}
        pp_ap = {}
        v_ln1_c = {}
        ppb = [xpb[0], xpb[1], x0b[0], x0b[1]]
        ppsem = ["x0", "x1", "st0", "st1"]
        for i in range(NT):
            if i >= 4:
                W("sync", "ve", pp_ve[i - 4])
                W("sync", "ap", pp_ap[i - 4])
            xs = ppsem[i % 4]
            vx = inc(xs, 16)
            E("sync", lambda e, i=i, xs=xs:
              e.dma_start(out=ppb[i % 4][:, :],
                          in_=dr["xp"][i * 128:(i + 1) * 128, :])
              .then_inc(S[xs], 16))
            W("vector", xs, vx)
            if i == 0:
                W("vector", "ve", v_lnz)
            E("vector", lambda e, i=i:
              e.reduce_sum(ln1[:, 0, i:i + 1], ppb[i % 4][:, :], axis=AX.X)
              .then_inc(S["ve"], 1))
            pp_ve[i] = inc("ve", 1)
            W("scalar", xs, vx)
            if i == 0:
                W("scalar", "ve", v_lnz)
            E("scalar", lambda e, i=i:
              e.activation(jkb[:, :], ppb[i % 4][:, :], AF.Square,
                           accum_out=ln1[:, 1, i:i + 1]).then_inc(S["ap"], 1))
            pp_ap[i] = inc("ap", 1)
            if i % 8 == 7:
                c = i // 8
                lo, hi = c * 8, c * 8 + 8
                W("vector", "ap", pp_ap[i])
                E("vector", lambda e, lo=lo, hi=hi:
                  e.tensor_scalar(ln1[:, 2, lo:hi], ln1[:, 0, lo:hi],
                                  -1.0 / D, None, ALU.mult))
                E("vector", lambda e, lo=lo, hi=hi:
                  e.tensor_scalar(ln1[:, 3, lo:hi], ln1[:, 1, lo:hi],
                                  1.0 / D, None, ALU.mult)
                  .then_inc(S["ve"], 1))
                W("vector", "ve", inc("ve", 1))
                E("vector", lambda e, lo=lo, hi=hi:
                  e.tensor_mul(ln1[:, 4, lo:hi], ln1[:, 2, lo:hi],
                               ln1[:, 2, lo:hi]).then_inc(S["ve"], 1))
                W("vector", "ve", inc("ve", 1))
                E("vector", lambda e, lo=lo, hi=hi:
                  e.scalar_tensor_tensor(ln1[:, 3, lo:hi], ln1[:, 3, lo:hi],
                                         EPS, ln1[:, 4, lo:hi], ALU.add,
                                         ALU.subtract).then_inc(S["ve"], 1))
                vv = inc("ve", 1)
                W("scalar", "ve", vv)
                E("scalar", lambda e, lo=lo, hi=hi:
                  e.activation(ln1[:, 4, lo:hi], ln1[:, 3, lo:hi], AF.Sqrt)
                  .then_inc(S["ap"], 1))
                va = inc("ap", 1)
                W("vector", "ap", va)
                E("vector", lambda e, lo=lo, hi=hi:
                  e.reciprocal(ln1[:, 3, lo:hi], ln1[:, 4, lo:hi])
                  .then_inc(S["ve"], 1))
                W("vector", "ve", inc("ve", 1))
                E("vector", lambda e, lo=lo, hi=hi:
                  e.tensor_mul(ln1[:, 2, lo:hi], ln1[:, 2, lo:hi],
                               ln1[:, 3, lo:hi]).then_inc(S["ve"], 1))
                v_ln1_c[c] = inc("ve", 1)
        v_ln1 = v_ln1_c[3]

        ap_apply = {}
        tr_xin = {}

        def ln1_apply_and_transpose(s, xinT_guard=None):
            for k in range(2):
                i = 2 * s + k
                if i >= 2:
                    W("sync", "ve", ap_apply[i - 2])
                else:
                    W("sync", "ve", pp_ve[NT - 2 + i])
                    W("sync", "ap", pp_ap[NT - 2 + i])
                xs = "st%d" % (i % 2)
                vx = inc(xs, 16)
                E("sync", lambda e, i=i, xs=xs:
                  e.dma_start(out=x0b[i % 2][:, :],
                              in_=dr["xp"][i * 128:(i + 1) * 128, :])
                  .then_inc(S[xs], 16))
                W("vector", xs, vx)
                W("vector", "ve", v_ln1_c[i // 8])
                if s >= 2:
                    W("vector", "t%d" % (s % 2), tr_xin[(s - 2, k)])
                else:
                    W("vector", "wvt", v_wvwo)
                E("vector", lambda e, i=i:
                  e.tensor_scalar(xinb[:, (i % 4) * 1024:(i % 4) * 1024 + 1024],
                                  x0b[i % 2][:, :], ln1[:, 3, i:i + 1],
                                  ln1[:, 2, i:i + 1], ALU.mult, ALU.add)
                  .then_inc(S["ve"], 1))
                ap_apply[i] = inc("ve", 1)
                W("sync", "ve", ap_apply[i])
                if xinT_guard is not None:
                    W("sync", "ve", xinT_guard)
                ts_ = "t%d" % (s % 2)
                E("sync", lambda e, s=s, k=k, i=i, ts_=ts_:
                  e.dma_start(out=xinT[s % 2][:, :, k * 128:k * 128 + 128],
                              in_=xinb[:, (i % 4) * 1024:(i % 4) * 1024 + 1024],
                              transpose=True).then_inc(S[ts_], 16))
                tr_xin[(s, k)] = inc(ts_, 16)

        def proj_slab(s):
            sb = s % 2
            W("tensor", "t%d" % sb, tr_xin[(s, 1)])
            W("tensor", "ap", v_wrot_ap)
            W("tensor", "wvt", v_wvwo)
            W("tensor", "ve", v_bkrot)
            ev = {}
            for cc in range(8):
                reg = 2048 + (cc % 2) * 1024
                if cc >= 2:
                    W("tensor", "ve", ev[cc - 2])
                for fc in range(8):
                    last = fc == 7
                    E("tensor", lambda e, cc=cc, fc=fc, reg=reg, sb=sb,
                      last=last:
                      (e.matmul(ps[:, reg:reg + 256],
                                wkT[:, fc, cc * 128:cc * 128 + 128],
                                xinT[sb][:, fc, 0:256],
                                start=(fc == 0), stop=True)
                       .then_inc(S["te"], 1)) if last else
                      e.matmul(ps[:, reg:reg + 256],
                               wkT[:, fc, cc * 128:cc * 128 + 128],
                               xinT[sb][:, fc, 0:256],
                               start=(fc == 0), stop=False))
                vt = inc("te", 1)
                W("vector", "te", vt)
                E("vector", lambda e, cc=cc, reg=reg, sb=sb:
                  e.tensor_scalar(Ks[sb][:, cc, :], ps[:, reg:reg + 256],
                                  bkr[:, cc:cc + 1], None, ALU.add)
                  .then_inc(S["ve"], 1))
                ev[cc] = inc("ve", 1)
            for rt in range(2):
                for n2 in range(2):
                    idx = rt * 2 + n2
                    reg = 2048 + (idx % 2) * 1024
                    W("tensor", "ve", ev[6 + idx] if idx < 2 else ev[8 + idx - 2])
                    E("tensor", lambda e, n2=n2, reg=reg:
                      e.matmul(ps[:, reg:reg + 512], onesb[0:1, 0:128],
                               brow[0:1, n2 * 512:n2 * 512 + 512],
                               start=True, stop=False))
                    for fc in range(8):
                        last = fc == 7
                        E("tensor", lambda e, rt=rt, n2=n2, fc=fc, reg=reg,
                          sb=sb, last=last:
                          (e.matmul(ps[:, reg:reg + 512],
                                    xinT[sb][:, fc, rt * 128:rt * 128 + 128],
                                    wvT[:, fc, n2 * 512:n2 * 512 + 512],
                                    start=False, stop=True)
                           .then_inc(S["te"], 1)) if last else
                          e.matmul(ps[:, reg:reg + 512],
                                   xinT[sb][:, fc, rt * 128:rt * 128 + 128],
                                   wvT[:, fc, n2 * 512:n2 * 512 + 512],
                                   start=False, stop=False))
                    vt = inc("te", 1)
                    W("vector", "te", vt)
                    E("vector", lambda e, rt=rt, n2=n2, reg=reg, sb=sb:
                      e.tensor_copy(
                          Vp[sb][:, rt, :].rearrange("p (a b) -> p a b", b=65)
                          [:, n2 * 8:(n2 + 1) * 8, 0:64],
                          ps[:, reg:reg + 512]
                          .rearrange("p (a b) -> p a b", b=64))
                      .then_inc(S["ve"], 1))
                    ev[8 + idx] = inc("ve", 1)
            v_v = val("ve")
            qap = {}
            for cc in range(8):
                reg = 2048 + (cc % 2) * 1024
                if cc < 2:
                    W("tensor", "ve", ev[10 + cc])
                else:
                    W("tensor", "ve", qap[cc - 2])
                for fc in range(8):
                    last = fc == 7
                    E("tensor", lambda e, cc=cc, fc=fc, reg=reg, sb=sb,
                      last=last:
                      (e.matmul(ps[:, reg:reg + 128],
                                wqT[:, fc, cc * 128:cc * 128 + 128],
                                xinT[sb][:, fc, 0:128],
                                start=(fc == 0), stop=True)
                       .then_inc(S["te"], 1)) if last else
                      e.matmul(ps[:, reg:reg + 128],
                               wqT[:, fc, cc * 128:cc * 128 + 128],
                               xinT[sb][:, fc, 0:128],
                               start=(fc == 0), stop=False))
                vt = inc("te", 1)
                W("vector", "te", vt)
                E("vector", lambda e, cc=cc, reg=reg, sb=sb:
                  e.tensor_scalar(QC[sb][0:64, cc * 128:cc * 128 + 128],
                                  ps[0:64, reg:reg + 128],
                                  bqr[0:64, cc:cc + 1], None, ALU.add))
                E("vector", lambda e, cc=cc, reg=reg, sb=sb:
                  e.tensor_scalar(QC[sb][64:128, cc * 128:cc * 128 + 128],
                                  ps[64:128, reg:reg + 128],
                                  bqr[64:128, cc:cc + 1], None, ALU.add)
                  .then_inc(S["ve"], 2))
                qap[cc] = inc("ve", 2)
            vq = val("ve")
            W("sync", "ve", vq)
            ts_ = "t%d" % sb
            E("sync", lambda e, sb=sb, ts_=ts_:
              e.dma_start(out=QW[sb][0:64, :], in_=QC[sb][64:128, :])
              .then_inc(S[ts_], 16))
            E("sync", lambda e, sb=sb, ts_=ts_:
              e.dma_start(out=QW[sb][64:128, :], in_=QC[sb][0:64, :])
              .then_inc(S[ts_], 16))
            vqw = inc(ts_, 32)
            return dict(k=ev[7], v=v_v, q=vq, qw=vqw, sb=sb)

        def attention_slab(s, pr, prev_mult, prev_pv, guard):
            sb = s % 2
            sc_d, ex_d, pv_d = {}, {}, {}

            def tile_of(n):
                P, g, t = n // 32, (n % 32) // 2, n % 2
                m, kh = g // 2, g % 2
                t_lo = 2 * m + t
                return P, t_lo, kh

            def scores(n):
                P, t_lo, kh = tile_of(n)
                pb = (t_lo % 2) * 64
                src_q = QC[sb] if (P == t_lo % 2) else QW[sb]
                bb = 4 + (n % 2) * 2
                if n == 0:
                    W("tensor", "ve", pr["k"])
                    W("tensor", "ve", pr["q"])
                    W("tensor", "t%d" % pr["sb"], pr["qw"])
                    if guard is not pr:
                        W("tensor", "ve", guard["q"])
                        W("tensor", "ve", guard["v"])
                if n >= 2:
                    W("tensor", "ex", ex_d[n - 2])
                for half in range(2):
                    last = half == 1
                    E("tensor", lambda e, P=P, t_lo=t_lo, kh=kh, pb=pb,
                      src_q=src_q, bb=bb, half=half, sb=sb, last=last:
                      (e.matmul(ps[:, (bb + half) * 512:(bb + half) * 512 + 512],
                                Ks[sb][pb:pb + 64, t_lo // 2,
                                       kh * 128:kh * 128 + 128],
                                src_q[pb:pb + 64,
                                      half * 512:half * 512 + 512],
                                start=True, stop=True)
                       .then_inc(S["sc"], 1)) if last else
                      e.matmul(ps[:, (bb + half) * 512:(bb + half) * 512 + 512],
                               Ks[sb][pb:pb + 64, t_lo // 2,
                                      kh * 128:kh * 128 + 128],
                               src_q[pb:pb + 64,
                                     half * 512:half * 512 + 512],
                               start=True, stop=True))
                sc_d[n] = inc("sc", 1)

            def expc(n):
                bb = 4 + (n % 2) * 2
                po = (n % 2) * 1024
                W("scalar", "sc", sc_d[n])
                if n >= 6:
                    W("scalar", "pv", pv_d[n - 6])
                elif prev_pv:
                    W("scalar", "pv", prev_pv)
                E("scalar", lambda e, n=n, bb=bb, po=po:
                  e.activation(PT[(n // 2) % 3][:, po:po + 1024],
                               ps[:, bb * 512:bb * 512 + 1024],
                               AF.Exp).then_inc(S["ex"], 1))
                ex_d[n] = inc("ex", 1)

            def pv(n):
                P, t_lo, kh = tile_of(n)
                po = (n % 2) * 1024
                W("tensor", "ex", ex_d[n])
                if n == 0:
                    W("tensor", "ve", prev_mult)
                    W("tensor", "ve", pr["v"])
                first = (n % 32) == 0
                last_g = (n % 32) == 31
                for half in range(2):
                    last = half == 1
                    E("tensor", lambda e, P=P, t_lo=t_lo, kh=kh, po=po, n=n,
                      half=half, sb=sb, first=first, last_g=last_g, last=last:
                      (e.matmul(ps[0:65, P * 1024 + half * 512:
                                   P * 1024 + half * 512 + 512],
                                Vp[sb][:, kh, t_lo * 65:t_lo * 65 + 65],
                                PT[(n // 2) % 3][:, po + half * 512:
                                                 po + half * 512 + 512],
                                start=first, stop=last_g)
                       .then_inc(S["pv"], 1)) if last else
                      e.matmul(ps[0:65, P * 1024 + half * 512:
                                  P * 1024 + half * 512 + 512],
                               Vp[sb][:, kh, t_lo * 65:t_lo * 65 + 65],
                               PT[(n // 2) % 3][:, po + half * 512:
                                                po + half * 512 + 512],
                               start=first, stop=last_g))
                pv_d[n] = inc("pv", 1)

            scores(0)
            scores(1)
            expc(0)
            expc(1)
            for n in range(2, 64, 2):
                scores(n)
                scores(n + 1)
                expc(n)
                expc(n + 1)
                pv(n - 2)
                pv(n - 1)
            pv(62)
            pv(63)
            return pv_d, ex_d

        def head_finish(s, pv_d):
            W("vector", "pv", pv_d[63])
            if s % 2 == 0:
                E("vector", lambda e, s=s:
                  e.tensor_copy(ocat[0:64, s // 2, :], ps[0:64, 0:2048])
                  .then_inc(S["ve"], 1))
            else:
                E("vector", lambda e:
                  e.tensor_copy(otmp[0:64, :], ps[0:64, 0:2048])
                  .then_inc(S["ve"], 1))
            inc("ve", 1)
            E("vector", lambda e: e.tensor_copy(oden[64:65, :],
                                                ps[64:65, 0:2048])
              .then_inc(S["ve"], 1))
            ve_evac = inc("ve", 1)
            if s % 2 == 1:
                W("sync", "ve", ve_evac)
                E("sync", lambda e, s=s:
                  e.dma_start(out=ocat[64:128, s // 2, :], in_=otmp[0:64, :])
                  .then_inc(S["tr"], 16))
                inc("tr", 16)
            W("sync", "ve", ve_evac)
            E("sync", lambda e, s=s:
              e.dma_start(out=dspr[:, s * 16:(s + 1) * 16],
                          in_=oden[64:65, :]).then_inc(S["tr"], 16))
            v_spread = inc("tr", 16)
            return dict(evac=ve_evac, spread=v_spread)

        def head_dbcast_mult(s, hf):
            W("tensor", "tr", hf["back"])
            W("tensor", "ve", hf["evac"])
            for c in range(4):
                last = c == 3
                E("tensor", lambda e, c=c, last=last:
                  (e.matmul(ps[0:64, c * 512:(c + 1) * 512], onesf[64:65, 0:64],
                            oden[64:65, c * 512:(c + 1) * 512],
                            start=True, stop=True).then_inc(S["te"], 1))
                  if last else
                  e.matmul(ps[0:64, c * 512:(c + 1) * 512], onesf[64:65, 0:64],
                           oden[64:65, c * 512:(c + 1) * 512],
                           start=True, stop=True))
            vt = inc("te", 1)
            W("vector", "te", vt)
            if s % 2 == 0:
                E("vector", lambda e, s=s:
                  e.tensor_mul(ocat[0:64, s // 2, :], oden[0:64, :],
                               ps[0:64, 0:2048]).then_inc(S["ve"], 1))
                vm = inc("ve", 1)
                return vm, None
            E("vector", lambda e:
              e.tensor_mul(otmp[0:64, :], oden[0:64, :], ps[0:64, 0:2048])
              .then_inc(S["ve"], 1))
            vm = inc("ve", 1)
            W("sync", "ve", vm)
            E("sync", lambda e, s=s:
              e.dma_start(out=ocat[64:128, s // 2, :], in_=otmp[0:64, :])
              .then_inc(S["tr"], 16))
            return vm, inc("tr", 16)

        prs = {}
        ln1_apply_and_transpose(0)
        prs[0] = proj_slab(0)
        ln1_apply_and_transpose(1)
        prs[1] = proj_slab(1)
        ln1_apply_and_transpose(2, xinT_guard=prs[0]["q"])
        ln1_apply_and_transpose(3, xinT_guard=prs[1]["q"])
        prev_mult = 0
        prev_pv = None
        for s in range(NSLAB):
            guard = prs[min(s + 1, NSLAB - 1)]
            pv_d, ex_d = attention_slab(s, prs[s], prev_mult, prev_pv, guard)
            hf = head_finish(s, pv_d)
            if s + 2 < NSLAB:
                prs[s + 2] = proj_slab(s + 2)
            if s + 4 < NSLAB:
                ln1_apply_and_transpose(s + 4, xinT_guard=prs[s + 2]["q"])
            prev_mult = hf["evac"]
            prev_pv = pv_d[63]
            last_spread = hf["spread"]
        W("vector", "tr", last_spread)
        E("vector", lambda e: e.reciprocal(dspr[:, 0:256], dspr[:, 0:256])
          .then_inc(S["ve"], 1))
        v_rec = inc("ve", 1)
        mult_done = {}
        for g in range(8):
            for hh in range(2):
                h = 2 * g + hh
                if g == 0 and hh == 0:
                    W("sync", "ve", v_rec)
                else:
                    W("sync", "te", val("te"))
                E("sync", lambda e, h=h:
                  e.dma_start(out=oden[64:65, :],
                              in_=dspr[:, h * 16:(h + 1) * 16])
                  .then_inc(S["tr"], 16))
                vback = inc("tr", 16)
                W("tensor", "tr", vback)
                if hh == 0:
                    if g == 0:
                        W("tensor", "ve", val("ve"))
                    else:
                        W("tensor", "ve", mult_done[g - 1])
                pb = hh * 64
                for c in range(4):
                    last = c == 3
                    E("tensor", lambda e, pb=pb, c=c, last=last:
                      (e.matmul(ps[pb:pb + 64, c * 512:(c + 1) * 512],
                                onesf[64:65, 0:64],
                                oden[64:65, c * 512:(c + 1) * 512],
                                start=True, stop=True).then_inc(S["te"], 1))
                      if last else
                      e.matmul(ps[pb:pb + 64, c * 512:(c + 1) * 512],
                               onesf[64:65, 0:64],
                               oden[64:65, c * 512:(c + 1) * 512],
                               start=True, stop=True))
                inc("te", 1)
            vt = val("te")
            W("vector", "te", vt)
            E("vector", lambda e, g=g:
              e.tensor_mul(ocat[:, g, :], ocat[:, g, :], ps[:, 0:2048])
              .then_inc(S["ve"], 1))
            mult_done[g] = inc("ve", 1)
        v_att_ve = val("ve")
        v_att_tr = val("tr")
        v_att_pv = val("pv")
        if stop_after == "attn":
            W("gpsimd", "ve", v_att_ve)
            W("gpsimd", "tr", v_att_tr)
            E("gpsimd", lambda e:
              e.dma_start(out=out_d[:, :],
                          in_=ocat[:, :, :]).then_inc(S["out"], 16))
            E("gpsimd", lambda e: e.wait_ge(S["out"], 16))
            _finish(block, K)
            return nc
        if stop_after == "proj0":
            W("gpsimd", "ve", prs[1]["v"])
            W("gpsimd", "ap", prs[1]["q"])
            E("gpsimd", lambda e:
              e.dma_start(out=out_d[0:128, 0:1024],
                          in_=QC[0][:, :]).then_inc(S["out"], 16))
            E("gpsimd", lambda e:
              e.dma_start(out=out_d[128:256, 0:1024],
                          in_=QC[1][:, :]).then_inc(S["out"], 16))
            E("gpsimd", lambda e:
              e.dma_start(out=out_d[256:384, 0:512],
                          in_=Ks[0][:, 0:2, :]).then_inc(S["out"], 16))
            E("gpsimd", lambda e:
              e.dma_start(out=out_d[384:512, 0:1024],
                          in_=Vp[0][:, 0, 0:1024]).then_inc(S["out"], 16))
            E("gpsimd", lambda e:
              e.dma_start(out=out_d[512:640, 0:512],
                          in_=xinT[0][:, 0:2, 0:256])
              .then_inc(S["out"], 16))
            E("gpsimd", lambda e: e.wait_ge(S["out"], 80))
            _finish(block, K)
            return nc

        psA_free = {}
        x1_store = {}
        xin2_done = {}
        x0_used = {}
        for tt in range(16):
            reg = (tt % 4) * 1024
            if tt == 0:
                W("tensor", "ve", v_att_ve)
                W("tensor", "tr", v_att_tr)
            if tt >= 4:
                W("tensor", "ve", psA_free[tt - 4][0])
                W("tensor", "ap", psA_free[tt - 4][1])
            E("tensor", lambda e, reg=reg:
              e.matmul(ps[:, reg:reg + 512], onesb[0:1, 0:128],
                       brow[0:1, 1024:1536], start=True, stop=False))
            E("tensor", lambda e, reg=reg:
              e.matmul(ps[:, reg + 512:reg + 1024], onesb[0:1, 0:128],
                       brow[0:1, 1536:2048], start=True, stop=False))
            for g in range(8):
                for n2 in range(2):
                    last = g == 7 and n2 == 1
                    E("tensor", lambda e, tt=tt, g=g, n2=n2, reg=reg,
                      last=last:
                      (e.matmul(ps[:, reg + n2 * 512:reg + n2 * 512 + 512],
                                ocat[:, g, tt * 128:tt * 128 + 128],
                                woT[:, g, n2 * 512:n2 * 512 + 512],
                                start=False, stop=(g == 7))
                       .then_inc(S["te"], 1)) if last else
                      e.matmul(ps[:, reg + n2 * 512:reg + n2 * 512 + 512],
                               ocat[:, g, tt * 128:tt * 128 + 128],
                               woT[:, g, n2 * 512:n2 * 512 + 512],
                               start=False, stop=(g == 7)))
            vt = inc("te", 1)
            if stop_after == "p4mm":
                psA_free[tt] = (0, 0)
                x0_used[tt] = 0
                x1_store[tt] = 0
                xin2_done[tt] = 0
                continue
            if tt >= 2:
                W("sync", "ve", x0_used[tt - 2])
            xs = "x%d" % (tt % 2)
            vx0 = inc(xs, 16)
            E("sync", lambda e, tt=tt, xs=xs:
              e.dma_start(out=x0b[tt % 2][:, :],
                          in_=dr["x0"][tt * 128:(tt + 1) * 128, :])
              .then_inc(S[xs], 16))
            W("vector", "te", vt)
            E("vector", lambda e, reg=reg:
              e.reduce_sum(stat[:, 0:1], ps[:, reg:reg + 1024], axis=AX.X))
            E("vector", lambda e: e.memset(stat[:, 1:2], 0.0)
              .then_inc(S["ve"], 1))
            vv0 = inc("ve", 1)
            W("scalar", "te", vt)
            W("scalar", "ve", vv0)
            E("scalar", lambda e, reg=reg:
              e.activation(jkb[:, :], ps[:, reg:reg + 1024], AF.Square,
                           accum_out=stat[:, 1:2]).then_inc(S["ap"], 1))
            va0 = inc("ap", 1)
            W("vector", "ap", va0)
            E("vector", lambda e: e.tensor_scalar(stat[:, 2:3], stat[:, 0:1],
                                                  -1.0 / D, None, ALU.mult))
            E("vector", lambda e: e.tensor_scalar(stat[:, 3:4], stat[:, 1:2],
                                                  1.0 / D, None, ALU.mult)
              .then_inc(S["ve"], 1))
            W("vector", "ve", inc("ve", 1))
            E("vector", lambda e: e.tensor_mul(stat[:, 4:5], stat[:, 2:3],
                                               stat[:, 2:3])
              .then_inc(S["ve"], 1))
            W("vector", "ve", inc("ve", 1))
            E("vector", lambda e: e.scalar_tensor_tensor(
                stat[:, 3:4], stat[:, 3:4], EPS, stat[:, 4:5], ALU.add,
                ALU.subtract).then_inc(S["ve"], 1))
            vv = inc("ve", 1)
            W("scalar", "ve", vv)
            E("scalar", lambda e: e.activation(stat[:, 4:5], stat[:, 3:4],
                                               AF.Sqrt).then_inc(S["ap"], 1))
            va = inc("ap", 1)
            W("vector", "ap", va)
            E("vector", lambda e: e.reciprocal(stat[:, 5:6], stat[:, 4:5])
              .then_inc(S["ve"], 1))
            W("vector", "ve", inc("ve", 1))
            E("vector", lambda e: e.tensor_mul(stat[:, 6:7], stat[:, 2:3],
                                               stat[:, 5:6])
              .then_inc(S["ve"], 1))
            vv = inc("ve", 1)
            W("scalar", "ve", vv)
            E("scalar", lambda e, reg=reg:
              e.activation(tmpL[:, :], ps[:, reg:reg + 1024], AF.Identity,
                           bias=stat[:, 6:7], scale=stat[:, 5:6])
              .then_inc(S["ap"], 1))
            va = inc("ap", 1)
            psA_free[tt] = (vv, va)
            W("vector", "ap", va)
            W("vector", xs, vx0)
            if tt >= 2:
                W("vector", "st%d" % (tt % 2), x1_store[tt - 2])
            E("vector", lambda e, tt=tt:
              e.tensor_add(x1b[tt % 2][:, :], tmpL[:, :], x0b[tt % 2][:, :])
              .then_inc(S["ve"], 1))
            vx1 = inc("ve", 1)
            x0_used[tt] = vx1
            W("sync", "ve", vx1)
            sts = "st%d" % (tt % 2)
            E("sync", lambda e, tt=tt, sts=sts:
              e.dma_start(out=x1_dram[tt * 128:(tt + 1) * 128, :],
                          in_=x1b[tt % 2][:, :]).then_inc(S[sts], 16))
            x1_store[tt] = inc(sts, 16)
            E("vector", lambda e, tt=tt:
              e.reduce_sum(stat[:, 8:9], x1b[tt % 2][:, :], axis=AX.X))
            E("vector", lambda e: e.memset(stat[:, 9:10], 0.0)
              .then_inc(S["ve"], 1))
            vv8 = inc("ve", 1)
            W("scalar", "ve", vv8)
            E("scalar", lambda e, tt=tt:
              e.activation(jkb[:, :], x1b[tt % 2][:, :], AF.Square,
                           accum_out=stat[:, 9:10]).then_inc(S["ap"], 1))
            va8 = inc("ap", 1)
            W("vector", "ap", va8)
            E("vector", lambda e:
              e.tensor_scalar(stat[:, 10:11], stat[:, 8:9], -1.0 / D, None,
                              ALU.mult))
            E("vector", lambda e:
              e.tensor_scalar(stat[:, 11:12], stat[:, 9:10], 1.0 / D, None,
                              ALU.mult).then_inc(S["ve"], 1))
            W("vector", "ve", inc("ve", 1))
            E("vector", lambda e: e.tensor_mul(stat[:, 12:13], stat[:, 10:11],
                                               stat[:, 10:11])
              .then_inc(S["ve"], 1))
            W("vector", "ve", inc("ve", 1))
            E("vector", lambda e: e.scalar_tensor_tensor(
                stat[:, 11:12], stat[:, 11:12], EPS, stat[:, 12:13], ALU.add,
                ALU.subtract).then_inc(S["ve"], 1))
            vv = inc("ve", 1)
            W("scalar", "ve", vv)
            E("scalar", lambda e: e.activation(stat[:, 12:13], stat[:, 11:12],
                                               AF.Sqrt).then_inc(S["ap"], 1))
            va = inc("ap", 1)
            W("vector", "ap", va)
            E("vector", lambda e: e.reciprocal(stat[:, 13:14], stat[:, 12:13])
              .then_inc(S["ve"], 1))
            W("vector", "ve", inc("ve", 1))
            E("vector", lambda e: e.tensor_mul(stat[:, 14:15], stat[:, 10:11],
                                               stat[:, 13:14])
              .then_inc(S["ve"], 1))
            vv = inc("ve", 1)
            W("scalar", "ve", vv)
            E("scalar", lambda e, tt=tt:
              e.activation(xin2[:, tt, :], x1b[tt % 2][:, :], AF.Identity,
                           bias=stat[:, 14:15], scale=stat[:, 13:14])
              .then_inc(S["ap"], 1))
            xin2_done[tt] = inc("ap", 1)

        if stop_after == "p4mm":
            W("vector", "te", val("te"))
            vv_ = inc("ve", 1)
            E("vector", lambda e: e.tensor_copy(oden[0:64, 0:1024],
                                                ps[0:64, 0:1024])
              .then_inc(S["ve"], 1))
            W("sync", "ve", val("ve"))
            W("sync", "te", val("te"))
            E("sync", lambda e:
              e.dma_start(out=out_d[0:64, :], in_=oden[0:64, 0:1024])
              .then_inc(S["out"], 16))
            E("sync", lambda e: e.wait_ge(S["out"], 16))
            _finish(block, K)
            return nc

        if stop_after == "p4ln2":
            W("gpsimd", "st0", val("st0"))
            W("gpsimd", "st1", val("st1"))
            E("gpsimd", lambda e:
              e.dma_start(out=out_d[:, :], in_=x1_dram[:, :])
              .then_inc(S["out"], 16))
            E("gpsimd", lambda e: e.wait_ge(S["out"], 16))
            _finish(block, K)
            return nc

        if stop_after == "p4":
            W("gpsimd", "st0", val("st0"))
            W("gpsimd", "st1", val("st1"))
            E("gpsimd", lambda e:
              e.dma_start(out=out_d[:, :], in_=x1_dram[:, :])
              .then_inc(S["out"], 16))
            E("gpsimd", lambda e: e.wait_ge(S["out"], 16))
            _finish(block, K)
            return nc

        w1_te = {}
        w2_te = {}
        gelu_done = {}
        psF_ap = {}
        out_tot = 0
        x1r_used = {}
        out_used = 0
        for tb in range(4):
            for k in range(4):
                tt = tb * 4 + k
                W("sync", "ap", xin2_done[tt])
                if tb >= 2:
                    W("sync", "te", w1_te[(tb - 2, 31)])
                ts_ = "t%d" % (tb % 2)
                E("sync", lambda e, tb=tb, k=k, tt=tt, ts_=ts_:
                  e.dma_start(out=xin2T[tb % 2][:, :, k * 128:k * 128 + 128],
                              in_=xin2[:, tt, :], transpose=True)
                  .then_inc(S[ts_], 16))
                inc(ts_, 16)
            v_x2t = val("t%d" % (tb % 2))
            if stop_after == "p5tr":
                break
            for jc in range(32):
                if (tb, jc - 2) in w1_te:
                    W("gpsimd", "te", w1_te[(tb, jc - 2)])
                elif tb > 0 and jc < 2:
                    W("gpsimd", "te", w1_te[(tb - 1, 30 + jc)])
                else:
                    W("gpsimd", "pv", v_att_pv)
                ws = "w5%d" % (jc % 2)
                vw = inc(ws, 16)
                E("gpsimd", lambda e, jc=jc, ws=ws:
                  e.dma_start(out=w1b[jc % 2][:, :, :],
                              in_=dr["w1"][:, jc * 128:(jc + 1) * 128]
                              .rearrange("(c p) j -> p c j", p=128))
                  .then_inc(S[ws], 16))
                reg = (jc % 2) * 512
                if stop_after == "p5w1":
                    continue
                W("tensor", ws, vw)
                W("tensor", "t%d" % (tb % 2), v_x2t)
                if (tb, jc - 2) in gelu_done:
                    W("tensor", "ap", gelu_done[(tb, jc - 2)])
                elif tb > 0 and jc < 2:
                    W("tensor", "ap", psF_ap[(tb - 1, 0)])
                for fc in range(8):
                    last = fc == 7
                    E("tensor", lambda e, jc=jc, fc=fc, reg=reg, tb=tb,
                      last=last:
                      (e.matmul(ps[:, reg:reg + 512], w1b[jc % 2][:, fc, :],
                                xin2T[tb % 2][:, fc, :],
                                start=(fc == 0), stop=True)
                       .then_inc(S["te"], 1)) if last else
                      e.matmul(ps[:, reg:reg + 512], w1b[jc % 2][:, fc, :],
                               xin2T[tb % 2][:, fc, :],
                               start=(fc == 0), stop=False))
                w1_te[(tb, jc)] = inc("te", 1)
                W("scalar", "te", w1_te[(tb, jc)])
                _gdst = (jkb[:, 0:512] if stop_after == "p5nog"
                         else gT[:, jc, :])
                E("scalar", lambda e, jc=jc, reg=reg, _gdst=_gdst:
                  e.activation(_gdst, ps[:, reg:reg + 512], AF.Gelu,
                               bias=b1c[:, jc:jc + 1]).then_inc(S["ap"], 1))
                gelu_done[(tb, jc)] = inc("ap", 1)
            if stop_after in ("p5w1", "p5nog", "p5mm1"):
                break
            W("tensor", "ap", gelu_done[(tb, 31)])
            for k in range(4):
                E("tensor", lambda e, k=k:
                  e.matmul(ps[:, k * 1024:k * 1024 + 512], onesb[0:1, 0:128],
                           brow[0:1, 2048:2560], start=True, stop=False))
                E("tensor", lambda e, k=k:
                  e.matmul(ps[:, k * 1024 + 512:k * 1024 + 1024],
                           onesb[0:1, 0:128], brow[0:1, 2560:3072],
                           start=True, stop=False))
            for jc in range(32):
                if (tb, jc - 2) in w2_te:
                    W("gpsimd", "te", w2_te[(tb, jc - 2)])
                else:
                    W("gpsimd", "te", w1_te[(tb, 30 + jc)])
                ws = "w5%d" % (jc % 2)
                vw = inc(ws, 16)
                E("gpsimd", lambda e, jc=jc, ws=ws:
                  e.dma_start(out=w2b[jc % 2][:, :],
                              in_=dr["w2"][jc * 128:(jc + 1) * 128, :])
                  .then_inc(S[ws], 16))
                W("tensor", ws, vw)
                for k in range(4):
                    for n2 in range(2):
                        last = k == 3 and n2 == 1
                        E("tensor", lambda e, jc=jc, k=k, n2=n2, last=last:
                          (e.matmul(ps[:, k * 1024 + n2 * 512:
                                       k * 1024 + n2 * 512 + 512],
                                    gT[:, jc, k * 128:k * 128 + 128],
                                    w2b[jc % 2][:, n2 * 512:n2 * 512 + 512],
                                    start=False, stop=(jc == 31))
                           .then_inc(S["te"], 1)) if last else
                          e.matmul(ps[:, k * 1024 + n2 * 512:
                                      k * 1024 + n2 * 512 + 512],
                                   gT[:, jc, k * 128:k * 128 + 128],
                                   w2b[jc % 2][:, n2 * 512:n2 * 512 + 512],
                                   start=False, stop=(jc == 31)))
                w2_te[(tb, jc)] = inc("te", 1)
            v_mm2 = val("te")
            if stop_after == "p5mm2":
                break
            last_ap = 0
            for k in range(4):
                W("vector", "te", v_mm2)
                E("vector", lambda e, k=k:
                  e.reduce_sum(stat[:, 32 + k:33 + k],
                               ps[:, k * 1024:k * 1024 + 1024], axis=AX.X))
                E("vector", lambda e, k=k: e.memset(stat[:, 36 + k:37 + k],
                                                    0.0).then_inc(S["ve"], 1))
                vv0 = inc("ve", 1)
                W("scalar", "te", v_mm2)
                W("scalar", "ve", vv0)
                E("scalar", lambda e, k=k:
                  e.activation(jkb[:, :], ps[:, k * 1024:k * 1024 + 1024],
                               AF.Square, accum_out=stat[:, 36 + k:37 + k])
                  .then_inc(S["ap"], 1))
                inc("ap", 1)
            W("vector", "ap", val("ap"))
            E("vector", lambda e:
              e.tensor_scalar(stat[:, 40:44], stat[:, 32:36], -1.0 / D, None,
                              ALU.mult))
            E("vector", lambda e:
              e.tensor_scalar(stat[:, 44:48], stat[:, 36:40], 1.0 / D, None,
                              ALU.mult).then_inc(S["ve"], 1))
            W("vector", "ve", inc("ve", 1))
            E("vector", lambda e:
              e.tensor_mul(stat[:, 48:52], stat[:, 40:44], stat[:, 40:44])
              .then_inc(S["ve"], 1))
            W("vector", "ve", inc("ve", 1))
            E("vector", lambda e: e.scalar_tensor_tensor(
                stat[:, 44:48], stat[:, 44:48], EPS, stat[:, 48:52], ALU.add,
                ALU.subtract).then_inc(S["ve"], 1))
            vv = inc("ve", 1)
            W("scalar", "ve", vv)
            E("scalar", lambda e: e.activation(stat[:, 48:52], stat[:, 44:48],
                                               AF.Sqrt).then_inc(S["ap"], 1))
            va = inc("ap", 1)
            W("vector", "ap", va)
            E("vector", lambda e: e.reciprocal(stat[:, 52:56], stat[:, 48:52])
              .then_inc(S["ve"], 1))
            W("vector", "ve", inc("ve", 1))
            E("vector", lambda e:
              e.tensor_mul(stat[:, 40:44], stat[:, 40:44], stat[:, 52:56])
              .then_inc(S["ve"], 1))
            v_st4 = inc("ve", 1)
            for k in range(4):
                tt = tb * 4 + k
                if tt >= 2:
                    W("sync", "ve", x1r_used.get(tt - 2, 0))
                W("sync", "st%d" % (tt % 2), x1_store[tt])
                xs = "x%d" % (tt % 2)
                vx1 = inc(xs, 16)
                E("sync", lambda e, tt=tt, xs=xs:
                  e.dma_start(out=x1rb[tt % 2][:, :],
                              in_=x1_dram[tt * 128:(tt + 1) * 128, :])
                  .then_inc(S[xs], 16))
                W("scalar", "ve", v_st4 if k == 0 else x1r_used[tt - 1])
                E("scalar", lambda e, k=k:
                  e.activation(junk[:, :], ps[:, k * 1024:k * 1024 + 1024],
                               AF.Identity, bias=stat[:, 40 + k:41 + k],
                               scale=stat[:, 52 + k:53 + k])
                  .then_inc(S["ap"], 1))
                va = inc("ap", 1)
                last_ap = va
                psF_ap[(tb, k)] = va
                W("vector", "ap", va)
                W("vector", xs, vx1)
                if out_used:
                    W("vector", "out", out_used)
                E("vector", lambda e, tt=tt:
                  e.tensor_add(outb[:, :], junk[:, :], x1rb[tt % 2][:, :])
                  .then_inc(S["ve"], 1))
                vo = inc("ve", 1)
                x1r_used[tt] = vo
                W("sync", "ve", vo)
                E("sync", lambda e, tt=tt:
                  e.dma_start(out=out_d[tt * 128:(tt + 1) * 128, :],
                              in_=outb[:, :]).then_inc(S["out"], 16))
                out_tot = inc("out", 16)
                out_used = out_tot
        W("sync", "out", out_tot)
        _finish(block, K)
        return nc


def _finish(block, K):
    @block.sync
    def _(e):
        for f in K.ops["sync"]:
            f(e)

    @block.tensor
    def _(e):
        for f in K.ops["tensor"]:
            f(e)

    @block.vector
    def _(e):
        for f in K.ops["vector"]:
            f(e)

    @block.scalar
    def _(e):
        for f in K.ops["scalar"]:
            f(e)

    @block.gpsimd
    def _(e):
        for f in K.ops["gpsimd"]:
            f(e)


_NC_CACHE = {}


def _get_nc():
    if "nc" not in _NC_CACHE:
        _NC_CACHE["nc"] = build_program()
    return _NC_CACHE["nc"]


def kernel(L0_attn_x, L0_attn_wq, L0_attn_bq, L0_attn_wk, L0_attn_bk,
           L0_attn_wv, L0_attn_bv, L0_attn_wo, L0_attn_bo, L0_attn_cos,
           L0_attn_sin, L0_ffn_w1, L0_ffn_b1, L0_ffn_w2, L0_ffn_b2):
    x = np.ascontiguousarray(np.asarray(L0_attn_x, dtype=np.float32))
    B = x.shape[0]
    eyeC = np.eye(128, dtype=np.float32)
    eyeU = np.zeros((128, 128), np.float32)
    eyeD = np.zeros((128, 128), np.float32)
    for p in range(0, 128, 2):
        eyeU[p, p + 1] = 1.0
    for p in range(1, 128, 2):
        eyeD[p, p - 1] = 1.0
    shared = dict(
        wq=np.ascontiguousarray(L0_attn_wq, np.float32),
        wk=np.ascontiguousarray(L0_attn_wk, np.float32),
        wv=np.ascontiguousarray(L0_attn_wv, np.float32),
        wo=np.ascontiguousarray(L0_attn_wo, np.float32),
        bq=np.ascontiguousarray(L0_attn_bq, np.float32),
        bk=np.ascontiguousarray(L0_attn_bk, np.float32),
        bv=np.ascontiguousarray(L0_attn_bv, np.float32),
        bo=np.ascontiguousarray(L0_attn_bo, np.float32),
        cosr=np.ascontiguousarray(np.asarray(L0_attn_cos)[3199], np.float32),
        sinr=np.ascontiguousarray(np.asarray(L0_attn_sin)[3199], np.float32),
        w1=np.ascontiguousarray(L0_ffn_w1, np.float32),
        b1=np.ascontiguousarray(L0_ffn_b1, np.float32),
        w2=np.ascontiguousarray(L0_ffn_w2, np.float32),
        b2=np.ascontiguousarray(L0_ffn_b2, np.float32),
        eyeC=eyeC, eyeU=eyeU, eyeD=eyeD,
    )
    in_maps = []
    for c in range(8):
        b, half = c // 2, c % 2
        xb = x[b]
        xpc = xb.reshape(16, 2, 128, D)[:, [half, 1 - half]].reshape(T, D)
        x0c = xb[half * TOWN + _TMAP]
        m = dict(shared)
        m["xp"] = np.ascontiguousarray(xpc)
        m["x0"] = np.ascontiguousarray(x0c)
        in_maps.append(m)

    nc = _get_nc()
    res = run_bass_kernel_spmd(nc, in_maps, core_ids=list(range(8)))
    out = np.empty((B, T, D), np.float32)
    inv = np.empty_like(_TMAP)
    inv[_TMAP] = np.arange(2048)
    for c in range(8):
        b, half = c // 2, c % 2
        out[b, half * TOWN:(half + 1) * TOWN] = res.results[c]["out"][inv]
    return out


# version 09
# speedup vs baseline: 1.0904x; 1.0100x over previous
import numpy as np
import ml_dtypes
import concourse.bass as bass
import concourse.mybir as mybir
from concourse.bass_utils import run_bass_kernel_spmd

F32 = mybir.dt.float32
BF16 = mybir.dt.bfloat16
AF = mybir.ActivationFunctionType
ALU = mybir.AluOpType
AX = mybir.AxisListType

D = 1024
T = 4096
TOWN = 2048
NSLAB = 16
NT = 32
EPS = 1e-5


def _tmap():
    q = np.arange(2048)
    P, u, j = q // 1024, (q % 1024) // 128, q % 128
    return j * 16 + 2 * u + P


_TMAP = _tmap()


class _K:

    def __init__(self):
        self.ops = {e: [] for e in ("sync", "tensor", "vector", "scalar",
                                    "gpsimd")}
        self.ct = {}

    def E(self, eng, fn):
        self.ops[eng].append(fn)

    def inc(self, sem, n):
        self.ct[sem] = self.ct.get(sem, 0) + n
        return self.ct[sem]

    def val(self, sem):
        return self.ct.get(sem, 0)


def build_program(stop_after=None):
    nc = bass.Bass("TRN2", target_bir_lowering=False,
                   detect_race_conditions=False)
    K = _K()
    E, inc, val = K.E, K.inc, K.val

    dr = {}
    for nm, shp in [("xp", [T, D]), ("x0", [TOWN, D]),
                    ("wq", [D, D]), ("wk", [D, D]), ("wv", [D, D]),
                    ("wo", [D, D]), ("bq", [D]), ("bk", [D]), ("bv", [D]),
                    ("bo", [D]), ("cosr", [64]), ("sinr", [64]),
                    ("w1p", [32, 128, D]), ("b1", [4096]),
                    ("w2", [4096, D]),
                    ("b2", [D]), ("eyeC", [128, 128]), ("eyeU", [128, 128]),
                    ("eyeD", [128, 128])]:
        dr[nm] = nc.dram_tensor(nm, shp, F32, kind="ExternalInput").ap()
    out_d = nc.dram_tensor("out", [TOWN, D], F32, kind="ExternalOutput").ap()
    x1_dram = nc.dram_tensor("x1s", [TOWN, D], F32, kind="Internal").ap()
    w1q_d = nc.dram_tensor("w1q", [32, 128, D], BF16, kind="Internal").ap()
    w2q_d = nc.dram_tensor("w2q", [4096, D], BF16, kind="Internal").ap()

    NBF = 81088
    NF32 = 12352

    bo_ = [0]

    def ab(n):
        o = bo_[0]
        bo_[0] += n
        return o

    o_woT = ab(8 * 1024)
    o_ocat = ab(8 * 2048)
    o_otmp = ab(2048)
    o_xinb = ab(4 * 1024)
    o_onesb = ab(128)
    o_brow = ab(3 * 1024)
    _pbf = bo_[0]
    o_wqT = ab(8 * 1024)
    o_wkT = ab(8 * 1024)
    o_wvT = ab(8 * 1024)
    o_xinT = [ab(8 * 256), ab(8 * 256)]
    o_Ks = [ab(8 * 256), ab(8 * 256)]
    o_Vp = [ab(2 * 1040), ab(2 * 1040)]
    o_QC = [ab(1024), ab(1024)]
    o_QW = [ab(1024), ab(1024)]
    o_PT = [ab(2048), ab(2048), ab(2048)]
    _endA = bo_[0]
    bo_[0] = _pbf
    o_xin2 = ab(16 * 1024)
    o_gT = ab(32 * 512)
    o_xin2T = [ab(8 * 512), ab(8 * 512)]
    o_w1b = [ab(8 * 128), ab(8 * 128)]
    o_w2b = [ab(1024), ab(1024)]
    _endB = bo_[0]
    assert max(_endA, _endB) <= NBF, (_endA, _endB)

    fo_ = [0]

    def af(n):
        o = fo_[0]
        fo_[0] += n
        return o

    o_oden = af(2048)
    o_dspr = af(16)
    o_stat = af(64)
    o_ln1 = af(6 * 32)
    o_bqr = af(8)
    o_bkr = af(8)
    o_b1c = af(32)
    o_cs = af(8)
    o_onesf = af(64)
    o_mrt = af(128)
    o_mrtq = af(128)
    o_eye = af(3 * 128)
    _pf = fo_[0]
    o_xpb = [af(1024), af(1024)]
    o_wst = af(1024)
    _endFA = fo_[0]
    fo_[0] = _pf
    o_x1rb = [af(1024), af(1024)]
    o_outb = af(1024)
    o_junk = af(1024)
    _endFA2 = fo_[0]
    fo_[0] = max(_endFA, _endFA2)
    o_x0b = [af(1024), af(1024)]
    o_x1b = [af(1024), af(1024)]
    o_tmpL = af(1024)
    assert fo_[0] <= NF32, fo_[0]

    import contextlib
    with contextlib.ExitStack() as _st:
        ABh = _st.enter_context(nc.sbuf_tensor("AB", [128, NBF], BF16))
        AFh = _st.enter_context(nc.sbuf_tensor("AF", [128, NF32], F32))
        ps = _st.enter_context(nc.psum_tensor("ps", [128, 4096], F32))
        S = {}
        for _snm in ("wg", "wq2", "wvt", "t0", "t1", "x0", "x1", "st0",
                     "st1", "w50", "w51", "tr", "te", "sc", "pv", "ex",
                     "ve", "ap", "out", "rp", "rq"):
            S[_snm] = _st.enter_context(nc.semaphore("s_" + _snm))
        block = _st.enter_context(nc.Block())

        def B(off, n):
            return ABh[:, off:off + n]

        def Fv(off, n):
            return AFh[:, off:off + n]

        woT = B(o_woT, 8192).rearrange("p (c n) -> p c n", c=8)
        ocat = B(o_ocat, 16384).rearrange("p (c n) -> p c n", c=8)
        otmp = B(o_otmp, 2048)
        xinb = B(o_xinb, 4096)
        onesb = B(o_onesb, 128)
        brow = B(o_brow, 3072)
        wqT = B(o_wqT, 8192).rearrange("p (c n) -> p c n", c=8)
        wkT = B(o_wkT, 8192).rearrange("p (c n) -> p c n", c=8)
        wvT = B(o_wvT, 8192).rearrange("p (c n) -> p c n", c=8)
        xinT = [B(o, 2048).rearrange("p (c n) -> p c n", c=8) for o in o_xinT]
        Ks = [B(o, 2048).rearrange("p (c n) -> p c n", c=8) for o in o_Ks]
        Vp = [B(o, 2080).rearrange("p (c n) -> p c n", c=2) for o in o_Vp]
        QC = [B(o, 1024) for o in o_QC]
        QW = [B(o, 1024) for o in o_QW]
        PT = [B(o, 2048) for o in o_PT]
        xin2 = B(o_xin2, 16384).rearrange("p (c n) -> p c n", c=16)
        gT = B(o_gT, 16384).rearrange("p (c n) -> p c n", c=32)
        xin2T = [B(o, 4096).rearrange("p (c n) -> p c n", c=8)
                 for o in o_xin2T]
        w1b = [B(o, 1024).rearrange("p (c n) -> p c n", c=8) for o in o_w1b]
        w2b = [B(o, 1024) for o in o_w2b]
        jkb = B(o_otmp, 1024)

        oden = Fv(o_oden, 2048)
        dspr = Fv(o_eye, 256)
        stat = Fv(o_stat, 64)
        ln1 = Fv(o_ln1, 192).rearrange("p (c n) -> p c n", c=6)
        bqr = Fv(o_bqr, 8)
        bkr = Fv(o_bkr, 8)
        b1c = Fv(o_b1c, 32)
        cs = Fv(o_cs, 8)
        onesf = Fv(o_onesf, 64)
        mrt = Fv(o_mrt, 128)
        mrtq = Fv(o_mrtq, 128)
        eye = Fv(o_eye, 384)
        xpb = [Fv(o, 1024) for o in o_xpb]
        wst = Fv(o_wst, 1024)
        x1rb = [Fv(o, 1024) for o in o_x1rb]
        outb = Fv(o_outb, 1024)
        junk = Fv(o_junk, 1024)
        x0b = [Fv(o, 1024) for o in o_x0b]
        x1b = [Fv(o, 1024) for o in o_x1b]
        tmpL = Fv(o_tmpL, 1024)

        def W(eng, sem, v):
            if v > 0:
                E(eng, lambda e, s=S[sem], v=v: e.wait_ge(s, v))

        def gdma(dst, src, slow=False):
            v = inc("wg", 16)
            E("gpsimd", lambda e, d=dst, s=src, slow=slow:
              e.dma_start(out=d, in_=s, allow_slow_non_contiguous=slow)
              .then_inc(S["wg"], 16))
            return v

        gdma(cs[0:64, 0:1], dr["cosr"][:].rearrange("(a c) -> c a", a=1))
        gdma(cs[64:128, 0:1], dr["cosr"][:].rearrange("(a c) -> c a", a=1))
        gdma(cs[0:64, 3:4], dr["sinr"][:].rearrange("(a c) -> c a", a=1))
        gdma(cs[64:128, 3:4], dr["sinr"][:].rearrange("(a c) -> c a", a=1))
        gdma(eye[:, 0:128], dr["eyeC"][:, :])
        gdma(eye[:, 128:256], dr["eyeU"][:, :])
        gdma(eye[:, 256:384], dr["eyeD"][:, :])
        gdma(bqr[:, 0:8], dr["bq"][:].rearrange("(c p) -> p c", p=128), slow=True)
        gdma(bkr[:, 0:8], dr["bk"][:].rearrange("(c p) -> p c", p=128), slow=True)
        gdma(b1c[:, 0:32], dr["b1"][:].rearrange("(c p) -> p c", p=128), slow=True)
        gdma(brow[0:1, 0:1024], dr["bv"][:].rearrange("(a n) -> a n", a=1))
        gdma(brow[0:1, 1024:2048], dr["bo"][:].rearrange("(a n) -> a n", a=1))
        gdma(brow[0:1, 2048:3072], dr["b2"][:].rearrange("(a n) -> a n", a=1))
        v_bias = val("wg")

        E("vector", lambda e: e.memset(cs[:, 1:3], 0.0).then_inc(S["ve"], 1))
        v_cz = inc("ve", 1)
        W("sync", "ve", v_cz)
        W("sync", "wg", v_bias)
        E("sync", lambda e: e.dma_start(out=cs[0:127, 1:2], in_=cs[1:128, 3:4])
          .then_inc(S["tr"], 16))
        E("sync", lambda e: e.dma_start(out=cs[1:128, 2:3], in_=cs[0:127, 3:4])
          .then_inc(S["tr"], 16))
        v_csh = inc("tr", 32)

        E("vector", lambda e: e.memset(onesb[:, :], 1.0))
        E("vector", lambda e: e.memset(onesf[:, :], 1.0))
        E("vector", lambda e: e.memset(Vp[0][:, :, :], 1.0))
        E("vector", lambda e: e.memset(Vp[1][:, :, :], 1.0))
        W("vector", "wg", v_bias)
        W("vector", "tr", v_csh)
        E("vector", lambda e: e.tensor_scalar(mrt[:, :], eye[:, 0:128],
                                              cs[:, 0:1], None, ALU.mult))
        E("vector", lambda e: e.tensor_scalar(eye[:, 128:256], eye[:, 128:256],
                                              cs[:, 1:2], None, ALU.mult))
        E("vector", lambda e: e.tensor_add(mrt[:, :], mrt[:, :],
                                           eye[:, 128:256]))
        E("vector", lambda e: e.tensor_scalar(eye[:, 256:384], eye[:, 256:384],
                                              cs[:, 2:3], None, ALU.mult))
        E("vector", lambda e: e.tensor_sub(mrt[:, :], mrt[:, :],
                                           eye[:, 256:384]))
        E("vector", lambda e: e.tensor_scalar(mrtq[:, :], mrt[:, :], 0.125,
                                              None, ALU.mult)
          .then_inc(S["ve"], 1))
        v_mrot = inc("ve", 1)

        v_wv_tr = {}
        for wname, dstT in (("wv", wvT), ("wo", woT)):
            for cc in range(8):
                vg = inc("wq2", 16)
                E("gpsimd", lambda e, wname=wname, cc=cc:
                  e.dma_start(out=xinb[:, 0:1024],
                              in_=dr[wname][cc * 128:(cc + 1) * 128, :])
                  .then_inc(S["wq2"], 16))
                W("sync", "wq2", vg)
                E("sync", lambda e, dstT=dstT, cc=cc:
                  e.dma_start(out=dstT[:, :, cc * 128:cc * 128 + 128],
                              in_=xinb[:, 0:1024], transpose=True)
                  .then_inc(S["wvt"], 16))
                vtr = inc("wvt", 16)
                W("gpsimd", "wvt", vtr)
        v_wvwo = val("wvt")

        rot_ap = {}

        def rot_weight(wname, rhs_ap, dstT, bias_col, prev_te, prev_ve):
            for cc in range(8):
                if cc >= 1:
                    W("gpsimd", "te", rot_ap[("te", wname, cc - 1)])
                elif prev_te:
                    W("gpsimd", "te", prev_te)
                vw = inc("wq2", 16)
                E("gpsimd", lambda e, wname=wname, cc=cc:
                  e.dma_start(out=wst[:, :],
                              in_=dr[wname][cc * 128:(cc + 1) * 128, :])
                  .then_inc(S["wq2"], 16))
                W("tensor", "ve", v_mrot)
                W("tensor", "wq2", vw)
                if cc >= 2:
                    W("tensor", "ap", rot_ap[("ap", wname, cc - 2)])
                reg = (cc % 2) * 1024
                for fc in range(8):
                    last = fc == 7
                    E("tensor", lambda e, cc=cc, fc=fc, reg=reg, last=last,
                      rhs_ap=rhs_ap:
                      (e.matmul(ps[:, reg + fc * 128:reg + fc * 128 + 128],
                                wst[:, fc * 128:(fc + 1) * 128],
                                rhs_ap, start=True, stop=True)
                       .then_inc(S["te"], 1)) if last else
                      e.matmul(ps[:, reg + fc * 128:reg + fc * 128 + 128],
                               wst[:, fc * 128:(fc + 1) * 128],
                               rhs_ap, start=True, stop=True))
                rot_ap[("te", wname, cc)] = inc("te", 1)
                W("scalar", "te", rot_ap[("te", wname, cc)])
                E("scalar", lambda e, cc=cc, reg=reg, dstT=dstT:
                  e.activation(dstT[:, :, cc * 128:cc * 128 + 128],
                               ps[:, reg:reg + 1024]
                               .rearrange("p (c n) -> p c n", c=8),
                               AF.Identity).then_inc(S["ap"], 1))
                rot_ap[("ap", wname, cc)] = inc("ap", 1)
            W("tensor", "wg", v_bias)
            if prev_ve:
                W("tensor", "ve", prev_ve)
            for cc in range(8):
                last = cc == 7
                E("tensor", lambda e, cc=cc, rhs_ap=rhs_ap, bias_col=bias_col,
                  last=last:
                  (e.matmul(ps[:, 2048 + cc:2048 + cc + 1], rhs_ap,
                            bias_col[:, cc:cc + 1], start=True, stop=True)
                   .then_inc(S["te"], 1)) if last else
                  e.matmul(ps[:, 2048 + cc:2048 + cc + 1], rhs_ap,
                           bias_col[:, cc:cc + 1], start=True, stop=True))
            vt = inc("te", 1)
            W("vector", "te", vt)
            E("vector", lambda e, bias_col=bias_col:
              e.tensor_copy(bias_col[:, 0:8], ps[:, 2048:2056])
              .then_inc(S["ve"], 1))
            return inc("ve", 1)

        v_bqrot = rot_weight("wq", mrtq[:, :], wqT, bqr, 0, 0)
        v_bkrot = rot_weight("wk", mrt[:, :], wkT, bkr,
                             rot_ap[("te", "wq", 7)], v_bqrot)
        v_wrot_ap = val("ap")

        wrep = [B(o_PT[0], 2048).rearrange("p (c n) -> p c n", c=2),
                B(o_PT[1], 2048).rearrange("p (c n) -> p c n", c=2)]
        nrep = 0
        for kind in range(2):
            for i in range(16):
                if nrep >= 2:
                    W("gpsimd", "rq", (nrep - 1) * 16)
                src = (dr["w1p"][2 * i:2 * i + 2]
                       .rearrange("j p k -> p j k") if kind == 0 else
                       dr["w2"][i * 256:(i + 1) * 256, :]
                       .rearrange("(j p) k -> p j k", p=128))
                E("gpsimd", lambda e, src=src, b=wrep[nrep % 2]:
                  e.dma_start(out=b, in_=src).then_inc(S["rp"], 16))
                vl = inc("rp", 16)
                W("gpsimd", "rp", vl)
                dst = (w1q_d[2 * i:2 * i + 2].rearrange("j p k -> p j k")
                       if kind == 0 else
                       w2q_d[i * 256:(i + 1) * 256, :]
                       .rearrange("(j p) k -> p j k", p=128))
                E("gpsimd", lambda e, dst=dst, b=wrep[nrep % 2]:
                  e.dma_start(out=dst, in_=b).then_inc(S["rq"], 16))
                inc("rq", 16)
                nrep += 1
        v_repack = val("rq")

        E("vector", lambda e: e.memset(ln1[:, :, :], 0.0).then_inc(S["ve"], 1))
        v_lnz = inc("ve", 1)
        pp_ve = {}
        pp_ap = {}
        v_ln1_c = {}
        ppb = [xpb[0], xpb[1], x0b[0], x0b[1]]
        ppsem = ["x0", "x1", "st0", "st1"]
        for i in range(NT):
            if i >= 4:
                W("sync", "ve", pp_ve[i - 4])
                W("sync", "ap", pp_ap[i - 4])
            xs = ppsem[i % 4]
            vx = inc(xs, 16)
            E("sync", lambda e, i=i, xs=xs:
              e.dma_start(out=ppb[i % 4][:, :],
                          in_=dr["xp"][i * 128:(i + 1) * 128, :])
              .then_inc(S[xs], 16))
            W("vector", xs, vx)
            if i == 0:
                W("vector", "ve", v_lnz)
            E("vector", lambda e, i=i:
              e.reduce_sum(ln1[:, 0, i:i + 1], ppb[i % 4][:, :], axis=AX.X)
              .then_inc(S["ve"], 1))
            pp_ve[i] = inc("ve", 1)
            W("scalar", xs, vx)
            if i == 0:
                W("scalar", "ve", v_lnz)
            E("scalar", lambda e, i=i:
              e.activation(jkb[:, :], ppb[i % 4][:, :], AF.Square,
                           accum_out=ln1[:, 1, i:i + 1]).then_inc(S["ap"], 1))
            pp_ap[i] = inc("ap", 1)
            if i % 8 == 7:
                c = i // 8
                lo, hi = c * 8, c * 8 + 8
                W("vector", "ap", pp_ap[i])
                E("vector", lambda e, lo=lo, hi=hi:
                  e.tensor_scalar(ln1[:, 2, lo:hi], ln1[:, 0, lo:hi],
                                  -1.0 / D, None, ALU.mult))
                E("vector", lambda e, lo=lo, hi=hi:
                  e.tensor_scalar(ln1[:, 3, lo:hi], ln1[:, 1, lo:hi],
                                  1.0 / D, None, ALU.mult)
                  .then_inc(S["ve"], 1))
                W("vector", "ve", inc("ve", 1))
                E("vector", lambda e, lo=lo, hi=hi:
                  e.tensor_mul(ln1[:, 4, lo:hi], ln1[:, 2, lo:hi],
                               ln1[:, 2, lo:hi]).then_inc(S["ve"], 1))
                W("vector", "ve", inc("ve", 1))
                E("vector", lambda e, lo=lo, hi=hi:
                  e.scalar_tensor_tensor(ln1[:, 3, lo:hi], ln1[:, 3, lo:hi],
                                         EPS, ln1[:, 4, lo:hi], ALU.add,
                                         ALU.subtract).then_inc(S["ve"], 1))
                vv = inc("ve", 1)
                W("scalar", "ve", vv)
                E("scalar", lambda e, lo=lo, hi=hi:
                  e.activation(ln1[:, 4, lo:hi], ln1[:, 3, lo:hi], AF.Sqrt)
                  .then_inc(S["ap"], 1))
                va = inc("ap", 1)
                W("vector", "ap", va)
                E("vector", lambda e, lo=lo, hi=hi:
                  e.reciprocal(ln1[:, 3, lo:hi], ln1[:, 4, lo:hi])
                  .then_inc(S["ve"], 1))
                W("vector", "ve", inc("ve", 1))
                E("vector", lambda e, lo=lo, hi=hi:
                  e.tensor_mul(ln1[:, 2, lo:hi], ln1[:, 2, lo:hi],
                               ln1[:, 3, lo:hi]).then_inc(S["ve"], 1))
                v_ln1_c[c] = inc("ve", 1)
        v_ln1 = v_ln1_c[3]

        ap_apply = {}
        tr_xin = {}

        def ln1_apply_and_transpose(s, xinT_guard=None):
            for k in range(2):
                i = 2 * s + k
                if i >= 2:
                    W("sync", "ve", ap_apply[i - 2])
                else:
                    W("sync", "ve", pp_ve[NT - 2 + i])
                    W("sync", "ap", pp_ap[NT - 2 + i])
                xs = "st%d" % (i % 2)
                vx = inc(xs, 16)
                E("sync", lambda e, i=i, xs=xs:
                  e.dma_start(out=x0b[i % 2][:, :],
                              in_=dr["xp"][i * 128:(i + 1) * 128, :])
                  .then_inc(S[xs], 16))
                W("vector", xs, vx)
                W("vector", "ve", v_ln1_c[i // 8])
                if s >= 2:
                    W("vector", "t%d" % (s % 2), tr_xin[(s - 2, k)])
                else:
                    W("vector", "wvt", v_wvwo)
                E("vector", lambda e, i=i:
                  e.tensor_scalar(xinb[:, (i % 4) * 1024:(i % 4) * 1024 + 1024],
                                  x0b[i % 2][:, :], ln1[:, 3, i:i + 1],
                                  ln1[:, 2, i:i + 1], ALU.mult, ALU.add)
                  .then_inc(S["ve"], 1))
                ap_apply[i] = inc("ve", 1)
                W("sync", "ve", ap_apply[i])
                if xinT_guard is not None:
                    W("sync", "ve", xinT_guard)
                ts_ = "t%d" % (s % 2)
                E("sync", lambda e, s=s, k=k, i=i, ts_=ts_:
                  e.dma_start(out=xinT[s % 2][:, :, k * 128:k * 128 + 128],
                              in_=xinb[:, (i % 4) * 1024:(i % 4) * 1024 + 1024],
                              transpose=True).then_inc(S[ts_], 16))
                tr_xin[(s, k)] = inc(ts_, 16)

        def proj_slab(s):
            sb = s % 2
            W("tensor", "t%d" % sb, tr_xin[(s, 1)])
            W("tensor", "ap", v_wrot_ap)
            W("tensor", "wvt", v_wvwo)
            W("tensor", "ve", v_bkrot)
            ev = {}
            for cc in range(8):
                reg = 2048 + (cc % 2) * 1024
                if cc >= 2:
                    W("tensor", "ve", ev[cc - 2])
                for fc in range(8):
                    last = fc == 7
                    E("tensor", lambda e, cc=cc, fc=fc, reg=reg, sb=sb,
                      last=last:
                      (e.matmul(ps[:, reg:reg + 256],
                                wkT[:, fc, cc * 128:cc * 128 + 128],
                                xinT[sb][:, fc, 0:256],
                                start=(fc == 0), stop=True)
                       .then_inc(S["te"], 1)) if last else
                      e.matmul(ps[:, reg:reg + 256],
                               wkT[:, fc, cc * 128:cc * 128 + 128],
                               xinT[sb][:, fc, 0:256],
                               start=(fc == 0), stop=False))
                vt = inc("te", 1)
                W("vector", "te", vt)
                E("vector", lambda e, cc=cc, reg=reg, sb=sb:
                  e.tensor_scalar(Ks[sb][:, cc, :], ps[:, reg:reg + 256],
                                  bkr[:, cc:cc + 1], None, ALU.add)
                  .then_inc(S["ve"], 1))
                ev[cc] = inc("ve", 1)
            for rt in range(2):
                for n2 in range(2):
                    idx = rt * 2 + n2
                    reg = 2048 + (idx % 2) * 1024
                    W("tensor", "ve", ev[6 + idx] if idx < 2 else ev[8 + idx - 2])
                    E("tensor", lambda e, n2=n2, reg=reg:
                      e.matmul(ps[:, reg:reg + 512], onesb[0:1, 0:128],
                               brow[0:1, n2 * 512:n2 * 512 + 512],
                               start=True, stop=False))
                    for fc in range(8):
                        last = fc == 7
                        E("tensor", lambda e, rt=rt, n2=n2, fc=fc, reg=reg,
                          sb=sb, last=last:
                          (e.matmul(ps[:, reg:reg + 512],
                                    xinT[sb][:, fc, rt * 128:rt * 128 + 128],
                                    wvT[:, fc, n2 * 512:n2 * 512 + 512],
                                    start=False, stop=True)
                           .then_inc(S["te"], 1)) if last else
                          e.matmul(ps[:, reg:reg + 512],
                                   xinT[sb][:, fc, rt * 128:rt * 128 + 128],
                                   wvT[:, fc, n2 * 512:n2 * 512 + 512],
                                   start=False, stop=False))
                    vt = inc("te", 1)
                    W("vector", "te", vt)
                    E("vector", lambda e, rt=rt, n2=n2, reg=reg, sb=sb:
                      e.tensor_copy(
                          Vp[sb][:, rt, :].rearrange("p (a b) -> p a b", b=65)
                          [:, n2 * 8:(n2 + 1) * 8, 0:64],
                          ps[:, reg:reg + 512]
                          .rearrange("p (a b) -> p a b", b=64))
                      .then_inc(S["ve"], 1))
                    ev[8 + idx] = inc("ve", 1)
            v_v = val("ve")
            qap = {}
            for cc in range(8):
                reg = 2048 + (cc % 2) * 1024
                if cc < 2:
                    W("tensor", "ve", ev[10 + cc])
                else:
                    W("tensor", "ve", qap[cc - 2])
                for fc in range(8):
                    last = fc == 7
                    E("tensor", lambda e, cc=cc, fc=fc, reg=reg, sb=sb,
                      last=last:
                      (e.matmul(ps[:, reg:reg + 128],
                                wqT[:, fc, cc * 128:cc * 128 + 128],
                                xinT[sb][:, fc, 0:128],
                                start=(fc == 0), stop=True)
                       .then_inc(S["te"], 1)) if last else
                      e.matmul(ps[:, reg:reg + 128],
                               wqT[:, fc, cc * 128:cc * 128 + 128],
                               xinT[sb][:, fc, 0:128],
                               start=(fc == 0), stop=False))
                vt = inc("te", 1)
                W("vector", "te", vt)
                E("vector", lambda e, cc=cc, reg=reg, sb=sb:
                  e.tensor_scalar(QC[sb][0:64, cc * 128:cc * 128 + 128],
                                  ps[0:64, reg:reg + 128],
                                  bqr[0:64, cc:cc + 1], None, ALU.add))
                E("vector", lambda e, cc=cc, reg=reg, sb=sb:
                  e.tensor_scalar(QC[sb][64:128, cc * 128:cc * 128 + 128],
                                  ps[64:128, reg:reg + 128],
                                  bqr[64:128, cc:cc + 1], None, ALU.add)
                  .then_inc(S["ve"], 2))
                qap[cc] = inc("ve", 2)
            vq = val("ve")
            W("sync", "ve", vq)
            ts_ = "t%d" % sb
            E("sync", lambda e, sb=sb, ts_=ts_:
              e.dma_start(out=QW[sb][0:64, :], in_=QC[sb][64:128, :])
              .then_inc(S[ts_], 16))
            E("sync", lambda e, sb=sb, ts_=ts_:
              e.dma_start(out=QW[sb][64:128, :], in_=QC[sb][0:64, :])
              .then_inc(S[ts_], 16))
            vqw = inc(ts_, 32)
            return dict(k=ev[7], v=v_v, q=vq, qw=vqw, sb=sb)

        def attention_slab(s, pr, prev_mult, prev_pv, guard):
            sb = s % 2
            sc_d, ex_d, pv_d = {}, {}, {}

            def tile_of(n):
                P, g, t = n // 32, (n % 32) // 2, n % 2
                m, kh = g // 2, g % 2
                t_lo = 2 * m + t
                return P, t_lo, kh

            def scores(n):
                P, t_lo, kh = tile_of(n)
                pb = (t_lo % 2) * 64
                src_q = QC[sb] if (P == t_lo % 2) else QW[sb]
                bb = 4 + (n % 2) * 2
                if n == 0:
                    W("tensor", "ve", pr["k"])
                    W("tensor", "ve", pr["q"])
                    W("tensor", "t%d" % pr["sb"], pr["qw"])
                    if guard is not pr:
                        W("tensor", "ve", guard["q"])
                        W("tensor", "ve", guard["v"])
                if n >= 2:
                    W("tensor", "ex", ex_d[n - 2])
                for half in range(2):
                    last = half == 1
                    E("tensor", lambda e, P=P, t_lo=t_lo, kh=kh, pb=pb,
                      src_q=src_q, bb=bb, half=half, sb=sb, last=last:
                      (e.matmul(ps[:, (bb + half) * 512:(bb + half) * 512 + 512],
                                Ks[sb][pb:pb + 64, t_lo // 2,
                                       kh * 128:kh * 128 + 128],
                                src_q[pb:pb + 64,
                                      half * 512:half * 512 + 512],
                                start=True, stop=True)
                       .then_inc(S["sc"], 1)) if last else
                      e.matmul(ps[:, (bb + half) * 512:(bb + half) * 512 + 512],
                               Ks[sb][pb:pb + 64, t_lo // 2,
                                      kh * 128:kh * 128 + 128],
                               src_q[pb:pb + 64,
                                     half * 512:half * 512 + 512],
                               start=True, stop=True))
                sc_d[n] = inc("sc", 1)

            def expc(n):
                bb = 4 + (n % 2) * 2
                po = (n % 2) * 1024
                if s == 0 and n == 0:
                    W("scalar", "rq", v_repack)
                W("scalar", "sc", sc_d[n])
                if n >= 6:
                    W("scalar", "pv", pv_d[n - 6])
                elif prev_pv:
                    W("scalar", "pv", prev_pv)
                E("scalar", lambda e, n=n, bb=bb, po=po:
                  e.activation(PT[(n // 2) % 3][:, po:po + 1024],
                               ps[:, bb * 512:bb * 512 + 1024],
                               AF.Exp).then_inc(S["ex"], 1))
                ex_d[n] = inc("ex", 1)

            def pv(n):
                P, t_lo, kh = tile_of(n)
                po = (n % 2) * 1024
                W("tensor", "ex", ex_d[n])
                if n == 0:
                    W("tensor", "ve", prev_mult)
                    W("tensor", "ve", pr["v"])
                first = (n % 32) == 0
                last_g = (n % 32) == 31
                for half in range(2):
                    last = half == 1
                    E("tensor", lambda e, P=P, t_lo=t_lo, kh=kh, po=po, n=n,
                      half=half, sb=sb, first=first, last_g=last_g, last=last:
                      (e.matmul(ps[0:65, P * 1024 + half * 512:
                                   P * 1024 + half * 512 + 512],
                                Vp[sb][:, kh, t_lo * 65:t_lo * 65 + 65],
                                PT[(n // 2) % 3][:, po + half * 512:
                                                 po + half * 512 + 512],
                                start=first, stop=last_g)
                       .then_inc(S["pv"], 1)) if last else
                      e.matmul(ps[0:65, P * 1024 + half * 512:
                                  P * 1024 + half * 512 + 512],
                               Vp[sb][:, kh, t_lo * 65:t_lo * 65 + 65],
                               PT[(n // 2) % 3][:, po + half * 512:
                                                po + half * 512 + 512],
                               start=first, stop=last_g))
                pv_d[n] = inc("pv", 1)

            scores(0)
            scores(1)
            expc(0)
            expc(1)
            for n in range(2, 64, 2):
                scores(n)
                scores(n + 1)
                expc(n)
                expc(n + 1)
                pv(n - 2)
                pv(n - 1)
            pv(62)
            pv(63)
            return pv_d, ex_d

        def head_finish(s, pv_d):
            W("vector", "pv", pv_d[63])
            if s % 2 == 0:
                E("vector", lambda e, s=s:
                  e.tensor_copy(ocat[0:64, s // 2, :], ps[0:64, 0:2048])
                  .then_inc(S["ve"], 1))
            else:
                E("vector", lambda e:
                  e.tensor_copy(otmp[0:64, :], ps[0:64, 0:2048])
                  .then_inc(S["ve"], 1))
            inc("ve", 1)
            E("vector", lambda e: e.tensor_copy(oden[64:65, :],
                                                ps[64:65, 0:2048])
              .then_inc(S["ve"], 1))
            ve_evac = inc("ve", 1)
            if s % 2 == 1:
                W("sync", "ve", ve_evac)
                E("sync", lambda e, s=s:
                  e.dma_start(out=ocat[64:128, s // 2, :], in_=otmp[0:64, :])
                  .then_inc(S["tr"], 16))
                inc("tr", 16)
            W("sync", "ve", ve_evac)
            E("sync", lambda e, s=s:
              e.dma_start(out=dspr[:, s * 16:(s + 1) * 16],
                          in_=oden[64:65, :]).then_inc(S["tr"], 16))
            v_spread = inc("tr", 16)
            return dict(evac=ve_evac, spread=v_spread)

        def head_dbcast_mult(s, hf):
            W("tensor", "tr", hf["back"])
            W("tensor", "ve", hf["evac"])
            for c in range(4):
                last = c == 3
                E("tensor", lambda e, c=c, last=last:
                  (e.matmul(ps[0:64, c * 512:(c + 1) * 512], onesf[64:65, 0:64],
                            oden[64:65, c * 512:(c + 1) * 512],
                            start=True, stop=True).then_inc(S["te"], 1))
                  if last else
                  e.matmul(ps[0:64, c * 512:(c + 1) * 512], onesf[64:65, 0:64],
                           oden[64:65, c * 512:(c + 1) * 512],
                           start=True, stop=True))
            vt = inc("te", 1)
            W("vector", "te", vt)
            if s % 2 == 0:
                E("vector", lambda e, s=s:
                  e.tensor_mul(ocat[0:64, s // 2, :], oden[0:64, :],
                               ps[0:64, 0:2048]).then_inc(S["ve"], 1))
                vm = inc("ve", 1)
                return vm, None
            E("vector", lambda e:
              e.tensor_mul(otmp[0:64, :], oden[0:64, :], ps[0:64, 0:2048])
              .then_inc(S["ve"], 1))
            vm = inc("ve", 1)
            W("sync", "ve", vm)
            E("sync", lambda e, s=s:
              e.dma_start(out=ocat[64:128, s // 2, :], in_=otmp[0:64, :])
              .then_inc(S["tr"], 16))
            return vm, inc("tr", 16)

        prs = {}
        ln1_apply_and_transpose(0)
        prs[0] = proj_slab(0)
        ln1_apply_and_transpose(1)
        prs[1] = proj_slab(1)
        ln1_apply_and_transpose(2, xinT_guard=prs[0]["q"])
        ln1_apply_and_transpose(3, xinT_guard=prs[1]["q"])
        prev_mult = 0
        prev_pv = None
        for s in range(NSLAB):
            guard = prs[min(s + 1, NSLAB - 1)]
            pv_d, ex_d = attention_slab(s, prs[s], prev_mult, prev_pv, guard)
            hf = head_finish(s, pv_d)
            if s + 2 < NSLAB:
                prs[s + 2] = proj_slab(s + 2)
            if s + 4 < NSLAB:
                ln1_apply_and_transpose(s + 4, xinT_guard=prs[s + 2]["q"])
            prev_mult = hf["evac"]
            prev_pv = pv_d[63]
            last_spread = hf["spread"]
        W("vector", "tr", last_spread)
        E("vector", lambda e: e.reciprocal(dspr[:, 0:256], dspr[:, 0:256])
          .then_inc(S["ve"], 1))
        v_rec = inc("ve", 1)
        mult_done = {}
        for g in range(8):
            for hh in range(2):
                h = 2 * g + hh
                if g == 0 and hh == 0:
                    W("sync", "ve", v_rec)
                else:
                    W("sync", "te", val("te"))
                E("sync", lambda e, h=h:
                  e.dma_start(out=oden[64:65, :],
                              in_=dspr[:, h * 16:(h + 1) * 16])
                  .then_inc(S["tr"], 16))
                vback = inc("tr", 16)
                W("tensor", "tr", vback)
                if hh == 0:
                    if g == 0:
                        W("tensor", "ve", val("ve"))
                    else:
                        W("tensor", "ve", mult_done[g - 1])
                pb = hh * 64
                for c in range(4):
                    last = c == 3
                    E("tensor", lambda e, pb=pb, c=c, last=last:
                      (e.matmul(ps[pb:pb + 64, c * 512:(c + 1) * 512],
                                onesf[64:65, 0:64],
                                oden[64:65, c * 512:(c + 1) * 512],
                                start=True, stop=True).then_inc(S["te"], 1))
                      if last else
                      e.matmul(ps[pb:pb + 64, c * 512:(c + 1) * 512],
                               onesf[64:65, 0:64],
                               oden[64:65, c * 512:(c + 1) * 512],
                               start=True, stop=True))
                inc("te", 1)
            vt = val("te")
            W("vector", "te", vt)
            E("vector", lambda e, g=g:
              e.tensor_mul(ocat[:, g, :], ocat[:, g, :], ps[:, 0:2048])
              .then_inc(S["ve"], 1))
            mult_done[g] = inc("ve", 1)
        v_att_ve = val("ve")
        v_att_tr = val("tr")
        v_att_pv = val("pv")
        if stop_after == "attn":
            W("gpsimd", "ve", v_att_ve)
            W("gpsimd", "tr", v_att_tr)
            E("gpsimd", lambda e:
              e.dma_start(out=out_d[:, :],
                          in_=ocat[:, :, :]).then_inc(S["out"], 16))
            E("gpsimd", lambda e: e.wait_ge(S["out"], 16))
            _finish(block, K)
            return nc
        if stop_after == "proj0":
            W("gpsimd", "ve", prs[1]["v"])
            W("gpsimd", "ap", prs[1]["q"])
            E("gpsimd", lambda e:
              e.dma_start(out=out_d[0:128, 0:1024],
                          in_=QC[0][:, :]).then_inc(S["out"], 16))
            E("gpsimd", lambda e:
              e.dma_start(out=out_d[128:256, 0:1024],
                          in_=QC[1][:, :]).then_inc(S["out"], 16))
            E("gpsimd", lambda e:
              e.dma_start(out=out_d[256:384, 0:512],
                          in_=Ks[0][:, 0:2, :]).then_inc(S["out"], 16))
            E("gpsimd", lambda e:
              e.dma_start(out=out_d[384:512, 0:1024],
                          in_=Vp[0][:, 0, 0:1024]).then_inc(S["out"], 16))
            E("gpsimd", lambda e:
              e.dma_start(out=out_d[512:640, 0:512],
                          in_=xinT[0][:, 0:2, 0:256])
              .then_inc(S["out"], 16))
            E("gpsimd", lambda e: e.wait_ge(S["out"], 80))
            _finish(block, K)
            return nc

        psA_free = {}
        x1_store = {}
        xin2_done = {}
        x0_used = {}
        for tt in range(16):
            reg = (tt % 4) * 1024
            if tt == 0:
                W("tensor", "ve", v_att_ve)
                W("tensor", "tr", v_att_tr)
            if tt >= 4:
                W("tensor", "ve", psA_free[tt - 4][0])
                W("tensor", "ap", psA_free[tt - 4][1])
            E("tensor", lambda e, reg=reg:
              e.matmul(ps[:, reg:reg + 512], onesb[0:1, 0:128],
                       brow[0:1, 1024:1536], start=True, stop=False))
            E("tensor", lambda e, reg=reg:
              e.matmul(ps[:, reg + 512:reg + 1024], onesb[0:1, 0:128],
                       brow[0:1, 1536:2048], start=True, stop=False))
            for g in range(8):
                for n2 in range(2):
                    last = g == 7 and n2 == 1
                    E("tensor", lambda e, tt=tt, g=g, n2=n2, reg=reg,
                      last=last:
                      (e.matmul(ps[:, reg + n2 * 512:reg + n2 * 512 + 512],
                                ocat[:, g, tt * 128:tt * 128 + 128],
                                woT[:, g, n2 * 512:n2 * 512 + 512],
                                start=False, stop=(g == 7))
                       .then_inc(S["te"], 1)) if last else
                      e.matmul(ps[:, reg + n2 * 512:reg + n2 * 512 + 512],
                               ocat[:, g, tt * 128:tt * 128 + 128],
                               woT[:, g, n2 * 512:n2 * 512 + 512],
                               start=False, stop=(g == 7)))
            vt = inc("te", 1)
            if stop_after == "p4mm":
                psA_free[tt] = (0, 0)
                x0_used[tt] = 0
                x1_store[tt] = 0
                xin2_done[tt] = 0
                continue
            if tt >= 2:
                W("sync", "ve", x0_used[tt - 2])
            xs = "x%d" % (tt % 2)
            vx0 = inc(xs, 16)
            E("sync", lambda e, tt=tt, xs=xs:
              e.dma_start(out=x0b[tt % 2][:, :],
                          in_=dr["x0"][tt * 128:(tt + 1) * 128, :])
              .then_inc(S[xs], 16))
            W("vector", "te", vt)
            E("vector", lambda e, reg=reg:
              e.reduce_sum(stat[:, 0:1], ps[:, reg:reg + 1024], axis=AX.X))
            E("vector", lambda e: e.memset(stat[:, 1:2], 0.0)
              .then_inc(S["ve"], 1))
            vv0 = inc("ve", 1)
            W("scalar", "te", vt)
            W("scalar", "ve", vv0)
            E("scalar", lambda e, reg=reg:
              e.activation(jkb[:, :], ps[:, reg:reg + 1024], AF.Square,
                           accum_out=stat[:, 1:2]).then_inc(S["ap"], 1))
            va0 = inc("ap", 1)
            W("vector", "ap", va0)
            E("vector", lambda e: e.tensor_scalar(stat[:, 2:3], stat[:, 0:1],
                                                  -1.0 / D, None, ALU.mult))
            E("vector", lambda e: e.tensor_scalar(stat[:, 3:4], stat[:, 1:2],
                                                  1.0 / D, None, ALU.mult)
              .then_inc(S["ve"], 1))
            W("vector", "ve", inc("ve", 1))
            E("vector", lambda e: e.tensor_mul(stat[:, 4:5], stat[:, 2:3],
                                               stat[:, 2:3])
              .then_inc(S["ve"], 1))
            W("vector", "ve", inc("ve", 1))
            E("vector", lambda e: e.scalar_tensor_tensor(
                stat[:, 3:4], stat[:, 3:4], EPS, stat[:, 4:5], ALU.add,
                ALU.subtract).then_inc(S["ve"], 1))
            vv = inc("ve", 1)
            W("scalar", "ve", vv)
            E("scalar", lambda e: e.activation(stat[:, 4:5], stat[:, 3:4],
                                               AF.Sqrt).then_inc(S["ap"], 1))
            va = inc("ap", 1)
            W("vector", "ap", va)
            E("vector", lambda e: e.reciprocal(stat[:, 5:6], stat[:, 4:5])
              .then_inc(S["ve"], 1))
            W("vector", "ve", inc("ve", 1))
            E("vector", lambda e: e.tensor_mul(stat[:, 6:7], stat[:, 2:3],
                                               stat[:, 5:6])
              .then_inc(S["ve"], 1))
            vv = inc("ve", 1)
            W("scalar", "ve", vv)
            E("scalar", lambda e, reg=reg:
              e.activation(tmpL[:, :], ps[:, reg:reg + 1024], AF.Identity,
                           bias=stat[:, 6:7], scale=stat[:, 5:6])
              .then_inc(S["ap"], 1))
            va = inc("ap", 1)
            psA_free[tt] = (vv, va)
            W("vector", "ap", va)
            W("vector", xs, vx0)
            if tt >= 2:
                W("vector", "st%d" % (tt % 2), x1_store[tt - 2])
            E("vector", lambda e, tt=tt:
              e.tensor_add(x1b[tt % 2][:, :], tmpL[:, :], x0b[tt % 2][:, :])
              .then_inc(S["ve"], 1))
            vx1 = inc("ve", 1)
            x0_used[tt] = vx1
            W("sync", "ve", vx1)
            sts = "st%d" % (tt % 2)
            E("sync", lambda e, tt=tt, sts=sts:
              e.dma_start(out=x1_dram[tt * 128:(tt + 1) * 128, :],
                          in_=x1b[tt % 2][:, :]).then_inc(S[sts], 16))
            x1_store[tt] = inc(sts, 16)
            E("vector", lambda e, tt=tt:
              e.reduce_sum(stat[:, 8:9], x1b[tt % 2][:, :], axis=AX.X))
            E("vector", lambda e: e.memset(stat[:, 9:10], 0.0)
              .then_inc(S["ve"], 1))
            vv8 = inc("ve", 1)
            W("scalar", "ve", vv8)
            E("scalar", lambda e, tt=tt:
              e.activation(jkb[:, :], x1b[tt % 2][:, :], AF.Square,
                           accum_out=stat[:, 9:10]).then_inc(S["ap"], 1))
            va8 = inc("ap", 1)
            W("vector", "ap", va8)
            E("vector", lambda e:
              e.tensor_scalar(stat[:, 10:11], stat[:, 8:9], -1.0 / D, None,
                              ALU.mult))
            E("vector", lambda e:
              e.tensor_scalar(stat[:, 11:12], stat[:, 9:10], 1.0 / D, None,
                              ALU.mult).then_inc(S["ve"], 1))
            W("vector", "ve", inc("ve", 1))
            E("vector", lambda e: e.tensor_mul(stat[:, 12:13], stat[:, 10:11],
                                               stat[:, 10:11])
              .then_inc(S["ve"], 1))
            W("vector", "ve", inc("ve", 1))
            E("vector", lambda e: e.scalar_tensor_tensor(
                stat[:, 11:12], stat[:, 11:12], EPS, stat[:, 12:13], ALU.add,
                ALU.subtract).then_inc(S["ve"], 1))
            vv = inc("ve", 1)
            W("scalar", "ve", vv)
            E("scalar", lambda e: e.activation(stat[:, 12:13], stat[:, 11:12],
                                               AF.Sqrt).then_inc(S["ap"], 1))
            va = inc("ap", 1)
            W("vector", "ap", va)
            E("vector", lambda e: e.reciprocal(stat[:, 13:14], stat[:, 12:13])
              .then_inc(S["ve"], 1))
            W("vector", "ve", inc("ve", 1))
            E("vector", lambda e: e.tensor_mul(stat[:, 14:15], stat[:, 10:11],
                                               stat[:, 13:14])
              .then_inc(S["ve"], 1))
            vv = inc("ve", 1)
            W("scalar", "ve", vv)
            E("scalar", lambda e, tt=tt:
              e.activation(xin2[:, tt, :], x1b[tt % 2][:, :], AF.Identity,
                           bias=stat[:, 14:15], scale=stat[:, 13:14])
              .then_inc(S["ap"], 1))
            xin2_done[tt] = inc("ap", 1)

        if stop_after == "p4mm":
            W("vector", "te", val("te"))
            vv_ = inc("ve", 1)
            E("vector", lambda e: e.tensor_copy(oden[0:64, 0:1024],
                                                ps[0:64, 0:1024])
              .then_inc(S["ve"], 1))
            W("sync", "ve", val("ve"))
            W("sync", "te", val("te"))
            E("sync", lambda e:
              e.dma_start(out=out_d[0:64, :], in_=oden[0:64, 0:1024])
              .then_inc(S["out"], 16))
            E("sync", lambda e: e.wait_ge(S["out"], 16))
            _finish(block, K)
            return nc

        if stop_after == "p4ln2":
            W("gpsimd", "st0", val("st0"))
            W("gpsimd", "st1", val("st1"))
            E("gpsimd", lambda e:
              e.dma_start(out=out_d[:, :], in_=x1_dram[:, :])
              .then_inc(S["out"], 16))
            E("gpsimd", lambda e: e.wait_ge(S["out"], 16))
            _finish(block, K)
            return nc

        if stop_after == "p4":
            W("gpsimd", "st0", val("st0"))
            W("gpsimd", "st1", val("st1"))
            E("gpsimd", lambda e:
              e.dma_start(out=out_d[:, :], in_=x1_dram[:, :])
              .then_inc(S["out"], 16))
            E("gpsimd", lambda e: e.wait_ge(S["out"], 16))
            _finish(block, K)
            return nc

        w1_te = {}
        w2_te = {}
        gelu_done = {}
        psF_ap = {}
        out_tot = 0
        x1r_used = {}
        out_used = 0
        for tb in range(4):
            for k in range(4):
                tt = tb * 4 + k
                W("sync", "ap", xin2_done[tt])
                if tb >= 2:
                    W("sync", "te", w1_te[(tb - 2, 31)])
                ts_ = "t%d" % (tb % 2)
                E("sync", lambda e, tb=tb, k=k, tt=tt, ts_=ts_:
                  e.dma_start(out=xin2T[tb % 2][:, :, k * 128:k * 128 + 128],
                              in_=xin2[:, tt, :], transpose=True)
                  .then_inc(S[ts_], 16))
                inc(ts_, 16)
            v_x2t = val("t%d" % (tb % 2))
            if stop_after == "p5tr":
                break
            for jc in range(32):
                if (tb, jc - 2) in w1_te:
                    W("gpsimd", "te", w1_te[(tb, jc - 2)])
                elif tb > 0 and jc < 2:
                    W("gpsimd", "te", w1_te[(tb - 1, 30 + jc)])
                else:
                    W("gpsimd", "pv", v_att_pv)
                    if tb == 0 and jc == 0:
                        W("gpsimd", "rq", v_repack)
                ws = "w5%d" % (jc % 2)
                vw = inc(ws, 16)
                E("gpsimd", lambda e, jc=jc, ws=ws:
                  e.dma_start(out=w1b[jc % 2][:, :, :],
                              in_=w1q_d[jc]).then_inc(S[ws], 16))
                reg = (jc % 2) * 512
                if stop_after == "p5w1":
                    continue
                W("tensor", ws, vw)
                W("tensor", "t%d" % (tb % 2), v_x2t)
                if (tb, jc - 2) in gelu_done:
                    W("tensor", "ap", gelu_done[(tb, jc - 2)])
                elif tb > 0 and jc < 2:
                    W("tensor", "ap", psF_ap[(tb - 1, 0)])
                for fc in range(8):
                    last = fc == 7
                    E("tensor", lambda e, jc=jc, fc=fc, reg=reg, tb=tb,
                      last=last:
                      (e.matmul(ps[:, reg:reg + 512], w1b[jc % 2][:, fc, :],
                                xin2T[tb % 2][:, fc, :],
                                start=(fc == 0), stop=True)
                       .then_inc(S["te"], 1)) if last else
                      e.matmul(ps[:, reg:reg + 512], w1b[jc % 2][:, fc, :],
                               xin2T[tb % 2][:, fc, :],
                               start=(fc == 0), stop=False))
                w1_te[(tb, jc)] = inc("te", 1)
                W("scalar", "te", w1_te[(tb, jc)])
                _gdst = (jkb[:, 0:512] if stop_after == "p5nog"
                         else gT[:, jc, :])
                E("scalar", lambda e, jc=jc, reg=reg, _gdst=_gdst:
                  e.activation(_gdst, ps[:, reg:reg + 512], AF.Gelu,
                               bias=b1c[:, jc:jc + 1]).then_inc(S["ap"], 1))
                gelu_done[(tb, jc)] = inc("ap", 1)
            if stop_after in ("p5w1", "p5nog", "p5mm1"):
                break
            W("tensor", "ap", gelu_done[(tb, 31)])
            for k in range(4):
                E("tensor", lambda e, k=k:
                  e.matmul(ps[:, k * 1024:k * 1024 + 512], onesb[0:1, 0:128],
                           brow[0:1, 2048:2560], start=True, stop=False))
                E("tensor", lambda e, k=k:
                  e.matmul(ps[:, k * 1024 + 512:k * 1024 + 1024],
                           onesb[0:1, 0:128], brow[0:1, 2560:3072],
                           start=True, stop=False))
            for jc in range(32):
                if (tb, jc - 2) in w2_te:
                    W("gpsimd", "te", w2_te[(tb, jc - 2)])
                else:
                    W("gpsimd", "te", w1_te[(tb, 30 + jc)])
                ws = "w5%d" % (jc % 2)
                vw = inc(ws, 16)
                E("gpsimd", lambda e, jc=jc, ws=ws:
                  e.dma_start(out=w2b[jc % 2][:, :],
                              in_=w2q_d[jc * 128:(jc + 1) * 128, :])
                  .then_inc(S[ws], 16))
                W("tensor", ws, vw)
                for k in range(4):
                    for n2 in range(2):
                        last = k == 3 and n2 == 1
                        E("tensor", lambda e, jc=jc, k=k, n2=n2, last=last:
                          (e.matmul(ps[:, k * 1024 + n2 * 512:
                                       k * 1024 + n2 * 512 + 512],
                                    gT[:, jc, k * 128:k * 128 + 128],
                                    w2b[jc % 2][:, n2 * 512:n2 * 512 + 512],
                                    start=False, stop=(jc == 31))
                           .then_inc(S["te"], 1)) if last else
                          e.matmul(ps[:, k * 1024 + n2 * 512:
                                      k * 1024 + n2 * 512 + 512],
                                   gT[:, jc, k * 128:k * 128 + 128],
                                   w2b[jc % 2][:, n2 * 512:n2 * 512 + 512],
                                   start=False, stop=(jc == 31)))
                w2_te[(tb, jc)] = inc("te", 1)
            v_mm2 = val("te")
            if stop_after == "p5mm2":
                break
            last_ap = 0
            for k in range(4):
                W("vector", "te", v_mm2)
                E("vector", lambda e, k=k:
                  e.reduce_sum(stat[:, 32 + k:33 + k],
                               ps[:, k * 1024:k * 1024 + 1024], axis=AX.X))
                E("vector", lambda e, k=k: e.memset(stat[:, 36 + k:37 + k],
                                                    0.0).then_inc(S["ve"], 1))
                vv0 = inc("ve", 1)
                W("scalar", "te", v_mm2)
                W("scalar", "ve", vv0)
                E("scalar", lambda e, k=k:
                  e.activation(jkb[:, :], ps[:, k * 1024:k * 1024 + 1024],
                               AF.Square, accum_out=stat[:, 36 + k:37 + k])
                  .then_inc(S["ap"], 1))
                inc("ap", 1)
            W("vector", "ap", val("ap"))
            E("vector", lambda e:
              e.tensor_scalar(stat[:, 40:44], stat[:, 32:36], -1.0 / D, None,
                              ALU.mult))
            E("vector", lambda e:
              e.tensor_scalar(stat[:, 44:48], stat[:, 36:40], 1.0 / D, None,
                              ALU.mult).then_inc(S["ve"], 1))
            W("vector", "ve", inc("ve", 1))
            E("vector", lambda e:
              e.tensor_mul(stat[:, 48:52], stat[:, 40:44], stat[:, 40:44])
              .then_inc(S["ve"], 1))
            W("vector", "ve", inc("ve", 1))
            E("vector", lambda e: e.scalar_tensor_tensor(
                stat[:, 44:48], stat[:, 44:48], EPS, stat[:, 48:52], ALU.add,
                ALU.subtract).then_inc(S["ve"], 1))
            vv = inc("ve", 1)
            W("scalar", "ve", vv)
            E("scalar", lambda e: e.activation(stat[:, 48:52], stat[:, 44:48],
                                               AF.Sqrt).then_inc(S["ap"], 1))
            va = inc("ap", 1)
            W("vector", "ap", va)
            E("vector", lambda e: e.reciprocal(stat[:, 52:56], stat[:, 48:52])
              .then_inc(S["ve"], 1))
            W("vector", "ve", inc("ve", 1))
            E("vector", lambda e:
              e.tensor_mul(stat[:, 40:44], stat[:, 40:44], stat[:, 52:56])
              .then_inc(S["ve"], 1))
            v_st4 = inc("ve", 1)
            for k in range(4):
                tt = tb * 4 + k
                if tt >= 2:
                    W("sync", "ve", x1r_used.get(tt - 2, 0))
                W("sync", "st%d" % (tt % 2), x1_store[tt])
                xs = "x%d" % (tt % 2)
                vx1 = inc(xs, 16)
                E("sync", lambda e, tt=tt, xs=xs:
                  e.dma_start(out=x1rb[tt % 2][:, :],
                              in_=x1_dram[tt * 128:(tt + 1) * 128, :])
                  .then_inc(S[xs], 16))
                W("scalar", "ve", v_st4 if k == 0 else x1r_used[tt - 1])
                E("scalar", lambda e, k=k:
                  e.activation(junk[:, :], ps[:, k * 1024:k * 1024 + 1024],
                               AF.Identity, bias=stat[:, 40 + k:41 + k],
                               scale=stat[:, 52 + k:53 + k])
                  .then_inc(S["ap"], 1))
                va = inc("ap", 1)
                last_ap = va
                psF_ap[(tb, k)] = va
                W("vector", "ap", va)
                W("vector", xs, vx1)
                if out_used:
                    W("vector", "out", out_used)
                E("vector", lambda e, tt=tt:
                  e.tensor_add(outb[:, :], junk[:, :], x1rb[tt % 2][:, :])
                  .then_inc(S["ve"], 1))
                vo = inc("ve", 1)
                x1r_used[tt] = vo
                W("sync", "ve", vo)
                E("sync", lambda e, tt=tt:
                  e.dma_start(out=out_d[tt * 128:(tt + 1) * 128, :],
                              in_=outb[:, :]).then_inc(S["out"], 16))
                out_tot = inc("out", 16)
                out_used = out_tot
        W("sync", "out", out_tot)
        _finish(block, K)
        return nc


def _finish(block, K):
    @block.sync
    def _(e):
        for f in K.ops["sync"]:
            f(e)

    @block.tensor
    def _(e):
        for f in K.ops["tensor"]:
            f(e)

    @block.vector
    def _(e):
        for f in K.ops["vector"]:
            f(e)

    @block.scalar
    def _(e):
        for f in K.ops["scalar"]:
            f(e)

    @block.gpsimd
    def _(e):
        for f in K.ops["gpsimd"]:
            f(e)


_NC_CACHE = {}


def _get_nc():
    if "nc" not in _NC_CACHE:
        _NC_CACHE["nc"] = build_program()
    return _NC_CACHE["nc"]


def kernel(L0_attn_x, L0_attn_wq, L0_attn_bq, L0_attn_wk, L0_attn_bk,
           L0_attn_wv, L0_attn_bv, L0_attn_wo, L0_attn_bo, L0_attn_cos,
           L0_attn_sin, L0_ffn_w1, L0_ffn_b1, L0_ffn_w2, L0_ffn_b2):
    x = np.ascontiguousarray(np.asarray(L0_attn_x, dtype=np.float32))
    B = x.shape[0]
    eyeC = np.eye(128, dtype=np.float32)
    eyeU = np.zeros((128, 128), np.float32)
    eyeD = np.zeros((128, 128), np.float32)
    for p in range(0, 128, 2):
        eyeU[p, p + 1] = 1.0
    for p in range(1, 128, 2):
        eyeD[p, p - 1] = 1.0
    shared = dict(
        wq=np.ascontiguousarray(L0_attn_wq, np.float32),
        wk=np.ascontiguousarray(L0_attn_wk, np.float32),
        wv=np.ascontiguousarray(L0_attn_wv, np.float32),
        wo=np.ascontiguousarray(L0_attn_wo, np.float32),
        bq=np.ascontiguousarray(L0_attn_bq, np.float32),
        bk=np.ascontiguousarray(L0_attn_bk, np.float32),
        bv=np.ascontiguousarray(L0_attn_bv, np.float32),
        bo=np.ascontiguousarray(L0_attn_bo, np.float32),
        cosr=np.ascontiguousarray(np.asarray(L0_attn_cos)[3199], np.float32),
        sinr=np.ascontiguousarray(np.asarray(L0_attn_sin)[3199], np.float32),
        w1p=np.ascontiguousarray(
            np.asarray(L0_ffn_w1, np.float32).reshape(8, 128, 32, 128)
            .transpose(2, 1, 0, 3)).reshape(32, 128, 1024),
        b1=np.ascontiguousarray(L0_ffn_b1, np.float32),
        w2=np.ascontiguousarray(L0_ffn_w2, np.float32),
        b2=np.ascontiguousarray(L0_ffn_b2, np.float32),
        eyeC=eyeC, eyeU=eyeU, eyeD=eyeD,
    )
    in_maps = []
    for c in range(8):
        b, half = c // 2, c % 2
        xb = x[b]
        xpc = xb.reshape(16, 2, 128, D)[:, [half, 1 - half]].reshape(T, D)
        x0c = xb[half * TOWN + _TMAP]
        m = dict(shared)
        m["xp"] = np.ascontiguousarray(xpc)
        m["x0"] = np.ascontiguousarray(x0c)
        in_maps.append(m)

    nc = _get_nc()
    res = run_bass_kernel_spmd(nc, in_maps, core_ids=list(range(8)))
    out = np.empty((B, T, D), np.float32)
    inv = np.empty_like(_TMAP)
    inv[_TMAP] = np.arange(2048)
    for c in range(8):
        b, half = c // 2, c % 2
        out[b, half * TOWN:(half + 1) * TOWN] = res.results[c]["out"][inv]
    return out


# version 15
# speedup vs baseline: 1.0931x; 1.0025x over previous
import numpy as np
import ml_dtypes
import concourse.bass as bass
import concourse.mybir as mybir
from concourse.bass_utils import run_bass_kernel_spmd

F32 = mybir.dt.float32
BF16 = mybir.dt.bfloat16
AF = mybir.ActivationFunctionType
ALU = mybir.AluOpType
AX = mybir.AxisListType

D = 1024
T = 4096
TOWN = 2048
NSLAB = 16
NT = 32
EPS = 1e-5


def _tmap():
    q = np.arange(2048)
    P, u, j = q // 1024, (q % 1024) // 128, q % 128
    return j * 16 + 2 * u + P


_TMAP = _tmap()


class _K:

    def __init__(self):
        self.ops = {e: [] for e in ("sync", "tensor", "vector", "scalar",
                                    "gpsimd")}
        self.ct = {}

    def E(self, eng, fn):
        self.ops[eng].append(fn)

    def inc(self, sem, n):
        self.ct[sem] = self.ct.get(sem, 0) + n
        return self.ct[sem]

    def val(self, sem):
        return self.ct.get(sem, 0)


def build_program(stop_after=None):
    nc = bass.Bass("TRN2", target_bir_lowering=False,
                   detect_race_conditions=False)
    K = _K()
    E, inc, val = K.E, K.inc, K.val

    dr = {}
    for nm, shp in [("xp", [T, D]), ("x0", [TOWN, D]),
                    ("wq", [D, D]), ("wk", [D, D]), ("wv", [D, D]),
                    ("wo", [D, D]), ("bq", [D]), ("bk", [D]), ("bv", [D]),
                    ("bo", [D]), ("cosr", [64]), ("sinr", [64]),
                    ("w1p", [32, 128, D]), ("b1", [4096]),
                    ("w2", [4096, D]),
                    ("b2", [D]), ("eyeC", [128, 128]), ("eyeU", [128, 128]),
                    ("eyeD", [128, 128])]:
        dr[nm] = nc.dram_tensor(nm, shp, F32, kind="ExternalInput").ap()
    out_d = nc.dram_tensor("out", [TOWN, D], F32, kind="ExternalOutput").ap()
    x1_dram = nc.dram_tensor("x1s", [TOWN, D], F32, kind="Internal").ap()
    w1q_d = nc.dram_tensor("w1q", [32, 128, D], BF16, kind="Internal").ap()
    w2q_d = nc.dram_tensor("w2q", [4096, D], BF16, kind="Internal").ap()

    NBF = 81088
    NF32 = 12352

    bo_ = [0]

    def ab(n):
        o = bo_[0]
        bo_[0] += n
        return o

    o_woT = ab(8 * 1024)
    o_ocat = ab(8 * 2048)
    o_otmp = ab(2048)
    o_xinb = ab(4 * 1024)
    o_onesb = ab(128)
    o_brow = ab(3 * 1024)
    _pbf = bo_[0]
    o_wqT = ab(8 * 1024)
    o_wkT = ab(8 * 1024)
    o_wvT = ab(8 * 1024)
    o_xinT = [ab(8 * 256), ab(8 * 256)]
    o_Ks = [ab(8 * 256), ab(8 * 256)]
    o_Vp = [ab(2 * 1040), ab(2 * 1040)]
    o_QC = [ab(1024), ab(1024)]
    o_QW = [ab(1024), ab(1024)]
    o_PT = [ab(2048), ab(2048), ab(2048)]
    _endA = bo_[0]
    bo_[0] = _pbf
    o_xin2 = ab(16 * 1024)
    o_gT = ab(32 * 512)
    o_xin2T = [ab(8 * 512), ab(8 * 512)]
    o_w1b = [ab(8 * 128), ab(8 * 128)]
    o_w2b = [ab(1024), ab(1024)]
    _endB = bo_[0]
    assert max(_endA, _endB) <= NBF, (_endA, _endB)

    fo_ = [0]

    def af(n):
        o = fo_[0]
        fo_[0] += n
        return o

    o_oden = af(2048)
    o_dspr = af(16)
    o_stat = af(64)
    o_ln1 = af(6 * 32)
    o_bqr = af(8)
    o_bkr = af(8)
    o_b1c = af(32)
    o_cs = af(8)
    o_onesf = af(64)
    o_mrt = af(128)
    o_mrtq = af(128)
    o_eye = af(3 * 128)
    _pf = fo_[0]
    o_xpb = [af(1024), af(1024)]
    o_wst = af(1024)
    _endFA = fo_[0]
    fo_[0] = _pf
    o_x1rb = [af(1024), af(1024)]
    o_outb = af(1024)
    o_junk = af(1024)
    _endFA2 = fo_[0]
    fo_[0] = max(_endFA, _endFA2)
    o_x0b = [af(1024), af(1024)]
    o_x1b = [af(1024), af(1024)]
    o_tmpL = af(1024)
    assert fo_[0] <= NF32, fo_[0]

    import contextlib
    with contextlib.ExitStack() as _st:
        ABh = _st.enter_context(nc.sbuf_tensor("AB", [128, NBF], BF16))
        AFh = _st.enter_context(nc.sbuf_tensor("AF", [128, NF32], F32))
        ps = _st.enter_context(nc.psum_tensor("ps", [128, 4096], F32))
        S = {}
        for _snm in ("wg", "wq2", "wvt", "t0", "t1", "x0", "x1", "st0",
                     "st1", "w50", "w51", "tr", "te", "sc", "pv", "ex",
                     "ve", "ap", "out", "rp", "rq"):
            S[_snm] = _st.enter_context(nc.semaphore("s_" + _snm))
        block = _st.enter_context(nc.Block())

        def B(off, n):
            return ABh[:, off:off + n]

        def Fv(off, n):
            return AFh[:, off:off + n]

        woT = B(o_woT, 8192).rearrange("p (c n) -> p c n", c=8)
        ocat = B(o_ocat, 16384).rearrange("p (c n) -> p c n", c=8)
        otmp = B(o_otmp, 2048)
        xinb = B(o_xinb, 4096)
        onesb = B(o_onesb, 128)
        brow = B(o_brow, 3072)
        wqT = B(o_wqT, 8192).rearrange("p (c n) -> p c n", c=8)
        wkT = B(o_wkT, 8192).rearrange("p (c n) -> p c n", c=8)
        wvT = B(o_wvT, 8192).rearrange("p (c n) -> p c n", c=8)
        xinT = [B(o, 2048).rearrange("p (c n) -> p c n", c=8) for o in o_xinT]
        Ks = [B(o, 2048).rearrange("p (c n) -> p c n", c=8) for o in o_Ks]
        Vp = [B(o, 2080).rearrange("p (c n) -> p c n", c=2) for o in o_Vp]
        QC = [B(o, 1024) for o in o_QC]
        QW = [B(o, 1024) for o in o_QW]
        PT = [B(o, 2048) for o in o_PT]
        xin2 = B(o_xin2, 16384).rearrange("p (c n) -> p c n", c=16)
        gT = B(o_gT, 16384).rearrange("p (c n) -> p c n", c=32)
        xin2T = [B(o, 4096).rearrange("p (c n) -> p c n", c=8)
                 for o in o_xin2T]
        w1b = [B(o, 1024).rearrange("p (c n) -> p c n", c=8) for o in o_w1b]
        w2b = [B(o, 1024) for o in o_w2b]
        jkb = B(o_otmp, 1024)

        oden = Fv(o_oden, 2048)
        dspr = Fv(o_eye, 256)
        stat = Fv(o_stat, 64)
        ln1 = Fv(o_ln1, 192).rearrange("p (c n) -> p c n", c=6)
        bqr = Fv(o_bqr, 8)
        bkr = Fv(o_bkr, 8)
        b1c = Fv(o_b1c, 32)
        cs = Fv(o_cs, 8)
        onesf = Fv(o_onesf, 64)
        mrt = Fv(o_mrt, 128)
        mrtq = Fv(o_mrtq, 128)
        eye = Fv(o_eye, 384)
        xpb = [Fv(o, 1024) for o in o_xpb]
        wst = Fv(o_wst, 1024)
        x1rb = [Fv(o, 1024) for o in o_x1rb]
        outb = Fv(o_outb, 1024)
        junk = Fv(o_junk, 1024)
        x0b = [Fv(o, 1024) for o in o_x0b]
        x1b = [Fv(o, 1024) for o in o_x1b]
        tmpL = Fv(o_tmpL, 1024)

        def W(eng, sem, v):
            if v > 0:
                E(eng, lambda e, s=S[sem], v=v: e.wait_ge(s, v))

        def gdma(dst, src, slow=False):
            v = inc("wg", 16)
            E("gpsimd", lambda e, d=dst, s=src, slow=slow:
              e.dma_start(out=d, in_=s, allow_slow_non_contiguous=slow)
              .then_inc(S["wg"], 16))
            return v

        gdma(cs[0:64, 0:1], dr["cosr"][:].rearrange("(a c) -> c a", a=1))
        gdma(cs[64:128, 0:1], dr["cosr"][:].rearrange("(a c) -> c a", a=1))
        gdma(cs[0:64, 3:4], dr["sinr"][:].rearrange("(a c) -> c a", a=1))
        gdma(cs[64:128, 3:4], dr["sinr"][:].rearrange("(a c) -> c a", a=1))
        gdma(eye[:, 0:128], dr["eyeC"][:, :])
        gdma(eye[:, 128:256], dr["eyeU"][:, :])
        gdma(eye[:, 256:384], dr["eyeD"][:, :])
        gdma(bqr[:, 0:8], dr["bq"][:].rearrange("(c p) -> p c", p=128), slow=True)
        gdma(bkr[:, 0:8], dr["bk"][:].rearrange("(c p) -> p c", p=128), slow=True)
        gdma(b1c[:, 0:32], dr["b1"][:].rearrange("(c p) -> p c", p=128), slow=True)
        gdma(brow[0:1, 0:1024], dr["bv"][:].rearrange("(a n) -> a n", a=1))
        gdma(brow[0:1, 1024:2048], dr["bo"][:].rearrange("(a n) -> a n", a=1))
        gdma(brow[0:1, 2048:3072], dr["b2"][:].rearrange("(a n) -> a n", a=1))
        v_bias = val("wg")

        E("vector", lambda e: e.memset(cs[:, 1:3], 0.0).then_inc(S["ve"], 1))
        v_cz = inc("ve", 1)
        W("sync", "ve", v_cz)
        W("sync", "wg", v_bias)
        E("sync", lambda e: e.dma_start(out=cs[0:127, 1:2], in_=cs[1:128, 3:4])
          .then_inc(S["tr"], 16))
        E("sync", lambda e: e.dma_start(out=cs[1:128, 2:3], in_=cs[0:127, 3:4])
          .then_inc(S["tr"], 16))
        v_csh = inc("tr", 32)

        E("vector", lambda e: e.memset(onesb[:, :], 1.0))
        E("vector", lambda e: e.memset(onesf[:, :], 1.0))
        E("vector", lambda e: e.memset(Vp[0][:, :, :], 1.0))
        E("vector", lambda e: e.memset(Vp[1][:, :, :], 1.0))
        W("vector", "wg", v_bias)
        W("vector", "tr", v_csh)
        E("vector", lambda e: e.tensor_scalar(mrt[:, :], eye[:, 0:128],
                                              cs[:, 0:1], None, ALU.mult))
        E("vector", lambda e: e.tensor_scalar(eye[:, 128:256], eye[:, 128:256],
                                              cs[:, 1:2], None, ALU.mult))
        E("vector", lambda e: e.tensor_add(mrt[:, :], mrt[:, :],
                                           eye[:, 128:256]))
        E("vector", lambda e: e.tensor_scalar(eye[:, 256:384], eye[:, 256:384],
                                              cs[:, 2:3], None, ALU.mult))
        E("vector", lambda e: e.tensor_sub(mrt[:, :], mrt[:, :],
                                           eye[:, 256:384]))
        E("vector", lambda e: e.tensor_scalar(mrtq[:, :], mrt[:, :], 0.125,
                                              None, ALU.mult)
          .then_inc(S["ve"], 1))
        v_mrot = inc("ve", 1)

        v_wv_tr = {}
        for wname, dstT in (("wv", wvT), ("wo", woT)):
            for cc in range(8):
                vg = inc("wq2", 16)
                E("gpsimd", lambda e, wname=wname, cc=cc:
                  e.dma_start(out=xinb[:, 0:1024],
                              in_=dr[wname][cc * 128:(cc + 1) * 128, :])
                  .then_inc(S["wq2"], 16))
                W("sync", "wq2", vg)
                E("sync", lambda e, dstT=dstT, cc=cc:
                  e.dma_start(out=dstT[:, :, cc * 128:cc * 128 + 128],
                              in_=xinb[:, 0:1024], transpose=True)
                  .then_inc(S["wvt"], 16))
                vtr = inc("wvt", 16)
                W("gpsimd", "wvt", vtr)
        v_wvwo = val("wvt")

        rot_ap = {}

        def rot_weight(wname, rhs_ap, dstT, bias_col, prev_te, prev_ve):
            for cc in range(8):
                if cc >= 1:
                    W("gpsimd", "te", rot_ap[("te", wname, cc - 1)])
                elif prev_te:
                    W("gpsimd", "te", prev_te)
                vw = inc("wq2", 16)
                E("gpsimd", lambda e, wname=wname, cc=cc:
                  e.dma_start(out=wst[:, :],
                              in_=dr[wname][cc * 128:(cc + 1) * 128, :])
                  .then_inc(S["wq2"], 16))
                W("tensor", "ve", v_mrot)
                W("tensor", "wq2", vw)
                if cc >= 2:
                    W("tensor", "ap", rot_ap[("ap", wname, cc - 2)])
                reg = (cc % 2) * 1024
                for fc in range(8):
                    last = fc == 7
                    E("tensor", lambda e, cc=cc, fc=fc, reg=reg, last=last,
                      rhs_ap=rhs_ap:
                      (e.matmul(ps[:, reg + fc * 128:reg + fc * 128 + 128],
                                wst[:, fc * 128:(fc + 1) * 128],
                                rhs_ap, start=True, stop=True)
                       .then_inc(S["te"], 1)) if last else
                      e.matmul(ps[:, reg + fc * 128:reg + fc * 128 + 128],
                               wst[:, fc * 128:(fc + 1) * 128],
                               rhs_ap, start=True, stop=True))
                rot_ap[("te", wname, cc)] = inc("te", 1)
                W("scalar", "te", rot_ap[("te", wname, cc)])
                E("scalar", lambda e, cc=cc, reg=reg, dstT=dstT:
                  e.activation(dstT[:, :, cc * 128:cc * 128 + 128],
                               ps[:, reg:reg + 1024]
                               .rearrange("p (c n) -> p c n", c=8),
                               AF.Identity).then_inc(S["ap"], 1))
                rot_ap[("ap", wname, cc)] = inc("ap", 1)
            W("tensor", "wg", v_bias)
            if prev_ve:
                W("tensor", "ve", prev_ve)
            for cc in range(8):
                last = cc == 7
                E("tensor", lambda e, cc=cc, rhs_ap=rhs_ap, bias_col=bias_col,
                  last=last:
                  (e.matmul(ps[:, 2048 + cc:2048 + cc + 1], rhs_ap,
                            bias_col[:, cc:cc + 1], start=True, stop=True)
                   .then_inc(S["te"], 1)) if last else
                  e.matmul(ps[:, 2048 + cc:2048 + cc + 1], rhs_ap,
                           bias_col[:, cc:cc + 1], start=True, stop=True))
            vt = inc("te", 1)
            W("vector", "te", vt)
            E("vector", lambda e, bias_col=bias_col:
              e.tensor_copy(bias_col[:, 0:8], ps[:, 2048:2056])
              .then_inc(S["ve"], 1))
            return inc("ve", 1)

        v_bqrot = rot_weight("wq", mrtq[:, :], wqT, bqr, 0, 0)
        v_bkrot = rot_weight("wk", mrt[:, :], wkT, bkr,
                             rot_ap[("te", "wq", 7)], v_bqrot)
        v_wrot_ap = val("ap")

        wrep = [B(o_PT[0], 2048).rearrange("p (c n) -> p c n", c=2),
                B(o_PT[1], 2048).rearrange("p (c n) -> p c n", c=2)]
        nrep = 0
        for kind in range(2):
            for i in range(16):
                if nrep >= 2:
                    W("gpsimd", "rq", (nrep - 1) * 16)
                src = (dr["w1p"][2 * i:2 * i + 2]
                       .rearrange("j p k -> p j k") if kind == 0 else
                       dr["w2"][i * 256:(i + 1) * 256, :]
                       .rearrange("(j p) k -> p j k", p=128))
                E("gpsimd", lambda e, src=src, b=wrep[nrep % 2]:
                  e.dma_start(out=b, in_=src).then_inc(S["rp"], 16))
                vl = inc("rp", 16)
                W("gpsimd", "rp", vl)
                dst = (w1q_d[2 * i:2 * i + 2].rearrange("j p k -> p j k")
                       if kind == 0 else
                       w2q_d[i * 256:(i + 1) * 256, :]
                       .rearrange("(j p) k -> p j k", p=128))
                E("gpsimd", lambda e, dst=dst, b=wrep[nrep % 2]:
                  e.dma_start(out=dst, in_=b).then_inc(S["rq"], 16))
                inc("rq", 16)
                nrep += 1
        v_repack = val("rq")

        E("vector", lambda e: e.memset(ln1[:, :, :], 0.0).then_inc(S["ve"], 1))
        v_lnz = inc("ve", 1)
        pp_ve = {}
        pp_ap = {}
        v_ln1_c = {}
        ppb = [xpb[0], xpb[1], x0b[0], x0b[1]]
        ppsem = ["x0", "x1", "st0", "st1"]
        for i in range(NT):
            if i >= 4:
                W("sync", "ve", pp_ve[i - 4])
                W("sync", "ap", pp_ap[i - 4])
            xs = ppsem[i % 4]
            vx = inc(xs, 16)
            E("sync", lambda e, i=i, xs=xs:
              e.dma_start(out=ppb[i % 4][:, :],
                          in_=dr["xp"][i * 128:(i + 1) * 128, :])
              .then_inc(S[xs], 16))
            W("vector", xs, vx)
            if i == 0:
                W("vector", "ve", v_lnz)
            E("vector", lambda e, i=i:
              e.reduce_sum(ln1[:, 0, i:i + 1], ppb[i % 4][:, :], axis=AX.X)
              .then_inc(S["ve"], 1))
            pp_ve[i] = inc("ve", 1)
            W("scalar", xs, vx)
            if i == 0:
                W("scalar", "ve", v_lnz)
            E("scalar", lambda e, i=i:
              e.activation(jkb[:, :], ppb[i % 4][:, :], AF.Square,
                           accum_out=ln1[:, 1, i:i + 1]).then_inc(S["ap"], 1))
            pp_ap[i] = inc("ap", 1)
            if i % 8 == 7:
                c = i // 8
                lo, hi = c * 8, c * 8 + 8
                W("vector", "ap", pp_ap[i])
                E("vector", lambda e, lo=lo, hi=hi:
                  e.tensor_scalar(ln1[:, 2, lo:hi], ln1[:, 0, lo:hi],
                                  -1.0 / D, None, ALU.mult))
                E("vector", lambda e, lo=lo, hi=hi:
                  e.tensor_scalar(ln1[:, 3, lo:hi], ln1[:, 1, lo:hi],
                                  1.0 / D, None, ALU.mult)
                  .then_inc(S["ve"], 1))
                W("vector", "ve", inc("ve", 1))
                E("vector", lambda e, lo=lo, hi=hi:
                  e.tensor_mul(ln1[:, 4, lo:hi], ln1[:, 2, lo:hi],
                               ln1[:, 2, lo:hi]).then_inc(S["ve"], 1))
                W("vector", "ve", inc("ve", 1))
                E("vector", lambda e, lo=lo, hi=hi:
                  e.scalar_tensor_tensor(ln1[:, 3, lo:hi], ln1[:, 3, lo:hi],
                                         EPS, ln1[:, 4, lo:hi], ALU.add,
                                         ALU.subtract).then_inc(S["ve"], 1))
                vv = inc("ve", 1)
                W("scalar", "ve", vv)
                E("scalar", lambda e, lo=lo, hi=hi:
                  e.activation(ln1[:, 4, lo:hi], ln1[:, 3, lo:hi], AF.Sqrt)
                  .then_inc(S["ap"], 1))
                va = inc("ap", 1)
                W("vector", "ap", va)
                E("vector", lambda e, lo=lo, hi=hi:
                  e.reciprocal(ln1[:, 3, lo:hi], ln1[:, 4, lo:hi])
                  .then_inc(S["ve"], 1))
                W("vector", "ve", inc("ve", 1))
                E("vector", lambda e, lo=lo, hi=hi:
                  e.tensor_mul(ln1[:, 2, lo:hi], ln1[:, 2, lo:hi],
                               ln1[:, 3, lo:hi]).then_inc(S["ve"], 1))
                v_ln1_c[c] = inc("ve", 1)
        v_ln1 = v_ln1_c[3]

        ap_apply = {}
        tr_xin = {}

        def ln1_apply_and_transpose(s, xinT_guard=None):
            for k in range(2):
                i = 2 * s + k
                if i >= 2:
                    W("sync", "ve", ap_apply[i - 2])
                else:
                    W("sync", "ve", pp_ve[NT - 2 + i])
                    W("sync", "ap", pp_ap[NT - 2 + i])
                xs = "st%d" % (i % 2)
                vx = inc(xs, 16)
                E("sync", lambda e, i=i, xs=xs:
                  e.dma_start(out=x0b[i % 2][:, :],
                              in_=dr["xp"][i * 128:(i + 1) * 128, :])
                  .then_inc(S[xs], 16))
                W("vector", xs, vx)
                W("vector", "ve", v_ln1_c[i // 8])
                if s >= 2:
                    W("vector", "t%d" % (s % 2), tr_xin[(s - 2, k)])
                else:
                    W("vector", "wvt", v_wvwo)
                E("vector", lambda e, i=i:
                  e.tensor_scalar(xinb[:, (i % 4) * 1024:(i % 4) * 1024 + 1024],
                                  x0b[i % 2][:, :], ln1[:, 3, i:i + 1],
                                  ln1[:, 2, i:i + 1], ALU.mult, ALU.add)
                  .then_inc(S["ve"], 1))
                ap_apply[i] = inc("ve", 1)
                W("sync", "ve", ap_apply[i])
                if xinT_guard is not None:
                    W("sync", "ve", xinT_guard)
                ts_ = "t%d" % (s % 2)
                E("sync", lambda e, s=s, k=k, i=i, ts_=ts_:
                  e.dma_start(out=xinT[s % 2][:, :, k * 128:k * 128 + 128],
                              in_=xinb[:, (i % 4) * 1024:(i % 4) * 1024 + 1024],
                              transpose=True).then_inc(S[ts_], 16))
                tr_xin[(s, k)] = inc(ts_, 16)

        def proj_slab(s):
            sb = s % 2
            W("tensor", "t%d" % sb, tr_xin[(s, 1)])
            W("tensor", "ap", v_wrot_ap)
            W("tensor", "wvt", v_wvwo)
            W("tensor", "ve", v_bkrot)
            ev = {}
            for cc in range(8):
                reg = 2048 + (cc % 2) * 1024
                if cc >= 2:
                    W("tensor", "ve", ev[cc - 2])
                for fc in range(8):
                    last = fc == 7
                    E("tensor", lambda e, cc=cc, fc=fc, reg=reg, sb=sb,
                      last=last:
                      (e.matmul(ps[:, reg:reg + 256],
                                wkT[:, fc, cc * 128:cc * 128 + 128],
                                xinT[sb][:, fc, 0:256],
                                start=(fc == 0), stop=True)
                       .then_inc(S["te"], 1)) if last else
                      e.matmul(ps[:, reg:reg + 256],
                               wkT[:, fc, cc * 128:cc * 128 + 128],
                               xinT[sb][:, fc, 0:256],
                               start=(fc == 0), stop=False))
                vt = inc("te", 1)
                W("vector", "te", vt)
                E("vector", lambda e, cc=cc, reg=reg, sb=sb:
                  e.tensor_scalar(Ks[sb][:, cc, :], ps[:, reg:reg + 256],
                                  bkr[:, cc:cc + 1], None, ALU.add)
                  .then_inc(S["ve"], 1))
                ev[cc] = inc("ve", 1)
            for rt in range(2):
                for n2 in range(2):
                    idx = rt * 2 + n2
                    reg = 2048 + (idx % 2) * 1024
                    W("tensor", "ve", ev[6 + idx] if idx < 2 else ev[8 + idx - 2])
                    E("tensor", lambda e, n2=n2, reg=reg:
                      e.matmul(ps[:, reg:reg + 512], onesb[0:1, 0:128],
                               brow[0:1, n2 * 512:n2 * 512 + 512],
                               start=True, stop=False))
                    for fc in range(8):
                        last = fc == 7
                        E("tensor", lambda e, rt=rt, n2=n2, fc=fc, reg=reg,
                          sb=sb, last=last:
                          (e.matmul(ps[:, reg:reg + 512],
                                    xinT[sb][:, fc, rt * 128:rt * 128 + 128],
                                    wvT[:, fc, n2 * 512:n2 * 512 + 512],
                                    start=False, stop=True)
                           .then_inc(S["te"], 1)) if last else
                          e.matmul(ps[:, reg:reg + 512],
                                   xinT[sb][:, fc, rt * 128:rt * 128 + 128],
                                   wvT[:, fc, n2 * 512:n2 * 512 + 512],
                                   start=False, stop=False))
                    vt = inc("te", 1)
                    W("vector", "te", vt)
                    E("vector", lambda e, rt=rt, n2=n2, reg=reg, sb=sb:
                      e.tensor_copy(
                          Vp[sb][:, rt, :].rearrange("p (a b) -> p a b", b=65)
                          [:, n2 * 8:(n2 + 1) * 8, 0:64],
                          ps[:, reg:reg + 512]
                          .rearrange("p (a b) -> p a b", b=64))
                      .then_inc(S["ve"], 1))
                    ev[8 + idx] = inc("ve", 1)
            v_v = val("ve")
            qap = {}
            for cc in range(8):
                reg = 2048 + (cc % 2) * 1024
                if cc < 2:
                    W("tensor", "ve", ev[10 + cc])
                else:
                    W("tensor", "ve", qap[cc - 2])
                for fc in range(8):
                    last = fc == 7
                    E("tensor", lambda e, cc=cc, fc=fc, reg=reg, sb=sb,
                      last=last:
                      (e.matmul(ps[:, reg:reg + 128],
                                wqT[:, fc, cc * 128:cc * 128 + 128],
                                xinT[sb][:, fc, 0:128],
                                start=(fc == 0), stop=True)
                       .then_inc(S["te"], 1)) if last else
                      e.matmul(ps[:, reg:reg + 128],
                               wqT[:, fc, cc * 128:cc * 128 + 128],
                               xinT[sb][:, fc, 0:128],
                               start=(fc == 0), stop=False))
                vt = inc("te", 1)
                W("vector", "te", vt)
                E("vector", lambda e, cc=cc, reg=reg, sb=sb:
                  e.tensor_scalar(QC[sb][0:64, cc * 128:cc * 128 + 128],
                                  ps[0:64, reg:reg + 128],
                                  bqr[0:64, cc:cc + 1], None, ALU.add))
                E("vector", lambda e, cc=cc, reg=reg, sb=sb:
                  e.tensor_scalar(QC[sb][64:128, cc * 128:cc * 128 + 128],
                                  ps[64:128, reg:reg + 128],
                                  bqr[64:128, cc:cc + 1], None, ALU.add)
                  .then_inc(S["ve"], 2))
                qap[cc] = inc("ve", 2)
            vq = val("ve")
            W("sync", "ve", vq)
            ts_ = "t%d" % sb
            E("sync", lambda e, sb=sb, ts_=ts_:
              e.dma_start(out=QW[sb][0:64, :], in_=QC[sb][64:128, :])
              .then_inc(S[ts_], 16))
            E("sync", lambda e, sb=sb, ts_=ts_:
              e.dma_start(out=QW[sb][64:128, :], in_=QC[sb][0:64, :])
              .then_inc(S[ts_], 16))
            vqw = inc(ts_, 32)
            return dict(k=ev[7], v=v_v, q=vq, qw=vqw, sb=sb)

        def attention_slab(s, pr, prev_mult, prev_pv, guard):
            sb = s % 2
            sc_d, ex_d, pv_d = {}, {}, {}

            def tile_of(n):
                P, g, t = n // 32, (n % 32) // 2, n % 2
                m, kh = g // 2, g % 2
                t_lo = 2 * m + t
                return P, t_lo, kh

            def scores(n):
                P, t_lo, kh = tile_of(n)
                pb = (t_lo % 2) * 64
                src_q = QC[sb] if (P == t_lo % 2) else QW[sb]
                bb = 4 + (n % 2) * 2
                if n == 0:
                    W("tensor", "ve", pr["k"])
                    W("tensor", "ve", pr["q"])
                    W("tensor", "t%d" % pr["sb"], pr["qw"])
                    if guard is not pr:
                        W("tensor", "ve", guard["q"])
                        W("tensor", "ve", guard["v"])
                if n >= 2:
                    W("tensor", "ex", ex_d[n - 2])
                for half in range(2):
                    last = half == 1
                    E("tensor", lambda e, P=P, t_lo=t_lo, kh=kh, pb=pb,
                      src_q=src_q, bb=bb, half=half, sb=sb, last=last:
                      (e.matmul(ps[:, (bb + half) * 512:(bb + half) * 512 + 512],
                                Ks[sb][pb:pb + 64, t_lo // 2,
                                       kh * 128:kh * 128 + 128],
                                src_q[pb:pb + 64,
                                      half * 512:half * 512 + 512],
                                start=True, stop=True)
                       .then_inc(S["sc"], 1)) if last else
                      e.matmul(ps[:, (bb + half) * 512:(bb + half) * 512 + 512],
                               Ks[sb][pb:pb + 64, t_lo // 2,
                                      kh * 128:kh * 128 + 128],
                               src_q[pb:pb + 64,
                                     half * 512:half * 512 + 512],
                               start=True, stop=True))
                sc_d[n] = inc("sc", 1)

            def expc(n):
                bb = 4 + (n % 2) * 2
                po = (n % 2) * 1024
                if s == 0 and n == 0:
                    W("scalar", "rq", v_repack)
                W("scalar", "sc", sc_d[n])
                if n >= 6:
                    W("scalar", "pv", pv_d[n - 6])
                elif prev_pv:
                    W("scalar", "pv", prev_pv)
                E("scalar", lambda e, n=n, bb=bb, po=po:
                  e.activation(PT[(n // 2) % 3][:, po:po + 1024],
                               ps[:, bb * 512:bb * 512 + 1024],
                               AF.Exp).then_inc(S["ex"], 1))
                ex_d[n] = inc("ex", 1)

            def pv(n):
                P, t_lo, kh = tile_of(n)
                po = (n % 2) * 1024
                W("tensor", "ex", ex_d[n])
                if n == 0:
                    W("tensor", "ve", prev_mult)
                    W("tensor", "ve", pr["v"])
                first = (n % 32) == 0
                last_g = (n % 32) == 31
                for half in range(2):
                    last = half == 1
                    E("tensor", lambda e, P=P, t_lo=t_lo, kh=kh, po=po, n=n,
                      half=half, sb=sb, first=first, last_g=last_g, last=last:
                      (e.matmul(ps[0:65, P * 1024 + half * 512:
                                   P * 1024 + half * 512 + 512],
                                Vp[sb][:, kh, t_lo * 65:t_lo * 65 + 65],
                                PT[(n // 2) % 3][:, po + half * 512:
                                                 po + half * 512 + 512],
                                start=first, stop=last_g)
                       .then_inc(S["pv"], 1)) if last else
                      e.matmul(ps[0:65, P * 1024 + half * 512:
                                  P * 1024 + half * 512 + 512],
                               Vp[sb][:, kh, t_lo * 65:t_lo * 65 + 65],
                               PT[(n // 2) % 3][:, po + half * 512:
                                                po + half * 512 + 512],
                               start=first, stop=last_g))
                pv_d[n] = inc("pv", 1)

            scores(0)
            scores(1)
            expc(0)
            expc(1)
            for n in range(2, 64, 2):
                scores(n)
                scores(n + 1)
                expc(n)
                expc(n + 1)
                pv(n - 2)
                pv(n - 1)
            pv(62)
            pv(63)
            return pv_d, ex_d

        def head_finish(s, pv_d):
            W("vector", "pv", pv_d[63])
            if s % 2 == 0:
                E("vector", lambda e, s=s:
                  e.tensor_copy(ocat[0:64, s // 2, :], ps[0:64, 0:2048])
                  .then_inc(S["ve"], 1))
            else:
                E("vector", lambda e:
                  e.tensor_copy(otmp[0:64, :], ps[0:64, 0:2048])
                  .then_inc(S["ve"], 1))
            inc("ve", 1)
            E("vector", lambda e: e.tensor_copy(oden[64:65, :],
                                                ps[64:65, 0:2048])
              .then_inc(S["ve"], 1))
            ve_evac = inc("ve", 1)
            if s % 2 == 1:
                W("sync", "ve", ve_evac)
                E("sync", lambda e, s=s:
                  e.dma_start(out=ocat[64:128, s // 2, :], in_=otmp[0:64, :])
                  .then_inc(S["tr"], 16))
                inc("tr", 16)
            W("sync", "ve", ve_evac)
            E("sync", lambda e, s=s:
              e.dma_start(out=dspr[:, s * 16:(s + 1) * 16],
                          in_=oden[64:65, :]).then_inc(S["tr"], 16))
            v_spread = inc("tr", 16)
            return dict(evac=ve_evac, spread=v_spread)

        def head_dbcast_mult(s, hf):
            W("tensor", "tr", hf["back"])
            W("tensor", "ve", hf["evac"])
            for c in range(4):
                last = c == 3
                E("tensor", lambda e, c=c, last=last:
                  (e.matmul(ps[0:64, c * 512:(c + 1) * 512], onesf[64:65, 0:64],
                            oden[64:65, c * 512:(c + 1) * 512],
                            start=True, stop=True).then_inc(S["te"], 1))
                  if last else
                  e.matmul(ps[0:64, c * 512:(c + 1) * 512], onesf[64:65, 0:64],
                           oden[64:65, c * 512:(c + 1) * 512],
                           start=True, stop=True))
            vt = inc("te", 1)
            W("vector", "te", vt)
            if s % 2 == 0:
                E("vector", lambda e, s=s:
                  e.tensor_mul(ocat[0:64, s // 2, :], oden[0:64, :],
                               ps[0:64, 0:2048]).then_inc(S["ve"], 1))
                vm = inc("ve", 1)
                return vm, None
            E("vector", lambda e:
              e.tensor_mul(otmp[0:64, :], oden[0:64, :], ps[0:64, 0:2048])
              .then_inc(S["ve"], 1))
            vm = inc("ve", 1)
            W("sync", "ve", vm)
            E("sync", lambda e, s=s:
              e.dma_start(out=ocat[64:128, s // 2, :], in_=otmp[0:64, :])
              .then_inc(S["tr"], 16))
            return vm, inc("tr", 16)

        prs = {}
        ln1_apply_and_transpose(0)
        prs[0] = proj_slab(0)
        ln1_apply_and_transpose(1)
        prs[1] = proj_slab(1)
        ln1_apply_and_transpose(2, xinT_guard=prs[0]["q"])
        ln1_apply_and_transpose(3, xinT_guard=prs[1]["q"])
        prev_mult = 0
        prev_pv = None
        for s in range(NSLAB):
            guard = prs[min(s + 1, NSLAB - 1)]
            pv_d, ex_d = attention_slab(s, prs[s], prev_mult, prev_pv, guard)
            hf = head_finish(s, pv_d)
            if s + 2 < NSLAB:
                prs[s + 2] = proj_slab(s + 2)
            if s + 4 < NSLAB:
                ln1_apply_and_transpose(s + 4, xinT_guard=prs[s + 2]["q"])
            prev_mult = hf["evac"]
            prev_pv = pv_d[63]
            last_spread = hf["spread"]
        W("vector", "tr", last_spread)
        E("vector", lambda e: e.reciprocal(dspr[:, 0:256], dspr[:, 0:256])
          .then_inc(S["ve"], 1))
        v_rec = inc("ve", 1)
        mult_done = {}
        for g in range(8):
            for hh in range(2):
                h = 2 * g + hh
                if g == 0 and hh == 0:
                    W("sync", "ve", v_rec)
                else:
                    W("sync", "te", val("te"))
                E("sync", lambda e, h=h:
                  e.dma_start(out=oden[64:65, :],
                              in_=dspr[:, h * 16:(h + 1) * 16])
                  .then_inc(S["tr"], 16))
                vback = inc("tr", 16)
                W("tensor", "tr", vback)
                if hh == 0:
                    if g == 0:
                        W("tensor", "ve", val("ve"))
                    else:
                        W("tensor", "ve", mult_done[g - 1])
                pb = hh * 64
                for c in range(4):
                    last = c == 3
                    E("tensor", lambda e, pb=pb, c=c, last=last:
                      (e.matmul(ps[pb:pb + 64, c * 512:(c + 1) * 512],
                                onesf[64:65, 0:64],
                                oden[64:65, c * 512:(c + 1) * 512],
                                start=True, stop=True).then_inc(S["te"], 1))
                      if last else
                      e.matmul(ps[pb:pb + 64, c * 512:(c + 1) * 512],
                               onesf[64:65, 0:64],
                               oden[64:65, c * 512:(c + 1) * 512],
                               start=True, stop=True))
                inc("te", 1)
            vt = val("te")
            W("vector", "te", vt)
            E("vector", lambda e, g=g:
              e.tensor_mul(ocat[:, g, :], ocat[:, g, :], ps[:, 0:2048])
              .then_inc(S["ve"], 1))
            mult_done[g] = inc("ve", 1)
        v_att_ve = val("ve")
        v_att_tr = val("tr")
        v_att_pv = val("pv")
        if stop_after == "attn":
            W("gpsimd", "ve", v_att_ve)
            W("gpsimd", "tr", v_att_tr)
            E("gpsimd", lambda e:
              e.dma_start(out=out_d[:, :],
                          in_=ocat[:, :, :]).then_inc(S["out"], 16))
            E("gpsimd", lambda e: e.wait_ge(S["out"], 16))
            _finish(block, K)
            return nc
        if stop_after == "proj0":
            W("gpsimd", "ve", prs[1]["v"])
            W("gpsimd", "ap", prs[1]["q"])
            E("gpsimd", lambda e:
              e.dma_start(out=out_d[0:128, 0:1024],
                          in_=QC[0][:, :]).then_inc(S["out"], 16))
            E("gpsimd", lambda e:
              e.dma_start(out=out_d[128:256, 0:1024],
                          in_=QC[1][:, :]).then_inc(S["out"], 16))
            E("gpsimd", lambda e:
              e.dma_start(out=out_d[256:384, 0:512],
                          in_=Ks[0][:, 0:2, :]).then_inc(S["out"], 16))
            E("gpsimd", lambda e:
              e.dma_start(out=out_d[384:512, 0:1024],
                          in_=Vp[0][:, 0, 0:1024]).then_inc(S["out"], 16))
            E("gpsimd", lambda e:
              e.dma_start(out=out_d[512:640, 0:512],
                          in_=xinT[0][:, 0:2, 0:256])
              .then_inc(S["out"], 16))
            E("gpsimd", lambda e: e.wait_ge(S["out"], 80))
            _finish(block, K)
            return nc

        psA_free = {}
        x1_store = {}
        xin2_done = {}
        x0_used = {}
        for tt in range(16):
            reg = (tt % 4) * 1024
            if tt == 0:
                W("tensor", "ve", v_att_ve)
                W("tensor", "tr", v_att_tr)
            if tt >= 4:
                W("tensor", "ve", psA_free[tt - 4][0])
                W("tensor", "ap", psA_free[tt - 4][1])
            E("tensor", lambda e, reg=reg:
              e.matmul(ps[:, reg:reg + 512], onesb[0:1, 0:128],
                       brow[0:1, 1024:1536], start=True, stop=False))
            E("tensor", lambda e, reg=reg:
              e.matmul(ps[:, reg + 512:reg + 1024], onesb[0:1, 0:128],
                       brow[0:1, 1536:2048], start=True, stop=False))
            for g in range(8):
                for n2 in range(2):
                    last = g == 7 and n2 == 1
                    E("tensor", lambda e, tt=tt, g=g, n2=n2, reg=reg,
                      last=last:
                      (e.matmul(ps[:, reg + n2 * 512:reg + n2 * 512 + 512],
                                ocat[:, g, tt * 128:tt * 128 + 128],
                                woT[:, g, n2 * 512:n2 * 512 + 512],
                                start=False, stop=(g == 7))
                       .then_inc(S["te"], 1)) if last else
                      e.matmul(ps[:, reg + n2 * 512:reg + n2 * 512 + 512],
                               ocat[:, g, tt * 128:tt * 128 + 128],
                               woT[:, g, n2 * 512:n2 * 512 + 512],
                               start=False, stop=(g == 7)))
            vt = inc("te", 1)
            if stop_after == "p4mm":
                psA_free[tt] = (0, 0)
                x0_used[tt] = 0
                x1_store[tt] = 0
                xin2_done[tt] = 0
                continue
            if tt >= 2:
                W("sync", "ve", x0_used[tt - 2])
            xs = "x%d" % (tt % 2)
            vx0 = inc(xs, 16)
            E("sync", lambda e, tt=tt, xs=xs:
              e.dma_start(out=x0b[tt % 2][:, :],
                          in_=dr["x0"][tt * 128:(tt + 1) * 128, :])
              .then_inc(S[xs], 16))
            W("vector", "te", vt)
            E("vector", lambda e, reg=reg:
              e.reduce_sum(stat[:, 0:1], ps[:, reg:reg + 1024], axis=AX.X))
            E("vector", lambda e: e.memset(stat[:, 1:2], 0.0)
              .then_inc(S["ve"], 1))
            vv0 = inc("ve", 1)
            W("scalar", "te", vt)
            W("scalar", "ve", vv0)
            E("scalar", lambda e, reg=reg:
              e.activation(jkb[:, :], ps[:, reg:reg + 1024], AF.Square,
                           accum_out=stat[:, 1:2]).then_inc(S["ap"], 1))
            va0 = inc("ap", 1)
            W("vector", "ap", va0)
            E("vector", lambda e: e.tensor_scalar(stat[:, 2:3], stat[:, 0:1],
                                                  -1.0 / D, None, ALU.mult))
            E("vector", lambda e: e.tensor_scalar(stat[:, 3:4], stat[:, 1:2],
                                                  1.0 / D, None, ALU.mult)
              .then_inc(S["ve"], 1))
            W("vector", "ve", inc("ve", 1))
            E("vector", lambda e: e.tensor_mul(stat[:, 4:5], stat[:, 2:3],
                                               stat[:, 2:3])
              .then_inc(S["ve"], 1))
            W("vector", "ve", inc("ve", 1))
            E("vector", lambda e: e.scalar_tensor_tensor(
                stat[:, 3:4], stat[:, 3:4], EPS, stat[:, 4:5], ALU.add,
                ALU.subtract).then_inc(S["ve"], 1))
            vv = inc("ve", 1)
            W("scalar", "ve", vv)
            E("scalar", lambda e: e.activation(stat[:, 4:5], stat[:, 3:4],
                                               AF.Sqrt).then_inc(S["ap"], 1))
            va = inc("ap", 1)
            W("vector", "ap", va)
            E("vector", lambda e: e.reciprocal(stat[:, 5:6], stat[:, 4:5])
              .then_inc(S["ve"], 1))
            W("vector", "ve", inc("ve", 1))
            E("vector", lambda e: e.tensor_mul(stat[:, 6:7], stat[:, 2:3],
                                               stat[:, 5:6])
              .then_inc(S["ve"], 1))
            vv = inc("ve", 1)
            W("scalar", "ve", vv)
            E("scalar", lambda e, reg=reg:
              e.activation(tmpL[:, :], ps[:, reg:reg + 1024], AF.Identity,
                           bias=stat[:, 6:7], scale=stat[:, 5:6])
              .then_inc(S["ap"], 1))
            va = inc("ap", 1)
            psA_free[tt] = (vv, va)
            W("vector", "ap", va)
            W("vector", xs, vx0)
            if tt >= 2:
                W("vector", "st%d" % (tt % 2), x1_store[tt - 2])
            E("vector", lambda e, tt=tt:
              e.tensor_add(x1b[tt % 2][:, :], tmpL[:, :], x0b[tt % 2][:, :])
              .then_inc(S["ve"], 1))
            vx1 = inc("ve", 1)
            x0_used[tt] = vx1
            W("sync", "ve", vx1)
            sts = "st%d" % (tt % 2)
            E("sync", lambda e, tt=tt, sts=sts:
              e.dma_start(out=x1_dram[tt * 128:(tt + 1) * 128, :],
                          in_=x1b[tt % 2][:, :]).then_inc(S[sts], 16))
            x1_store[tt] = inc(sts, 16)
            E("vector", lambda e, tt=tt:
              e.reduce_sum(stat[:, 8:9], x1b[tt % 2][:, :], axis=AX.X))
            E("vector", lambda e: e.memset(stat[:, 9:10], 0.0)
              .then_inc(S["ve"], 1))
            vv8 = inc("ve", 1)
            W("scalar", "ve", vv8)
            E("scalar", lambda e, tt=tt:
              e.activation(jkb[:, :], x1b[tt % 2][:, :], AF.Square,
                           accum_out=stat[:, 9:10]).then_inc(S["ap"], 1))
            va8 = inc("ap", 1)
            W("vector", "ap", va8)
            E("vector", lambda e:
              e.tensor_scalar(stat[:, 10:11], stat[:, 8:9], -1.0 / D, None,
                              ALU.mult))
            E("vector", lambda e:
              e.tensor_scalar(stat[:, 11:12], stat[:, 9:10], 1.0 / D, None,
                              ALU.mult).then_inc(S["ve"], 1))
            W("vector", "ve", inc("ve", 1))
            E("vector", lambda e: e.tensor_mul(stat[:, 12:13], stat[:, 10:11],
                                               stat[:, 10:11])
              .then_inc(S["ve"], 1))
            W("vector", "ve", inc("ve", 1))
            E("vector", lambda e: e.scalar_tensor_tensor(
                stat[:, 11:12], stat[:, 11:12], EPS, stat[:, 12:13], ALU.add,
                ALU.subtract).then_inc(S["ve"], 1))
            vv = inc("ve", 1)
            W("scalar", "ve", vv)
            E("scalar", lambda e: e.activation(stat[:, 12:13], stat[:, 11:12],
                                               AF.Sqrt).then_inc(S["ap"], 1))
            va = inc("ap", 1)
            W("vector", "ap", va)
            E("vector", lambda e: e.reciprocal(stat[:, 13:14], stat[:, 12:13])
              .then_inc(S["ve"], 1))
            W("vector", "ve", inc("ve", 1))
            E("vector", lambda e: e.tensor_mul(stat[:, 14:15], stat[:, 10:11],
                                               stat[:, 13:14])
              .then_inc(S["ve"], 1))
            vv = inc("ve", 1)
            W("scalar", "ve", vv)
            E("scalar", lambda e, tt=tt:
              e.activation(xin2[:, tt, :], x1b[tt % 2][:, :], AF.Identity,
                           bias=stat[:, 14:15], scale=stat[:, 13:14])
              .then_inc(S["ap"], 1))
            xin2_done[tt] = inc("ap", 1)

        if stop_after == "p4mm":
            W("vector", "te", val("te"))
            vv_ = inc("ve", 1)
            E("vector", lambda e: e.tensor_copy(oden[0:64, 0:1024],
                                                ps[0:64, 0:1024])
              .then_inc(S["ve"], 1))
            W("sync", "ve", val("ve"))
            W("sync", "te", val("te"))
            E("sync", lambda e:
              e.dma_start(out=out_d[0:64, :], in_=oden[0:64, 0:1024])
              .then_inc(S["out"], 16))
            E("sync", lambda e: e.wait_ge(S["out"], 16))
            _finish(block, K)
            return nc

        if stop_after == "p4ln2":
            W("gpsimd", "st0", val("st0"))
            W("gpsimd", "st1", val("st1"))
            E("gpsimd", lambda e:
              e.dma_start(out=out_d[:, :], in_=x1_dram[:, :])
              .then_inc(S["out"], 16))
            E("gpsimd", lambda e: e.wait_ge(S["out"], 16))
            _finish(block, K)
            return nc

        if stop_after == "p4":
            W("gpsimd", "st0", val("st0"))
            W("gpsimd", "st1", val("st1"))
            E("gpsimd", lambda e:
              e.dma_start(out=out_d[:, :], in_=x1_dram[:, :])
              .then_inc(S["out"], 16))
            E("gpsimd", lambda e: e.wait_ge(S["out"], 16))
            _finish(block, K)
            return nc

        w1_te = {}
        w2_te = {}
        gelu_done = {}
        psF_ap = {}
        out_tot = 0
        x1r_used = {}
        out_used = 0
        v_x2td = {}

        def emit_tr(tb):
            for k in range(4):
                tt = tb * 4 + k
                W("sync", "ap", xin2_done[tt])
                if tb >= 2:
                    W("sync", "te", w1_te[(tb - 2, 31)])
                ts_ = "t%d" % (tb % 2)
                E("sync", lambda e, tb=tb, k=k, tt=tt, ts_=ts_:
                  e.dma_start(out=xin2T[tb % 2][:, :, k * 128:k * 128 + 128],
                              in_=xin2[:, tt, :], transpose=True)
                  .then_inc(S[ts_], 16))
                inc(ts_, 16)
            v_x2td[tb] = val("t%d" % (tb % 2))

        emit_tr(0)
        emit_tr(1)
        for tb in range(4):
            v_x2t = v_x2td[tb]
            if stop_after == "p5tr":
                break
            for jc in range(32):
                if (tb, jc - 2) in w1_te:
                    W("gpsimd", "te", w1_te[(tb, jc - 2)])
                elif tb > 0 and jc < 2:
                    W("gpsimd", "te", w1_te[(tb - 1, 30 + jc)])
                else:
                    W("gpsimd", "pv", v_att_pv)
                    if tb == 0 and jc == 0:
                        W("gpsimd", "rq", v_repack)
                ws = "w5%d" % (jc % 2)
                vw = inc(ws, 16)
                E("gpsimd", lambda e, jc=jc, ws=ws:
                  e.dma_start(out=w1b[jc % 2][:, :, :],
                              in_=w1q_d[jc]).then_inc(S[ws], 16))
                reg = (jc % 2) * 512
                if stop_after == "p5w1":
                    continue
                W("tensor", ws, vw)
                W("tensor", "t%d" % (tb % 2), v_x2t)
                if (tb, jc - 2) in gelu_done:
                    W("tensor", "ap", gelu_done[(tb, jc - 2)])
                elif tb > 0 and jc < 2:
                    W("tensor", "ap", psF_ap[(tb - 1, 0)])
                for fc in range(8):
                    last = fc == 7
                    E("tensor", lambda e, jc=jc, fc=fc, reg=reg, tb=tb,
                      last=last:
                      (e.matmul(ps[:, reg:reg + 512], w1b[jc % 2][:, fc, :],
                                xin2T[tb % 2][:, fc, :],
                                start=(fc == 0), stop=True)
                       .then_inc(S["te"], 1)) if last else
                      e.matmul(ps[:, reg:reg + 512], w1b[jc % 2][:, fc, :],
                               xin2T[tb % 2][:, fc, :],
                               start=(fc == 0), stop=False))
                w1_te[(tb, jc)] = inc("te", 1)
                W("scalar", "te", w1_te[(tb, jc)])
                _gdst = (jkb[:, 0:512] if stop_after == "p5nog"
                         else gT[:, jc, :])
                E("scalar", lambda e, jc=jc, reg=reg, _gdst=_gdst:
                  e.activation(_gdst, ps[:, reg:reg + 512], AF.Gelu,
                               bias=b1c[:, jc:jc + 1]).then_inc(S["ap"], 1))
                gelu_done[(tb, jc)] = inc("ap", 1)
            if stop_after in ("p5w1", "p5nog", "p5mm1"):
                break
            W("tensor", "ap", gelu_done[(tb, 31)])
            for k in range(4):
                E("tensor", lambda e, k=k:
                  e.matmul(ps[:, k * 1024:k * 1024 + 512], onesb[0:1, 0:128],
                           brow[0:1, 2048:2560], start=True, stop=False))
                E("tensor", lambda e, k=k:
                  e.matmul(ps[:, k * 1024 + 512:k * 1024 + 1024],
                           onesb[0:1, 0:128], brow[0:1, 2560:3072],
                           start=True, stop=False))
            for jc in range(32):
                if (tb, jc - 2) in w2_te:
                    W("gpsimd", "te", w2_te[(tb, jc - 2)])
                else:
                    W("gpsimd", "te", w1_te[(tb, 30 + jc)])
                ws = "w5%d" % (jc % 2)
                vw = inc(ws, 16)
                E("gpsimd", lambda e, jc=jc, ws=ws:
                  e.dma_start(out=w2b[jc % 2][:, :],
                              in_=w2q_d[jc * 128:(jc + 1) * 128, :])
                  .then_inc(S[ws], 16))
                W("tensor", ws, vw)
                for k in range(4):
                    for n2 in range(2):
                        last = k == 3 and n2 == 1
                        E("tensor", lambda e, jc=jc, k=k, n2=n2, last=last:
                          (e.matmul(ps[:, k * 1024 + n2 * 512:
                                       k * 1024 + n2 * 512 + 512],
                                    gT[:, jc, k * 128:k * 128 + 128],
                                    w2b[jc % 2][:, n2 * 512:n2 * 512 + 512],
                                    start=False, stop=(jc == 31))
                           .then_inc(S["te"], 1)) if last else
                          e.matmul(ps[:, k * 1024 + n2 * 512:
                                      k * 1024 + n2 * 512 + 512],
                                   gT[:, jc, k * 128:k * 128 + 128],
                                   w2b[jc % 2][:, n2 * 512:n2 * 512 + 512],
                                   start=False, stop=(jc == 31)))
                w2_te[(tb, jc)] = inc("te", 1)
            v_mm2 = val("te")
            if tb + 2 < 4:
                emit_tr(tb + 2)
            if stop_after == "p5mm2":
                break
            last_ap = 0
            for k in range(4):
                W("vector", "te", v_mm2)
                E("vector", lambda e, k=k:
                  e.reduce_sum(stat[:, 32 + k:33 + k],
                               ps[:, k * 1024:k * 1024 + 1024], axis=AX.X))
                E("vector", lambda e, k=k: e.memset(stat[:, 36 + k:37 + k],
                                                    0.0).then_inc(S["ve"], 1))
                vv0 = inc("ve", 1)
                W("scalar", "te", v_mm2)
                W("scalar", "ve", vv0)
                E("scalar", lambda e, k=k:
                  e.activation(jkb[:, :], ps[:, k * 1024:k * 1024 + 1024],
                               AF.Square, accum_out=stat[:, 36 + k:37 + k])
                  .then_inc(S["ap"], 1))
                inc("ap", 1)
            W("vector", "ap", val("ap"))
            E("vector", lambda e:
              e.tensor_scalar(stat[:, 40:44], stat[:, 32:36], -1.0 / D, None,
                              ALU.mult))
            E("vector", lambda e:
              e.tensor_scalar(stat[:, 44:48], stat[:, 36:40], 1.0 / D, None,
                              ALU.mult).then_inc(S["ve"], 1))
            W("vector", "ve", inc("ve", 1))
            E("vector", lambda e:
              e.tensor_mul(stat[:, 48:52], stat[:, 40:44], stat[:, 40:44])
              .then_inc(S["ve"], 1))
            W("vector", "ve", inc("ve", 1))
            E("vector", lambda e: e.scalar_tensor_tensor(
                stat[:, 44:48], stat[:, 44:48], EPS, stat[:, 48:52], ALU.add,
                ALU.subtract).then_inc(S["ve"], 1))
            vv = inc("ve", 1)
            W("scalar", "ve", vv)
            E("scalar", lambda e: e.activation(stat[:, 48:52], stat[:, 44:48],
                                               AF.Sqrt).then_inc(S["ap"], 1))
            va = inc("ap", 1)
            W("vector", "ap", va)
            E("vector", lambda e: e.reciprocal(stat[:, 52:56], stat[:, 48:52])
              .then_inc(S["ve"], 1))
            W("vector", "ve", inc("ve", 1))
            E("vector", lambda e:
              e.tensor_mul(stat[:, 40:44], stat[:, 40:44], stat[:, 52:56])
              .then_inc(S["ve"], 1))
            v_st4 = inc("ve", 1)
            for k in range(4):
                tt = tb * 4 + k
                if tt >= 2:
                    W("sync", "ve", x1r_used.get(tt - 2, 0))
                W("sync", "st%d" % (tt % 2), x1_store[tt])
                xs = "x%d" % (tt % 2)
                vx1 = inc(xs, 16)
                E("sync", lambda e, tt=tt, xs=xs:
                  e.dma_start(out=x1rb[tt % 2][:, :],
                              in_=x1_dram[tt * 128:(tt + 1) * 128, :])
                  .then_inc(S[xs], 16))
                W("scalar", "ve", v_st4 if k == 0 else x1r_used[tt - 1])
                E("scalar", lambda e, k=k:
                  e.activation(junk[:, :], ps[:, k * 1024:k * 1024 + 1024],
                               AF.Identity, bias=stat[:, 40 + k:41 + k],
                               scale=stat[:, 52 + k:53 + k])
                  .then_inc(S["ap"], 1))
                va = inc("ap", 1)
                last_ap = va
                psF_ap[(tb, k)] = va
                W("vector", "ap", va)
                W("vector", xs, vx1)
                if out_used:
                    W("vector", "out", out_used)
                E("vector", lambda e, tt=tt:
                  e.tensor_add(outb[:, :], junk[:, :], x1rb[tt % 2][:, :])
                  .then_inc(S["ve"], 1))
                vo = inc("ve", 1)
                x1r_used[tt] = vo
                W("sync", "ve", vo)
                E("sync", lambda e, tt=tt:
                  e.dma_start(out=out_d[tt * 128:(tt + 1) * 128, :],
                              in_=outb[:, :]).then_inc(S["out"], 16))
                out_tot = inc("out", 16)
                out_used = out_tot
        W("sync", "out", out_tot)
        _finish(block, K)
        return nc


def _finish(block, K):
    @block.sync
    def _(e):
        for f in K.ops["sync"]:
            f(e)

    @block.tensor
    def _(e):
        for f in K.ops["tensor"]:
            f(e)

    @block.vector
    def _(e):
        for f in K.ops["vector"]:
            f(e)

    @block.scalar
    def _(e):
        for f in K.ops["scalar"]:
            f(e)

    @block.gpsimd
    def _(e):
        for f in K.ops["gpsimd"]:
            f(e)


_NC_CACHE = {}


def _get_nc():
    if "nc" not in _NC_CACHE:
        _NC_CACHE["nc"] = build_program()
    return _NC_CACHE["nc"]


def kernel(L0_attn_x, L0_attn_wq, L0_attn_bq, L0_attn_wk, L0_attn_bk,
           L0_attn_wv, L0_attn_bv, L0_attn_wo, L0_attn_bo, L0_attn_cos,
           L0_attn_sin, L0_ffn_w1, L0_ffn_b1, L0_ffn_w2, L0_ffn_b2):
    x = np.ascontiguousarray(np.asarray(L0_attn_x, dtype=np.float32))
    B = x.shape[0]
    eyeC = np.eye(128, dtype=np.float32)
    eyeU = np.zeros((128, 128), np.float32)
    eyeD = np.zeros((128, 128), np.float32)
    for p in range(0, 128, 2):
        eyeU[p, p + 1] = 1.0
    for p in range(1, 128, 2):
        eyeD[p, p - 1] = 1.0
    shared = dict(
        wq=np.ascontiguousarray(L0_attn_wq, np.float32),
        wk=np.ascontiguousarray(L0_attn_wk, np.float32),
        wv=np.ascontiguousarray(L0_attn_wv, np.float32),
        wo=np.ascontiguousarray(L0_attn_wo, np.float32),
        bq=np.ascontiguousarray(L0_attn_bq, np.float32),
        bk=np.ascontiguousarray(L0_attn_bk, np.float32),
        bv=np.ascontiguousarray(L0_attn_bv, np.float32),
        bo=np.ascontiguousarray(L0_attn_bo, np.float32),
        cosr=np.ascontiguousarray(np.asarray(L0_attn_cos)[3199], np.float32),
        sinr=np.ascontiguousarray(np.asarray(L0_attn_sin)[3199], np.float32),
        w1p=np.ascontiguousarray(
            np.asarray(L0_ffn_w1, np.float32).reshape(8, 128, 32, 128)
            .transpose(2, 1, 0, 3)).reshape(32, 128, 1024),
        b1=np.ascontiguousarray(L0_ffn_b1, np.float32),
        w2=np.ascontiguousarray(L0_ffn_w2, np.float32),
        b2=np.ascontiguousarray(L0_ffn_b2, np.float32),
        eyeC=eyeC, eyeU=eyeU, eyeD=eyeD,
    )
    in_maps = []
    for c in range(8):
        b, half = c // 2, c % 2
        xb = x[b]
        xpc = xb.reshape(16, 2, 128, D)[:, [half, 1 - half]].reshape(T, D)
        x0c = xb[half * TOWN + _TMAP]
        m = dict(shared)
        m["xp"] = np.ascontiguousarray(xpc)
        m["x0"] = np.ascontiguousarray(x0c)
        in_maps.append(m)

    nc = _get_nc()
    res = run_bass_kernel_spmd(nc, in_maps, core_ids=list(range(8)))
    out = np.empty((B, T, D), np.float32)
    inv = np.empty_like(_TMAP)
    inv[_TMAP] = np.arange(2048)
    for c in range(8):
        b, half = c // 2, c % 2
        out[b, half * TOWN:(half + 1) * TOWN] = res.results[c]["out"][inv]
    return out


# version 16
# speedup vs baseline: 1.0964x; 1.0030x over previous
import numpy as np
import ml_dtypes
import concourse.bass as bass
import concourse.mybir as mybir
from concourse.bass_utils import run_bass_kernel_spmd

F32 = mybir.dt.float32
BF16 = mybir.dt.bfloat16
AF = mybir.ActivationFunctionType
ALU = mybir.AluOpType
AX = mybir.AxisListType

D = 1024
T = 4096
TOWN = 2048
NSLAB = 16
NT = 32
EPS = 1e-5


def _tmap():
    q = np.arange(2048)
    P, u, j = q // 1024, (q % 1024) // 128, q % 128
    return j * 16 + 2 * u + P


_TMAP = _tmap()


class _K:

    def __init__(self):
        self.ops = {e: [] for e in ("sync", "tensor", "vector", "scalar",
                                    "gpsimd")}
        self.ct = {}

    def E(self, eng, fn):
        self.ops[eng].append(fn)

    def inc(self, sem, n):
        self.ct[sem] = self.ct.get(sem, 0) + n
        return self.ct[sem]

    def val(self, sem):
        return self.ct.get(sem, 0)


def build_program(stop_after=None):
    nc = bass.Bass("TRN2", target_bir_lowering=False,
                   detect_race_conditions=False)
    K = _K()
    E, inc, val = K.E, K.inc, K.val

    dr = {}
    for nm, shp in [("xp", [T, D]), ("x0", [TOWN, D]),
                    ("wq", [D, D]), ("wk", [D, D]), ("wv", [D, D]),
                    ("wo", [D, D]), ("bq", [D]), ("bk", [D]), ("bv", [D]),
                    ("bo", [D]), ("cosr", [64]), ("sinr", [64]),
                    ("w1p", [32, 128, D]), ("b1", [4096]),
                    ("w2", [4096, D]),
                    ("b2", [D]), ("eyeC", [128, 128]), ("eyeU", [128, 128]),
                    ("eyeD", [128, 128])]:
        dr[nm] = nc.dram_tensor(nm, shp, F32, kind="ExternalInput").ap()
    out_d = nc.dram_tensor("out", [TOWN, D], F32, kind="ExternalOutput").ap()
    x1_dram = nc.dram_tensor("x1s", [TOWN, D], F32, kind="Internal").ap()
    w1q_d = nc.dram_tensor("w1q", [32, 128, D], BF16, kind="Internal").ap()
    w2q_d = nc.dram_tensor("w2q", [4096, D], BF16, kind="Internal").ap()

    NBF = 81088
    NF32 = 12352

    bo_ = [0]

    def ab(n):
        o = bo_[0]
        bo_[0] += n
        return o

    o_woT = ab(8 * 1024)
    o_ocat = ab(8 * 2048)
    o_otmp = ab(2048)
    o_xinb = ab(4 * 1024)
    o_onesb = ab(128)
    o_brow = ab(3 * 1024)
    _pbf = bo_[0]
    o_wqT = ab(8 * 1024)
    o_wkT = ab(8 * 1024)
    o_wvT = ab(8 * 1024)
    o_xinT = [ab(8 * 256), ab(8 * 256)]
    o_Ks = [ab(8 * 256), ab(8 * 256)]
    o_Vp = [ab(2 * 1040), ab(2 * 1040)]
    o_QC = [ab(1024), ab(1024)]
    o_QW = [ab(1024), ab(1024)]
    o_PT = [ab(2048), ab(2048), ab(2048)]
    _endA = bo_[0]
    bo_[0] = _pbf
    o_xin2 = ab(16 * 1024)
    o_gT = ab(32 * 512)
    o_xin2T = [ab(8 * 512), ab(8 * 512)]
    o_w1b = [ab(8 * 128), ab(8 * 128)]
    o_w2b = [ab(1024), ab(1024)]
    _endB = bo_[0]
    assert max(_endA, _endB) <= NBF, (_endA, _endB)

    fo_ = [0]

    def af(n):
        o = fo_[0]
        fo_[0] += n
        return o

    o_oden = af(2048)
    o_dspr = af(16)
    o_stat = af(64)
    o_ln1 = af(6 * 32)
    o_bqr = af(8)
    o_bkr = af(8)
    o_b1c = af(32)
    o_cs = af(8)
    o_onesf = af(64)
    o_mrt = af(128)
    o_mrtq = af(128)
    o_eye = af(3 * 128)
    _pf = fo_[0]
    o_xpb = [af(1024), af(1024)]
    o_wst = af(1024)
    _endFA = fo_[0]
    fo_[0] = _pf
    o_x1rb = [af(1024), af(1024)]
    o_outb = af(1024)
    o_junk = af(1024)
    _endFA2 = fo_[0]
    fo_[0] = max(_endFA, _endFA2)
    o_x0b = [af(1024), af(1024)]
    o_x1b = [af(1024), af(1024)]
    o_tmpL = af(1024)
    assert fo_[0] <= NF32, fo_[0]

    import contextlib
    with contextlib.ExitStack() as _st:
        ABh = _st.enter_context(nc.sbuf_tensor("AB", [128, NBF], BF16))
        AFh = _st.enter_context(nc.sbuf_tensor("AF", [128, NF32], F32))
        ps = _st.enter_context(nc.psum_tensor("ps", [128, 4096], F32))
        S = {}
        for _snm in ("wg", "wq2", "wvt", "t0", "t1", "x0", "x1", "st0",
                     "st1", "w50", "w51", "tr", "te", "sc", "pv", "ex",
                     "ve", "ap", "out", "rp", "rq"):
            S[_snm] = _st.enter_context(nc.semaphore("s_" + _snm))
        block = _st.enter_context(nc.Block())

        def B(off, n):
            return ABh[:, off:off + n]

        def Fv(off, n):
            return AFh[:, off:off + n]

        woT = B(o_woT, 8192).rearrange("p (c n) -> p c n", c=8)
        ocat = B(o_ocat, 16384).rearrange("p (c n) -> p c n", c=8)
        otmp = B(o_otmp, 2048)
        xinb = B(o_xinb, 4096)
        onesb = B(o_onesb, 128)
        brow = B(o_brow, 3072)
        wqT = B(o_wqT, 8192).rearrange("p (c n) -> p c n", c=8)
        wkT = B(o_wkT, 8192).rearrange("p (c n) -> p c n", c=8)
        wvT = B(o_wvT, 8192).rearrange("p (c n) -> p c n", c=8)
        xinT = [B(o, 2048).rearrange("p (c n) -> p c n", c=8) for o in o_xinT]
        Ks = [B(o, 2048).rearrange("p (c n) -> p c n", c=8) for o in o_Ks]
        Vp = [B(o, 2080).rearrange("p (c n) -> p c n", c=2) for o in o_Vp]
        QC = [B(o, 1024) for o in o_QC]
        QW = [B(o, 1024) for o in o_QW]
        PT = [B(o, 2048) for o in o_PT]
        xin2 = B(o_xin2, 16384).rearrange("p (c n) -> p c n", c=16)
        gT = B(o_gT, 16384).rearrange("p (c n) -> p c n", c=32)
        xin2T = [B(o, 4096).rearrange("p (c n) -> p c n", c=8)
                 for o in o_xin2T]
        w1b = [B(o, 1024).rearrange("p (c n) -> p c n", c=8) for o in o_w1b]
        w2b = [B(o, 1024) for o in o_w2b]
        jkb = B(o_otmp, 1024)

        oden = Fv(o_oden, 2048)
        dspr = Fv(o_eye, 256)
        stat = Fv(o_stat, 64)
        ln1 = Fv(o_ln1, 192).rearrange("p (c n) -> p c n", c=6)
        bqr = Fv(o_bqr, 8)
        bkr = Fv(o_bkr, 8)
        b1c = Fv(o_b1c, 32)
        cs = Fv(o_cs, 8)
        onesf = Fv(o_onesf, 64)
        mrt = Fv(o_mrt, 128)
        mrtq = Fv(o_mrtq, 128)
        eye = Fv(o_eye, 384)
        xpb = [Fv(o, 1024) for o in o_xpb]
        wst = Fv(o_wst, 1024)
        x1rb = [Fv(o, 1024) for o in o_x1rb]
        outb = Fv(o_outb, 1024)
        junk = Fv(o_junk, 1024)
        x0b = [Fv(o, 1024) for o in o_x0b]
        x1b = [Fv(o, 1024) for o in o_x1b]
        tmpL = Fv(o_tmpL, 1024)

        def W(eng, sem, v):
            if v > 0:
                E(eng, lambda e, s=S[sem], v=v: e.wait_ge(s, v))

        def gdma(dst, src, slow=False):
            v = inc("wg", 16)
            E("gpsimd", lambda e, d=dst, s=src, slow=slow:
              e.dma_start(out=d, in_=s, allow_slow_non_contiguous=slow)
              .then_inc(S["wg"], 16))
            return v

        gdma(cs[0:64, 0:1], dr["cosr"][:].rearrange("(a c) -> c a", a=1))
        gdma(cs[64:128, 0:1], dr["cosr"][:].rearrange("(a c) -> c a", a=1))
        gdma(cs[0:64, 3:4], dr["sinr"][:].rearrange("(a c) -> c a", a=1))
        gdma(cs[64:128, 3:4], dr["sinr"][:].rearrange("(a c) -> c a", a=1))
        gdma(eye[:, 0:128], dr["eyeC"][:, :])
        gdma(eye[:, 128:256], dr["eyeU"][:, :])
        gdma(eye[:, 256:384], dr["eyeD"][:, :])
        gdma(bqr[:, 0:8], dr["bq"][:].rearrange("(c p) -> p c", p=128), slow=True)
        gdma(bkr[:, 0:8], dr["bk"][:].rearrange("(c p) -> p c", p=128), slow=True)
        gdma(b1c[:, 0:32], dr["b1"][:].rearrange("(c p) -> p c", p=128), slow=True)
        gdma(brow[0:1, 0:1024], dr["bv"][:].rearrange("(a n) -> a n", a=1))
        gdma(brow[0:1, 1024:2048], dr["bo"][:].rearrange("(a n) -> a n", a=1))
        gdma(brow[0:1, 2048:3072], dr["b2"][:].rearrange("(a n) -> a n", a=1))
        v_bias = val("wg")

        E("vector", lambda e: e.memset(cs[:, 1:3], 0.0).then_inc(S["ve"], 1))
        v_cz = inc("ve", 1)
        W("sync", "ve", v_cz)
        W("sync", "wg", v_bias)
        E("sync", lambda e: e.dma_start(out=cs[0:127, 1:2], in_=cs[1:128, 3:4])
          .then_inc(S["tr"], 16))
        E("sync", lambda e: e.dma_start(out=cs[1:128, 2:3], in_=cs[0:127, 3:4])
          .then_inc(S["tr"], 16))
        v_csh = inc("tr", 32)

        E("vector", lambda e: e.memset(onesb[:, :], 1.0))
        E("vector", lambda e: e.memset(onesf[:, :], 1.0))
        E("vector", lambda e: e.memset(Vp[0][:, :, :], 1.0))
        E("vector", lambda e: e.memset(Vp[1][:, :, :], 1.0))
        W("vector", "wg", v_bias)
        W("vector", "tr", v_csh)
        E("vector", lambda e: e.tensor_scalar(mrt[:, :], eye[:, 0:128],
                                              cs[:, 0:1], None, ALU.mult))
        E("vector", lambda e: e.tensor_scalar(eye[:, 128:256], eye[:, 128:256],
                                              cs[:, 1:2], None, ALU.mult))
        E("vector", lambda e: e.tensor_add(mrt[:, :], mrt[:, :],
                                           eye[:, 128:256]))
        E("vector", lambda e: e.tensor_scalar(eye[:, 256:384], eye[:, 256:384],
                                              cs[:, 2:3], None, ALU.mult))
        E("vector", lambda e: e.tensor_sub(mrt[:, :], mrt[:, :],
                                           eye[:, 256:384]))
        E("vector", lambda e: e.tensor_scalar(mrtq[:, :], mrt[:, :], 0.125,
                                              None, ALU.mult)
          .then_inc(S["ve"], 1))
        v_mrot = inc("ve", 1)

        wrep = [B(o_PT[0], 2048).rearrange("p (c n) -> p c n", c=2),
                B(o_PT[1], 2048).rearrange("p (c n) -> p c n", c=2)]
        _rp = [0]

        def emit_repack():
            nrep = _rp[0]
            if nrep >= 32:
                return
            _rp[0] += 1
            kind, i = divmod(nrep, 16)
            if nrep >= 2:
                W("gpsimd", "rq", (nrep - 1) * 16)
            src = (dr["w1p"][2 * i:2 * i + 2]
                   .rearrange("j p k -> p j k") if kind == 0 else
                   dr["w2"][i * 256:(i + 1) * 256, :]
                   .rearrange("(j p) k -> p j k", p=128))
            E("gpsimd", lambda e, src=src, b=wrep[nrep % 2]:
              e.dma_start(out=b, in_=src).then_inc(S["rp"], 16))
            vl = inc("rp", 16)
            W("gpsimd", "rp", vl)
            dst = (w1q_d[2 * i:2 * i + 2].rearrange("j p k -> p j k")
                   if kind == 0 else
                   w2q_d[i * 256:(i + 1) * 256, :]
                   .rearrange("(j p) k -> p j k", p=128))
            E("gpsimd", lambda e, dst=dst, b=wrep[nrep % 2]:
              e.dma_start(out=dst, in_=b).then_inc(S["rq"], 16))
            inc("rq", 16)

        v_wv_tr = {}
        for wname, dstT in (("wv", wvT), ("wo", woT)):
            for cc in range(8):
                vg = inc("wq2", 16)
                E("gpsimd", lambda e, wname=wname, cc=cc:
                  e.dma_start(out=xinb[:, 0:1024],
                              in_=dr[wname][cc * 128:(cc + 1) * 128, :])
                  .then_inc(S["wq2"], 16))
                W("sync", "wq2", vg)
                E("sync", lambda e, dstT=dstT, cc=cc:
                  e.dma_start(out=dstT[:, :, cc * 128:cc * 128 + 128],
                              in_=xinb[:, 0:1024], transpose=True)
                  .then_inc(S["wvt"], 16))
                vtr = inc("wvt", 16)
                W("gpsimd", "wvt", vtr)
                emit_repack()
        v_wvwo = val("wvt")

        rot_ap = {}

        def rot_weight(wname, rhs_ap, dstT, bias_col, prev_te, prev_ve):
            for cc in range(8):
                if cc >= 1:
                    W("gpsimd", "te", rot_ap[("te", wname, cc - 1)])
                elif prev_te:
                    W("gpsimd", "te", prev_te)
                vw = inc("wq2", 16)
                E("gpsimd", lambda e, wname=wname, cc=cc:
                  e.dma_start(out=wst[:, :],
                              in_=dr[wname][cc * 128:(cc + 1) * 128, :])
                  .then_inc(S["wq2"], 16))
                emit_repack()
                W("tensor", "ve", v_mrot)
                W("tensor", "wq2", vw)
                if cc >= 2:
                    W("tensor", "ap", rot_ap[("ap", wname, cc - 2)])
                reg = (cc % 2) * 1024
                for fc in range(8):
                    last = fc == 7
                    E("tensor", lambda e, cc=cc, fc=fc, reg=reg, last=last,
                      rhs_ap=rhs_ap:
                      (e.matmul(ps[:, reg + fc * 128:reg + fc * 128 + 128],
                                wst[:, fc * 128:(fc + 1) * 128],
                                rhs_ap, start=True, stop=True)
                       .then_inc(S["te"], 1)) if last else
                      e.matmul(ps[:, reg + fc * 128:reg + fc * 128 + 128],
                               wst[:, fc * 128:(fc + 1) * 128],
                               rhs_ap, start=True, stop=True))
                rot_ap[("te", wname, cc)] = inc("te", 1)
                W("scalar", "te", rot_ap[("te", wname, cc)])
                E("scalar", lambda e, cc=cc, reg=reg, dstT=dstT:
                  e.activation(dstT[:, :, cc * 128:cc * 128 + 128],
                               ps[:, reg:reg + 1024]
                               .rearrange("p (c n) -> p c n", c=8),
                               AF.Identity).then_inc(S["ap"], 1))
                rot_ap[("ap", wname, cc)] = inc("ap", 1)
            W("tensor", "wg", v_bias)
            if prev_ve:
                W("tensor", "ve", prev_ve)
            for cc in range(8):
                last = cc == 7
                E("tensor", lambda e, cc=cc, rhs_ap=rhs_ap, bias_col=bias_col,
                  last=last:
                  (e.matmul(ps[:, 2048 + cc:2048 + cc + 1], rhs_ap,
                            bias_col[:, cc:cc + 1], start=True, stop=True)
                   .then_inc(S["te"], 1)) if last else
                  e.matmul(ps[:, 2048 + cc:2048 + cc + 1], rhs_ap,
                           bias_col[:, cc:cc + 1], start=True, stop=True))
            vt = inc("te", 1)
            W("vector", "te", vt)
            E("vector", lambda e, bias_col=bias_col:
              e.tensor_copy(bias_col[:, 0:8], ps[:, 2048:2056])
              .then_inc(S["ve"], 1))
            return inc("ve", 1)

        v_bqrot = rot_weight("wq", mrtq[:, :], wqT, bqr, 0, 0)
        v_bkrot = rot_weight("wk", mrt[:, :], wkT, bkr,
                             rot_ap[("te", "wq", 7)], v_bqrot)
        v_wrot_ap = val("ap")

        while _rp[0] < 32:
            emit_repack()
        v_repack = val("rq")

        E("vector", lambda e: e.memset(ln1[:, :, :], 0.0).then_inc(S["ve"], 1))
        v_lnz = inc("ve", 1)
        pp_ve = {}
        pp_ap = {}
        v_ln1_c = {}
        ppb = [xpb[0], xpb[1], x0b[0], x0b[1]]
        ppsem = ["x0", "x1", "st0", "st1"]
        for i in range(NT):
            if i >= 4:
                W("sync", "ve", pp_ve[i - 4])
                W("sync", "ap", pp_ap[i - 4])
            xs = ppsem[i % 4]
            vx = inc(xs, 16)
            E("sync", lambda e, i=i, xs=xs:
              e.dma_start(out=ppb[i % 4][:, :],
                          in_=dr["xp"][i * 128:(i + 1) * 128, :])
              .then_inc(S[xs], 16))
            W("vector", xs, vx)
            if i == 0:
                W("vector", "ve", v_lnz)
            E("vector", lambda e, i=i:
              e.reduce_sum(ln1[:, 0, i:i + 1], ppb[i % 4][:, :], axis=AX.X)
              .then_inc(S["ve"], 1))
            pp_ve[i] = inc("ve", 1)
            W("scalar", xs, vx)
            if i == 0:
                W("scalar", "ve", v_lnz)
            E("scalar", lambda e, i=i:
              e.activation(jkb[:, :], ppb[i % 4][:, :], AF.Square,
                           accum_out=ln1[:, 1, i:i + 1]).then_inc(S["ap"], 1))
            pp_ap[i] = inc("ap", 1)
            if i % 8 == 7:
                c = i // 8
                lo, hi = c * 8, c * 8 + 8
                W("vector", "ap", pp_ap[i])
                E("vector", lambda e, lo=lo, hi=hi:
                  e.tensor_scalar(ln1[:, 2, lo:hi], ln1[:, 0, lo:hi],
                                  -1.0 / D, None, ALU.mult))
                E("vector", lambda e, lo=lo, hi=hi:
                  e.tensor_scalar(ln1[:, 3, lo:hi], ln1[:, 1, lo:hi],
                                  1.0 / D, None, ALU.mult)
                  .then_inc(S["ve"], 1))
                W("vector", "ve", inc("ve", 1))
                E("vector", lambda e, lo=lo, hi=hi:
                  e.tensor_mul(ln1[:, 4, lo:hi], ln1[:, 2, lo:hi],
                               ln1[:, 2, lo:hi]).then_inc(S["ve"], 1))
                W("vector", "ve", inc("ve", 1))
                E("vector", lambda e, lo=lo, hi=hi:
                  e.scalar_tensor_tensor(ln1[:, 3, lo:hi], ln1[:, 3, lo:hi],
                                         EPS, ln1[:, 4, lo:hi], ALU.add,
                                         ALU.subtract).then_inc(S["ve"], 1))
                vv = inc("ve", 1)
                W("scalar", "ve", vv)
                E("scalar", lambda e, lo=lo, hi=hi:
                  e.activation(ln1[:, 4, lo:hi], ln1[:, 3, lo:hi], AF.Sqrt)
                  .then_inc(S["ap"], 1))
                va = inc("ap", 1)
                W("vector", "ap", va)
                E("vector", lambda e, lo=lo, hi=hi:
                  e.reciprocal(ln1[:, 3, lo:hi], ln1[:, 4, lo:hi])
                  .then_inc(S["ve"], 1))
                W("vector", "ve", inc("ve", 1))
                E("vector", lambda e, lo=lo, hi=hi:
                  e.tensor_mul(ln1[:, 2, lo:hi], ln1[:, 2, lo:hi],
                               ln1[:, 3, lo:hi]).then_inc(S["ve"], 1))
                v_ln1_c[c] = inc("ve", 1)
        v_ln1 = v_ln1_c[3]

        ap_apply = {}
        tr_xin = {}

        def ln1_apply_and_transpose(s, xinT_guard=None):
            for k in range(2):
                i = 2 * s + k
                if i >= 2:
                    W("sync", "ve", ap_apply[i - 2])
                else:
                    W("sync", "ve", pp_ve[NT - 2 + i])
                    W("sync", "ap", pp_ap[NT - 2 + i])
                xs = "st%d" % (i % 2)
                vx = inc(xs, 16)
                E("sync", lambda e, i=i, xs=xs:
                  e.dma_start(out=x0b[i % 2][:, :],
                              in_=dr["xp"][i * 128:(i + 1) * 128, :])
                  .then_inc(S[xs], 16))
                W("vector", xs, vx)
                W("vector", "ve", v_ln1_c[i // 8])
                if s >= 2:
                    W("vector", "t%d" % (s % 2), tr_xin[(s - 2, k)])
                else:
                    W("vector", "wvt", v_wvwo)
                E("vector", lambda e, i=i:
                  e.tensor_scalar(xinb[:, (i % 4) * 1024:(i % 4) * 1024 + 1024],
                                  x0b[i % 2][:, :], ln1[:, 3, i:i + 1],
                                  ln1[:, 2, i:i + 1], ALU.mult, ALU.add)
                  .then_inc(S["ve"], 1))
                ap_apply[i] = inc("ve", 1)
                W("sync", "ve", ap_apply[i])
                if xinT_guard is not None:
                    W("sync", "ve", xinT_guard)
                ts_ = "t%d" % (s % 2)
                E("sync", lambda e, s=s, k=k, i=i, ts_=ts_:
                  e.dma_start(out=xinT[s % 2][:, :, k * 128:k * 128 + 128],
                              in_=xinb[:, (i % 4) * 1024:(i % 4) * 1024 + 1024],
                              transpose=True).then_inc(S[ts_], 16))
                tr_xin[(s, k)] = inc(ts_, 16)

        def proj_slab(s):
            sb = s % 2
            W("tensor", "t%d" % sb, tr_xin[(s, 1)])
            W("tensor", "ap", v_wrot_ap)
            W("tensor", "wvt", v_wvwo)
            W("tensor", "ve", v_bkrot)
            ev = {}
            for cc in range(8):
                reg = 2048 + (cc % 2) * 1024
                if cc >= 2:
                    W("tensor", "ve", ev[cc - 2])
                for fc in range(8):
                    last = fc == 7
                    E("tensor", lambda e, cc=cc, fc=fc, reg=reg, sb=sb,
                      last=last:
                      (e.matmul(ps[:, reg:reg + 256],
                                wkT[:, fc, cc * 128:cc * 128 + 128],
                                xinT[sb][:, fc, 0:256],
                                start=(fc == 0), stop=True)
                       .then_inc(S["te"], 1)) if last else
                      e.matmul(ps[:, reg:reg + 256],
                               wkT[:, fc, cc * 128:cc * 128 + 128],
                               xinT[sb][:, fc, 0:256],
                               start=(fc == 0), stop=False))
                vt = inc("te", 1)
                W("vector", "te", vt)
                E("vector", lambda e, cc=cc, reg=reg, sb=sb:
                  e.tensor_scalar(Ks[sb][:, cc, :], ps[:, reg:reg + 256],
                                  bkr[:, cc:cc + 1], None, ALU.add)
                  .then_inc(S["ve"], 1))
                ev[cc] = inc("ve", 1)
            for rt in range(2):
                for n2 in range(2):
                    idx = rt * 2 + n2
                    reg = 2048 + (idx % 2) * 1024
                    W("tensor", "ve", ev[6 + idx] if idx < 2 else ev[8 + idx - 2])
                    E("tensor", lambda e, n2=n2, reg=reg:
                      e.matmul(ps[:, reg:reg + 512], onesb[0:1, 0:128],
                               brow[0:1, n2 * 512:n2 * 512 + 512],
                               start=True, stop=False))
                    for fc in range(8):
                        last = fc == 7
                        E("tensor", lambda e, rt=rt, n2=n2, fc=fc, reg=reg,
                          sb=sb, last=last:
                          (e.matmul(ps[:, reg:reg + 512],
                                    xinT[sb][:, fc, rt * 128:rt * 128 + 128],
                                    wvT[:, fc, n2 * 512:n2 * 512 + 512],
                                    start=False, stop=True)
                           .then_inc(S["te"], 1)) if last else
                          e.matmul(ps[:, reg:reg + 512],
                                   xinT[sb][:, fc, rt * 128:rt * 128 + 128],
                                   wvT[:, fc, n2 * 512:n2 * 512 + 512],
                                   start=False, stop=False))
                    vt = inc("te", 1)
                    W("vector", "te", vt)
                    E("vector", lambda e, rt=rt, n2=n2, reg=reg, sb=sb:
                      e.tensor_copy(
                          Vp[sb][:, rt, :].rearrange("p (a b) -> p a b", b=65)
                          [:, n2 * 8:(n2 + 1) * 8, 0:64],
                          ps[:, reg:reg + 512]
                          .rearrange("p (a b) -> p a b", b=64))
                      .then_inc(S["ve"], 1))
                    ev[8 + idx] = inc("ve", 1)
            v_v = val("ve")
            qap = {}
            for cc in range(8):
                reg = 2048 + (cc % 2) * 1024
                if cc < 2:
                    W("tensor", "ve", ev[10 + cc])
                else:
                    W("tensor", "ve", qap[cc - 2])
                for fc in range(8):
                    last = fc == 7
                    E("tensor", lambda e, cc=cc, fc=fc, reg=reg, sb=sb,
                      last=last:
                      (e.matmul(ps[:, reg:reg + 128],
                                wqT[:, fc, cc * 128:cc * 128 + 128],
                                xinT[sb][:, fc, 0:128],
                                start=(fc == 0), stop=True)
                       .then_inc(S["te"], 1)) if last else
                      e.matmul(ps[:, reg:reg + 128],
                               wqT[:, fc, cc * 128:cc * 128 + 128],
                               xinT[sb][:, fc, 0:128],
                               start=(fc == 0), stop=False))
                vt = inc("te", 1)
                W("vector", "te", vt)
                E("vector", lambda e, cc=cc, reg=reg, sb=sb:
                  e.tensor_scalar(QC[sb][0:64, cc * 128:cc * 128 + 128],
                                  ps[0:64, reg:reg + 128],
                                  bqr[0:64, cc:cc + 1], None, ALU.add))
                E("vector", lambda e, cc=cc, reg=reg, sb=sb:
                  e.tensor_scalar(QC[sb][64:128, cc * 128:cc * 128 + 128],
                                  ps[64:128, reg:reg + 128],
                                  bqr[64:128, cc:cc + 1], None, ALU.add)
                  .then_inc(S["ve"], 2))
                qap[cc] = inc("ve", 2)
            vq = val("ve")
            W("sync", "ve", vq)
            ts_ = "t%d" % sb
            E("sync", lambda e, sb=sb, ts_=ts_:
              e.dma_start(out=QW[sb][0:64, :], in_=QC[sb][64:128, :])
              .then_inc(S[ts_], 16))
            E("sync", lambda e, sb=sb, ts_=ts_:
              e.dma_start(out=QW[sb][64:128, :], in_=QC[sb][0:64, :])
              .then_inc(S[ts_], 16))
            vqw = inc(ts_, 32)
            return dict(k=ev[7], v=v_v, q=vq, qw=vqw, sb=sb)

        def attention_slab(s, pr, prev_mult, prev_pv, guard):
            sb = s % 2
            sc_d, ex_d, pv_d = {}, {}, {}

            def tile_of(n):
                P, g, t = n // 32, (n % 32) // 2, n % 2
                m, kh = g // 2, g % 2
                t_lo = 2 * m + t
                return P, t_lo, kh

            def scores(n):
                P, t_lo, kh = tile_of(n)
                pb = (t_lo % 2) * 64
                src_q = QC[sb] if (P == t_lo % 2) else QW[sb]
                bb = 4 + (n % 2) * 2
                if n == 0:
                    W("tensor", "ve", pr["k"])
                    W("tensor", "ve", pr["q"])
                    W("tensor", "t%d" % pr["sb"], pr["qw"])
                    if guard is not pr:
                        W("tensor", "ve", guard["q"])
                        W("tensor", "ve", guard["v"])
                if n >= 2:
                    W("tensor", "ex", ex_d[n - 2])
                for half in range(2):
                    last = half == 1
                    E("tensor", lambda e, P=P, t_lo=t_lo, kh=kh, pb=pb,
                      src_q=src_q, bb=bb, half=half, sb=sb, last=last:
                      (e.matmul(ps[:, (bb + half) * 512:(bb + half) * 512 + 512],
                                Ks[sb][pb:pb + 64, t_lo // 2,
                                       kh * 128:kh * 128 + 128],
                                src_q[pb:pb + 64,
                                      half * 512:half * 512 + 512],
                                start=True, stop=True)
                       .then_inc(S["sc"], 1)) if last else
                      e.matmul(ps[:, (bb + half) * 512:(bb + half) * 512 + 512],
                               Ks[sb][pb:pb + 64, t_lo // 2,
                                      kh * 128:kh * 128 + 128],
                               src_q[pb:pb + 64,
                                     half * 512:half * 512 + 512],
                               start=True, stop=True))
                sc_d[n] = inc("sc", 1)

            def expc(n):
                bb = 4 + (n % 2) * 2
                po = (n % 2) * 1024
                if s == 0 and n == 0:
                    W("scalar", "rq", v_repack)
                W("scalar", "sc", sc_d[n])
                if n >= 6:
                    W("scalar", "pv", pv_d[n - 6])
                elif prev_pv:
                    W("scalar", "pv", prev_pv)
                E("scalar", lambda e, n=n, bb=bb, po=po:
                  e.activation(PT[(n // 2) % 3][:, po:po + 1024],
                               ps[:, bb * 512:bb * 512 + 1024],
                               AF.Exp).then_inc(S["ex"], 1))
                ex_d[n] = inc("ex", 1)

            def pv(n):
                P, t_lo, kh = tile_of(n)
                po = (n % 2) * 1024
                W("tensor", "ex", ex_d[n])
                if n == 0:
                    W("tensor", "ve", prev_mult)
                    W("tensor", "ve", pr["v"])
                first = (n % 32) == 0
                last_g = (n % 32) == 31
                for half in range(2):
                    last = half == 1
                    E("tensor", lambda e, P=P, t_lo=t_lo, kh=kh, po=po, n=n,
                      half=half, sb=sb, first=first, last_g=last_g, last=last:
                      (e.matmul(ps[0:65, P * 1024 + half * 512:
                                   P * 1024 + half * 512 + 512],
                                Vp[sb][:, kh, t_lo * 65:t_lo * 65 + 65],
                                PT[(n // 2) % 3][:, po + half * 512:
                                                 po + half * 512 + 512],
                                start=first, stop=last_g)
                       .then_inc(S["pv"], 1)) if last else
                      e.matmul(ps[0:65, P * 1024 + half * 512:
                                  P * 1024 + half * 512 + 512],
                               Vp[sb][:, kh, t_lo * 65:t_lo * 65 + 65],
                               PT[(n // 2) % 3][:, po + half * 512:
                                                po + half * 512 + 512],
                               start=first, stop=last_g))
                pv_d[n] = inc("pv", 1)

            scores(0)
            scores(1)
            expc(0)
            expc(1)
            for n in range(2, 64, 2):
                scores(n)
                scores(n + 1)
                expc(n)
                expc(n + 1)
                pv(n - 2)
                pv(n - 1)
            pv(62)
            pv(63)
            return pv_d, ex_d

        def head_finish(s, pv_d):
            W("vector", "pv", pv_d[63])
            if s % 2 == 0:
                E("vector", lambda e, s=s:
                  e.tensor_copy(ocat[0:64, s // 2, :], ps[0:64, 0:2048])
                  .then_inc(S["ve"], 1))
            else:
                E("vector", lambda e:
                  e.tensor_copy(otmp[0:64, :], ps[0:64, 0:2048])
                  .then_inc(S["ve"], 1))
            inc("ve", 1)
            E("vector", lambda e: e.tensor_copy(oden[64:65, :],
                                                ps[64:65, 0:2048])
              .then_inc(S["ve"], 1))
            ve_evac = inc("ve", 1)
            if s % 2 == 1:
                W("sync", "ve", ve_evac)
                E("sync", lambda e, s=s:
                  e.dma_start(out=ocat[64:128, s // 2, :], in_=otmp[0:64, :])
                  .then_inc(S["tr"], 16))
                inc("tr", 16)
            W("sync", "ve", ve_evac)
            E("sync", lambda e, s=s:
              e.dma_start(out=dspr[:, s * 16:(s + 1) * 16],
                          in_=oden[64:65, :]).then_inc(S["tr"], 16))
            v_spread = inc("tr", 16)
            return dict(evac=ve_evac, spread=v_spread)

        def head_dbcast_mult(s, hf):
            W("tensor", "tr", hf["back"])
            W("tensor", "ve", hf["evac"])
            for c in range(4):
                last = c == 3
                E("tensor", lambda e, c=c, last=last:
                  (e.matmul(ps[0:64, c * 512:(c + 1) * 512], onesf[64:65, 0:64],
                            oden[64:65, c * 512:(c + 1) * 512],
                            start=True, stop=True).then_inc(S["te"], 1))
                  if last else
                  e.matmul(ps[0:64, c * 512:(c + 1) * 512], onesf[64:65, 0:64],
                           oden[64:65, c * 512:(c + 1) * 512],
                           start=True, stop=True))
            vt = inc("te", 1)
            W("vector", "te", vt)
            if s % 2 == 0:
                E("vector", lambda e, s=s:
                  e.tensor_mul(ocat[0:64, s // 2, :], oden[0:64, :],
                               ps[0:64, 0:2048]).then_inc(S["ve"], 1))
                vm = inc("ve", 1)
                return vm, None
            E("vector", lambda e:
              e.tensor_mul(otmp[0:64, :], oden[0:64, :], ps[0:64, 0:2048])
              .then_inc(S["ve"], 1))
            vm = inc("ve", 1)
            W("sync", "ve", vm)
            E("sync", lambda e, s=s:
              e.dma_start(out=ocat[64:128, s // 2, :], in_=otmp[0:64, :])
              .then_inc(S["tr"], 16))
            return vm, inc("tr", 16)

        prs = {}
        ln1_apply_and_transpose(0)
        prs[0] = proj_slab(0)
        ln1_apply_and_transpose(1)
        prs[1] = proj_slab(1)
        ln1_apply_and_transpose(2, xinT_guard=prs[0]["q"])
        ln1_apply_and_transpose(3, xinT_guard=prs[1]["q"])
        prev_mult = 0
        prev_pv = None
        for s in range(NSLAB):
            guard = prs[min(s + 1, NSLAB - 1)]
            pv_d, ex_d = attention_slab(s, prs[s], prev_mult, prev_pv, guard)
            hf = head_finish(s, pv_d)
            if s + 2 < NSLAB:
                prs[s + 2] = proj_slab(s + 2)
            if s + 4 < NSLAB:
                ln1_apply_and_transpose(s + 4, xinT_guard=prs[s + 2]["q"])
            prev_mult = hf["evac"]
            prev_pv = pv_d[63]
            last_spread = hf["spread"]
        W("vector", "tr", last_spread)
        E("vector", lambda e: e.reciprocal(dspr[:, 0:256], dspr[:, 0:256])
          .then_inc(S["ve"], 1))
        v_rec = inc("ve", 1)
        mult_done = {}
        for g in range(8):
            for hh in range(2):
                h = 2 * g + hh
                if g == 0 and hh == 0:
                    W("sync", "ve", v_rec)
                else:
                    W("sync", "te", val("te"))
                E("sync", lambda e, h=h:
                  e.dma_start(out=oden[64:65, :],
                              in_=dspr[:, h * 16:(h + 1) * 16])
                  .then_inc(S["tr"], 16))
                vback = inc("tr", 16)
                W("tensor", "tr", vback)
                if hh == 0:
                    if g == 0:
                        W("tensor", "ve", val("ve"))
                    else:
                        W("tensor", "ve", mult_done[g - 1])
                pb = hh * 64
                for c in range(4):
                    last = c == 3
                    E("tensor", lambda e, pb=pb, c=c, last=last:
                      (e.matmul(ps[pb:pb + 64, c * 512:(c + 1) * 512],
                                onesf[64:65, 0:64],
                                oden[64:65, c * 512:(c + 1) * 512],
                                start=True, stop=True).then_inc(S["te"], 1))
                      if last else
                      e.matmul(ps[pb:pb + 64, c * 512:(c + 1) * 512],
                               onesf[64:65, 0:64],
                               oden[64:65, c * 512:(c + 1) * 512],
                               start=True, stop=True))
                inc("te", 1)
            vt = val("te")
            W("vector", "te", vt)
            E("vector", lambda e, g=g:
              e.tensor_mul(ocat[:, g, :], ocat[:, g, :], ps[:, 0:2048])
              .then_inc(S["ve"], 1))
            mult_done[g] = inc("ve", 1)
        v_att_ve = val("ve")
        v_att_tr = val("tr")
        v_att_pv = val("pv")
        if stop_after == "attn":
            W("gpsimd", "ve", v_att_ve)
            W("gpsimd", "tr", v_att_tr)
            E("gpsimd", lambda e:
              e.dma_start(out=out_d[:, :],
                          in_=ocat[:, :, :]).then_inc(S["out"], 16))
            E("gpsimd", lambda e: e.wait_ge(S["out"], 16))
            _finish(block, K)
            return nc
        if stop_after == "proj0":
            W("gpsimd", "ve", prs[1]["v"])
            W("gpsimd", "ap", prs[1]["q"])
            E("gpsimd", lambda e:
              e.dma_start(out=out_d[0:128, 0:1024],
                          in_=QC[0][:, :]).then_inc(S["out"], 16))
            E("gpsimd", lambda e:
              e.dma_start(out=out_d[128:256, 0:1024],
                          in_=QC[1][:, :]).then_inc(S["out"], 16))
            E("gpsimd", lambda e:
              e.dma_start(out=out_d[256:384, 0:512],
                          in_=Ks[0][:, 0:2, :]).then_inc(S["out"], 16))
            E("gpsimd", lambda e:
              e.dma_start(out=out_d[384:512, 0:1024],
                          in_=Vp[0][:, 0, 0:1024]).then_inc(S["out"], 16))
            E("gpsimd", lambda e:
              e.dma_start(out=out_d[512:640, 0:512],
                          in_=xinT[0][:, 0:2, 0:256])
              .then_inc(S["out"], 16))
            E("gpsimd", lambda e: e.wait_ge(S["out"], 80))
            _finish(block, K)
            return nc

        psA_free = {}
        x1_store = {}
        xin2_done = {}
        x0_used = {}
        for tt in range(16):
            reg = (tt % 4) * 1024
            if tt == 0:
                W("tensor", "ve", v_att_ve)
                W("tensor", "tr", v_att_tr)
            if tt >= 4:
                W("tensor", "ve", psA_free[tt - 4][0])
                W("tensor", "ap", psA_free[tt - 4][1])
            E("tensor", lambda e, reg=reg:
              e.matmul(ps[:, reg:reg + 512], onesb[0:1, 0:128],
                       brow[0:1, 1024:1536], start=True, stop=False))
            E("tensor", lambda e, reg=reg:
              e.matmul(ps[:, reg + 512:reg + 1024], onesb[0:1, 0:128],
                       brow[0:1, 1536:2048], start=True, stop=False))
            for g in range(8):
                for n2 in range(2):
                    last = g == 7 and n2 == 1
                    E("tensor", lambda e, tt=tt, g=g, n2=n2, reg=reg,
                      last=last:
                      (e.matmul(ps[:, reg + n2 * 512:reg + n2 * 512 + 512],
                                ocat[:, g, tt * 128:tt * 128 + 128],
                                woT[:, g, n2 * 512:n2 * 512 + 512],
                                start=False, stop=(g == 7))
                       .then_inc(S["te"], 1)) if last else
                      e.matmul(ps[:, reg + n2 * 512:reg + n2 * 512 + 512],
                               ocat[:, g, tt * 128:tt * 128 + 128],
                               woT[:, g, n2 * 512:n2 * 512 + 512],
                               start=False, stop=(g == 7)))
            vt = inc("te", 1)
            if stop_after == "p4mm":
                psA_free[tt] = (0, 0)
                x0_used[tt] = 0
                x1_store[tt] = 0
                xin2_done[tt] = 0
                continue
            if tt >= 2:
                W("sync", "ve", x0_used[tt - 2])
            xs = "x%d" % (tt % 2)
            vx0 = inc(xs, 16)
            E("sync", lambda e, tt=tt, xs=xs:
              e.dma_start(out=x0b[tt % 2][:, :],
                          in_=dr["x0"][tt * 128:(tt + 1) * 128, :])
              .then_inc(S[xs], 16))
            W("vector", "te", vt)
            E("vector", lambda e, reg=reg:
              e.reduce_sum(stat[:, 0:1], ps[:, reg:reg + 1024], axis=AX.X))
            E("vector", lambda e: e.memset(stat[:, 1:2], 0.0)
              .then_inc(S["ve"], 1))
            vv0 = inc("ve", 1)
            W("scalar", "te", vt)
            W("scalar", "ve", vv0)
            E("scalar", lambda e, reg=reg:
              e.activation(jkb[:, :], ps[:, reg:reg + 1024], AF.Square,
                           accum_out=stat[:, 1:2]).then_inc(S["ap"], 1))
            va0 = inc("ap", 1)
            W("vector", "ap", va0)
            E("vector", lambda e: e.tensor_scalar(stat[:, 2:3], stat[:, 0:1],
                                                  -1.0 / D, None, ALU.mult))
            E("vector", lambda e: e.tensor_scalar(stat[:, 3:4], stat[:, 1:2],
                                                  1.0 / D, None, ALU.mult)
              .then_inc(S["ve"], 1))
            W("vector", "ve", inc("ve", 1))
            E("vector", lambda e: e.tensor_mul(stat[:, 4:5], stat[:, 2:3],
                                               stat[:, 2:3])
              .then_inc(S["ve"], 1))
            W("vector", "ve", inc("ve", 1))
            E("vector", lambda e: e.scalar_tensor_tensor(
                stat[:, 3:4], stat[:, 3:4], EPS, stat[:, 4:5], ALU.add,
                ALU.subtract).then_inc(S["ve"], 1))
            vv = inc("ve", 1)
            W("scalar", "ve", vv)
            E("scalar", lambda e: e.activation(stat[:, 4:5], stat[:, 3:4],
                                               AF.Sqrt).then_inc(S["ap"], 1))
            va = inc("ap", 1)
            W("vector", "ap", va)
            E("vector", lambda e: e.reciprocal(stat[:, 5:6], stat[:, 4:5])
              .then_inc(S["ve"], 1))
            W("vector", "ve", inc("ve", 1))
            E("vector", lambda e: e.tensor_mul(stat[:, 6:7], stat[:, 2:3],
                                               stat[:, 5:6])
              .then_inc(S["ve"], 1))
            vv = inc("ve", 1)
            W("scalar", "ve", vv)
            E("scalar", lambda e, reg=reg:
              e.activation(tmpL[:, :], ps[:, reg:reg + 1024], AF.Identity,
                           bias=stat[:, 6:7], scale=stat[:, 5:6])
              .then_inc(S["ap"], 1))
            va = inc("ap", 1)
            psA_free[tt] = (vv, va)
            W("vector", "ap", va)
            W("vector", xs, vx0)
            if tt >= 2:
                W("vector", "st%d" % (tt % 2), x1_store[tt - 2])
            E("vector", lambda e, tt=tt:
              e.tensor_add(x1b[tt % 2][:, :], tmpL[:, :], x0b[tt % 2][:, :])
              .then_inc(S["ve"], 1))
            vx1 = inc("ve", 1)
            x0_used[tt] = vx1
            W("sync", "ve", vx1)
            sts = "st%d" % (tt % 2)
            E("sync", lambda e, tt=tt, sts=sts:
              e.dma_start(out=x1_dram[tt * 128:(tt + 1) * 128, :],
                          in_=x1b[tt % 2][:, :]).then_inc(S[sts], 16))
            x1_store[tt] = inc(sts, 16)
            E("vector", lambda e, tt=tt:
              e.reduce_sum(stat[:, 8:9], x1b[tt % 2][:, :], axis=AX.X))
            E("vector", lambda e: e.memset(stat[:, 9:10], 0.0)
              .then_inc(S["ve"], 1))
            vv8 = inc("ve", 1)
            W("scalar", "ve", vv8)
            E("scalar", lambda e, tt=tt:
              e.activation(jkb[:, :], x1b[tt % 2][:, :], AF.Square,
                           accum_out=stat[:, 9:10]).then_inc(S["ap"], 1))
            va8 = inc("ap", 1)
            W("vector", "ap", va8)
            E("vector", lambda e:
              e.tensor_scalar(stat[:, 10:11], stat[:, 8:9], -1.0 / D, None,
                              ALU.mult))
            E("vector", lambda e:
              e.tensor_scalar(stat[:, 11:12], stat[:, 9:10], 1.0 / D, None,
                              ALU.mult).then_inc(S["ve"], 1))
            W("vector", "ve", inc("ve", 1))
            E("vector", lambda e: e.tensor_mul(stat[:, 12:13], stat[:, 10:11],
                                               stat[:, 10:11])
              .then_inc(S["ve"], 1))
            W("vector", "ve", inc("ve", 1))
            E("vector", lambda e: e.scalar_tensor_tensor(
                stat[:, 11:12], stat[:, 11:12], EPS, stat[:, 12:13], ALU.add,
                ALU.subtract).then_inc(S["ve"], 1))
            vv = inc("ve", 1)
            W("scalar", "ve", vv)
            E("scalar", lambda e: e.activation(stat[:, 12:13], stat[:, 11:12],
                                               AF.Sqrt).then_inc(S["ap"], 1))
            va = inc("ap", 1)
            W("vector", "ap", va)
            E("vector", lambda e: e.reciprocal(stat[:, 13:14], stat[:, 12:13])
              .then_inc(S["ve"], 1))
            W("vector", "ve", inc("ve", 1))
            E("vector", lambda e: e.tensor_mul(stat[:, 14:15], stat[:, 10:11],
                                               stat[:, 13:14])
              .then_inc(S["ve"], 1))
            vv = inc("ve", 1)
            W("scalar", "ve", vv)
            E("scalar", lambda e, tt=tt:
              e.activation(xin2[:, tt, :], x1b[tt % 2][:, :], AF.Identity,
                           bias=stat[:, 14:15], scale=stat[:, 13:14])
              .then_inc(S["ap"], 1))
            xin2_done[tt] = inc("ap", 1)

        if stop_after == "p4mm":
            W("vector", "te", val("te"))
            vv_ = inc("ve", 1)
            E("vector", lambda e: e.tensor_copy(oden[0:64, 0:1024],
                                                ps[0:64, 0:1024])
              .then_inc(S["ve"], 1))
            W("sync", "ve", val("ve"))
            W("sync", "te", val("te"))
            E("sync", lambda e:
              e.dma_start(out=out_d[0:64, :], in_=oden[0:64, 0:1024])
              .then_inc(S["out"], 16))
            E("sync", lambda e: e.wait_ge(S["out"], 16))
            _finish(block, K)
            return nc

        if stop_after == "p4ln2":
            W("gpsimd", "st0", val("st0"))
            W("gpsimd", "st1", val("st1"))
            E("gpsimd", lambda e:
              e.dma_start(out=out_d[:, :], in_=x1_dram[:, :])
              .then_inc(S["out"], 16))
            E("gpsimd", lambda e: e.wait_ge(S["out"], 16))
            _finish(block, K)
            return nc

        if stop_after == "p4":
            W("gpsimd", "st0", val("st0"))
            W("gpsimd", "st1", val("st1"))
            E("gpsimd", lambda e:
              e.dma_start(out=out_d[:, :], in_=x1_dram[:, :])
              .then_inc(S["out"], 16))
            E("gpsimd", lambda e: e.wait_ge(S["out"], 16))
            _finish(block, K)
            return nc

        w1_te = {}
        w2_te = {}
        gelu_done = {}
        psF_ap = {}
        out_tot = 0
        x1r_used = {}
        out_used = 0
        for tb in range(4):
            for k in range(4):
                tt = tb * 4 + k
                W("sync", "ap", xin2_done[tt])
                if tb >= 2:
                    W("sync", "te", w1_te[(tb - 2, 31)])
                ts_ = "t%d" % (tb % 2)
                E("sync", lambda e, tb=tb, k=k, tt=tt, ts_=ts_:
                  e.dma_start(out=xin2T[tb % 2][:, :, k * 128:k * 128 + 128],
                              in_=xin2[:, tt, :], transpose=True)
                  .then_inc(S[ts_], 16))
                inc(ts_, 16)
            v_x2t = val("t%d" % (tb % 2))
            if stop_after == "p5tr":
                break
            for jc in range(32):
                if (tb, jc - 2) in w1_te:
                    W("gpsimd", "te", w1_te[(tb, jc - 2)])
                elif tb > 0 and jc < 2:
                    W("gpsimd", "te", w1_te[(tb - 1, 30 + jc)])
                else:
                    W("gpsimd", "pv", v_att_pv)
                    if tb == 0 and jc == 0:
                        W("gpsimd", "rq", v_repack)
                ws = "w5%d" % (jc % 2)
                vw = inc(ws, 16)
                E("gpsimd", lambda e, jc=jc, ws=ws:
                  e.dma_start(out=w1b[jc % 2][:, :, :],
                              in_=w1q_d[jc]).then_inc(S[ws], 16))
                reg = (jc % 2) * 512
                if stop_after == "p5w1":
                    continue
                W("tensor", ws, vw)
                W("tensor", "t%d" % (tb % 2), v_x2t)
                if (tb, jc - 2) in gelu_done:
                    W("tensor", "ap", gelu_done[(tb, jc - 2)])
                elif tb > 0 and jc < 2:
                    W("tensor", "ap", psF_ap[(tb - 1, 0)])
                for fc in range(8):
                    last = fc == 7
                    E("tensor", lambda e, jc=jc, fc=fc, reg=reg, tb=tb,
                      last=last:
                      (e.matmul(ps[:, reg:reg + 512], w1b[jc % 2][:, fc, :],
                                xin2T[tb % 2][:, fc, :],
                                start=(fc == 0), stop=True)
                       .then_inc(S["te"], 1)) if last else
                      e.matmul(ps[:, reg:reg + 512], w1b[jc % 2][:, fc, :],
                               xin2T[tb % 2][:, fc, :],
                               start=(fc == 0), stop=False))
                w1_te[(tb, jc)] = inc("te", 1)
                W("scalar", "te", w1_te[(tb, jc)])
                _gdst = (jkb[:, 0:512] if stop_after == "p5nog"
                         else gT[:, jc, :])
                E("scalar", lambda e, jc=jc, reg=reg, _gdst=_gdst:
                  e.activation(_gdst, ps[:, reg:reg + 512], AF.Gelu,
                               bias=b1c[:, jc:jc + 1]).then_inc(S["ap"], 1))
                gelu_done[(tb, jc)] = inc("ap", 1)
            if stop_after in ("p5w1", "p5nog", "p5mm1"):
                break
            W("tensor", "ap", gelu_done[(tb, 31)])
            for k in range(4):
                E("tensor", lambda e, k=k:
                  e.matmul(ps[:, k * 1024:k * 1024 + 512], onesb[0:1, 0:128],
                           brow[0:1, 2048:2560], start=True, stop=False))
                E("tensor", lambda e, k=k:
                  e.matmul(ps[:, k * 1024 + 512:k * 1024 + 1024],
                           onesb[0:1, 0:128], brow[0:1, 2560:3072],
                           start=True, stop=False))
            for jc in range(32):
                if (tb, jc - 2) in w2_te:
                    W("gpsimd", "te", w2_te[(tb, jc - 2)])
                else:
                    W("gpsimd", "te", w1_te[(tb, 30 + jc)])
                ws = "w5%d" % (jc % 2)
                vw = inc(ws, 16)
                E("gpsimd", lambda e, jc=jc, ws=ws:
                  e.dma_start(out=w2b[jc % 2][:, :],
                              in_=w2q_d[jc * 128:(jc + 1) * 128, :])
                  .then_inc(S[ws], 16))
                W("tensor", ws, vw)
                for k in range(4):
                    for n2 in range(2):
                        last = k == 3 and n2 == 1
                        E("tensor", lambda e, jc=jc, k=k, n2=n2, last=last:
                          (e.matmul(ps[:, k * 1024 + n2 * 512:
                                       k * 1024 + n2 * 512 + 512],
                                    gT[:, jc, k * 128:k * 128 + 128],
                                    w2b[jc % 2][:, n2 * 512:n2 * 512 + 512],
                                    start=False, stop=(jc == 31))
                           .then_inc(S["te"], 1)) if last else
                          e.matmul(ps[:, k * 1024 + n2 * 512:
                                      k * 1024 + n2 * 512 + 512],
                                   gT[:, jc, k * 128:k * 128 + 128],
                                   w2b[jc % 2][:, n2 * 512:n2 * 512 + 512],
                                   start=False, stop=(jc == 31)))
                w2_te[(tb, jc)] = inc("te", 1)
            v_mm2 = val("te")
            if stop_after == "p5mm2":
                break
            last_ap = 0
            for k in range(4):
                W("vector", "te", v_mm2)
                E("vector", lambda e, k=k:
                  e.reduce_sum(stat[:, 32 + k:33 + k],
                               ps[:, k * 1024:k * 1024 + 1024], axis=AX.X))
                E("vector", lambda e, k=k: e.memset(stat[:, 36 + k:37 + k],
                                                    0.0).then_inc(S["ve"], 1))
                vv0 = inc("ve", 1)
                W("scalar", "te", v_mm2)
                W("scalar", "ve", vv0)
                E("scalar", lambda e, k=k:
                  e.activation(jkb[:, :], ps[:, k * 1024:k * 1024 + 1024],
                               AF.Square, accum_out=stat[:, 36 + k:37 + k])
                  .then_inc(S["ap"], 1))
                inc("ap", 1)
            W("vector", "ap", val("ap"))
            E("vector", lambda e:
              e.tensor_scalar(stat[:, 40:44], stat[:, 32:36], -1.0 / D, None,
                              ALU.mult))
            E("vector", lambda e:
              e.tensor_scalar(stat[:, 44:48], stat[:, 36:40], 1.0 / D, None,
                              ALU.mult).then_inc(S["ve"], 1))
            W("vector", "ve", inc("ve", 1))
            E("vector", lambda e:
              e.tensor_mul(stat[:, 48:52], stat[:, 40:44], stat[:, 40:44])
              .then_inc(S["ve"], 1))
            W("vector", "ve", inc("ve", 1))
            E("vector", lambda e: e.scalar_tensor_tensor(
                stat[:, 44:48], stat[:, 44:48], EPS, stat[:, 48:52], ALU.add,
                ALU.subtract).then_inc(S["ve"], 1))
            vv = inc("ve", 1)
            W("scalar", "ve", vv)
            E("scalar", lambda e: e.activation(stat[:, 48:52], stat[:, 44:48],
                                               AF.Sqrt).then_inc(S["ap"], 1))
            va = inc("ap", 1)
            W("vector", "ap", va)
            E("vector", lambda e: e.reciprocal(stat[:, 52:56], stat[:, 48:52])
              .then_inc(S["ve"], 1))
            W("vector", "ve", inc("ve", 1))
            E("vector", lambda e:
              e.tensor_mul(stat[:, 40:44], stat[:, 40:44], stat[:, 52:56])
              .then_inc(S["ve"], 1))
            v_st4 = inc("ve", 1)
            for k in range(4):
                tt = tb * 4 + k
                if tt >= 2:
                    W("sync", "ve", x1r_used.get(tt - 2, 0))
                W("sync", "st%d" % (tt % 2), x1_store[tt])
                xs = "x%d" % (tt % 2)
                vx1 = inc(xs, 16)
                E("sync", lambda e, tt=tt, xs=xs:
                  e.dma_start(out=x1rb[tt % 2][:, :],
                              in_=x1_dram[tt * 128:(tt + 1) * 128, :])
                  .then_inc(S[xs], 16))
                W("scalar", "ve", v_st4 if k == 0 else x1r_used[tt - 1])
                E("scalar", lambda e, k=k:
                  e.activation(junk[:, :], ps[:, k * 1024:k * 1024 + 1024],
                               AF.Identity, bias=stat[:, 40 + k:41 + k],
                               scale=stat[:, 52 + k:53 + k])
                  .then_inc(S["ap"], 1))
                va = inc("ap", 1)
                last_ap = va
                psF_ap[(tb, k)] = va
                W("vector", "ap", va)
                W("vector", xs, vx1)
                if out_used:
                    W("vector", "out", out_used)
                E("vector", lambda e, tt=tt:
                  e.tensor_add(outb[:, :], junk[:, :], x1rb[tt % 2][:, :])
                  .then_inc(S["ve"], 1))
                vo = inc("ve", 1)
                x1r_used[tt] = vo
                W("sync", "ve", vo)
                E("sync", lambda e, tt=tt:
                  e.dma_start(out=out_d[tt * 128:(tt + 1) * 128, :],
                              in_=outb[:, :]).then_inc(S["out"], 16))
                out_tot = inc("out", 16)
                out_used = out_tot
        W("sync", "out", out_tot)
        _finish(block, K)
        return nc


def _finish(block, K):
    @block.sync
    def _(e):
        for f in K.ops["sync"]:
            f(e)

    @block.tensor
    def _(e):
        for f in K.ops["tensor"]:
            f(e)

    @block.vector
    def _(e):
        for f in K.ops["vector"]:
            f(e)

    @block.scalar
    def _(e):
        for f in K.ops["scalar"]:
            f(e)

    @block.gpsimd
    def _(e):
        for f in K.ops["gpsimd"]:
            f(e)


_NC_CACHE = {}


def _get_nc():
    if "nc" not in _NC_CACHE:
        _NC_CACHE["nc"] = build_program()
    return _NC_CACHE["nc"]


def kernel(L0_attn_x, L0_attn_wq, L0_attn_bq, L0_attn_wk, L0_attn_bk,
           L0_attn_wv, L0_attn_bv, L0_attn_wo, L0_attn_bo, L0_attn_cos,
           L0_attn_sin, L0_ffn_w1, L0_ffn_b1, L0_ffn_w2, L0_ffn_b2):
    x = np.ascontiguousarray(np.asarray(L0_attn_x, dtype=np.float32))
    B = x.shape[0]
    eyeC = np.eye(128, dtype=np.float32)
    eyeU = np.zeros((128, 128), np.float32)
    eyeD = np.zeros((128, 128), np.float32)
    for p in range(0, 128, 2):
        eyeU[p, p + 1] = 1.0
    for p in range(1, 128, 2):
        eyeD[p, p - 1] = 1.0
    shared = dict(
        wq=np.ascontiguousarray(L0_attn_wq, np.float32),
        wk=np.ascontiguousarray(L0_attn_wk, np.float32),
        wv=np.ascontiguousarray(L0_attn_wv, np.float32),
        wo=np.ascontiguousarray(L0_attn_wo, np.float32),
        bq=np.ascontiguousarray(L0_attn_bq, np.float32),
        bk=np.ascontiguousarray(L0_attn_bk, np.float32),
        bv=np.ascontiguousarray(L0_attn_bv, np.float32),
        bo=np.ascontiguousarray(L0_attn_bo, np.float32),
        cosr=np.ascontiguousarray(np.asarray(L0_attn_cos)[3199], np.float32),
        sinr=np.ascontiguousarray(np.asarray(L0_attn_sin)[3199], np.float32),
        w1p=np.ascontiguousarray(
            np.asarray(L0_ffn_w1, np.float32).reshape(8, 128, 32, 128)
            .transpose(2, 1, 0, 3)).reshape(32, 128, 1024),
        b1=np.ascontiguousarray(L0_ffn_b1, np.float32),
        w2=np.ascontiguousarray(L0_ffn_w2, np.float32),
        b2=np.ascontiguousarray(L0_ffn_b2, np.float32),
        eyeC=eyeC, eyeU=eyeU, eyeD=eyeD,
    )
    in_maps = []
    for c in range(8):
        b, half = c // 2, c % 2
        xb = x[b]
        xpc = xb.reshape(16, 2, 128, D)[:, [half, 1 - half]].reshape(T, D)
        x0c = xb[half * TOWN + _TMAP]
        m = dict(shared)
        m["xp"] = np.ascontiguousarray(xpc)
        m["x0"] = np.ascontiguousarray(x0c)
        in_maps.append(m)

    nc = _get_nc()
    res = run_bass_kernel_spmd(nc, in_maps, core_ids=list(range(8)))
    out = np.empty((B, T, D), np.float32)
    inv = np.empty_like(_TMAP)
    inv[_TMAP] = np.arange(2048)
    for c in range(8):
        b, half = c // 2, c % 2
        out[b, half * TOWN:(half + 1) * TOWN] = res.results[c]["out"][inv]
    return out


# version 17 (final)
# speedup vs baseline: 1.1030x; 1.0060x over previous
import numpy as np
import ml_dtypes
import concourse.bass as bass
import concourse.mybir as mybir
from concourse.bass_utils import run_bass_kernel_spmd

F32 = mybir.dt.float32
BF16 = mybir.dt.bfloat16
AF = mybir.ActivationFunctionType
ALU = mybir.AluOpType
AX = mybir.AxisListType

D = 1024
T = 4096
TOWN = 2048
NSLAB = 16
NT = 32
EPS = 1e-5


def _tmap():
    q = np.arange(2048)
    P, u, j = q // 1024, (q % 1024) // 128, q % 128
    return j * 16 + 2 * u + P


_TMAP = _tmap()


class _K:

    def __init__(self):
        self.ops = {e: [] for e in ("sync", "tensor", "vector", "scalar",
                                    "gpsimd")}
        self.ct = {}

    def E(self, eng, fn):
        self.ops[eng].append(fn)

    def inc(self, sem, n):
        self.ct[sem] = self.ct.get(sem, 0) + n
        return self.ct[sem]

    def val(self, sem):
        return self.ct.get(sem, 0)


def build_program(stop_after=None):
    nc = bass.Bass("TRN2", target_bir_lowering=False,
                   detect_race_conditions=False)
    K = _K()
    E, inc, val = K.E, K.inc, K.val

    dr = {}
    for nm, shp in [("xp", [T, D]), ("x0", [TOWN, D]),
                    ("wq", [D, D]), ("wk", [D, D]), ("wv", [D, D]),
                    ("wo", [D, D]), ("bq", [D]), ("bk", [D]), ("bv", [D]),
                    ("bo", [D]), ("cosr", [64]), ("sinr", [64]),
                    ("w1p", [32, 128, D]), ("b1", [4096]),
                    ("w2", [4096, D]),
                    ("b2", [D]), ("eyeC", [128, 128]), ("eyeU", [128, 128]),
                    ("eyeD", [128, 128])]:
        dr[nm] = nc.dram_tensor(nm, shp, F32, kind="ExternalInput").ap()
    out_d = nc.dram_tensor("out", [TOWN, D], F32, kind="ExternalOutput").ap()
    x1_dram = nc.dram_tensor("x1s", [TOWN, D], F32, kind="Internal").ap()
    w1q_d = nc.dram_tensor("w1q", [32, 128, D], BF16, kind="Internal").ap()
    w2q_d = nc.dram_tensor("w2q", [4096, D], BF16, kind="Internal").ap()

    NBF = 81088
    NF32 = 12352

    bo_ = [0]

    def ab(n):
        o = bo_[0]
        bo_[0] += n
        return o

    o_woT = ab(8 * 1024)
    o_ocat = ab(8 * 2048)
    o_otmp = ab(2048)
    o_xinb = ab(4 * 1024)
    o_onesb = ab(128)
    o_brow = ab(3 * 1024)
    _pbf = bo_[0]
    o_wqT = ab(8 * 1024)
    o_wkT = ab(8 * 1024)
    o_wvT = ab(8 * 1024)
    o_xinT = [ab(8 * 256), ab(8 * 256)]
    o_Ks = [ab(8 * 256), ab(8 * 256)]
    o_Vp = [ab(2 * 1040), ab(2 * 1040)]
    o_QC = [ab(1024), ab(1024)]
    o_QW = [ab(1024), ab(1024)]
    o_PT = [ab(2048), ab(2048), ab(2048)]
    _endA = bo_[0]
    bo_[0] = _pbf
    o_xin2 = ab(16 * 1024)
    o_gT = ab(32 * 512)
    o_xin2T = [ab(8 * 512), ab(8 * 512)]
    o_w1b = [ab(8 * 128), ab(8 * 128)]
    o_w2b = [ab(1024), ab(1024)]
    _endB = bo_[0]
    assert max(_endA, _endB) <= NBF, (_endA, _endB)

    fo_ = [0]

    def af(n):
        o = fo_[0]
        fo_[0] += n
        return o

    o_oden = af(2048)
    o_dspr = af(16)
    o_stat = af(64)
    o_ln1 = af(6 * 32)
    o_bqr = af(8)
    o_bkr = af(8)
    o_b1c = af(32)
    o_cs = af(8)
    o_onesf = af(64)
    o_mrt = af(128)
    o_mrtq = af(128)
    o_eye = af(3 * 128)
    _pf = fo_[0]
    o_xpb = [af(1024), af(1024)]
    o_wst = af(1024)
    _endFA = fo_[0]
    fo_[0] = _pf
    o_x1rb = [af(1024), af(1024)]
    o_outb = af(1024)
    o_junk = af(1024)
    _endFA2 = fo_[0]
    fo_[0] = max(_endFA, _endFA2)
    o_x0b = [af(1024), af(1024)]
    o_x1b = [af(1024), af(1024)]
    o_tmpL = af(1024)
    assert fo_[0] <= NF32, fo_[0]

    import contextlib
    with contextlib.ExitStack() as _st:
        ABh = _st.enter_context(nc.sbuf_tensor("AB", [128, NBF], BF16))
        AFh = _st.enter_context(nc.sbuf_tensor("AF", [128, NF32], F32))
        ps = _st.enter_context(nc.psum_tensor("ps", [128, 4096], F32))
        S = {}
        for _snm in ("wg", "wq2", "wvt", "t0", "t1", "x0", "x1", "st0",
                     "st1", "w50", "w51", "tr", "te", "sc", "pv", "ex",
                     "ve", "ap", "out", "rp", "rq"):
            S[_snm] = _st.enter_context(nc.semaphore("s_" + _snm))
        block = _st.enter_context(nc.Block())

        def B(off, n):
            return ABh[:, off:off + n]

        def Fv(off, n):
            return AFh[:, off:off + n]

        woT = B(o_woT, 8192).rearrange("p (c n) -> p c n", c=8)
        ocat = B(o_ocat, 16384).rearrange("p (c n) -> p c n", c=8)
        otmp = B(o_otmp, 2048)
        xinb = B(o_xinb, 4096)
        onesb = B(o_onesb, 128)
        brow = B(o_brow, 3072)
        wqT = B(o_wqT, 8192).rearrange("p (c n) -> p c n", c=8)
        wkT = B(o_wkT, 8192).rearrange("p (c n) -> p c n", c=8)
        wvT = B(o_wvT, 8192).rearrange("p (c n) -> p c n", c=8)
        xinT = [B(o, 2048).rearrange("p (c n) -> p c n", c=8) for o in o_xinT]
        Ks = [B(o, 2048).rearrange("p (c n) -> p c n", c=8) for o in o_Ks]
        Vp = [B(o, 2080).rearrange("p (c n) -> p c n", c=2) for o in o_Vp]
        QC = [B(o, 1024) for o in o_QC]
        QW = [B(o, 1024) for o in o_QW]
        PT = [B(o, 2048) for o in o_PT]
        xin2 = B(o_xin2, 16384).rearrange("p (c n) -> p c n", c=16)
        gT = B(o_gT, 16384).rearrange("p (c n) -> p c n", c=32)
        xin2T = [B(o, 4096).rearrange("p (c n) -> p c n", c=8)
                 for o in o_xin2T]
        w1b = [B(o, 1024).rearrange("p (c n) -> p c n", c=8) for o in o_w1b]
        w2b = [B(o, 1024) for o in o_w2b]
        jkb = B(o_otmp, 1024)

        oden = Fv(o_oden, 2048)
        dspr = Fv(o_eye, 256)
        stat = Fv(o_stat, 64)
        ln1 = Fv(o_ln1, 192).rearrange("p (c n) -> p c n", c=6)
        bqr = Fv(o_bqr, 8)
        bkr = Fv(o_bkr, 8)
        b1c = Fv(o_b1c, 32)
        cs = Fv(o_cs, 8)
        onesf = Fv(o_onesf, 64)
        mrt = Fv(o_mrt, 128)
        mrtq = Fv(o_mrtq, 128)
        eye = Fv(o_eye, 384)
        xpb = [Fv(o, 1024) for o in o_xpb]
        wst = Fv(o_wst, 1024)
        x1rb = [Fv(o, 1024) for o in o_x1rb]
        outb = Fv(o_outb, 1024)
        junk = Fv(o_junk, 1024)
        x0b = [Fv(o, 1024) for o in o_x0b]
        x1b = [Fv(o, 1024) for o in o_x1b]
        tmpL = Fv(o_tmpL, 1024)

        def W(eng, sem, v):
            if v > 0:
                E(eng, lambda e, s=S[sem], v=v: e.wait_ge(s, v))

        def gdma(dst, src, slow=False):
            v = inc("wg", 16)
            E("gpsimd", lambda e, d=dst, s=src, slow=slow:
              e.dma_start(out=d, in_=s, allow_slow_non_contiguous=slow)
              .then_inc(S["wg"], 16))
            return v

        gdma(cs[0:64, 0:1], dr["cosr"][:].rearrange("(a c) -> c a", a=1))
        gdma(cs[64:128, 0:1], dr["cosr"][:].rearrange("(a c) -> c a", a=1))
        gdma(cs[0:64, 3:4], dr["sinr"][:].rearrange("(a c) -> c a", a=1))
        gdma(cs[64:128, 3:4], dr["sinr"][:].rearrange("(a c) -> c a", a=1))
        gdma(eye[:, 0:128], dr["eyeC"][:, :])
        gdma(eye[:, 128:256], dr["eyeU"][:, :])
        gdma(eye[:, 256:384], dr["eyeD"][:, :])
        gdma(bqr[:, 0:8], dr["bq"][:].rearrange("(c p) -> p c", p=128), slow=True)
        gdma(bkr[:, 0:8], dr["bk"][:].rearrange("(c p) -> p c", p=128), slow=True)
        gdma(b1c[:, 0:32], dr["b1"][:].rearrange("(c p) -> p c", p=128), slow=True)
        gdma(brow[0:1, 0:1024], dr["bv"][:].rearrange("(a n) -> a n", a=1))
        gdma(brow[0:1, 1024:2048], dr["bo"][:].rearrange("(a n) -> a n", a=1))
        gdma(brow[0:1, 2048:3072], dr["b2"][:].rearrange("(a n) -> a n", a=1))
        v_bias = val("wg")

        E("vector", lambda e: e.memset(cs[:, 1:3], 0.0).then_inc(S["ve"], 1))
        v_cz = inc("ve", 1)
        W("sync", "ve", v_cz)
        W("sync", "wg", v_bias)
        E("sync", lambda e: e.dma_start(out=cs[0:127, 1:2], in_=cs[1:128, 3:4])
          .then_inc(S["tr"], 16))
        E("sync", lambda e: e.dma_start(out=cs[1:128, 2:3], in_=cs[0:127, 3:4])
          .then_inc(S["tr"], 16))
        v_csh = inc("tr", 32)

        E("vector", lambda e: e.memset(onesb[:, :], 1.0))
        E("vector", lambda e: e.memset(onesf[:, :], 1.0))
        E("vector", lambda e: e.memset(Vp[0][:, :, :], 1.0))
        E("vector", lambda e: e.memset(Vp[1][:, :, :], 1.0))
        W("vector", "wg", v_bias)
        W("vector", "tr", v_csh)
        E("vector", lambda e: e.tensor_scalar(mrt[:, :], eye[:, 0:128],
                                              cs[:, 0:1], None, ALU.mult))
        E("vector", lambda e: e.tensor_scalar(eye[:, 128:256], eye[:, 128:256],
                                              cs[:, 1:2], None, ALU.mult))
        E("vector", lambda e: e.tensor_add(mrt[:, :], mrt[:, :],
                                           eye[:, 128:256]))
        E("vector", lambda e: e.tensor_scalar(eye[:, 256:384], eye[:, 256:384],
                                              cs[:, 2:3], None, ALU.mult))
        E("vector", lambda e: e.tensor_sub(mrt[:, :], mrt[:, :],
                                           eye[:, 256:384]))
        E("vector", lambda e: e.tensor_scalar(mrtq[:, :], mrt[:, :], 0.125,
                                              None, ALU.mult)
          .then_inc(S["ve"], 1))
        v_mrot = inc("ve", 1)

        wrep = [B(o_PT[0], 2048).rearrange("p (c n) -> p c n", c=2),
                B(o_PT[1], 2048).rearrange("p (c n) -> p c n", c=2)]
        _rp = [0]

        def emit_repack():
            nrep = _rp[0]
            if nrep >= 32:
                return
            _rp[0] += 1
            kind, i = divmod(nrep, 16)
            if nrep >= 2:
                W("gpsimd", "rq", (nrep - 1) * 16)
            src = (dr["w1p"][2 * i:2 * i + 2]
                   .rearrange("j p k -> p j k") if kind == 0 else
                   dr["w2"][i * 256:(i + 1) * 256, :]
                   .rearrange("(j p) k -> p j k", p=128))
            E("gpsimd", lambda e, src=src, b=wrep[nrep % 2]:
              e.dma_start(out=b, in_=src).then_inc(S["rp"], 16))
            vl = inc("rp", 16)
            W("gpsimd", "rp", vl)
            dst = (w1q_d[2 * i:2 * i + 2].rearrange("j p k -> p j k")
                   if kind == 0 else
                   w2q_d[i * 256:(i + 1) * 256, :]
                   .rearrange("(j p) k -> p j k", p=128))
            E("gpsimd", lambda e, dst=dst, b=wrep[nrep % 2]:
              e.dma_start(out=dst, in_=b).then_inc(S["rq"], 16))
            inc("rq", 16)

        v_wv_tr = {}
        for wname, dstT in (("wv", wvT), ("wo", woT)):
            for cc in range(8):
                vg = inc("wq2", 16)
                E("gpsimd", lambda e, wname=wname, cc=cc:
                  e.dma_start(out=xinb[:, 0:1024],
                              in_=dr[wname][cc * 128:(cc + 1) * 128, :])
                  .then_inc(S["wq2"], 16))
                W("sync", "wq2", vg)
                E("sync", lambda e, dstT=dstT, cc=cc:
                  e.dma_start(out=dstT[:, :, cc * 128:cc * 128 + 128],
                              in_=xinb[:, 0:1024], transpose=True)
                  .then_inc(S["wvt"], 16))
                vtr = inc("wvt", 16)
                W("gpsimd", "wvt", vtr)
                emit_repack()
        v_wvwo = val("wvt")

        rot_ap = {}

        def rot_weight(wname, rhs_ap, dstT, bias_col, prev_te, prev_ve):
            for cc in range(8):
                if cc >= 1:
                    W("gpsimd", "te", rot_ap[("te", wname, cc - 1)])
                elif prev_te:
                    W("gpsimd", "te", prev_te)
                vw = inc("wq2", 16)
                E("gpsimd", lambda e, wname=wname, cc=cc:
                  e.dma_start(out=wst[:, :],
                              in_=dr[wname][cc * 128:(cc + 1) * 128, :])
                  .then_inc(S["wq2"], 16))
                emit_repack()
                W("tensor", "ve", v_mrot)
                W("tensor", "wq2", vw)
                if cc >= 2:
                    W("tensor", "ap", rot_ap[("ap", wname, cc - 2)])
                reg = (cc % 2) * 1024
                for fc in range(8):
                    last = fc == 7
                    E("tensor", lambda e, cc=cc, fc=fc, reg=reg, last=last,
                      rhs_ap=rhs_ap:
                      (e.matmul(ps[:, reg + fc * 128:reg + fc * 128 + 128],
                                wst[:, fc * 128:(fc + 1) * 128],
                                rhs_ap, start=True, stop=True)
                       .then_inc(S["te"], 1)) if last else
                      e.matmul(ps[:, reg + fc * 128:reg + fc * 128 + 128],
                               wst[:, fc * 128:(fc + 1) * 128],
                               rhs_ap, start=True, stop=True))
                rot_ap[("te", wname, cc)] = inc("te", 1)
                W("scalar", "te", rot_ap[("te", wname, cc)])
                E("scalar", lambda e, cc=cc, reg=reg, dstT=dstT:
                  e.activation(dstT[:, :, cc * 128:cc * 128 + 128],
                               ps[:, reg:reg + 1024]
                               .rearrange("p (c n) -> p c n", c=8),
                               AF.Identity).then_inc(S["ap"], 1))
                rot_ap[("ap", wname, cc)] = inc("ap", 1)
            W("tensor", "wg", v_bias)
            if prev_ve:
                W("tensor", "ve", prev_ve)
            for cc in range(8):
                last = cc == 7
                E("tensor", lambda e, cc=cc, rhs_ap=rhs_ap, bias_col=bias_col,
                  last=last:
                  (e.matmul(ps[:, 2048 + cc:2048 + cc + 1], rhs_ap,
                            bias_col[:, cc:cc + 1], start=True, stop=True)
                   .then_inc(S["te"], 1)) if last else
                  e.matmul(ps[:, 2048 + cc:2048 + cc + 1], rhs_ap,
                           bias_col[:, cc:cc + 1], start=True, stop=True))
            vt = inc("te", 1)
            W("vector", "te", vt)
            E("vector", lambda e, bias_col=bias_col:
              e.tensor_copy(bias_col[:, 0:8], ps[:, 2048:2056])
              .then_inc(S["ve"], 1))
            return inc("ve", 1)

        v_bqrot = rot_weight("wq", mrtq[:, :], wqT, bqr, 0, 0)
        v_bkrot = rot_weight("wk", mrt[:, :], wkT, bkr,
                             rot_ap[("te", "wq", 7)], v_bqrot)
        v_wrot_ap = val("ap")

        while _rp[0] < 32:
            emit_repack()
        v_repack = val("rq")

        E("vector", lambda e: e.memset(ln1[:, :, :], 0.0).then_inc(S["ve"], 1))
        v_lnz = inc("ve", 1)
        pp_ve = {}
        pp_ap = {}
        v_ln1_c = {}
        ppb = [xpb[0], xpb[1], x0b[0], x0b[1]]
        ppsem = ["x0", "x1", "st0", "st1"]
        for i in range(NT):
            if i >= 4:
                W("sync", "ve", pp_ve[i - 4])
                W("sync", "ap", pp_ap[i - 4])
            xs = ppsem[i % 4]
            vx = inc(xs, 16)
            E("sync", lambda e, i=i, xs=xs:
              e.dma_start(out=ppb[i % 4][:, :],
                          in_=dr["xp"][i * 128:(i + 1) * 128, :])
              .then_inc(S[xs], 16))
            W("vector", xs, vx)
            if i == 0:
                W("vector", "ve", v_lnz)
            E("vector", lambda e, i=i:
              e.reduce_sum(ln1[:, 0, i:i + 1], ppb[i % 4][:, :], axis=AX.X)
              .then_inc(S["ve"], 1))
            pp_ve[i] = inc("ve", 1)
            W("scalar", xs, vx)
            if i == 0:
                W("scalar", "ve", v_lnz)
            E("scalar", lambda e, i=i:
              e.activation(jkb[:, :], ppb[i % 4][:, :], AF.Square,
                           accum_out=ln1[:, 1, i:i + 1]).then_inc(S["ap"], 1))
            pp_ap[i] = inc("ap", 1)
            if i % 8 == 7:
                c = i // 8
                lo, hi = c * 8, c * 8 + 8
                W("vector", "ap", pp_ap[i])
                E("vector", lambda e, lo=lo, hi=hi:
                  e.tensor_scalar(ln1[:, 2, lo:hi], ln1[:, 0, lo:hi],
                                  -1.0 / D, None, ALU.mult))
                E("vector", lambda e, lo=lo, hi=hi:
                  e.tensor_scalar(ln1[:, 3, lo:hi], ln1[:, 1, lo:hi],
                                  1.0 / D, None, ALU.mult)
                  .then_inc(S["ve"], 1))
                W("vector", "ve", inc("ve", 1))
                E("vector", lambda e, lo=lo, hi=hi:
                  e.tensor_mul(ln1[:, 4, lo:hi], ln1[:, 2, lo:hi],
                               ln1[:, 2, lo:hi]).then_inc(S["ve"], 1))
                W("vector", "ve", inc("ve", 1))
                E("vector", lambda e, lo=lo, hi=hi:
                  e.scalar_tensor_tensor(ln1[:, 3, lo:hi], ln1[:, 3, lo:hi],
                                         EPS, ln1[:, 4, lo:hi], ALU.add,
                                         ALU.subtract).then_inc(S["ve"], 1))
                vv = inc("ve", 1)
                W("scalar", "ve", vv)
                E("scalar", lambda e, lo=lo, hi=hi:
                  e.activation(ln1[:, 4, lo:hi], ln1[:, 3, lo:hi], AF.Sqrt)
                  .then_inc(S["ap"], 1))
                va = inc("ap", 1)
                W("vector", "ap", va)
                E("vector", lambda e, lo=lo, hi=hi:
                  e.reciprocal(ln1[:, 3, lo:hi], ln1[:, 4, lo:hi])
                  .then_inc(S["ve"], 1))
                W("vector", "ve", inc("ve", 1))
                E("vector", lambda e, lo=lo, hi=hi:
                  e.tensor_mul(ln1[:, 2, lo:hi], ln1[:, 2, lo:hi],
                               ln1[:, 3, lo:hi]).then_inc(S["ve"], 1))
                v_ln1_c[c] = inc("ve", 1)
        v_ln1 = v_ln1_c[3]

        ap_apply = {}
        tr_xin = {}

        def ln1_apply_and_transpose(s, xinT_guard=None):
            for k in range(2):
                i = 2 * s + k
                if i >= 2:
                    W("sync", "ve", ap_apply[i - 2])
                else:
                    W("sync", "ve", pp_ve[NT - 2 + i])
                    W("sync", "ap", pp_ap[NT - 2 + i])
                xs = "st%d" % (i % 2)
                vx = inc(xs, 16)
                E("sync", lambda e, i=i, xs=xs:
                  e.dma_start(out=x0b[i % 2][:, :],
                              in_=dr["xp"][i * 128:(i + 1) * 128, :])
                  .then_inc(S[xs], 16))
                W("vector", xs, vx)
                W("vector", "ve", v_ln1_c[i // 8])
                if s >= 2:
                    W("vector", "t%d" % (s % 2), tr_xin[(s - 2, k)])
                else:
                    W("vector", "wvt", v_wvwo)
                E("vector", lambda e, i=i:
                  e.tensor_scalar(xinb[:, (i % 4) * 1024:(i % 4) * 1024 + 1024],
                                  x0b[i % 2][:, :], ln1[:, 3, i:i + 1],
                                  ln1[:, 2, i:i + 1], ALU.mult, ALU.add)
                  .then_inc(S["ve"], 1))
                ap_apply[i] = inc("ve", 1)
                W("sync", "ve", ap_apply[i])
                if xinT_guard is not None:
                    W("sync", "ve", xinT_guard)
                ts_ = "t%d" % (s % 2)
                E("sync", lambda e, s=s, k=k, i=i, ts_=ts_:
                  e.dma_start(out=xinT[s % 2][:, :, k * 128:k * 128 + 128],
                              in_=xinb[:, (i % 4) * 1024:(i % 4) * 1024 + 1024],
                              transpose=True).then_inc(S[ts_], 16))
                tr_xin[(s, k)] = inc(ts_, 16)

        def proj_slab(s):
            sb = s % 2
            W("tensor", "t%d" % sb, tr_xin[(s, 1)])
            W("tensor", "ap", v_wrot_ap)
            W("tensor", "wvt", v_wvwo)
            W("tensor", "ve", v_bkrot)
            ev = {}
            for cc in range(8):
                reg = 2048 + (cc % 2) * 1024
                if cc >= 2:
                    W("tensor", "ve", ev[cc - 2])
                for fc in range(8):
                    last = fc == 7
                    E("tensor", lambda e, cc=cc, fc=fc, reg=reg, sb=sb,
                      last=last:
                      (e.matmul(ps[:, reg:reg + 256],
                                wkT[:, fc, cc * 128:cc * 128 + 128],
                                xinT[sb][:, fc, 0:256],
                                start=(fc == 0), stop=True)
                       .then_inc(S["te"], 1)) if last else
                      e.matmul(ps[:, reg:reg + 256],
                               wkT[:, fc, cc * 128:cc * 128 + 128],
                               xinT[sb][:, fc, 0:256],
                               start=(fc == 0), stop=False))
                vt = inc("te", 1)
                W("vector", "te", vt)
                E("vector", lambda e, cc=cc, reg=reg, sb=sb:
                  e.tensor_scalar(Ks[sb][:, cc, :], ps[:, reg:reg + 256],
                                  bkr[:, cc:cc + 1], None, ALU.add)
                  .then_inc(S["ve"], 1))
                ev[cc] = inc("ve", 1)
            for rt in range(2):
                for n2 in range(2):
                    idx = rt * 2 + n2
                    reg = 2048 + (idx % 2) * 1024
                    W("tensor", "ve", ev[6 + idx] if idx < 2 else ev[8 + idx - 2])
                    E("tensor", lambda e, n2=n2, reg=reg:
                      e.matmul(ps[:, reg:reg + 512], onesb[0:1, 0:128],
                               brow[0:1, n2 * 512:n2 * 512 + 512],
                               start=True, stop=False))
                    for fc in range(8):
                        last = fc == 7
                        E("tensor", lambda e, rt=rt, n2=n2, fc=fc, reg=reg,
                          sb=sb, last=last:
                          (e.matmul(ps[:, reg:reg + 512],
                                    xinT[sb][:, fc, rt * 128:rt * 128 + 128],
                                    wvT[:, fc, n2 * 512:n2 * 512 + 512],
                                    start=False, stop=True)
                           .then_inc(S["te"], 1)) if last else
                          e.matmul(ps[:, reg:reg + 512],
                                   xinT[sb][:, fc, rt * 128:rt * 128 + 128],
                                   wvT[:, fc, n2 * 512:n2 * 512 + 512],
                                   start=False, stop=False))
                    vt = inc("te", 1)
                    W("vector", "te", vt)
                    E("vector", lambda e, rt=rt, n2=n2, reg=reg, sb=sb:
                      e.tensor_copy(
                          Vp[sb][:, rt, :].rearrange("p (a b) -> p a b", b=65)
                          [:, n2 * 8:(n2 + 1) * 8, 0:64],
                          ps[:, reg:reg + 512]
                          .rearrange("p (a b) -> p a b", b=64))
                      .then_inc(S["ve"], 1))
                    ev[8 + idx] = inc("ve", 1)
            v_v = val("ve")
            qap = {}
            for cc in range(8):
                reg = 2048 + (cc % 2) * 1024
                if cc < 2:
                    W("tensor", "ve", ev[10 + cc])
                else:
                    W("tensor", "ve", qap[cc - 2])
                for fc in range(8):
                    last = fc == 7
                    E("tensor", lambda e, cc=cc, fc=fc, reg=reg, sb=sb,
                      last=last:
                      (e.matmul(ps[:, reg:reg + 128],
                                wqT[:, fc, cc * 128:cc * 128 + 128],
                                xinT[sb][:, fc, 0:128],
                                start=(fc == 0), stop=True)
                       .then_inc(S["te"], 1)) if last else
                      e.matmul(ps[:, reg:reg + 128],
                               wqT[:, fc, cc * 128:cc * 128 + 128],
                               xinT[sb][:, fc, 0:128],
                               start=(fc == 0), stop=False))
                vt = inc("te", 1)
                W("vector", "te", vt)
                E("vector", lambda e, cc=cc, reg=reg, sb=sb:
                  e.tensor_scalar(QC[sb][0:64, cc * 128:cc * 128 + 128],
                                  ps[0:64, reg:reg + 128],
                                  bqr[0:64, cc:cc + 1], None, ALU.add))
                E("vector", lambda e, cc=cc, reg=reg, sb=sb:
                  e.tensor_scalar(QC[sb][64:128, cc * 128:cc * 128 + 128],
                                  ps[64:128, reg:reg + 128],
                                  bqr[64:128, cc:cc + 1], None, ALU.add)
                  .then_inc(S["ve"], 2))
                qap[cc] = inc("ve", 2)
            vq = val("ve")
            W("sync", "ve", vq)
            ts_ = "t%d" % sb
            E("sync", lambda e, sb=sb, ts_=ts_:
              e.dma_start(out=QW[sb][0:64, :], in_=QC[sb][64:128, :])
              .then_inc(S[ts_], 16))
            E("sync", lambda e, sb=sb, ts_=ts_:
              e.dma_start(out=QW[sb][64:128, :], in_=QC[sb][0:64, :])
              .then_inc(S[ts_], 16))
            vqw = inc(ts_, 32)
            return dict(k=ev[7], v=v_v, q=vq, qw=vqw, sb=sb)

        def attention_slab(s, pr, prev_mult, prev_pv, guard):
            sb = s % 2
            sc_d, ex_d, pv_d = {}, {}, {}

            def tile_of(n):
                P, g, t = n // 32, (n % 32) // 2, n % 2
                m, kh = g // 2, g % 2
                t_lo = 2 * m + t
                return P, t_lo, kh

            def scores(n):
                P, t_lo, kh = tile_of(n)
                pb = (t_lo % 2) * 64
                src_q = QC[sb] if (P == t_lo % 2) else QW[sb]
                bb = 4 + (n % 2) * 2
                if n == 0:
                    W("tensor", "ve", pr["k"])
                    W("tensor", "ve", pr["q"])
                    W("tensor", "t%d" % pr["sb"], pr["qw"])
                    if guard is not pr:
                        W("tensor", "ve", guard["q"])
                        W("tensor", "ve", guard["v"])
                if n >= 2:
                    W("tensor", "ex", ex_d[n - 2])
                for half in range(2):
                    last = half == 1
                    E("tensor", lambda e, P=P, t_lo=t_lo, kh=kh, pb=pb,
                      src_q=src_q, bb=bb, half=half, sb=sb, last=last:
                      (e.matmul(ps[:, (bb + half) * 512:(bb + half) * 512 + 512],
                                Ks[sb][pb:pb + 64, t_lo // 2,
                                       kh * 128:kh * 128 + 128],
                                src_q[pb:pb + 64,
                                      half * 512:half * 512 + 512],
                                start=True, stop=True)
                       .then_inc(S["sc"], 1)) if last else
                      e.matmul(ps[:, (bb + half) * 512:(bb + half) * 512 + 512],
                               Ks[sb][pb:pb + 64, t_lo // 2,
                                      kh * 128:kh * 128 + 128],
                               src_q[pb:pb + 64,
                                     half * 512:half * 512 + 512],
                               start=True, stop=True))
                sc_d[n] = inc("sc", 1)

            def expc(n):
                bb = 4 + (n % 2) * 2
                po = (n % 2) * 1024
                if s == 0 and n == 0:
                    W("scalar", "rq", v_repack)
                W("scalar", "sc", sc_d[n])
                if n >= 6:
                    W("scalar", "pv", pv_d[n - 6])
                elif prev_pv:
                    W("scalar", "pv", prev_pv)
                E("scalar", lambda e, n=n, bb=bb, po=po:
                  e.activation(PT[(n // 2) % 3][:, po:po + 1024],
                               ps[:, bb * 512:bb * 512 + 1024],
                               AF.Exp).then_inc(S["ex"], 1))
                ex_d[n] = inc("ex", 1)

            def pv(n):
                P, t_lo, kh = tile_of(n)
                po = (n % 2) * 1024
                W("tensor", "ex", ex_d[n])
                if n == 0:
                    W("tensor", "ve", prev_mult)
                    W("tensor", "ve", pr["v"])
                first = (n % 32) == 0
                last_g = (n % 32) == 31
                for half in range(2):
                    last = half == 1
                    E("tensor", lambda e, P=P, t_lo=t_lo, kh=kh, po=po, n=n,
                      half=half, sb=sb, first=first, last_g=last_g, last=last:
                      (e.matmul(ps[0:65, P * 1024 + half * 512:
                                   P * 1024 + half * 512 + 512],
                                Vp[sb][:, kh, t_lo * 65:t_lo * 65 + 65],
                                PT[(n // 2) % 3][:, po + half * 512:
                                                 po + half * 512 + 512],
                                start=first, stop=last_g)
                       .then_inc(S["pv"], 1)) if last else
                      e.matmul(ps[0:65, P * 1024 + half * 512:
                                  P * 1024 + half * 512 + 512],
                               Vp[sb][:, kh, t_lo * 65:t_lo * 65 + 65],
                               PT[(n // 2) % 3][:, po + half * 512:
                                                po + half * 512 + 512],
                               start=first, stop=last_g))
                pv_d[n] = inc("pv", 1)

            scores(0)
            scores(1)
            expc(0)
            expc(1)
            for n in range(2, 64, 2):
                scores(n)
                scores(n + 1)
                expc(n)
                expc(n + 1)
                pv(n - 2)
                pv(n - 1)
            pv(62)
            pv(63)
            return pv_d, ex_d

        def head_finish(s, pv_d):
            W("vector", "pv", pv_d[63])
            if s % 2 == 0:
                E("vector", lambda e, s=s:
                  e.tensor_copy(ocat[0:64, s // 2, :], ps[0:64, 0:2048])
                  .then_inc(S["ve"], 1))
            else:
                E("vector", lambda e:
                  e.tensor_copy(otmp[0:64, :], ps[0:64, 0:2048])
                  .then_inc(S["ve"], 1))
            inc("ve", 1)
            E("vector", lambda e: e.tensor_copy(oden[64:65, :],
                                                ps[64:65, 0:2048])
              .then_inc(S["ve"], 1))
            ve_evac = inc("ve", 1)
            if s % 2 == 1:
                W("sync", "ve", ve_evac)
                E("sync", lambda e, s=s:
                  e.dma_start(out=ocat[64:128, s // 2, :], in_=otmp[0:64, :])
                  .then_inc(S["tr"], 16))
                inc("tr", 16)
            W("sync", "ve", ve_evac)
            E("sync", lambda e, s=s:
              e.dma_start(out=dspr[:, s * 16:(s + 1) * 16],
                          in_=oden[64:65, :]).then_inc(S["tr"], 16))
            v_spread = inc("tr", 16)
            return dict(evac=ve_evac, spread=v_spread)

        def head_dbcast_mult(s, hf):
            W("tensor", "tr", hf["back"])
            W("tensor", "ve", hf["evac"])
            for c in range(4):
                last = c == 3
                E("tensor", lambda e, c=c, last=last:
                  (e.matmul(ps[0:64, c * 512:(c + 1) * 512], onesf[64:65, 0:64],
                            oden[64:65, c * 512:(c + 1) * 512],
                            start=True, stop=True).then_inc(S["te"], 1))
                  if last else
                  e.matmul(ps[0:64, c * 512:(c + 1) * 512], onesf[64:65, 0:64],
                           oden[64:65, c * 512:(c + 1) * 512],
                           start=True, stop=True))
            vt = inc("te", 1)
            W("vector", "te", vt)
            if s % 2 == 0:
                E("vector", lambda e, s=s:
                  e.tensor_mul(ocat[0:64, s // 2, :], oden[0:64, :],
                               ps[0:64, 0:2048]).then_inc(S["ve"], 1))
                vm = inc("ve", 1)
                return vm, None
            E("vector", lambda e:
              e.tensor_mul(otmp[0:64, :], oden[0:64, :], ps[0:64, 0:2048])
              .then_inc(S["ve"], 1))
            vm = inc("ve", 1)
            W("sync", "ve", vm)
            E("sync", lambda e, s=s:
              e.dma_start(out=ocat[64:128, s // 2, :], in_=otmp[0:64, :])
              .then_inc(S["tr"], 16))
            return vm, inc("tr", 16)

        prs = {}
        ln1_apply_and_transpose(0)
        prs[0] = proj_slab(0)
        ln1_apply_and_transpose(1)
        prs[1] = proj_slab(1)
        ln1_apply_and_transpose(2, xinT_guard=prs[0]["q"])
        ln1_apply_and_transpose(3, xinT_guard=prs[1]["q"])
        prev_mult = 0
        prev_pv = None
        for s in range(NSLAB):
            guard = prs[min(s + 1, NSLAB - 1)]
            pv_d, ex_d = attention_slab(s, prs[s], prev_mult, prev_pv, guard)
            hf = head_finish(s, pv_d)
            if s + 2 < NSLAB:
                prs[s + 2] = proj_slab(s + 2)
            if s + 4 < NSLAB:
                ln1_apply_and_transpose(s + 4, xinT_guard=prs[s + 2]["q"])
            prev_mult = hf["evac"]
            prev_pv = pv_d[63]
            last_spread = hf["spread"]
        W("vector", "tr", last_spread)
        E("vector", lambda e: e.reciprocal(dspr[:, 0:256], dspr[:, 0:256])
          .then_inc(S["ve"], 1))
        v_rec = inc("ve", 1)
        mult_done = {}
        for g in range(8):
            for hh in range(2):
                h = 2 * g + hh
                if g == 0 and hh == 0:
                    W("sync", "ve", v_rec)
                else:
                    W("sync", "te", val("te"))
                E("sync", lambda e, h=h:
                  e.dma_start(out=oden[64:65, :],
                              in_=dspr[:, h * 16:(h + 1) * 16])
                  .then_inc(S["tr"], 16))
                vback = inc("tr", 16)
                W("tensor", "tr", vback)
                if hh == 0:
                    if g == 0:
                        W("tensor", "ve", val("ve"))
                    else:
                        W("tensor", "ve", mult_done[g - 1])
                pb = hh * 64
                for c in range(4):
                    last = c == 3
                    E("tensor", lambda e, pb=pb, c=c, last=last:
                      (e.matmul(ps[pb:pb + 64, c * 512:(c + 1) * 512],
                                onesf[64:65, 0:64],
                                oden[64:65, c * 512:(c + 1) * 512],
                                start=True, stop=True).then_inc(S["te"], 1))
                      if last else
                      e.matmul(ps[pb:pb + 64, c * 512:(c + 1) * 512],
                               onesf[64:65, 0:64],
                               oden[64:65, c * 512:(c + 1) * 512],
                               start=True, stop=True))
                inc("te", 1)
            vt = val("te")
            W("vector", "te", vt)
            E("vector", lambda e, g=g:
              e.tensor_mul(ocat[:, g, :], ocat[:, g, :], ps[:, 0:2048])
              .then_inc(S["ve"], 1))
            mult_done[g] = inc("ve", 1)
        v_att_ve = val("ve")
        v_att_tr = val("tr")
        v_att_pv = val("pv")
        if stop_after == "attn":
            W("gpsimd", "ve", v_att_ve)
            W("gpsimd", "tr", v_att_tr)
            E("gpsimd", lambda e:
              e.dma_start(out=out_d[:, :],
                          in_=ocat[:, :, :]).then_inc(S["out"], 16))
            E("gpsimd", lambda e: e.wait_ge(S["out"], 16))
            _finish(block, K)
            return nc
        if stop_after == "proj0":
            W("gpsimd", "ve", prs[1]["v"])
            W("gpsimd", "ap", prs[1]["q"])
            E("gpsimd", lambda e:
              e.dma_start(out=out_d[0:128, 0:1024],
                          in_=QC[0][:, :]).then_inc(S["out"], 16))
            E("gpsimd", lambda e:
              e.dma_start(out=out_d[128:256, 0:1024],
                          in_=QC[1][:, :]).then_inc(S["out"], 16))
            E("gpsimd", lambda e:
              e.dma_start(out=out_d[256:384, 0:512],
                          in_=Ks[0][:, 0:2, :]).then_inc(S["out"], 16))
            E("gpsimd", lambda e:
              e.dma_start(out=out_d[384:512, 0:1024],
                          in_=Vp[0][:, 0, 0:1024]).then_inc(S["out"], 16))
            E("gpsimd", lambda e:
              e.dma_start(out=out_d[512:640, 0:512],
                          in_=xinT[0][:, 0:2, 0:256])
              .then_inc(S["out"], 16))
            E("gpsimd", lambda e: e.wait_ge(S["out"], 80))
            _finish(block, K)
            return nc

        psA_free = {}
        x1_store = {}
        xin2_done = {}
        x0_used = {}
        for tt in range(16):
            reg = (tt % 4) * 1024
            if tt == 0:
                W("tensor", "ve", v_att_ve)
                W("tensor", "tr", v_att_tr)
            if tt >= 4:
                W("tensor", "ve", psA_free[tt - 4][0])
                W("tensor", "ap", psA_free[tt - 4][1])
            E("tensor", lambda e, reg=reg:
              e.matmul(ps[:, reg:reg + 512], onesb[0:1, 0:128],
                       brow[0:1, 1024:1536], start=True, stop=False))
            E("tensor", lambda e, reg=reg:
              e.matmul(ps[:, reg + 512:reg + 1024], onesb[0:1, 0:128],
                       brow[0:1, 1536:2048], start=True, stop=False))
            for g in range(8):
                for n2 in range(2):
                    last = g == 7 and n2 == 1
                    E("tensor", lambda e, tt=tt, g=g, n2=n2, reg=reg,
                      last=last:
                      (e.matmul(ps[:, reg + n2 * 512:reg + n2 * 512 + 512],
                                ocat[:, g, tt * 128:tt * 128 + 128],
                                woT[:, g, n2 * 512:n2 * 512 + 512],
                                start=False, stop=(g == 7))
                       .then_inc(S["te"], 1)) if last else
                      e.matmul(ps[:, reg + n2 * 512:reg + n2 * 512 + 512],
                               ocat[:, g, tt * 128:tt * 128 + 128],
                               woT[:, g, n2 * 512:n2 * 512 + 512],
                               start=False, stop=(g == 7)))
            vt = inc("te", 1)
            if stop_after == "p4mm":
                psA_free[tt] = (0, 0)
                x0_used[tt] = 0
                x1_store[tt] = 0
                xin2_done[tt] = 0
                continue
            if tt >= 2:
                W("sync", "ve", x0_used[tt - 2])
            xs = "x%d" % (tt % 2)
            vx0 = inc(xs, 16)
            E("sync", lambda e, tt=tt, xs=xs:
              e.dma_start(out=x0b[tt % 2][:, :],
                          in_=dr["x0"][tt * 128:(tt + 1) * 128, :])
              .then_inc(S[xs], 16))
            W("vector", "te", vt)
            E("vector", lambda e, reg=reg:
              e.reduce_sum(stat[:, 0:1], ps[:, reg:reg + 1024], axis=AX.X))
            E("vector", lambda e: e.memset(stat[:, 1:2], 0.0)
              .then_inc(S["ve"], 1))
            vv0 = inc("ve", 1)
            W("scalar", "te", vt)
            W("scalar", "ve", vv0)
            E("scalar", lambda e, reg=reg:
              e.activation(jkb[:, :], ps[:, reg:reg + 1024], AF.Square,
                           accum_out=stat[:, 1:2]).then_inc(S["ap"], 1))
            va0 = inc("ap", 1)
            W("vector", "ap", va0)
            E("vector", lambda e: e.tensor_scalar(stat[:, 2:3], stat[:, 0:1],
                                                  -1.0 / D, None, ALU.mult))
            E("vector", lambda e: e.tensor_scalar(stat[:, 3:4], stat[:, 1:2],
                                                  1.0 / D, None, ALU.mult)
              .then_inc(S["ve"], 1))
            W("vector", "ve", inc("ve", 1))
            E("vector", lambda e: e.tensor_mul(stat[:, 4:5], stat[:, 2:3],
                                               stat[:, 2:3])
              .then_inc(S["ve"], 1))
            W("vector", "ve", inc("ve", 1))
            E("vector", lambda e: e.scalar_tensor_tensor(
                stat[:, 3:4], stat[:, 3:4], EPS, stat[:, 4:5], ALU.add,
                ALU.subtract).then_inc(S["ve"], 1))
            vv = inc("ve", 1)
            W("scalar", "ve", vv)
            E("scalar", lambda e: e.activation(stat[:, 4:5], stat[:, 3:4],
                                               AF.Sqrt).then_inc(S["ap"], 1))
            va = inc("ap", 1)
            W("vector", "ap", va)
            E("vector", lambda e: e.reciprocal(stat[:, 5:6], stat[:, 4:5])
              .then_inc(S["ve"], 1))
            W("vector", "ve", inc("ve", 1))
            E("vector", lambda e: e.tensor_mul(stat[:, 6:7], stat[:, 2:3],
                                               stat[:, 5:6])
              .then_inc(S["ve"], 1))
            vv = inc("ve", 1)
            W("scalar", "ve", vv)
            E("scalar", lambda e, reg=reg:
              e.activation(tmpL[:, :], ps[:, reg:reg + 1024], AF.Identity,
                           bias=stat[:, 6:7], scale=stat[:, 5:6])
              .then_inc(S["ap"], 1))
            va = inc("ap", 1)
            psA_free[tt] = (vv, va)
            W("vector", "ap", va)
            W("vector", xs, vx0)
            if tt >= 2:
                W("vector", "st%d" % (tt % 2), x1_store[tt - 2])
            E("vector", lambda e, tt=tt:
              e.tensor_add(x1b[tt % 2][:, :], tmpL[:, :], x0b[tt % 2][:, :])
              .then_inc(S["ve"], 1))
            vx1 = inc("ve", 1)
            x0_used[tt] = vx1
            W("sync", "ve", vx1)
            sts = "st%d" % (tt % 2)
            E("sync", lambda e, tt=tt, sts=sts:
              e.dma_start(out=x1_dram[tt * 128:(tt + 1) * 128, :],
                          in_=x1b[tt % 2][:, :]).then_inc(S[sts], 16))
            x1_store[tt] = inc(sts, 16)
            E("vector", lambda e, tt=tt:
              e.reduce_sum(stat[:, 8:9], x1b[tt % 2][:, :], axis=AX.X))
            E("vector", lambda e: e.memset(stat[:, 9:10], 0.0)
              .then_inc(S["ve"], 1))
            vv8 = inc("ve", 1)
            W("scalar", "ve", vv8)
            E("scalar", lambda e, tt=tt:
              e.activation(jkb[:, :], x1b[tt % 2][:, :], AF.Square,
                           accum_out=stat[:, 9:10]).then_inc(S["ap"], 1))
            va8 = inc("ap", 1)
            W("vector", "ap", va8)
            E("vector", lambda e:
              e.tensor_scalar(stat[:, 10:11], stat[:, 8:9], -1.0 / D, None,
                              ALU.mult))
            E("vector", lambda e:
              e.tensor_scalar(stat[:, 11:12], stat[:, 9:10], 1.0 / D, None,
                              ALU.mult).then_inc(S["ve"], 1))
            W("vector", "ve", inc("ve", 1))
            E("vector", lambda e: e.tensor_mul(stat[:, 12:13], stat[:, 10:11],
                                               stat[:, 10:11])
              .then_inc(S["ve"], 1))
            W("vector", "ve", inc("ve", 1))
            E("vector", lambda e: e.scalar_tensor_tensor(
                stat[:, 11:12], stat[:, 11:12], EPS, stat[:, 12:13], ALU.add,
                ALU.subtract).then_inc(S["ve"], 1))
            vv = inc("ve", 1)
            W("scalar", "ve", vv)
            E("scalar", lambda e: e.activation(stat[:, 12:13], stat[:, 11:12],
                                               AF.Sqrt).then_inc(S["ap"], 1))
            va = inc("ap", 1)
            W("vector", "ap", va)
            E("vector", lambda e: e.reciprocal(stat[:, 13:14], stat[:, 12:13])
              .then_inc(S["ve"], 1))
            W("vector", "ve", inc("ve", 1))
            E("vector", lambda e: e.tensor_mul(stat[:, 14:15], stat[:, 10:11],
                                               stat[:, 13:14])
              .then_inc(S["ve"], 1))
            vv = inc("ve", 1)
            W("scalar", "ve", vv)
            E("scalar", lambda e, tt=tt:
              e.activation(xin2[:, tt, :], x1b[tt % 2][:, :], AF.Identity,
                           bias=stat[:, 14:15], scale=stat[:, 13:14])
              .then_inc(S["ap"], 1))
            xin2_done[tt] = inc("ap", 1)

        if stop_after == "p4mm":
            W("vector", "te", val("te"))
            vv_ = inc("ve", 1)
            E("vector", lambda e: e.tensor_copy(oden[0:64, 0:1024],
                                                ps[0:64, 0:1024])
              .then_inc(S["ve"], 1))
            W("sync", "ve", val("ve"))
            W("sync", "te", val("te"))
            E("sync", lambda e:
              e.dma_start(out=out_d[0:64, :], in_=oden[0:64, 0:1024])
              .then_inc(S["out"], 16))
            E("sync", lambda e: e.wait_ge(S["out"], 16))
            _finish(block, K)
            return nc

        if stop_after == "p4ln2":
            W("gpsimd", "st0", val("st0"))
            W("gpsimd", "st1", val("st1"))
            E("gpsimd", lambda e:
              e.dma_start(out=out_d[:, :], in_=x1_dram[:, :])
              .then_inc(S["out"], 16))
            E("gpsimd", lambda e: e.wait_ge(S["out"], 16))
            _finish(block, K)
            return nc

        if stop_after == "p4":
            W("gpsimd", "st0", val("st0"))
            W("gpsimd", "st1", val("st1"))
            E("gpsimd", lambda e:
              e.dma_start(out=out_d[:, :], in_=x1_dram[:, :])
              .then_inc(S["out"], 16))
            E("gpsimd", lambda e: e.wait_ge(S["out"], 16))
            _finish(block, K)
            return nc

        w1_te = {}
        w2_te = {}
        gelu_done = {}
        psF_ap = {}
        out_tot = 0
        x1r_used = {}
        out_used = 0
        v_x2td = {}

        def emit_tr(tb):
            for k in range(4):
                tt = tb * 4 + k
                W("sync", "ap", xin2_done[tt])
                if tb >= 2:
                    W("sync", "te", w1_te[(tb - 2, 31)])
                ts_ = "t%d" % (tb % 2)
                E("sync", lambda e, tb=tb, k=k, tt=tt, ts_=ts_:
                  e.dma_start(out=xin2T[tb % 2][:, :, k * 128:k * 128 + 128],
                              in_=xin2[:, tt, :], transpose=True)
                  .then_inc(S[ts_], 16))
                inc(ts_, 16)
            v_x2td[tb] = val("t%d" % (tb % 2))

        emit_tr(0)
        emit_tr(1)
        for tb in range(4):
            v_x2t = v_x2td[tb]
            if stop_after == "p5tr":
                break
            for jc in range(32):
                if (tb, jc - 2) in w1_te:
                    W("gpsimd", "te", w1_te[(tb, jc - 2)])
                elif tb > 0 and jc < 2:
                    W("gpsimd", "te", w1_te[(tb - 1, 30 + jc)])
                else:
                    W("gpsimd", "pv", v_att_pv)
                    if tb == 0 and jc == 0:
                        W("gpsimd", "rq", v_repack)
                ws = "w5%d" % (jc % 2)
                vw = inc(ws, 16)
                E("gpsimd", lambda e, jc=jc, ws=ws:
                  e.dma_start(out=w1b[jc % 2][:, :, :],
                              in_=w1q_d[jc]).then_inc(S[ws], 16))
                reg = (jc % 2) * 512
                if stop_after == "p5w1":
                    continue
                W("tensor", ws, vw)
                W("tensor", "t%d" % (tb % 2), v_x2t)
                if (tb, jc - 2) in gelu_done:
                    W("tensor", "ap", gelu_done[(tb, jc - 2)])
                elif tb > 0 and jc < 2:
                    W("tensor", "ap", psF_ap[(tb - 1, 0)])
                for fc in range(8):
                    last = fc == 7
                    E("tensor", lambda e, jc=jc, fc=fc, reg=reg, tb=tb,
                      last=last:
                      (e.matmul(ps[:, reg:reg + 512], w1b[jc % 2][:, fc, :],
                                xin2T[tb % 2][:, fc, :],
                                start=(fc == 0), stop=True)
                       .then_inc(S["te"], 1)) if last else
                      e.matmul(ps[:, reg:reg + 512], w1b[jc % 2][:, fc, :],
                               xin2T[tb % 2][:, fc, :],
                               start=(fc == 0), stop=False))
                w1_te[(tb, jc)] = inc("te", 1)
                W("scalar", "te", w1_te[(tb, jc)])
                _gdst = (jkb[:, 0:512] if stop_after == "p5nog"
                         else gT[:, jc, :])
                E("scalar", lambda e, jc=jc, reg=reg, _gdst=_gdst:
                  e.activation(_gdst, ps[:, reg:reg + 512], AF.Gelu,
                               bias=b1c[:, jc:jc + 1]).then_inc(S["ap"], 1))
                gelu_done[(tb, jc)] = inc("ap", 1)
            if stop_after in ("p5w1", "p5nog", "p5mm1"):
                break
            W("tensor", "ap", gelu_done[(tb, 31)])
            for k in range(4):
                E("tensor", lambda e, k=k:
                  e.matmul(ps[:, k * 1024:k * 1024 + 512], onesb[0:1, 0:128],
                           brow[0:1, 2048:2560], start=True, stop=False))
                E("tensor", lambda e, k=k:
                  e.matmul(ps[:, k * 1024 + 512:k * 1024 + 1024],
                           onesb[0:1, 0:128], brow[0:1, 2560:3072],
                           start=True, stop=False))
            for jc in range(32):
                if (tb, jc - 2) in w2_te:
                    W("gpsimd", "te", w2_te[(tb, jc - 2)])
                else:
                    W("gpsimd", "te", w1_te[(tb, 30 + jc)])
                ws = "w5%d" % (jc % 2)
                vw = inc(ws, 16)
                E("gpsimd", lambda e, jc=jc, ws=ws:
                  e.dma_start(out=w2b[jc % 2][:, :],
                              in_=w2q_d[jc * 128:(jc + 1) * 128, :])
                  .then_inc(S[ws], 16))
                W("tensor", ws, vw)
                for k in range(4):
                    for n2 in range(2):
                        last = k == 3 and n2 == 1
                        E("tensor", lambda e, jc=jc, k=k, n2=n2, last=last:
                          (e.matmul(ps[:, k * 1024 + n2 * 512:
                                       k * 1024 + n2 * 512 + 512],
                                    gT[:, jc, k * 128:k * 128 + 128],
                                    w2b[jc % 2][:, n2 * 512:n2 * 512 + 512],
                                    start=False, stop=(jc == 31))
                           .then_inc(S["te"], 1)) if last else
                          e.matmul(ps[:, k * 1024 + n2 * 512:
                                      k * 1024 + n2 * 512 + 512],
                                   gT[:, jc, k * 128:k * 128 + 128],
                                   w2b[jc % 2][:, n2 * 512:n2 * 512 + 512],
                                   start=False, stop=(jc == 31)))
                w2_te[(tb, jc)] = inc("te", 1)
            v_mm2 = val("te")
            if tb + 2 < 4:
                emit_tr(tb + 2)
            if stop_after == "p5mm2":
                break
            last_ap = 0
            for k in range(4):
                W("vector", "te", v_mm2)
                E("vector", lambda e, k=k:
                  e.reduce_sum(stat[:, 32 + k:33 + k],
                               ps[:, k * 1024:k * 1024 + 1024], axis=AX.X))
                E("vector", lambda e, k=k: e.memset(stat[:, 36 + k:37 + k],
                                                    0.0).then_inc(S["ve"], 1))
                vv0 = inc("ve", 1)
                W("scalar", "te", v_mm2)
                W("scalar", "ve", vv0)
                E("scalar", lambda e, k=k:
                  e.activation(jkb[:, :], ps[:, k * 1024:k * 1024 + 1024],
                               AF.Square, accum_out=stat[:, 36 + k:37 + k])
                  .then_inc(S["ap"], 1))
                inc("ap", 1)
            W("vector", "ap", val("ap"))
            E("vector", lambda e:
              e.tensor_scalar(stat[:, 40:44], stat[:, 32:36], -1.0 / D, None,
                              ALU.mult))
            E("vector", lambda e:
              e.tensor_scalar(stat[:, 44:48], stat[:, 36:40], 1.0 / D, None,
                              ALU.mult).then_inc(S["ve"], 1))
            W("vector", "ve", inc("ve", 1))
            E("vector", lambda e:
              e.tensor_mul(stat[:, 48:52], stat[:, 40:44], stat[:, 40:44])
              .then_inc(S["ve"], 1))
            W("vector", "ve", inc("ve", 1))
            E("vector", lambda e: e.scalar_tensor_tensor(
                stat[:, 44:48], stat[:, 44:48], EPS, stat[:, 48:52], ALU.add,
                ALU.subtract).then_inc(S["ve"], 1))
            vv = inc("ve", 1)
            W("scalar", "ve", vv)
            E("scalar", lambda e: e.activation(stat[:, 48:52], stat[:, 44:48],
                                               AF.Sqrt).then_inc(S["ap"], 1))
            va = inc("ap", 1)
            W("vector", "ap", va)
            E("vector", lambda e: e.reciprocal(stat[:, 52:56], stat[:, 48:52])
              .then_inc(S["ve"], 1))
            W("vector", "ve", inc("ve", 1))
            E("vector", lambda e:
              e.tensor_mul(stat[:, 40:44], stat[:, 40:44], stat[:, 52:56])
              .then_inc(S["ve"], 1))
            v_st4 = inc("ve", 1)
            for k in range(4):
                tt = tb * 4 + k
                if tt >= 2:
                    W("sync", "ve", x1r_used.get(tt - 2, 0))
                W("sync", "st%d" % (tt % 2), x1_store[tt])
                xs = "x%d" % (tt % 2)
                vx1 = inc(xs, 16)
                E("sync", lambda e, tt=tt, xs=xs:
                  e.dma_start(out=x1rb[tt % 2][:, :],
                              in_=x1_dram[tt * 128:(tt + 1) * 128, :])
                  .then_inc(S[xs], 16))
                W("scalar", "ve", v_st4 if k == 0 else x1r_used[tt - 1])
                E("scalar", lambda e, k=k:
                  e.activation(junk[:, :], ps[:, k * 1024:k * 1024 + 1024],
                               AF.Identity, bias=stat[:, 40 + k:41 + k],
                               scale=stat[:, 52 + k:53 + k])
                  .then_inc(S["ap"], 1))
                va = inc("ap", 1)
                last_ap = va
                psF_ap[(tb, k)] = va
                W("vector", "ap", va)
                W("vector", xs, vx1)
                if out_used:
                    W("vector", "out", out_used)
                E("vector", lambda e, tt=tt:
                  e.tensor_add(outb[:, :], junk[:, :], x1rb[tt % 2][:, :])
                  .then_inc(S["ve"], 1))
                vo = inc("ve", 1)
                x1r_used[tt] = vo
                W("sync", "ve", vo)
                E("sync", lambda e, tt=tt:
                  e.dma_start(out=out_d[tt * 128:(tt + 1) * 128, :],
                              in_=outb[:, :]).then_inc(S["out"], 16))
                out_tot = inc("out", 16)
                out_used = out_tot
        W("sync", "out", out_tot)
        _finish(block, K)
        return nc


def _finish(block, K):
    @block.sync
    def _(e):
        for f in K.ops["sync"]:
            f(e)

    @block.tensor
    def _(e):
        for f in K.ops["tensor"]:
            f(e)

    @block.vector
    def _(e):
        for f in K.ops["vector"]:
            f(e)

    @block.scalar
    def _(e):
        for f in K.ops["scalar"]:
            f(e)

    @block.gpsimd
    def _(e):
        for f in K.ops["gpsimd"]:
            f(e)


_NC_CACHE = {}


def _get_nc():
    if "nc" not in _NC_CACHE:
        _NC_CACHE["nc"] = build_program()
    return _NC_CACHE["nc"]


def kernel(L0_attn_x, L0_attn_wq, L0_attn_bq, L0_attn_wk, L0_attn_bk,
           L0_attn_wv, L0_attn_bv, L0_attn_wo, L0_attn_bo, L0_attn_cos,
           L0_attn_sin, L0_ffn_w1, L0_ffn_b1, L0_ffn_w2, L0_ffn_b2):
    x = np.ascontiguousarray(np.asarray(L0_attn_x, dtype=np.float32))
    B = x.shape[0]
    eyeC = np.eye(128, dtype=np.float32)
    eyeU = np.zeros((128, 128), np.float32)
    eyeD = np.zeros((128, 128), np.float32)
    for p in range(0, 128, 2):
        eyeU[p, p + 1] = 1.0
    for p in range(1, 128, 2):
        eyeD[p, p - 1] = 1.0
    shared = dict(
        wq=np.ascontiguousarray(L0_attn_wq, np.float32),
        wk=np.ascontiguousarray(L0_attn_wk, np.float32),
        wv=np.ascontiguousarray(L0_attn_wv, np.float32),
        wo=np.ascontiguousarray(L0_attn_wo, np.float32),
        bq=np.ascontiguousarray(L0_attn_bq, np.float32),
        bk=np.ascontiguousarray(L0_attn_bk, np.float32),
        bv=np.ascontiguousarray(L0_attn_bv, np.float32),
        bo=np.ascontiguousarray(L0_attn_bo, np.float32),
        cosr=np.ascontiguousarray(np.asarray(L0_attn_cos)[3199], np.float32),
        sinr=np.ascontiguousarray(np.asarray(L0_attn_sin)[3199], np.float32),
        w1p=np.ascontiguousarray(
            np.asarray(L0_ffn_w1, np.float32).reshape(8, 128, 32, 128)
            .transpose(2, 1, 0, 3)).reshape(32, 128, 1024),
        b1=np.ascontiguousarray(L0_ffn_b1, np.float32),
        w2=np.ascontiguousarray(L0_ffn_w2, np.float32),
        b2=np.ascontiguousarray(L0_ffn_b2, np.float32),
        eyeC=eyeC, eyeU=eyeU, eyeD=eyeD,
    )
    in_maps = []
    for c in range(8):
        b, half = c // 2, c % 2
        xb = x[b]
        xpc = xb.reshape(16, 2, 128, D)[:, [half, 1 - half]].reshape(T, D)
        x0c = xb[half * TOWN + _TMAP]
        m = dict(shared)
        m["xp"] = np.ascontiguousarray(xpc)
        m["x0"] = np.ascontiguousarray(x0c)
        in_maps.append(m)

    nc = _get_nc()
    res = run_bass_kernel_spmd(nc, in_maps, core_ids=list(range(8)))
    out = np.empty((B, T, D), np.float32)
    inv = np.empty_like(_TMAP)
    inv[_TMAP] = np.arange(2048)
    for c in range(8):
        b, half = c // 2, c % 2
        out[b, half * TOWN:(half + 1) * TOWN] = res.results[c]["out"][inv]
    return out
